# version 37
# baseline (speedup 1.0000x reference)
import numpy as np
from contextlib import ExitStack
import concourse.bass as bass
import concourse.mybir as mybir
from concourse.bass_utils import run_bass_kernel_spmd

F32 = mybir.dt.float32
BF16 = mybir.dt.bfloat16
I32 = mybir.dt.int32
AF = mybir.ActivationFunctionType
ALU = mybir.AluOpType
AX = mybir.AxisListType

T = 2048
D = 1024
NCORES = 8
NSEQ = 2
DCH = D // 128
NT = T // 128
KBINS = 513
KPAD = 640
KCH = KPAD // 128
LN_EPS = 1e-5

ENGS = ("pe", "dve", "act", "pool", "sp")



class Tok:
    __slots__ = ("sem", "val", "snap")

    def __init__(self, sem, val, snap):
        self.sem, self.val, self.snap = sem, val, snap


_NO_AUTO = {"dma_start", "indirect_dma_start", "wait_ge", "nop", "sem_inc",
            "print", "dma_start_transpose"}


class EngCtx:

    _FIELDS = ("ctx", "name", "sem_name", "count", "clock", "recs", "auto",
               "self_waited", "uninced")

    def __init__(self, ctx, name, auto):
        self.ctx, self.name = ctx, name
        self.sem_name = "s_" + name
        self.count = 0
        self.clock = {}
        self.recs = []
        self.auto = auto
        self.self_waited = 0
        self.uninced = False

    def _rec(self, fn, auto_inced=False):
        entry = [fn, [], auto_inced]
        self.recs.append(entry)
        return entry

    def __getattr__(self, attr):
        if attr.startswith("_") or attr in EngCtx._FIELDS:
            raise AttributeError(attr)

        def wrap(*a, **k):
            if self.auto and attr not in _NO_AUTO and self.count > self.self_waited:
                sem, val = self.sem_name, self.count
                self._rec(lambda eng: eng.wait_ge(self.ctx.sems[sem], val))
                self.self_waited = val
            auto_inc = self.auto and attr not in _NO_AUTO
            e = self._rec(lambda eng: getattr(eng, attr)(*a, **k), auto_inc)
            if auto_inc:
                self.count += 1
                e[1].append((self.sem_name, 1))
                self.clock[self.sem_name] = self.count
            elif attr not in _NO_AUTO:
                self.uninced = True
            return None
        return wrap

    def wait(self, tok):
        if tok is None:
            return
        if tok.sem == self.sem_name:
            if tok.val <= self.self_waited:
                return
            sem, val = tok.sem, tok.val
            self._rec(lambda eng: eng.wait_ge(self.ctx.sems[sem], val))
            self.self_waited = val
            return
        if self.clock.get(tok.sem, 0) >= tok.val:
            return
        sem, val = tok.sem, tok.val
        self._rec(lambda eng: eng.wait_ge(self.ctx.sems[sem], val))
        for s, v in tok.snap.items():
            if s == self.sem_name:
                if v > self.self_waited:
                    self.self_waited = v
                continue
            if self.clock.get(s, 0) < v:
                self.clock[s] = v
        if self.clock.get(tok.sem, 0) < tok.val:
            self.clock[tok.sem] = tok.val

    def wait_all(self, toks):
        for t in toks:
            self.wait(t)

    def sig(self):
        assert self.recs, "sig() with no recorded instruction"
        if self.recs[-1][2] or any(s == self.sem_name for s, _ in self.recs[-1][1]):
            return Tok(self.sem_name, self.count, dict(self.clock))
        self.count += 1
        self.recs[-1][1].append((self.sem_name, 1))
        self.clock[self.sem_name] = self.count
        self.uninced = False
        return Tok(self.sem_name, self.count, dict(self.clock))

    def sig_dma(self, stream):
        c = self.ctx.dma_counts[stream] = self.ctx.dma_counts.get(stream, 0) + 16
        self.recs[-1][1].append((stream, 16))
        snap = dict(self.clock)
        snap[stream] = c
        return Tok(stream, c, snap)


DEBUG = False


class Ctx:
    def __init__(self, nc, stack):
        self.dbg = []
        self.dbg_names = {}
        self.nc = nc
        self.stack = stack
        self.sems = {}
        self.dma_counts = {}
        self.eng = {n: EngCtx(self, n, auto=(n in ("dve", "act", "pool")))
                    for n in ENGS}
        for n in ENGS:
            self.sems["s_" + n] = stack.enter_context(nc.semaphore("s_" + n))
        for s in ("dW", "dX", "dO", "dS"):
            self.sems[s] = stack.enter_context(nc.semaphore(s))
        self.pe = self.eng["pe"]
        self.dve = self.eng["dve"]
        self.act = self.eng["act"]
        self.pool = self.eng["pool"]
        self.sp = self.eng["sp"]
        self._sbuf_guards = []
        self._psum_guards = []
        self._tensors = {}
        self._phase_guards = []
        self._phase_ctr = 0
        self._phase_stack = []

    def sbuf(self, name, shape, dtype, side="left"):
        if name in self._tensors:
            return self._tensors[name]
        g = self.nc.sbuf_tensor(name, shape, dtype)
        h = g.__enter__()
        self._sbuf_guards.append(g)
        self._tensors[name] = h
        return h

    def psum(self, name, shape, dtype=F32):
        if name in self._tensors:
            return self._tensors[name]
        g = self.nc.psum_tensor(name, shape, dtype)
        h = g.__enter__()
        self._psum_guards.append(g)
        self._tensors[name] = h
        return h

    def barrier(self):
        toks = []
        for n in ENGS:
            e = self.eng[n]
            if e.uninced:
                toks.append(e.sig())
            elif e.count > 0:
                toks.append(Tok(e.sem_name, e.count, dict(e.clock)))
        for s, c in self.dma_counts.items():
            toks.append(Tok(s, c, {s: c}))
        for n in ENGS:
            self.eng[n].wait_all(toks)

    def phase_psum(self, name, shape, dtype=F32):
        self._phase_ctr += 1
        g = self.nc.psum_tensor(f"{name}_{self._phase_ctr}", shape, dtype)
        h = g.__enter__()
        self._phase_guards.append(g)
        return h

    def phase_sbuf(self, name, shape, dtype):
        self._phase_ctr += 1
        g = self.nc.sbuf_tensor(f"{name}_{self._phase_ctr}", shape, dtype)
        h = g.__enter__()
        self._phase_guards.append(g)
        return h

    def push_phase(self):
        self._phase_stack.append(len(self._phase_guards))

    def pop_phase(self):
        k = self._phase_stack.pop() if self._phase_stack else 0
        for g in reversed(self._phase_guards[k:]):
            g.__exit__(None, None, None)
        del self._phase_guards[k:]

    def end_phase(self):
        assert not self._phase_stack, "unbalanced push_phase"
        for g in reversed(self._phase_guards):
            g.__exit__(None, None, None)
        self._phase_guards = []

    def dump(self, name, ap, deps=()):
        if not DEBUG or name in self.dbg_names:
            return
        self.dbg_names[name] = True
        shape = list(ap.shape)
        dt = ap.dtype
        dst = self.nc.dram_tensor("dbg_" + name, shape, dt, kind="ExternalOutput").ap()
        self.sp.wait_all(list(deps))
        self.sp.dma_start(out=dst, in_=ap)
        self.sp.sig_dma("dO")

    def replay(self, block):
        raws = {"pe": block.tensor, "dve": block.vector, "act": block.scalar,
                "pool": block.gpsimd, "sp": block.sync}

        def make_body(name):
            def body(eng):
                for fn, incs, _auto in self.eng[name].recs:
                    inst = fn(eng)
                    for (sem, v) in incs:
                        assert inst is not None, f"then_inc on None inst ({name})"
                        inst.then_inc(self.sems[sem], v)
            return body
        for n in ENGS:
            if self.eng[n].recs:
                raws[n](make_body(n))




def host_constants():
    c = {}
    e = np.arange(D)[:, None]
    k = np.arange(KBINS)[None, :]
    ang = 2.0 * np.pi * ((e * k) % D).astype(np.float64) / D
    c["dftC"] = np.cos(ang).astype(np.float32)
    c["dftS"] = (-np.sin(ang)).astype(np.float32)
    w = np.full(KBINS, 2.0)
    w[0] = 1.0
    w[512] = 1.0
    ki = np.arange(KBINS)[:, None]
    ee = np.arange(D)[None, :]
    ang2 = 2.0 * np.pi * ((ki * ee) % D).astype(np.float64) / D
    cw = (w[:, None] * np.cos(ang2) / D).astype(np.float32)
    sw = (-w[:, None] * np.sin(ang2) / D).astype(np.float32)
    c["idftC"] = np.zeros((KPAD, D), np.float32)
    c["idftC"][:KBINS] = cw
    c["idftS"] = np.zeros((KPAD, D), np.float32)
    c["idftS"][:KBINS] = sw
    c["ident"] = np.eye(128, dtype=np.float32)
    tri = (np.arange(128)[:, None] <= np.arange(128)[None, :]).astype(np.float32)
    c["trimask"] = tri
    c["utri"] = tri.copy()
    c["ones128"] = np.ones((128, 128), np.float32)
    c["onesrow"] = np.ones((1, 128), np.float32)
    c["shamt"] = np.tile(np.array([0, 2, 4, 6], np.int32), 32)[:, None].copy()
    c["mask3"] = np.full((128, 1), 3, np.int32)
    bandmask = np.zeros((128, 16), np.float32)
    for h in range(16):
        bandmask[8 * h:8 * h + 6, h] = 1.0
    c["bandmask"] = bandmask
    tri16 = np.tile(tri, (1, 16))
    c["trimask16"] = np.ascontiguousarray(tri16)
    half0 = np.zeros((1, 128), np.float32)
    half0[0, :64] = 1.0
    half1 = np.zeros((1, 128), np.float32)
    half1[0, 64:] = 1.0
    c["half0"] = half0
    c["half1"] = half1
    c["ones16"] = np.ones((128, 16), np.float32)
    return c


def stage_inputs(inputs):
    g = {}
    g["l0a_wq"] = np.ascontiguousarray(
        np.concatenate([inputs["L0_attn_W1_q"], inputs["L0_attn_W2_q"]], 0).T)
    g["l0a_wk"] = np.ascontiguousarray(
        np.concatenate([inputs["L0_attn_W1_k"], inputs["L0_attn_W2_k"]], 0).T)
    g["l0a_wv"] = np.ascontiguousarray(np.asarray(inputs["L0_attn_Wv"]).T)
    g["l0a_bv"] = np.asarray(inputs["L0_attn_bv"]).reshape(1, D)
    g["l0a_wo"] = np.ascontiguousarray(np.asarray(inputs["L0_attn_Wo"]).T)
    g["l0a_bo"] = np.asarray(inputs["L0_attn_bo"]).reshape(1, D)
    g["l0f_w1"] = np.ascontiguousarray(np.concatenate(
        [np.asarray(inputs["L0_ffn_W1"]), np.asarray(inputs["L0_ffn_Ws1"])[None]], 0))
    g["l0f_w2"] = np.ascontiguousarray(np.concatenate(
        [np.asarray(inputs["L0_ffn_W2"]), np.asarray(inputs["L0_ffn_Ws2"])[None]], 0))
    g["l0f_wr"] = np.ascontiguousarray(np.asarray(inputs["L0_ffn_Wr"]))
    g["l1a_wq"] = np.ascontiguousarray(np.asarray(inputs["L1_attn_Wq"]))
    g["l1a_wk"] = np.ascontiguousarray(np.asarray(inputs["L1_attn_Wk"]))
    g["l1a_wv"] = np.ascontiguousarray(np.asarray(inputs["L1_attn_Wv"]))
    g["l1a_bq"] = np.ascontiguousarray(np.asarray(inputs["L1_attn_bq"]).reshape(DCH, 128).T)
    g["l1a_bk"] = np.ascontiguousarray(np.asarray(inputs["L1_attn_bk"]).reshape(DCH, 128).T)
    g["l1a_bv"] = np.ascontiguousarray(np.asarray(inputs["L1_attn_bv"]).reshape(DCH, 128).T)
    pk1 = np.asarray(inputs["L1_ffn_packed_weight"]).reshape(256, D)
    g["l1f_pk"] = np.ascontiguousarray(np.tile(pk1.T, (1, 4)))
    g["l1f_ws"] = np.asarray(inputs["L1_ffn_weight_scale"]).reshape(1, 1).astype(np.float32)
    g["l1f_bias"] = np.ascontiguousarray(
        np.asarray(inputs["L1_ffn_bias"]).reshape(DCH, 128).T)
    g.update(host_constants())
    x = np.asarray(inputs["L0_attn_x"])
    shards = [np.ascontiguousarray(x[2 * c:2 * c + 2].transpose(0, 2, 1))
              for c in range(NCORES)]
    return g, shards


BLOCK_INPUTS = {
    0: ("l0a_wq", "l0a_wk", "l0a_wv", "l0a_bv", "l0a_wo", "l0a_bo",
        "trimask16", "bandmask", "half0", "half1"),
    1: ("l0f_w1", "l0f_w2", "l0f_wr"),
    2: ("l1a_wq", "l1a_wk", "l1a_wv", "l1a_bq", "l1a_bk", "l1a_bv",
        "dftC", "dftS", "idftC", "idftS", "utri"),
    3: ("l1f_pk", "l1f_ws", "l1f_bias", "shamt", "mask3"),
}


def declare_dram(nc, g, launch_blocks):
    d = {}

    def inp(name):
        arr = g[name]
        dt = I32 if arr.dtype == np.int32 else F32
        d[name] = nc.dram_tensor(name, list(arr.shape), dt, kind="ExternalInput").ap()

    d["xin"] = nc.dram_tensor("xin", [NSEQ, D, T], F32, kind="ExternalInput").ap()
    d["xout"] = nc.dram_tensor("xout", [NSEQ, D, T], F32, kind="ExternalOutput").ap()
    for k in ("ident", "ones128", "onesrow"):
        inp(k)
    names = set()
    for blk in launch_blocks:
        names.update(BLOCK_INPUTS[blk])
    for k in sorted(names):
        inp(k)
    if 2 in launch_blocks:
        d["wtil"] = nc.dram_tensor("wtil", [6, D, KPAD], BF16, kind="Internal").ap()
        d["btil"] = nc.dram_tensor("btil", [1, 6 * KPAD], BF16, kind="Internal").ap()
    return d


class Bufs:
    pass


def alloc_persistent(ctx, launch_blocks=()):
    b = Bufs()
    b.x = ctx.sbuf("x_m", [128, DCH, T], F32)
    b.xln = ctx.sbuf("xln", [128, DCH, T + 2], BF16)
    b.ident = ctx.sbuf("c_ident", [128, 128], BF16)
    b.ones128 = ctx.sbuf("c_ones", [128, 128], BF16)
    b.onesrow = ctx.sbuf("c_onesrow", [1, 128], BF16)
    b.mrep = ctx.sbuf("mrep", [128, T], F32)
    b.rrep = ctx.sbuf("rrep", [128, T], F32)
    b.t1 = ctx.sbuf("t1rep", [128, T], F32)
    b.sq = ctx.sbuf("sqch", [128, 2, 512], BF16)
    b.s16 = ctx.sbuf("s16", [128, 2 * T], BF16)
    b.zbuf = ctx.sbuf("zbuf", [128, DCH, T], BF16)
    b.cpool = ctx.sbuf("cpool", [128, 8], F32)
    return b


CONST_VALS = (-1.0, LN_EPS)


def emit_load_const(ctx, dram, b):
    sp, pool = ctx.sp, ctx.pool
    toks = []
    for name, dst in (("ident", b.ident), ("ones128", b.ones128), ("onesrow", b.onesrow)):
        pool.dma_start(out=dst[:], in_=dram[name])
        toks.append(pool.sig_dma("dW"))
    for j, v in enumerate(CONST_VALS):
        pool.memset(b.cpool[:, j:j + 1], v)
        ctx.nc.const_aps.register(F32, v, b.cpool[:, j:j + 1])
    toks.append(pool.sig())
    return toks


def emit_load_x(ctx, dram, b, seq):
    sp = ctx.sp
    sp.dma_start(out=b.x[:], in_=dram["xin"][seq].rearrange("(c p) t -> p c t", p=128))
    return sp.sig_dma("dX")


def emit_store_x(ctx, dram, b, seq, deps):
    sp = ctx.sp
    sp.wait_all(deps)
    sp.dma_start(out=dram["xout"][seq].rearrange("(c p) t -> p c t", p=128), in_=b.x[:])
    return sp.sig_dma("dO")




def emit_stats_rows(ctx, b, src_get, psA, psB):
    pe, dve, act = ctx.pe, ctx.dve, ctx.act
    last = None
    sq_read = None
    for tg in range(T // 512):
        sl = slice(512 * tg, 512 * tg + 512)
        for c in range(DCH):
            src, rdy = src_get(c)
            pe.wait(rdy)
            if c == 0 and last is not None:
                pe.wait(last)
            pe.matmul(psA[:, 0:512], b.ones128[:], src[:, sl],
                      start=(c == 0), stop=(c == DCH - 1))
            act.wait(rdy)
            if c == 0 and last is not None:
                act.wait(last)
            slq = c % 2
            if sq_read is not None:
                act.wait(sq_read)
            act.activation(b.sq[:, slq, :], src[:, sl], AF.Square)
            sqt = act.sig()
            pe.wait(sqt)
            pe.matmul(psB[:, 0:512], b.ones128[:], b.sq[:, slq, :],
                      start=(c == 0), stop=(c == DCH - 1))
            if c >= 1:
                sq_read = pe.sig()
        tokp = pe.sig()
        dve.wait(tokp)
        dve.tensor_single_scalar(b.mrep[:, sl], psA[:, 0:512], 1.0 / D, ALU.mult)
        dve.tensor_single_scalar(b.t1[:, sl], psB[:, 0:512], 1.0 / D, ALU.mult)
        last = dve.sig()
    dve.tensor_tensor(b.rrep[:], b.mrep[:], b.mrep[:], ALU.mult)
    dve.tensor_tensor(b.t1[:], b.t1[:], b.rrep[:], ALU.subtract)
    tokv = dve.sig()
    act.wait(tokv)
    act.activation(b.t1[:], b.t1[:], AF.Ln, bias=LN_EPS)
    act.activation(b.rrep[:], b.t1[:], AF.Exp, scale=-0.5)
    last = act.sig()
    pe.wait(last)
    return last


def emit_preln(ctx, b, deps):
    pe, dve, act, pool = ctx.pe, ctx.dve, ctx.act, ctx.pool
    psA = ctx.phase_psum("lnA", [128, 512], F32)
    psB = ctx.phase_psum("lnB", [128, 512], F32)
    pool.wait_all(deps)
    pool.memset(b.xln[:, :, 0:2], 0.0)
    mz = pool.sig()
    ctoks = {}
    for c in range(DCH):
        act.wait_all(deps)
        act.copy(b.xln[:, c, 2:], b.x[:, c, :])
        ctoks[c] = act.sig()
    st = emit_stats_rows(ctx, b, lambda c: (b.xln[:, c, 2:], ctoks[c]), psA, psB)
    out = []
    for c in range(DCH):
        dve.wait(st)
        dve.wait_all(deps)
        dve.tensor_tensor(b.t1[:], b.x[:, c, :], b.mrep[:], ALU.subtract)
        dve.tensor_tensor(b.xln[:, c, 2:], b.t1[:], b.rrep[:], ALU.mult)
        out.append(dve.sig())
    out.append(mz)
    ctx.dump("xln", b.xln[:], out)
    ctx.dump("mrep", b.mrep[:], out)
    ctx.dump("rrep", b.rrep[:], out)
    return out


def _z_src_get(b, z_toks):
    if not isinstance(z_toks, (list, tuple)):
        z_toks = [z_toks]
    if len(z_toks) == DCH:
        return lambda c: (b.zbuf[:, c, :], z_toks[c])
    if len(z_toks) == 4 * DCH:
        return lambda c: (b.zbuf[:, c, :], z_toks[4 * c + 3])
    return lambda c: (b.zbuf[:, c, :], z_toks[-1])


def emit_transition(ctx, b, z_toks):
    pe, dve, act, pool = ctx.pe, ctx.dve, ctx.act, ctx.pool
    psA = ctx.phase_psum("trA", [128, 512], F32)
    psB = ctx.phase_psum("trB", [128, 512], F32)
    psC = ctx.phase_psum("trC", [128, 512], F32)
    psD = ctx.phase_psum("trD", [128, 512], F32)
    st = emit_stats_rows(ctx, b, _z_src_get(b, z_toks), psA, psB)
    pool.memset(b.xln[:, :, 0:2], 0.0)
    mz = pool.sig()
    ctoks = {}
    for c in range(DCH):
        dve.wait(st)
        dve.tensor_tensor(b.t1[:], b.zbuf[:, c, :], b.mrep[:], ALU.subtract)
        dve.tensor_tensor(b.t1[:], b.t1[:], b.rrep[:], ALU.mult)
        dve.tensor_tensor(b.x[:, c, :], b.x[:, c, :], b.t1[:], ALU.add)
        xt = dve.sig()
        act.wait(xt)
        act.copy(b.xln[:, c, 2:], b.x[:, c, :])
        ctoks[c] = act.sig()
    st2 = emit_stats_rows(ctx, b, lambda c: (b.xln[:, c, 2:], ctoks[c]), psC, psD)
    out = []
    for c in range(DCH):
        dve.wait(st2)
        dve.tensor_tensor(b.t1[:], b.x[:, c, :], b.mrep[:], ALU.subtract)
        dve.tensor_tensor(b.xln[:, c, 2:], b.t1[:], b.rrep[:], ALU.mult)
        out.append(dve.sig())
    out.append(mz)
    return out


def emit_postln_residual(ctx, b, z_toks):
    pe, dve, act, pool = ctx.pe, ctx.dve, ctx.act, ctx.pool
    psA = ctx.phase_psum("plA", [128, 512], F32)
    psB = ctx.phase_psum("plB", [128, 512], F32)
    st = emit_stats_rows(ctx, b, _z_src_get(b, z_toks), psA, psB)
    ctx.dump("pl_mrep", b.mrep[:], [st])
    ctx.dump("pl_rrep", b.rrep[:], [st])
    out = []
    for c in range(DCH):
        dve.wait(st)
        dve.tensor_tensor(b.t1[:], b.zbuf[:, c, :], b.mrep[:], ALU.subtract)
        dve.tensor_tensor(b.t1[:], b.t1[:], b.rrep[:], ALU.mult)
        dve.tensor_tensor(b.x[:, c, :], b.x[:, c, :], b.t1[:], ALU.add)
        out.append(dve.sig())
    ctx.dump("pl_x", b.x[:], out)
    return out




def emit_l1f_weights(ctx, dram, b):
    dve, act, sp = ctx.dve, ctx.act, ctx.sp
    b.l1f_wt = ctx.phase_sbuf("l1f_wt", [128, DCH, D], BF16)
    pk_s = ctx.phase_sbuf("l1f_pks", [128, 2, D], I32)
    sh_s = ctx.phase_sbuf("l1f_sh", [128, 1], I32)
    m3_s = ctx.phase_sbuf("l1f_m3", [128, 1], I32)
    b.l1f_ws = ctx.phase_sbuf("l1f_wss", [1, 1], F32)
    b.l1f_bias = ctx.phase_sbuf("l1f_bs", [128, DCH], F32)
    sp.dma_start(out=sh_s[:], in_=dram["shamt"])
    sp.sig_dma("dW")
    sp.dma_start(out=m3_s[:], in_=dram["mask3"])
    sp.sig_dma("dW")
    sp.dma_start(out=b.l1f_ws[:], in_=dram["l1f_ws"])
    sp.sig_dma("dW")
    sp.dma_start(out=b.l1f_bias[:], in_=dram["l1f_bias"])
    tl = sp.sig_dma("dW")
    pkv = dram["l1f_pk"].rearrange("(c p) o -> c p o", p=128)
    toks = []
    prev = {}
    for c in range(DCH):
        sl = c % 2
        if prev.get(sl) is not None:
            sp.wait(prev[sl])
        sp.dma_start(out=pk_s[:, sl, :], in_=pkv[c])
        tp = sp.sig_dma("dW")
        dve.wait(tp)
        dve.wait(tl)
        for s4 in range(1, 4):
            dve.tensor_single_scalar(pk_s[:, sl, 256 * s4:256 * (s4 + 1)],
                                     pk_s[:, sl, 256 * s4:256 * (s4 + 1)],
                                     2 * s4, ALU.logical_shift_right)
            dve.sig()
        dve.tensor_single_scalar(pk_s[:, sl, :], pk_s[:, sl, :], m3_s[:, 0:1],
                                 ALU.bitwise_and)
        tc = dve.sig()
        act.wait(tc)
        act.activation(b.l1f_wt[:, c, :], pk_s[:, sl, :], AF.Identity, bias=-1.0)
        ta = act.sig()
        prev[sl] = ta
        toks.append(ta)
    return toks


def emit_l1f(ctx, b, xln_toks, wt_toks):
    pe, dve, act, pool, sp = ctx.pe, ctx.dve, ctx.act, ctx.pool, ctx.sp
    xt = b.xln
    act.wait_all(xln_toks)
    act.activation(b.t1[:], xt[:, 0, 2:], AF.Abs)
    t_a = act.sig()
    dve.wait(t_a)
    for c in range(1, DCH):
        act.wait_all(xln_toks)
        act.activation(b.rrep[:], xt[:, c, 2:], AF.Abs)
        t_a = act.sig()
        dve.wait(t_a)
        dve.tensor_max(b.t1[:], b.t1[:], b.rrep[:])
        t_d = dve.sig()
        act.wait(t_d)
    fold_bf = b.s16[:, 0:T]
    dve.tensor_copy(fold_bf, b.t1[:])
    tfold = dve.sig()
    ps_t = ctx.phase_psum("l1f_pst", [128, 128], BF16)
    cols = ctx.phase_sbuf("l1f_cols", [128, NT], F32)
    colsb = ctx.phase_sbuf("l1f_colsb", [128, NT], BF16)
    rows = ctx.phase_sbuf("l1f_rows", [NT, 128], F32)
    tcol = None
    for j in range(NT):
        pe.wait(tfold)
        if tcol is not None:
            pe.wait(tcol)
        pe.transpose(ps_t[:], fold_bf[:, 128 * j:128 * j + 128], b.ident[:])
        tt = pe.sig()
        dve.wait(tt)
        dve.reduce_max(cols[:, j:j + 1], ps_t[:], AX.X)
        tcol = dve.sig()
    dve.tensor_copy(colsb[:], cols[:])
    tcb = dve.sig()
    ps_r = ctx.phase_psum("l1f_psr", [128, 128], BF16)
    pe.wait(tcb)
    pe.transpose(ps_r[0:NT, :], colsb[:], b.ident[:])
    tr = pe.sig()
    dve.wait(tr)
    dve.tensor_copy(rows[:], ps_r[0:NT, :])
    trows = dve.sig()
    amaxrow = b.t1[0:1, 0:T]
    asrow = b.rrep[0:1, 0:T]
    asrow_bf = b.s16[0:1, T:T + T]
    wsasrow_bf = b.s16[0:1, 0:T]
    sp.wait(trows)
    sp.dma_start(out=amaxrow, in_=rows[:])
    tdma = sp.sig_dma("dS")
    dve.wait(tdma)
    dve.tensor_single_scalar(amaxrow, amaxrow, 1e-5, ALU.max)
    dve.sig()
    dve.reciprocal(asrow, amaxrow)
    dve.sig()
    dve.tensor_single_scalar(asrow, asrow, 127.0, ALU.mult)
    dve.sig()
    dve.tensor_copy(asrow_bf, asrow)
    tas = dve.sig()
    dve.tensor_single_scalar(amaxrow, amaxrow, b.l1f_ws[0:1, 0:1], ALU.mult)
    dve.sig()
    dve.tensor_single_scalar(amaxrow, amaxrow, 1.0 / 127.0, ALU.mult)
    dve.sig()
    dve.tensor_copy(wsasrow_bf, amaxrow)
    twsas = dve.sig()
    ps_bc = ctx.phase_psum("l1f_psbc", [128, 512], F32)
    asrep = b.mrep
    ev = None
    for tg in range(T // 512):
        sl = slice(512 * tg, 512 * tg + 512)
        pe.wait(tas)
        if ev is not None:
            pe.wait(ev)
        pe.matmul(ps_bc[:], b.onesrow[:], asrow_bf[0:1, sl], start=True, stop=True)
        t_ = pe.sig()
        dve.wait(t_)
        dve.tensor_copy(asrep[:, sl], ps_bc[:])
        ev = dve.sig()
    ctx.dump("asrep", asrep[:], [ev])
    qi = ctx.phase_sbuf("l1f_qi", [128, T], I32)
    qtoks = []
    for c in range(DCH):
        dve.wait(ev)
        dve.wait_all(xln_toks)
        dve.tensor_tensor(b.t1[:], xt[:, c, 2:], asrep[:], ALU.mult)
        dve.sig()
        dve.tensor_scalar(b.t1[:], b.t1[:], -128.0, 127.0, ALU.max, ALU.min)
        dve.sig()
        dve.tensor_copy(qi[:], b.t1[:])
        tq = dve.sig()
        act.wait(tq)
        act.copy(xt[:, c, 2:], qi[:])
        qtoks.append(act.sig())
    ctx.dump("qbuf", xt[:], qtoks)
    wsasrep = b.s16[:, T:2 * T]
    ev2 = None
    for tg in range(T // 512):
        sl = slice(512 * tg, 512 * tg + 512)
        pe.wait(twsas)
        pe.wait(ev2 if ev2 is not None else ev)
        pe.matmul(ps_bc[:], b.onesrow[:], wsasrow_bf[0:1, sl], start=True, stop=True)
        t_ = pe.sig()
        dve.wait(t_)
        dve.tensor_copy(wsasrep[:, sl], ps_bc[:])
        ev2 = dve.sig()
    trep = ev2
    ps_z = [ctx.phase_psum("l1f_z0", [128, 512], F32), ctx.phase_psum("l1f_z1", [128, 512], F32)]
    ev_prev = [None, None]
    ztoks = []
    for m in range(DCH):
        for tg in range(T // 512):
            sl = slice(512 * tg, 512 * tg + 512)
            idx = (m * (T // 512) + tg) % 2
            pz = ps_z[idx]
            for c in range(DCH):
                pe.wait(qtoks[c])
                pe.wait_all(wt_toks)
                if c == 0 and ev_prev[idx] is not None:
                    pe.wait(ev_prev[idx])
                pe.matmul(pz[:], b.l1f_wt[:, c, 128 * m:128 * (m + 1)],
                          xt[:, c, 2 + 512 * tg:2 + 512 * (tg + 1)],
                          start=(c == 0), stop=(c == DCH - 1))
            tmm = pe.sig()
            dve.wait(tmm)
            dve.wait(trep)
            dve.tensor_tensor(b.t1[:, 0:512], pz[:], wsasrep[:, sl], ALU.mult)
            dve.sig()
            dve.tensor_scalar(b.zbuf[:, m, sl], b.t1[:, 0:512],
                              b.l1f_bias[:, m:m + 1], None, ALU.add)
            zt = dve.sig()
            ev_prev[idx] = zt
            ztoks.append(zt)
    ctx.dump("zbuf", b.zbuf[:], ztoks)
    ctx.dump("wsasrep", wsasrep[:], [trep])
    ctx.dump("l1fwt", b.l1f_wt[:], wt_toks)
    return ztoks




def emit_l0f(ctx, dram, b, xln_toks):
    pe, dve, act, pool, sp = ctx.pe, ctx.dve, ctx.act, ctx.pool, ctx.sp
    NE = 9
    b.mask_fm = ctx.phase_sbuf("l0f_maskfm", [8, T], BF16)
    ctx.push_phase()
    wr = ctx.phase_sbuf("l0f_wr", [128, DCH, 8], BF16)
    pool.dma_start(out=wr[:], in_=dram["l0f_wr"].rearrange("(c p) e -> p c e", p=128))
    twr = pool.sig_dma("dW")
    ps_r = ctx.phase_psum("l0f_psr", [8, 512], F32)
    rlog = ctx.phase_sbuf("l0f_rlog", [8, T], BF16)
    ev = None
    for tg in range(T // 512):
        sl2 = slice(2 + 512 * tg, 2 + 512 * (tg + 1))
        for c in range(DCH):
            pe.wait(twr)
            pe.wait_all(xln_toks)
            if c == 0 and ev is not None:
                pe.wait(ev)
            pe.matmul(ps_r[:], wr[:, c, :], b.xln[:, c, sl2],
                      start=(c == 0), stop=(c == DCH - 1))
        tm_ = pe.sig()
        act.wait(tm_)
        act.copy(rlog[:, 512 * tg:512 * (tg + 1)], ps_r[:])
        ev = act.sig()
    trlog = ev
    ps_g = ctx.phase_psum("l0f_psg", [128, 8], BF16)
    ps_gb = ctx.phase_psum("l0f_psgb", [8, 128], BF16)
    ltm = ctx.phase_sbuf("l0f_ltm", [128, 8], F32)
    g1 = ctx.phase_sbuf("l0f_g1", [128, 8], F32)
    g2 = ctx.phase_sbuf("l0f_g2", [128, 8], F32)
    gc = ctx.phase_sbuf("l0f_gc", [128, 12], F32)
    mask_tm = ctx.phase_sbuf("l0f_masktm", [128, 8], BF16)
    mask_fm = b.mask_fm
    tmask = None
    tp_g = None
    tp_gb = None
    for j in range(NT):
        pe.wait(trlog)
        if tp_g is not None:
            pe.wait(tp_g)
        pe.transpose(ps_g[:], rlog[:, 128 * j:128 * (j + 1)], b.ident[0:8, 0:8])
        tt = pe.sig()
        act.wait(tt)
        act.copy(ltm[:], ps_g[:])
        tp_g = act.sig()
        dve.wait(tp_g)
        dve.reduce_max(gc[:, 0:1], ltm[:], AX.X)
        dve.tensor_single_scalar(g1[:], ltm[:], gc[:, 0:1], ALU.is_ge)
        dve.reduce_sum(gc[:, 1:2], g1[:], AX.X)
        dve.scalar_tensor_tensor(g2[:], g1[:], -1e30, ltm[:], ALU.mult, ALU.add)
        dve.reduce_max(gc[:, 2:3], g2[:], AX.X)
        dve.tensor_single_scalar(gc[:, 3:4], gc[:, 1:2], 1.5, ALU.is_gt)
        dve.tensor_tensor(gc[:, 4:5], gc[:, 0:1], gc[:, 2:3], ALU.subtract)
        dve.tensor_tensor(gc[:, 5:6], gc[:, 3:4], gc[:, 4:5], ALU.mult)
        dve.tensor_tensor(gc[:, 6:7], gc[:, 2:3], gc[:, 5:6], ALU.add)
        dve.tensor_tensor(gc[:, 7:8], gc[:, 6:7], gc[:, 0:1], ALU.subtract)
        td = dve.sig()
        act.wait(td)
        act.activation(gc[:, 8:9], gc[:, 7:8], AF.Exp)
        te = act.sig()
        dve.wait(te)
        dve.tensor_single_scalar(gc[:, 9:10], gc[:, 8:9], 1.0, ALU.add)
        dve.tensor_single_scalar(gc[:, 10:11], gc[:, 9:10], 1.0, ALU.is_equal)
        dve.tensor_single_scalar(mask_tm[:], g1[:], gc[:, 10:11], ALU.mult)
        tmtm = dve.sig()
        pe.wait(tmtm)
        if tp_gb is not None:
            pe.wait(tp_gb)
        pe.transpose(ps_gb[0:8, :], mask_tm[:], b.ident[:])
        tb = pe.sig()
        act.wait(tb)
        act.copy(mask_fm[0:8, 128 * j:128 * (j + 1)], ps_gb[0:8, :])
        tp_gb = act.sig()
        tmask = tp_gb
    ctx.barrier()
    ctx.pop_phase()
    w1b = ctx.phase_sbuf("l0f_w1b", [128, 2, DCH, 512], BF16)
    w2b = ctx.phase_sbuf("l0f_w2b", [128, 2, 4, D], BF16)
    rowbuf = ctx.phase_sbuf("l0f_rowb", [1, T], BF16)
    maskrep = ctx.phase_sbuf("l0f_mrep", [128, T], BF16)
    hraw = b.s16[:, 0:T]
    hm = b.s16[:, T:2 * T]
    _h01 = b.t1[:].bitcast(BF16)
    _h23 = b.mrep[:].bitcast(BF16)
    hsq = lambda fc, sl=slice(0, T): (_h01 if fc < 2 else _h23)[
        :, T * (fc % 2) + (sl.start or 0):T * (fc % 2) + (sl.stop if sl.stop is not None else T)]
    ps_h = ctx.phase_psum("l0f_psh", [128, T], F32)
    ps_z = ctx.phase_psum("l0f_psz", [128, T], F32)
    w_used = [None, None]
    tmr_used = None
    tmr = None
    th_evac = None
    tz_evac = None
    hsq_read = [None] * 4
    ztoks = []
    for e in range(NE):
        slot = e % 2
        pool.wait(w_used[slot])
        pool.dma_start(out=w1b[:, slot], in_=dram["l0f_w1"][e].rearrange(
            "(c p) f -> p c f", p=128))
        pool.sig_dma("dW")
        pool.dma_start(out=w2b[:, slot], in_=dram["l0f_w2"][e].rearrange(
            "(c p) o -> p c o", p=128))
        tw = pool.sig_dma("dW")
        if e < 8:
            sp.wait(tmask)
            sp.wait(tmr)
            sp.dma_start(out=rowbuf[:], in_=mask_fm[e:e + 1, :])
            trow = sp.sig_dma("dS")
            pe.wait(trow)
            pe.wait(th_evac)
            if tmr is not None:
                pe.wait(tmr)
            for tg in range(T // 512):
                sl = slice(512 * tg, 512 * (tg + 1))
                pe.matmul(ps_h[:, sl], b.onesrow[:], rowbuf[0:1, sl],
                          start=True, stop=True)
            tb_ = pe.sig()
            dve.wait(tb_)
            dve.wait(tmr_used)
            for tg in range(T // 512):
                sl = slice(512 * tg, 512 * (tg + 1))
                dve.tensor_copy(maskrep[:, sl], ps_h[:, sl])
            tmr = dve.sig()
        hsq_toks = []
        for fc in range(4):
            for d in range(DCH):
                pe.wait(tw)
                pe.wait_all(xln_toks)
                if d == 0:
                    pe.wait(th_evac)
                    pe.wait(tmr)
                lh = w1b[:, slot, d, 128 * fc:128 * (fc + 1)]
                for tg in range(T // 512):
                    pe.matmul(ps_h[:, 512 * tg:512 * (tg + 1)], lh,
                              b.xln[:, d, 2 + 512 * tg:2 + 512 * (tg + 1)],
                              start=(d == 0), stop=(d == DCH - 1))
            tmm = pe.sig()
            act.wait(tmm)
            if hsq_toks:
                act.wait(hsq_toks[-1])
            elif tmr_used is not None:
                act.wait(tmr_used)
            act.activation(hraw[:], ps_h[:], AF.Relu)
            th_evac = act.sig()
            dve.wait(th_evac)
            dve.wait(hsq_read[fc])
            if e < 8:
                dve.wait(tmr)
                dve.tensor_tensor(hm[:], hraw[:], maskrep[:], ALU.mult)
                dve.tensor_tensor(hsq(fc), hm[:], hraw[:], ALU.mult)
            else:
                dve.tensor_tensor(hsq(fc), hraw[:], hraw[:], ALU.mult)
            hsq_toks.append(dve.sig())
        tmr_used = hsq_toks[-1]
        if e == 8:
            pass
        for m in range(DCH):
            for fc in range(4):
                pe.wait(hsq_toks[fc])
                if fc == 0:
                    pe.wait(tz_evac)
                lh = w2b[:, slot, fc, 128 * m:128 * (m + 1)]
                for tg in range(T // 512):
                    pe.matmul(ps_z[:, 512 * tg:512 * (tg + 1)], lh,
                              hsq(fc, slice(512 * tg, 512 * (tg + 1))),
                              start=(fc == 0), stop=(fc == 3))
            tmm = pe.sig()
            dve.wait(tmm)
            if e == 0:
                dve.tensor_copy(b.zbuf[:, m, :], ps_z[:])
            else:
                dve.tensor_tensor(b.zbuf[:, m, :], b.zbuf[:, m, :], ps_z[:], ALU.add)
            tz_evac = dve.sig()
            if e == NE - 1:
                ztoks.append(tz_evac)
        for fc in range(4):
            hsq_read[fc] = tz_evac
        w_used[slot] = tz_evac
    ctx.dump("moe_maskfm", mask_fm[:], [tmask])
    ctx.dump("moe_z", b.zbuf[:], ztoks)
    return ztoks



PAIRS = ((0, 1), (0, 2), (0, 3), (1, 2), (1, 3), (2, 3))
JSIGN = (1, -1, 1, 1, -1, 1)
ATT_SCALE = float(np.float32(6.0 ** (-0.5)))


def emit_l0a(ctx, dram, b, xln_toks):
    pe, dve, act, pool, sp = ctx.pe, ctx.dve, ctx.act, ctx.pool, ctx.sp
    bandm = ctx.phase_sbuf("l0a_band", [128, 16], F32)
    tri16 = ctx.phase_sbuf("l0a_tri16", [128, T], BF16)
    half01 = ctx.phase_sbuf("l0a_half", [1, 256], BF16)
    sp.dma_start(out=bandm[:], in_=dram["bandmask"])
    sp.sig_dma("dW")
    pool.dma_start(out=tri16[:], in_=dram["trimask16"])
    pool.sig_dma("dW")
    pool.dma_start(out=half01[0:1, 0:128], in_=dram["half0"])
    pool.sig_dma("dW")
    pool.dma_start(out=half01[0:1, 128:256], in_=dram["half1"])
    tw0 = pool.sig_dma("dW")
    vtm = b.zbuf[:].rearrange("p c t -> p (c t)").rearrange("p (j o) -> p j o", o=D)
    qspec = ctx.phase_sbuf("l0a_qspec", [128, T], BF16)
    kspec = ctx.phase_sbuf("l0a_kspec", [128, T], BF16)
    ctx.push_phase()
    wq = ctx.phase_sbuf("l0a_wqb", [128, 16, 128], BF16)
    wk = ctx.phase_sbuf("l0a_wkb", [128, DCH, 128], BF16)
    wv = ctx.phase_sbuf("l0a_wvb", [128, DCH, D], BF16)
    bvr = ctx.phase_sbuf("l0a_bvr", [1, D], BF16)
    pool.dma_start(out=wq[:], in_=dram["l0a_wq"].rearrange("(c p) o -> p c o", p=128))
    pool.sig_dma("dW")
    pool.dma_start(out=wk[:], in_=dram["l0a_wk"].rearrange("(c p) o -> p c o", p=128))
    pool.sig_dma("dW")
    pool.dma_start(out=wv[:], in_=dram["l0a_wv"].rearrange("(c p) o -> p c o", p=128))
    pool.sig_dma("dW")
    pool.dma_start(out=bvr[:], in_=dram["l0a_bv"])
    tw = pool.sig_dma("dW")
    q12f = ctx.phase_sbuf("l0a_q12f", [128, T], BF16)
    k12f = ctx.phase_sbuf("l0a_k12f", [128, T], BF16)
    _rbf = b.rrep[:].bitcast(BF16)
    ps_p = ctx.phase_psum("l0a_psp", [128, 2, 512], F32)
    ps_t = ctx.phase_psum("l0a_pst", [128, 128], BF16)
    ps_v = ctx.phase_psum("l0a_psv", [128, 2, 512], F32)
    tvev = {}
    for j in range(NT):
        tsl = slice(2 + 128 * j, 2 + 128 * (j + 1))
        for half in range(2):
            pv = ps_v[:, half, :]
            osl = slice(512 * half, 512 * (half + 1))
            for dc in range(DCH):
                pe.wait(tw)
                pe.wait_all(xln_toks)
                if dc == 0 and j >= 1:
                    pe.wait(tvev[(j - 1, half)])
                pe.matmul(pv[:], b.xln[:, dc, tsl], wv[:, dc, osl],
                          start=(dc == 0), stop=False)
            pe.matmul(pv[:], b.onesrow[:, 0:128], bvr[0:1, osl],
                      start=False, stop=True)
            tmm = pe.sig()
            act.wait(tmm)
            act.copy(vtm[:, j, osl], pv[:])
            tvev[(j, half)] = act.sig()
    tqf = None
    for tg in range(T // 512):
        sl = slice(512 * tg, 512 * (tg + 1))
        for src, wgt, nch, off in ((q12f, wq, 16, None), (k12f, wk, DCH, None)):
            pp = ps_p[:, 0 if src is q12f else 1, :]
            for c in range(nch):
                pe.wait(tw)
                pe.wait_all(xln_toks)
                if c == 0 and tqf is not None:
                    pe.wait(tqf)
                if src is q12f and c >= DCH:
                    rhs = b.xln[:, c - DCH, 1 + 512 * tg:1 + 512 * (tg + 1)]
                else:
                    rhs = b.xln[:, c, 2 + 512 * tg:2 + 512 * (tg + 1)]
                pe.matmul(pp[:], wgt[:, c, :], rhs, start=(c == 0), stop=(c == nch - 1))
            tmm = pe.sig()
            act.wait(tmm)
            act.copy(src[:, sl], pp[:])
            tqf = act.sig()
    q12tmF = _rbf[:, 0:T]
    k12tmF = _rbf[:, T:2 * T]
    _tbf = b.t1[:].bitcast(BF16)
    qlinF = _tbf[:, 0:T]
    klinF = _tbf[:, T:2 * T]
    pool.memset(qlinF, 0.0)
    pool.memset(klinF, 0.0)
    tlz = pool.sig()
    ttm = None
    for j in range(NT):
        for which, (f_, dstF) in enumerate(((q12f, q12tmF), (k12f, k12tmF))):
            pe.wait(tqf)
            if ttm is not None:
                pe.wait(ttm)
            pe.transpose(ps_t[:], f_[:, 128 * j:128 * (j + 1)], b.ident[:])
            tt = pe.sig()
            act.wait(tt)
            act.copy(dstF[:, 128 * j:128 * (j + 1)], ps_t[:])
            ttm = act.sig()
    qa = q12tmF.rearrange("p (j x) -> p j x", x=128)[:, :, 0:64]\
        .rearrange("p j (h i) -> p j h i", i=4)
    qb2 = q12tmF.rearrange("p (j x) -> p j x", x=128)[:, :, 64:128]\
        .rearrange("p j (h i) -> p j h i", i=4)
    ka = k12tmF.rearrange("p (j x) -> p j x", x=128)[:, :, 0:64]\
        .rearrange("p j (h i) -> p j h i", i=4)
    kb2 = k12tmF.rearrange("p (j x) -> p j x", x=128)[:, :, 64:128]\
        .rearrange("p j (h i) -> p j h i", i=4)
    lvq = qlinF.rearrange("p (j x) -> p j x", x=128)\
        .rearrange("p j (h c) -> p j h c", c=8)
    lvk = klinF.rearrange("p (j x) -> p j x", x=128)\
        .rearrange("p j (h c) -> p j h c", c=8)
    t_a = b.mrep[:, 0:256].rearrange("p (j h u) -> p j h u", h=16, u=1)
    t_b = b.mrep[:, 256:512].rearrange("p (j h u) -> p j h u", h=16, u=1)
    dve.wait(ttm)
    dve.wait(tlz)
    for which in (0, 1):
        a1, a2, lv = (qa, qb2, lvq) if which == 0 else (ka, kb2, lvk)
        for c6 in range(6):
            if which == 0:
                i_, j_ = PAIRS[c6]
                neg = False
            else:
                i_, j_ = PAIRS[5 - c6]
                neg = (JSIGN[c6] < 0)
            dve.tensor_tensor(t_a, a1[:, :, :, i_:i_ + 1], a2[:, :, :, j_:j_ + 1], ALU.mult)
            dve.tensor_tensor(t_b, a1[:, :, :, j_:j_ + 1], a2[:, :, :, i_:i_ + 1], ALU.mult)
            if neg:
                dve.tensor_tensor(lv[:, :, :, c6:c6 + 1], t_b, t_a, ALU.subtract)
            else:
                dve.tensor_tensor(lv[:, :, :, c6:c6 + 1], t_a, t_b, ALU.subtract)
    tpl = dve.sig()
    ncol = b.rrep
    for which in (0, 1):
        linF = qlinF if which == 0 else klinF
        lv = lvq if which == 0 else lvk
        dve.wait(tpl)
        dve.tensor_tensor(b.mrep[:], linF, linF, ALU.mult)
        dve.tensor_reduce(ncol[:, 0:256].rearrange("p (x u) -> p x u", u=1),
                          b.mrep[:].rearrange("p (x c) -> p x c", c=8), AX.X, ALU.add)
        tn_ = dve.sig()
        act.wait(tn_)
        act.activation(ncol[:, 256:512], ncol[:, 0:256], AF.Sqrt)
        tsq2 = act.sig()
        dve.wait(tsq2)
        dve.tensor_single_scalar(ncol[:, 256:512], ncol[:, 256:512], 1e-12, ALU.max)
        dve.reciprocal(ncol[:, 512:768], ncol[:, 256:512])
        dve.tensor_tensor(lv, lv, ncol[:, 512:768]
                          .rearrange("p (j h u) -> p j h u", h=16, u=1)
                          .broadcast_to([128, NT, 16, 8]), ALU.mult)
        tnorm = dve.sig()
    tspec = None
    for j in range(NT):
        for which in (0, 1):
            linF = qlinF if which == 0 else klinF
            pe.wait(tnorm)
            if tspec is not None:
                pe.wait(tspec)
            pe.transpose(ps_t[:], linF[:, 128 * j:128 * (j + 1)], b.ident[:])
            tt2 = pe.sig()
            act.wait(tt2)
            act.copy((qspec if which == 0 else kspec)[:, 128 * j:128 * (j + 1)], ps_t[:])
            tspec = act.sig()
    ctx.barrier()
    ctx.pop_phase()
    ctx.push_phase()
    psb = [b.s16[:, 0:T], b.s16[:, T:2 * T]]
    qb = ctx.phase_sbuf("l0a_qb", [128, T], BF16)
    brd_all = ctx.phase_sbuf("l0a_brdall", [128, 2, 512], BF16)
    zrow = ctx.phase_sbuf("l0a_zrow", [1, 512], BF16)
    pool.memset(zrow[:], 0.0)
    tzr = pool.sig()
    rrow = b.t1[0:1, 0:T]
    rrowb = ctx.phase_sbuf("l0a_rrowb", [1, T], BF16)
    ps_l = ctx.phase_psum("l0a_psl", [128, 2, 512], F32)
    ps_oa = ctx.phase_psum("l0a_psoa", [128, 4, 128], F32)
    ps_ob = ctx.phase_psum("l0a_psob", [128, 4, 128], F32)
    ps_s = ctx.phase_psum("l0a_pss", [128, T], F32)
    oview = [b.xln[:, p, 2:2 + T] for p in range(DCH)]

    def oview_a(q):
        return b.xln[:, 0:4, 2 + 128 * q:2 + 128 * (q + 1)]

    def oview_b(q):
        return b.xln[:, 4:8, 2 + 128 * q:2 + 128 * (q + 1)]
    tqb = None
    texp = [None, None]
    tpread = [None, None]
    pslw = [None, None]
    eread = [None, None]
    ebuf = b.mrep[:].bitcast(BF16)
    tsum_ev = None
    to_ev = None
    tbrd_ev = None
    ztoks_oready = []
    for q in range(NT):
        dve.wait(tspec)
        dve.wait(tqb)
        for h in range(16):
            dve.tensor_single_scalar(qb[:, 128 * h:128 * (h + 1)],
                                     qspec[:, 128 * q:128 * (q + 1)],
                                     bandm[:, h:h + 1], ALU.mult)
        tqbv = dve.sig()
        pe.wait(tzr)
        pe.wait(to_ev)
        pe.matmul(ps_oa[:].rearrange("p a b -> p (a b)"), b.onesrow[:, 0:128],
                  zrow[0:1, :], start=True, stop=True)
        pe.matmul(ps_ob[:].rearrange("p a b -> p (a b)"), b.onesrow[:, 0:128],
                  zrow[0:1, :], start=True, stop=True)

        def emit_sums_attnv(k):
            pb2 = k % 2
            pe.wait(texp[pb2])
            if k == 0:
                pe.wait(tsum_ev)
            for sg in range(T // 512):
                pe.matmul(ps_s[:, 512 * sg:512 * (sg + 1)], b.ones128[:],
                          psb[pb2][:, 512 * sg:512 * (sg + 1)],
                          start=(k == 0), stop=(k == q))
            for p in range(DCH):
                po = (ps_oa if p < 4 else ps_ob)[:, p % 4, :]
                pe.matmul(po[0:64, :], vtm[:, k, 128 * p:128 * p + 64],
                          psb[pb2][:, 256 * p:256 * p + 128],
                          start=False, stop=(k == q), tile_position=(0, 0))
                pe.matmul(po[64:128, :], vtm[:, k, 128 * p + 64:128 * (p + 1)],
                          psb[pb2][:, 256 * p + 128:256 * (p + 1)],
                          start=False, stop=(k == q), tile_position=(0, 64))
            tpread[pb2] = pe.sig()

        for k in range(q + 1):
            pb = k % 2
            for chp in range(2):
                pe.wait(tqbv)
                pe.wait(tspec)
                pe.wait(pslw[0])
                for sl_ in range(2):
                    ch = 2 * chp + sl_
                    pe.matmul(ps_l[:, sl_, :],
                              kspec[:, 128 * k:128 * (k + 1)],
                              qb[:, 512 * ch:512 * (ch + 1)],
                              start=True, stop=True)
                tmm = pe.sig()
                dve.wait(tmm)
                dve.wait(eread[chp])
                dve.tensor_copy(ebuf[:, 1024 * chp:1024 * (chp + 1)],
                                ps_l[:].rearrange("p s x -> p (s x)"))
                pslw[0] = pslw[1] = dve.sig()
                act.wait(pslw[0])
                act.wait(tpread[pb])
                act.activation(psb[pb][:, 1024 * chp:1024 * (chp + 1)],
                               ebuf[:, 1024 * chp:1024 * (chp + 1)],
                               AF.Exp, scale=ATT_SCALE)
                eread[chp] = act.sig()
            texp[pb] = eread[1]
            if k == q:
                dve.wait(texp[pb])
                dve.tensor_tensor(psb[pb][:], psb[pb][:], tri16[:], ALU.mult)
                texp[pb] = dve.sig()
            if q == 3 and k == 1:
                ctx.dump("l0a_P31", psb[pb][:, :], [texp[pb]])
            if k >= 1:
                emit_sums_attnv(k - 1)
        emit_sums_attnv(q)
        tka = tpread[q % 2]
        dve.wait(tka)
        dve.tensor_copy(rrow[0:1, :], ps_s[0:1, :])
        tsr = dve.sig()
        act.wait(tsr)
        act.activation(rrow[0:1, :], rrow[0:1, :], AF.Ln)
        act.activation(rrowb[0:1, :], rrow[0:1, :], AF.Exp, scale=-1.0)
        trr = act.sig()
        if q == 3:
            ctx.dump("l0a_r3", rrowb[0:1, :], [trr])
        tsum_ev = trr
        rview = rrowb[0:1, :].rearrange("u (a b q) -> u a b q", a=8, b=2)
        for half in range(2):
            pe.wait(trr)
            pe.wait(pslw[half])
            pe.wait(tbrd_ev)
            pe.matmul(ps_l[:, half, :], half01[0:1, 0:128],
                      rview[:, 4 * half:4 * (half + 1), 0, :], start=True, stop=True)
            pe.matmul(ps_l[:, half, :], half01[0:1, 128:256],
                      rview[:, 4 * half:4 * (half + 1), 1, :], start=False, stop=True)
        tbb = pe.sig()
        dve.wait(tbb)
        dve.wait(to_ev)
        dve.tensor_copy(brd_all[:], ps_l[:])
        tbrd_ev = dve.sig()
        pslw = [tbrd_ev, tbrd_ev]
        dve.wait(tbrd_ev)
        dve.wait(tka)
        dve.tensor_tensor(oview_a(q), ps_oa[:],
                          brd_all[:, 0, :].rearrange("p (a q) -> p a q", a=4), ALU.mult)
        dve.tensor_tensor(oview_b(q), ps_ob[:],
                          brd_all[:, 1, :].rearrange("p (a q) -> p a q", a=4), ALU.mult)
        to_ev = dve.sig()
        ztoks_oready.append(to_ev)
    ctx.dump("l0a_o", b.xln[:], ztoks_oready)
    ctx.barrier()
    ctx.pop_phase()
    ctx.push_phase()
    wo = ctx.phase_sbuf("l0a_wob", [128, DCH, D], BF16)
    bor = ctx.phase_sbuf("l0a_bor", [1, D], BF16)
    pool.dma_start(out=wo[:], in_=dram["l0a_wo"].rearrange("(c p) o -> p c o", p=128))
    pool.sig_dma("dW")
    pool.dma_start(out=bor[:], in_=dram["l0a_bo"])
    two = pool.sig_dma("dW")
    ones512 = ctx.phase_sbuf("l0a_ones512", [1, 512], BF16)
    pool.memset(ones512[:], 1.0)
    ton = pool.sig()
    ps_z = ctx.phase_psum("l0a_psz", [128, T], F32)
    ztoks = []
    tz_ev = None
    for m in range(DCH):
        for p in range(DCH):
            pe.wait(two)
            pe.wait_all(ztoks_oready)
            if p == 0:
                pe.wait(tz_ev)
            lh = wo[:, p, 128 * m:128 * (m + 1)]
            for tg in range(T // 512):
                pe.matmul(ps_z[:, 512 * tg:512 * (tg + 1)], lh,
                          oview[p][:, 512 * tg:512 * (tg + 1)],
                          start=(p == 0), stop=False)
        for tg in range(T // 512):
            pe.wait(ton)
            pe.matmul(ps_z[:, 512 * tg:512 * (tg + 1)],
                      bor[0:1, 128 * m:128 * (m + 1)], ones512[0:1, :],
                      start=False, stop=True)
        tmm = pe.sig()
        dve.wait(tmm)
        dve.tensor_copy(b.zbuf[:, m, :], ps_z[:])
        tz_ev = dve.sig()
        ztoks.append(tz_ev)
    ctx.barrier()
    ctx.pop_phase()
    ctx.dump("l0a_z", b.zbuf[:], ztoks)
    return ztoks



L1A_HALVES = ((0, 1, 2), (3, 4))


def emit_l1a_fold(ctx, dram, b):
    pe, dve, act, pool, sp = ctx.pe, ctx.dve, ctx.act, ctx.pool, ctx.sp
    ctx.push_phase()
    b.bias_tilde = ctx.phase_sbuf("l1a_btil", [1, 6 * KPAD], BF16)
    wxb = ctx.phase_sbuf("l1a_wxb", [128, DCH, D], BF16)
    cb = ctx.phase_sbuf("l1a_cb", [128, DCH, KBINS], BF16)
    bcol = ctx.phase_sbuf("l1a_bcol", [128, DCH], BF16)
    stg = ctx.phase_sbuf("l1a_stg", [128, KPAD], BF16)
    ps_f = ctx.phase_psum("l1a_psf", [128, KBINS], F32)
    ps_b = ctx.phase_psum("l1a_psb", [1, KBINS], F32)
    pool.memset(stg[:], 0.0)
    tz = pool.sig()
    tprev_dma = None
    tstg_free = tz
    for wi, (wname, bname, plane) in enumerate((
            ("l1a_wq", "l1a_bq", "dftC"), ("l1a_wq", "l1a_bq", "dftS"),
            ("l1a_wk", "l1a_bk", "dftC"), ("l1a_wk", "l1a_bk", "dftS"),
            ("l1a_wv", "l1a_bv", "dftC"), ("l1a_wv", "l1a_bv", "dftS"))):
        pool.wait(tprev_dma)
        pool.dma_start(out=wxb[:], in_=dram[wname].rearrange("(c p) d -> p c d", p=128))
        pool.sig_dma("dW")
        pool.dma_start(out=cb[:], in_=dram[plane].rearrange("(c p) k -> p c k", p=128))
        pool.sig_dma("dW")
        pool.dma_start(out=bcol[:], in_=dram[bname])
        tld = pool.sig_dma("dW")
        tev = None
        for dc in range(DCH):
            for ec in range(DCH):
                pe.wait(tld)
                if ec == 0 and tev is not None:
                    pe.wait(tev)
                pe.matmul(ps_f[:, 0:512], wxb[:, ec, 128 * dc:128 * (dc + 1)],
                          cb[:, ec, 0:512], start=(ec == 0), stop=(ec == DCH - 1))
                pe.matmul(ps_f[:, 512:513], wxb[:, ec, 128 * dc:128 * (dc + 1)],
                          cb[:, ec, 512:513], start=(ec == 0), stop=(ec == DCH - 1))
            tmm = pe.sig()
            act.wait(tmm)
            act.wait(tstg_free)
            act.copy(stg[:, 0:KBINS], ps_f[:])
            tev = act.sig()
            sp.wait(tev)
            sp.dma_start(out=dram["wtil"][wi, 128 * dc:128 * (dc + 1), :], in_=stg[:])
            tstg_free = sp.sig_dma("dS")
        tevb = None
        for ec in range(DCH):
            pe.wait(tld)
            pe.matmul(ps_b[:, 0:512], bcol[:, ec:ec + 1], cb[:, ec, 0:512],
                      start=(ec == 0), stop=(ec == DCH - 1))
            pe.matmul(ps_b[:, 512:513], bcol[:, ec:ec + 1], cb[:, ec, 512:513],
                      start=(ec == 0), stop=(ec == DCH - 1))
        tmm = pe.sig()
        dve.wait(tmm)
        dve.memset(b.bias_tilde[0:1, KPAD * wi:KPAD * (wi + 1)], 0.0)
        dve.sig()
        dve.tensor_copy(b.bias_tilde[0:1, KPAD * wi:KPAD * wi + KBINS], ps_b[:])
        tbt = dve.sig()
        pe.wait(tbt)
        tprev_dma = Tok(tld.sem, tld.val, tld.snap)
        tprev_dma = tev
    ctx.dump("l1a_btil", b.bias_tilde[:], [tbt])
    sp.wait(tbt)
    sp.dma_start(out=dram["btil"], in_=b.bias_tilde[:])
    tst = sp.sig_dma("dS")
    for e_ in ENGS:
        ctx.eng[e_].wait(tst)
    ctx.pop_phase()
    return None


def _complex_norm(ctx, b, re_src, im_src, re_dst, im_dst, width, deps):
    dve, act = ctx.dve, ctx.act
    n2 = b.t1[:, 0:width]
    t2 = b.mrep[:, 0:width]
    dve.wait_all(deps)
    dve.tensor_tensor(n2, re_src, re_src, ALU.mult)
    dve.tensor_tensor(t2, im_src, im_src, ALU.mult)
    dve.tensor_tensor(n2, n2, t2, ALU.add)
    tn = dve.sig()
    act.wait(tn)
    dve.tensor_single_scalar(n2, n2, 1e-30, ALU.max)
    tm_ = dve.sig()
    act.wait(tm_)
    act.activation(n2, n2, AF.Ln)
    act.activation(n2, n2, AF.Exp, scale=0.5)
    ts_ = act.sig()
    dve.wait(ts_)
    dve.tensor_single_scalar(n2, n2, 1e-8, ALU.add)
    tb_ = dve.sig()
    act.wait(tb_)
    act.activation(n2, n2, AF.Ln)
    act.activation(t2, n2, AF.Exp, scale=-1.0)
    tc_ = act.sig()
    dve.wait(tc_)
    dve.tensor_tensor(re_dst, re_src, t2, ALU.mult)
    dve.sig()
    dve.tensor_tensor(im_dst, im_src, t2, ALU.mult)
    return dve.sig()


L1A_PARTS = ((0, 1), (2, 3), (4,))


def emit_l1a(ctx, dram, b, xln_toks, seq):
    pe, dve, act, pool, sp = ctx.pe, ctx.dve, ctx.act, ctx.pool, ctx.sp
    onesT = ctx.phase_sbuf("l1a_onesT", [1, 512], BF16)
    utri = ctx.phase_sbuf("l1a_utri", [128, 128], BF16)
    carry = ctx.phase_sbuf("l1a_carry", [128, 12], F32)
    pool.memset(onesT[:], 1.0)
    tone = pool.sig()
    pool.dma_start(out=utri[:], in_=dram["utri"])
    tutri = pool.sig_dma("dW")
    zfirst = [True]
    ztoks = []
    for part_i, kcs in enumerate(L1A_PARTS):
        nk = len(kcs)
        W = 128 * nk
        ko = 128 * kcs[0]
        ctx.push_phase()
        btlp = ctx.phase_sbuf("l1a_btlp", [1, 6 * W], BF16)
        sp.dma_start(
            out=btlp[:].rearrange("u (w k) -> u w k", k=W),
            in_=dram["btil"].rearrange("u (w k) -> u w k", k=KPAD)[:, :, ko:ko + W])
        tbtl = sp.sig_dma("dW")
        xk = ctx.phase_sbuf("l1a_xk", [128, NT, 2 * W], BF16)
        _qre = b.rrep[:].bitcast(BF16).rearrange("p (a t) -> p a t", t=T)[:, 0:nk, :]
        _qim = ctx.phase_sbuf("l1a_qhi", [128, nk, T], BF16)

        def qh(ci):
            return _qre[:, ci, :] if ci < nk else _qim[:, ci - nk, :]
        ctx.push_phase()
        wkb = ctx.phase_sbuf("l1a_wkb", [128, DCH, 4, W], BF16)
        for j, wi in enumerate((2, 3, 4, 5)):
            sp.dma_start(out=wkb[:, :, j, :], in_=dram["wtil"][wi, :, ko:ko + W]
                         .rearrange("(c p) k -> p c k", p=128))
            twk = sp.sig_dma("dW")

        dve.wait(tbtl)
        dve.engine_nop()
        tbc = tbtl
        ps4d = [ctx.phase_psum("l1a_ps4a", [128, 4 * W], F32),
                ctx.phase_psum("l1a_ps4b", [128, 4 * W], F32)]
        kvtd = [b.sq[:].rearrange("p a w -> p (a w)")[:, 0:4 * W]
                .rearrange("p (a w) -> p a w", a=4),
                b.t1[:].bitcast(BF16)[:, 1024:1024 + 4 * W]
                .rearrange("p (a w) -> p a w", a=4)]
        ps_free = [None, None]
        kvt_free = [None, None]
        tbind_prev = None
        for j in range(NT):
            sl2 = j % 2
            ps4 = ps4d[sl2]
            kvt = kvtd[sl2]
            tsl = slice(2 + 128 * j, 2 + 128 * (j + 1))
            for dc in range(DCH):
                pe.wait(twk)
                pe.wait_all(xln_toks)
                pe.wait(tone)
                pe.wait(tbc)
                if dc == 0:
                    pe.wait(ps_free[sl2])
                wr_ = wkb[:, dc, :, :].rearrange("p a w -> p (a w)")
                for sg in range(0, 4 * W, 512):
                    pe.matmul(ps4[:, sg:sg + 512], b.xln[:, dc, tsl],
                              wr_[:, sg:sg + 512], start=(dc == 0), stop=False)
            for sg in range(0, 4 * W, 512):
                pe.matmul(ps4[:, sg:sg + 512], onesT[0:1, 0:128],
                          btlp[0:1, 2 * W + sg:2 * W + sg + 512],
                          start=False, stop=True)
            tmm = pe.sig()
            act.wait(tmm)
            act.wait(kvt_free[sl2])
            act.copy(kvt[:], ps4[:])
            ps_free[sl2] = act.sig()
            kv4 = kvt[:].rearrange("p (a b) w -> p a b w", b=2)
            res_ = kv4[:, :, 0, :]
            ims_ = kv4[:, :, 1, :]
            n2 = b.t1[:, 0:2 * W].rearrange("p (a w) -> p a w", a=2)
            t2 = b.mrep[:, 0:2 * W].rearrange("p (a w) -> p a w", a=2)
            dve.wait(ps_free[sl2])
            dve.tensor_tensor(n2, res_, res_, ALU.mult)
            dve.tensor_tensor(t2, ims_, ims_, ALU.mult)
            dve.tensor_tensor(n2, n2, t2, ALU.add)
            dve.tensor_single_scalar(n2, n2, 1e-30, ALU.max)
            tn_ = dve.sig()
            act.wait(tn_)
            act.activation(n2, n2, AF.Ln)
            act.activation(n2, n2, AF.Exp, scale=0.5)
            ts_ = act.sig()
            dve.wait(ts_)
            dve.tensor_single_scalar(n2, n2, 1e-8, ALU.add)
            tb_ = dve.sig()
            act.wait(tb_)
            act.activation(n2, n2, AF.Ln)
            act.activation(t2, n2, AF.Exp, scale=-1.0)
            tc_ = act.sig()
            dve.wait(tc_)
            dve.tensor_tensor(res_, res_, t2, ALU.mult)
            dve.tensor_tensor(ims_, ims_, t2, ALU.mult)
            dve.sig()
            ta = b.s16[:, 0:W]
            tb_ = b.s16[:, W:2 * W]
            dve.tensor_tensor(ta, kvt[:, 0, :], kvt[:, 2, :], ALU.mult)
            dve.tensor_tensor(tb_, kvt[:, 1, :], kvt[:, 3, :], ALU.mult)
            dve.tensor_tensor(xk[:, j, 0:W], ta, tb_, ALU.subtract)
            dve.tensor_tensor(ta, kvt[:, 0, :], kvt[:, 3, :], ALU.mult)
            dve.tensor_tensor(tb_, kvt[:, 1, :], kvt[:, 2, :], ALU.mult)
            dve.tensor_tensor(xk[:, j, W:2 * W], ta, tb_, ALU.add)
            tbind_prev = dve.sig()
            kvt_free[sl2] = tbind_prev
        ctx.barrier()
        ctx.pop_phase()
        ctx.push_phase()
        wqb = ctx.phase_sbuf("l1a_wqb", [128, 2, DCH, W], BF16)
        for j, wi in enumerate((0, 1)):
            sp.dma_start(out=wqb[:, j], in_=dram["wtil"][wi, :, ko:ko + W]
                         .rearrange("(c p) k -> p c k", p=128))
            twq = sp.sig_dma("dW")
        ps_q = ctx.phase_psum("l1a_psq", [128, T], F32)
        tq_free = None
        for ci in range(2 * nk):
            pl, kci = (0, ci) if ci < nk else (1, ci - nk)
            for dc in range(DCH):
                pe.wait(twq)
                pe.wait_all(xln_toks)
                if dc == 0:
                    pe.wait(tq_free)
                lh = wqb[:, pl, dc, 128 * kci:128 * (kci + 1)]
                for tg in range(T // 512):
                    pe.matmul(ps_q[:, 512 * tg:512 * (tg + 1)], lh,
                              b.xln[:, dc, 2 + 512 * tg:2 + 512 * (tg + 1)],
                              start=(dc == 0), stop=False)
            pe.wait(tbtl)
            seg = btlp[0:1, W * pl + 128 * kci:W * pl + 128 * (kci + 1)]
            for tg in range(T // 512):
                pe.matmul(ps_q[:, 512 * tg:512 * (tg + 1)], seg,
                          onesT[0:1, 0:512], start=False, stop=True)
            tmm = pe.sig()
            act.wait(tmm)
            act.copy(qh(ci), ps_q[:])
            tq_free = act.sig()
        tqn = []
        for kci in range(nk):
            tn = _complex_norm(ctx, b, qh(kci), qh(nk + kci),
                               qh(kci), qh(nk + kci), T, [tq_free])
            tqn.append(tn)
        ctx.barrier()
        ctx.pop_phase()
        ctx.push_phase()
        ps_c = [ctx.phase_psum(f"l1a_psc{i}", [128, 128], F32) for i in range(4)]
        ps_ts = ctx.phase_psum("l1a_psts", [128, 2 * nk, NT], F32)
        tsum = ctx.phase_sbuf("l1a_tsum", [128, 2 * nk, NT + 1], F32)
        onecol = ctx.phase_sbuf("l1a_onecol", [128, 1], BF16)
        pool.memset(onecol[:], 1.0)
        toc = pool.sig()
        pe.wait_all(tqn)
        pe.wait(toc)
        for ci in range(2 * nk):
            for j in range(NT):
                pe.matmul(ps_ts[:, ci, j:j + 1], xk[:, j, 128 * ci:128 * (ci + 1)],
                          onecol[:], start=True, stop=True)
        tts = pe.sig()
        tsum2 = ctx.phase_sbuf("l1a_tsum2", [128, 2 * nk, NT + 1], F32)
        dve.wait(tts)
        dve.memset(tsum[:, :, 0:1], 0.0)
        dve.memset(tsum2[:, :, 0:1], 0.0)
        dve.tensor_copy(tsum[:, :, 1:NT + 1], ps_ts[:, :, :])
        dve.sig()
        bufs = (tsum, tsum2)
        for si, sh in enumerate((1, 2, 4, 8)):
            a_, b2_ = bufs[si % 2], bufs[(si + 1) % 2]
            dve.tensor_copy(b2_[:, :, 1:sh + 1], a_[:, :, 1:sh + 1])
            dve.tensor_tensor(b2_[:, :, sh + 1:NT + 1], a_[:, :, sh + 1:NT + 1],
                              a_[:, :, 1:NT + 1 - sh], ALU.add)
            dve.sig()
        tsum = bufs[0]
        tpfx = dve.sig()
        tev4 = [None] * 4
        for ci in range(2 * nk):
            for j in range(NT):
                sl4 = (ci * NT + j) % 4
                pe.wait(tutri)
                pe.wait(tev4[sl4])
                pe.matmul(ps_c[sl4][:], xk[:, j, 128 * ci:128 * (ci + 1)], utri[:],
                          start=True, stop=True)
                tmm = pe.sig()
                dve.wait(tmm)
                dve.wait(tpfx)
                dve.tensor_single_scalar(xk[:, j, 128 * ci:128 * (ci + 1)],
                                         ps_c[sl4][:], tsum[:, ci, j:j + 1], ALU.add)
                tev4[sl4] = dve.sig()
        ctx.barrier()
        ctx.pop_phase()
        ctx.push_phase()
        cwb = ctx.phase_sbuf("l1a_cwb", [128, 2, nk, D], BF16)
        for j, nm in enumerate(("idftC", "idftS")):
            pool.dma_start(out=cwb[:, j], in_=dram[nm][ko:ko + W, :]
                           .rearrange("(c p) e -> p c e", p=128))
            tcw = pool.sig_dma("dW")
        ps_o = ctx.phase_psum("l1a_pso", [128, T], F32)
        gtmp = ctx.phase_sbuf("l1a_gtmp", [128, 2, T], BF16)
        gr = b.s16[:, 0:T]
        gi = b.s16[:, T:2 * T]
        tz_ev = None
        for kci in range(nk):
            sre = xk[:, :, 128 * kci:128 * (kci + 1)]
            sim = xk[:, :, 128 * (nk + kci):128 * (nk + kci + 1)]
            qre = qh(kci).rearrange("p (j q) -> p j q", q=128)
            qim = qh(nk + kci).rearrange("p (j q) -> p j q", q=128)
            grv = gr.rearrange("p (j q) -> p j q", q=128)
            giv = gi.rearrange("p (j q) -> p j q", q=128)
            ta = gtmp[:, 0, :].rearrange("p (j q) -> p j q", q=128)
            tb_ = gtmp[:, 1, :].rearrange("p (j q) -> p j q", q=128)
            dve.wait(tz_ev)
            dve.tensor_tensor(ta, sre, qre, ALU.mult)
            dve.tensor_tensor(tb_, sim, qim, ALU.mult)
            dve.tensor_tensor(grv, ta, tb_, ALU.add)
            dve.tensor_tensor(ta, sim, qre, ALU.mult)
            dve.tensor_tensor(tb_, sre, qim, ALU.mult)
            dve.tensor_tensor(giv, ta, tb_, ALU.subtract)
            tg_ = dve.sig()
            for m in range(DCH):
                for pl, g_ in ((0, gr), (1, gi)):
                    for tg2 in range(T // 512):
                        pe.wait(tcw)
                        pe.wait(tg_)
                        if pl == 0 and tg2 == 0:
                            pe.wait(tz_ev)
                        pe.matmul(ps_o[:, 512 * tg2:512 * (tg2 + 1)],
                                  cwb[:, pl, kci, 128 * m:128 * (m + 1)],
                                  g_[:, 512 * tg2:512 * (tg2 + 1)],
                                  start=(pl == 0), stop=(pl == 1))
                tmm = pe.sig()
                dve.wait(tmm)
                if zfirst[0]:
                    dve.tensor_copy(b.zbuf[:, m, :], ps_o[:])
                else:
                    dve.tensor_tensor(b.zbuf[:, m, :], b.zbuf[:, m, :], ps_o[:], ALU.add)
                tz_ev = dve.sig()
                if part_i == len(L1A_PARTS) - 1 and kci == nk - 1:
                    ztoks.append(tz_ev)
            zfirst[0] = False
        ctx.barrier()
        ctx.pop_phase()
        ctx.pop_phase()
    ctx.dump("l1a_z", b.zbuf[:], ztoks)
    return ztoks


def emit_launch(ctx, dram, launch_blocks):
    b = alloc_persistent(ctx, launch_blocks)
    ctoks = emit_load_const(ctx, dram, b)
    for e in ENGS:
        ctx.eng[e].wait_all(ctoks)
    if 2 in launch_blocks:
        emit_l1a_fold(ctx, dram, b)
        ctx.barrier()
        ctx.end_phase()
    for seq in range(NSEQ):
        xtok = emit_load_x(ctx, dram, b, seq)
        ctx.barrier()
        ln_toks = emit_preln(ctx, b, [xtok])
        ctx.barrier()
        ctx.end_phase()
        for i, blk in enumerate(launch_blocks):
            if blk == 0:
                ztoks = emit_l0a(ctx, dram, b, ln_toks)
            elif blk == 1:
                ztoks = emit_l0f(ctx, dram, b, ln_toks)
            elif blk == 2:
                ztoks = emit_l1a(ctx, dram, b, ln_toks, seq)
            elif blk == 3:
                wt_toks = emit_l1f_weights(ctx, dram, b)
                ztoks = emit_l1f(ctx, b, ln_toks, wt_toks)
            else:
                raise NotImplementedError(f"block {blk}")
            ctx.barrier()
            ctx.end_phase()
            if i < len(launch_blocks) - 1:
                ln_toks = emit_transition(ctx, b, ztoks)
            else:
                xdeps = emit_postln_residual(ctx, b, ztoks)
            ctx.barrier()
            ctx.end_phase()
        st = emit_store_x(ctx, dram, b, seq, xdeps)
        for e in ENGS:
            ctx.eng[e].wait(st)
    ctx.sp.wait(Tok("dO", ctx.dma_counts["dO"], {}))


def build_launch(g, launch_blocks):
    nc = bass.Bass("TRN2", target_bir_lowering=False)
    dram = declare_dram(nc, g, launch_blocks)
    stack = ExitStack()
    with stack:
        ctx = Ctx(nc, stack)
        emit_launch(ctx, dram, launch_blocks)
        block = stack.enter_context(nc.Block())
        ctx.replay(block)
    return nc


LAUNCH_PLAN = [[0, 1, 2, 3]]


def kernel(**inputs):
    g, shards = stage_inputs(inputs)
    cur = shards
    for launch_blocks in LAUNCH_PLAN:
        nc = build_launch(g, launch_blocks)
        in_maps = []
        need = {"xin"}
        need.update(("ident", "ones128", "onesrow"))
        for blk in launch_blocks:
            need.update(BLOCK_INPUTS[blk])
        for c in range(NCORES):
            m = {k: g[k] for k in need if k != "xin"}
            m["xin"] = cur[c]
            in_maps.append(m)
        res = run_bass_kernel_spmd(nc, in_maps, core_ids=list(range(NCORES)))
        cur = [r["xout"] for r in res.results]
    out = np.empty((16, T, D), np.float32)
    for c in range(NCORES):
        out[2 * c:2 * c + 2] = cur[c].transpose(0, 2, 1)
    return out


# version 38
# speedup vs baseline: 1.0315x; 1.0315x over previous
import numpy as np
from contextlib import ExitStack
import concourse.bass as bass
import concourse.mybir as mybir
from concourse.bass_utils import run_bass_kernel_spmd

F32 = mybir.dt.float32
BF16 = mybir.dt.bfloat16
I32 = mybir.dt.int32
AF = mybir.ActivationFunctionType
ALU = mybir.AluOpType
AX = mybir.AxisListType

T = 2048
D = 1024
NCORES = 8
NSEQ = 2
DCH = D // 128
NT = T // 128
KBINS = 513
KPAD = 640
KCH = KPAD // 128
LN_EPS = 1e-5

ENGS = ("pe", "dve", "act", "pool", "sp")



class Tok:
    __slots__ = ("sem", "val", "snap")

    def __init__(self, sem, val, snap):
        self.sem, self.val, self.snap = sem, val, snap


_NO_AUTO = {"dma_start", "indirect_dma_start", "wait_ge", "nop", "sem_inc",
            "print", "dma_start_transpose"}


class EngCtx:

    _FIELDS = ("ctx", "name", "sem_name", "count", "clock", "recs", "auto",
               "self_waited", "uninced")

    def __init__(self, ctx, name, auto):
        self.ctx, self.name = ctx, name
        self.sem_name = "s_" + name
        self.count = 0
        self.clock = {}
        self.recs = []
        self.auto = auto
        self.self_waited = 0
        self.uninced = False

    def _rec(self, fn, auto_inced=False):
        entry = [fn, [], auto_inced]
        self.recs.append(entry)
        return entry

    def __getattr__(self, attr):
        if attr.startswith("_") or attr in EngCtx._FIELDS:
            raise AttributeError(attr)

        def wrap(*a, **k):
            if self.auto and attr not in _NO_AUTO and self.count > self.self_waited:
                sem, val = self.sem_name, self.count
                self._rec(lambda eng: eng.wait_ge(self.ctx.sems[sem], val))
                self.self_waited = val
            auto_inc = self.auto and attr not in _NO_AUTO
            e = self._rec(lambda eng: getattr(eng, attr)(*a, **k), auto_inc)
            if auto_inc:
                self.count += 1
                e[1].append((self.sem_name, 1))
                self.clock[self.sem_name] = self.count
            elif attr not in _NO_AUTO:
                self.uninced = True
            return None
        return wrap

    def wait(self, tok):
        if tok is None:
            return
        if tok.sem == self.sem_name:
            if tok.val <= self.self_waited:
                return
            sem, val = tok.sem, tok.val
            self._rec(lambda eng: eng.wait_ge(self.ctx.sems[sem], val))
            self.self_waited = val
            return
        if self.clock.get(tok.sem, 0) >= tok.val:
            return
        sem, val = tok.sem, tok.val
        self._rec(lambda eng: eng.wait_ge(self.ctx.sems[sem], val))
        for s, v in tok.snap.items():
            if s == self.sem_name:
                if v > self.self_waited:
                    self.self_waited = v
                continue
            if self.clock.get(s, 0) < v:
                self.clock[s] = v
        if self.clock.get(tok.sem, 0) < tok.val:
            self.clock[tok.sem] = tok.val

    def wait_all(self, toks):
        for t in toks:
            self.wait(t)

    def sig(self):
        assert self.recs, "sig() with no recorded instruction"
        if self.recs[-1][2] or any(s == self.sem_name for s, _ in self.recs[-1][1]):
            return Tok(self.sem_name, self.count, dict(self.clock))
        self.count += 1
        self.recs[-1][1].append((self.sem_name, 1))
        self.clock[self.sem_name] = self.count
        self.uninced = False
        return Tok(self.sem_name, self.count, dict(self.clock))

    def sig_dma(self, stream):
        c = self.ctx.dma_counts[stream] = self.ctx.dma_counts.get(stream, 0) + 16
        self.recs[-1][1].append((stream, 16))
        snap = dict(self.clock)
        snap[stream] = c
        return Tok(stream, c, snap)


DEBUG = False


class Ctx:
    def __init__(self, nc, stack):
        self.dbg = []
        self.dbg_names = {}
        self.nc = nc
        self.stack = stack
        self.sems = {}
        self.dma_counts = {}
        self.eng = {n: EngCtx(self, n, auto=(n in ("dve", "act", "pool")))
                    for n in ENGS}
        for n in ENGS:
            self.sems["s_" + n] = stack.enter_context(nc.semaphore("s_" + n))
        for s in ("dW", "dX", "dO", "dS"):
            self.sems[s] = stack.enter_context(nc.semaphore(s))
        self.pe = self.eng["pe"]
        self.dve = self.eng["dve"]
        self.act = self.eng["act"]
        self.pool = self.eng["pool"]
        self.sp = self.eng["sp"]
        self._sbuf_guards = []
        self._psum_guards = []
        self._tensors = {}
        self._phase_guards = []
        self._phase_ctr = 0
        self._phase_stack = []

    def sbuf(self, name, shape, dtype, side="left"):
        if name in self._tensors:
            return self._tensors[name]
        g = self.nc.sbuf_tensor(name, shape, dtype)
        h = g.__enter__()
        self._sbuf_guards.append(g)
        self._tensors[name] = h
        return h

    def psum(self, name, shape, dtype=F32):
        if name in self._tensors:
            return self._tensors[name]
        g = self.nc.psum_tensor(name, shape, dtype)
        h = g.__enter__()
        self._psum_guards.append(g)
        self._tensors[name] = h
        return h

    def barrier(self):
        toks = []
        for n in ENGS:
            e = self.eng[n]
            if e.uninced:
                toks.append(e.sig())
            elif e.count > 0:
                toks.append(Tok(e.sem_name, e.count, dict(e.clock)))
        for s, c in self.dma_counts.items():
            toks.append(Tok(s, c, {s: c}))
        for n in ENGS:
            self.eng[n].wait_all(toks)

    def phase_psum(self, name, shape, dtype=F32):
        self._phase_ctr += 1
        g = self.nc.psum_tensor(f"{name}_{self._phase_ctr}", shape, dtype)
        h = g.__enter__()
        self._phase_guards.append(g)
        return h

    def phase_sbuf(self, name, shape, dtype):
        self._phase_ctr += 1
        g = self.nc.sbuf_tensor(f"{name}_{self._phase_ctr}", shape, dtype)
        h = g.__enter__()
        self._phase_guards.append(g)
        return h

    def push_phase(self):
        self._phase_stack.append(len(self._phase_guards))

    def pop_phase(self):
        k = self._phase_stack.pop() if self._phase_stack else 0
        for g in reversed(self._phase_guards[k:]):
            g.__exit__(None, None, None)
        del self._phase_guards[k:]

    def end_phase(self):
        assert not self._phase_stack, "unbalanced push_phase"
        for g in reversed(self._phase_guards):
            g.__exit__(None, None, None)
        self._phase_guards = []

    def dump(self, name, ap, deps=()):
        if not DEBUG or name in self.dbg_names:
            return
        self.dbg_names[name] = True
        shape = list(ap.shape)
        dt = ap.dtype
        dst = self.nc.dram_tensor("dbg_" + name, shape, dt, kind="ExternalOutput").ap()
        self.sp.wait_all(list(deps))
        self.sp.dma_start(out=dst, in_=ap)
        self.sp.sig_dma("dO")

    def replay(self, block):
        raws = {"pe": block.tensor, "dve": block.vector, "act": block.scalar,
                "pool": block.gpsimd, "sp": block.sync}

        def make_body(name):
            def body(eng):
                for fn, incs, _auto in self.eng[name].recs:
                    inst = fn(eng)
                    for (sem, v) in incs:
                        assert inst is not None, f"then_inc on None inst ({name})"
                        inst.then_inc(self.sems[sem], v)
            return body
        for n in ENGS:
            if self.eng[n].recs:
                raws[n](make_body(n))




def host_constants():
    c = {}
    e = np.arange(D)[:, None]
    k = np.arange(KBINS)[None, :]
    ang = 2.0 * np.pi * ((e * k) % D).astype(np.float64) / D
    c["dftC"] = np.cos(ang).astype(np.float32)
    c["dftS"] = (-np.sin(ang)).astype(np.float32)
    w = np.full(KBINS, 2.0)
    w[0] = 1.0
    w[512] = 1.0
    ki = np.arange(KBINS)[:, None]
    ee = np.arange(D)[None, :]
    ang2 = 2.0 * np.pi * ((ki * ee) % D).astype(np.float64) / D
    cw = (w[:, None] * np.cos(ang2) / D).astype(np.float32)
    sw = (-w[:, None] * np.sin(ang2) / D).astype(np.float32)
    c["idftC"] = np.zeros((KPAD, D), np.float32)
    c["idftC"][:KBINS] = cw
    c["idftS"] = np.zeros((KPAD, D), np.float32)
    c["idftS"][:KBINS] = sw
    c["ident"] = np.eye(128, dtype=np.float32)
    tri = (np.arange(128)[:, None] <= np.arange(128)[None, :]).astype(np.float32)
    c["trimask"] = tri
    c["utri"] = tri.copy()
    c["ones128"] = np.ones((128, 128), np.float32)
    c["onesrow"] = np.ones((1, 128), np.float32)
    c["shamt"] = np.tile(np.array([0, 2, 4, 6], np.int32), 32)[:, None].copy()
    c["mask3"] = np.full((128, 1), 3, np.int32)
    bandmask = np.zeros((128, 16), np.float32)
    for h in range(16):
        bandmask[8 * h:8 * h + 6, h] = 1.0
    c["bandmask"] = bandmask
    tri16 = np.tile(tri, (1, 16))
    c["trimask16"] = np.ascontiguousarray(tri16)
    half0 = np.zeros((1, 128), np.float32)
    half0[0, :64] = 1.0
    half1 = np.zeros((1, 128), np.float32)
    half1[0, 64:] = 1.0
    c["half0"] = half0
    c["half1"] = half1
    c["ones16"] = np.ones((128, 16), np.float32)
    return c


def stage_inputs(inputs):
    g = {}
    g["l0a_wq"] = np.ascontiguousarray(
        np.concatenate([inputs["L0_attn_W1_q"], inputs["L0_attn_W2_q"]], 0).T)
    g["l0a_wk"] = np.ascontiguousarray(
        np.concatenate([inputs["L0_attn_W1_k"], inputs["L0_attn_W2_k"]], 0).T)
    g["l0a_wv"] = np.ascontiguousarray(np.asarray(inputs["L0_attn_Wv"]).T)
    g["l0a_bv"] = np.asarray(inputs["L0_attn_bv"]).reshape(1, D)
    g["l0a_wo"] = np.ascontiguousarray(np.asarray(inputs["L0_attn_Wo"]).T)
    g["l0a_bo"] = np.asarray(inputs["L0_attn_bo"]).reshape(1, D)
    g["l0f_w1"] = np.ascontiguousarray(np.concatenate(
        [np.asarray(inputs["L0_ffn_W1"]), np.asarray(inputs["L0_ffn_Ws1"])[None]], 0))
    g["l0f_w2"] = np.ascontiguousarray(np.concatenate(
        [np.asarray(inputs["L0_ffn_W2"]), np.asarray(inputs["L0_ffn_Ws2"])[None]], 0))
    g["l0f_wr"] = np.ascontiguousarray(np.asarray(inputs["L0_ffn_Wr"]))
    g["l1a_wq"] = np.ascontiguousarray(np.asarray(inputs["L1_attn_Wq"]))
    g["l1a_wk"] = np.ascontiguousarray(np.asarray(inputs["L1_attn_Wk"]))
    g["l1a_wv"] = np.ascontiguousarray(np.asarray(inputs["L1_attn_Wv"]))
    g["l1a_bq"] = np.ascontiguousarray(np.asarray(inputs["L1_attn_bq"]).reshape(DCH, 128).T)
    g["l1a_bk"] = np.ascontiguousarray(np.asarray(inputs["L1_attn_bk"]).reshape(DCH, 128).T)
    g["l1a_bv"] = np.ascontiguousarray(np.asarray(inputs["L1_attn_bv"]).reshape(DCH, 128).T)
    pk1 = np.asarray(inputs["L1_ffn_packed_weight"]).reshape(256, D)
    g["l1f_pk"] = np.ascontiguousarray(np.tile(pk1.T, (1, 4)))
    g["l1f_ws"] = np.asarray(inputs["L1_ffn_weight_scale"]).reshape(1, 1).astype(np.float32)
    g["l1f_bias"] = np.ascontiguousarray(
        np.asarray(inputs["L1_ffn_bias"]).reshape(DCH, 128).T)
    g.update(host_constants())
    x = np.asarray(inputs["L0_attn_x"])
    shards = [np.ascontiguousarray(x[2 * c:2 * c + 2].transpose(0, 2, 1))
              for c in range(NCORES)]
    return g, shards


BLOCK_INPUTS = {
    0: ("l0a_wq", "l0a_wk", "l0a_wv", "l0a_bv", "l0a_wo", "l0a_bo",
        "trimask16", "bandmask", "half0", "half1"),
    1: ("l0f_w1", "l0f_w2", "l0f_wr"),
    2: ("l1a_wq", "l1a_wk", "l1a_wv", "l1a_bq", "l1a_bk", "l1a_bv",
        "dftC", "dftS", "idftC", "idftS", "utri"),
    3: ("l1f_pk", "l1f_ws", "l1f_bias", "shamt", "mask3"),
}


def declare_dram(nc, g, launch_blocks):
    d = {}

    def inp(name):
        arr = g[name]
        dt = I32 if arr.dtype == np.int32 else F32
        d[name] = nc.dram_tensor(name, list(arr.shape), dt, kind="ExternalInput").ap()

    d["xin"] = nc.dram_tensor("xin", [NSEQ, D, T], F32, kind="ExternalInput").ap()
    d["xout"] = nc.dram_tensor("xout", [NSEQ, D, T], F32, kind="ExternalOutput").ap()
    for k in ("ident", "ones128", "onesrow"):
        inp(k)
    names = set()
    for blk in launch_blocks:
        names.update(BLOCK_INPUTS[blk])
    for k in sorted(names):
        inp(k)
    if 2 in launch_blocks:
        d["wtil"] = nc.dram_tensor("wtil", [6, D, KPAD], BF16, kind="Internal").ap()
        d["btil"] = nc.dram_tensor("btil", [1, 6 * KPAD], BF16, kind="Internal").ap()
    return d


class Bufs:
    pass


def alloc_persistent(ctx, launch_blocks=()):
    b = Bufs()
    b.x = ctx.sbuf("x_m", [128, DCH, T], F32)
    b.xln = ctx.sbuf("xln", [128, DCH, T + 2], BF16)
    b.ident = ctx.sbuf("c_ident", [128, 128], BF16)
    b.ones128 = ctx.sbuf("c_ones", [128, 128], BF16)
    b.onesrow = ctx.sbuf("c_onesrow", [1, 128], BF16)
    b.mrep = ctx.sbuf("mrep", [128, T], F32)
    b.rrep = ctx.sbuf("rrep", [128, T], F32)
    b.t1 = ctx.sbuf("t1rep", [128, T], F32)
    b.sq = ctx.sbuf("sqch", [128, 2, 512], BF16)
    b.s16 = ctx.sbuf("s16", [128, 2 * T], BF16)
    b.zbuf = ctx.sbuf("zbuf", [128, DCH, T], BF16)
    b.cpool = ctx.sbuf("cpool", [128, 8], F32)
    return b


CONST_VALS = (-1.0, LN_EPS, 1e-8)


def emit_load_const(ctx, dram, b):
    sp, pool = ctx.sp, ctx.pool
    toks = []
    for name, dst in (("ident", b.ident), ("ones128", b.ones128), ("onesrow", b.onesrow)):
        pool.dma_start(out=dst[:], in_=dram[name])
        toks.append(pool.sig_dma("dW"))
    for j, v in enumerate(CONST_VALS):
        pool.memset(b.cpool[:, j:j + 1], v)
        ctx.nc.const_aps.register(F32, v, b.cpool[:, j:j + 1])
    toks.append(pool.sig())
    return toks


def emit_load_x(ctx, dram, b, seq):
    sp = ctx.sp
    sp.dma_start(out=b.x[:], in_=dram["xin"][seq].rearrange("(c p) t -> p c t", p=128))
    return sp.sig_dma("dX")


def emit_store_x(ctx, dram, b, seq, deps):
    sp = ctx.sp
    sp.wait_all(deps)
    sp.dma_start(out=dram["xout"][seq].rearrange("(c p) t -> p c t", p=128), in_=b.x[:])
    return sp.sig_dma("dO")




def emit_stats_rows(ctx, b, src_get, psA, psB):
    pe, dve, act = ctx.pe, ctx.dve, ctx.act
    last = None
    sq_read = None
    for tg in range(T // 512):
        sl = slice(512 * tg, 512 * tg + 512)
        for c in range(DCH):
            src, rdy = src_get(c)
            pe.wait(rdy)
            if c == 0 and last is not None:
                pe.wait(last)
            pe.matmul(psA[:, 0:512], b.ones128[:], src[:, sl],
                      start=(c == 0), stop=(c == DCH - 1))
            act.wait(rdy)
            if c == 0 and last is not None:
                act.wait(last)
            slq = c % 2
            if sq_read is not None:
                act.wait(sq_read)
            act.activation(b.sq[:, slq, :], src[:, sl], AF.Square)
            sqt = act.sig()
            pe.wait(sqt)
            pe.matmul(psB[:, 0:512], b.ones128[:], b.sq[:, slq, :],
                      start=(c == 0), stop=(c == DCH - 1))
            if c >= 1:
                sq_read = pe.sig()
        tokp = pe.sig()
        dve.wait(tokp)
        dve.tensor_single_scalar(b.mrep[:, sl], psA[:, 0:512], 1.0 / D, ALU.mult)
        dve.tensor_single_scalar(b.t1[:, sl], psB[:, 0:512], 1.0 / D, ALU.mult)
        last = dve.sig()
    dve.tensor_tensor(b.rrep[:], b.mrep[:], b.mrep[:], ALU.mult)
    dve.tensor_tensor(b.t1[:], b.t1[:], b.rrep[:], ALU.subtract)
    tokv = dve.sig()
    act.wait(tokv)
    act.activation(b.t1[:], b.t1[:], AF.Ln, bias=LN_EPS)
    act.activation(b.rrep[:], b.t1[:], AF.Exp, scale=-0.5)
    last = act.sig()
    pe.wait(last)
    return last


def emit_preln(ctx, b, deps):
    pe, dve, act, pool = ctx.pe, ctx.dve, ctx.act, ctx.pool
    psA = ctx.phase_psum("lnA", [128, 512], F32)
    psB = ctx.phase_psum("lnB", [128, 512], F32)
    pool.wait_all(deps)
    pool.memset(b.xln[:, :, 0:2], 0.0)
    mz = pool.sig()
    ctoks = {}
    for c in range(DCH):
        act.wait_all(deps)
        act.copy(b.xln[:, c, 2:], b.x[:, c, :])
        ctoks[c] = act.sig()
    st = emit_stats_rows(ctx, b, lambda c: (b.xln[:, c, 2:], ctoks[c]), psA, psB)
    out = []
    for c in range(DCH):
        dve.wait(st)
        dve.wait_all(deps)
        dve.tensor_tensor(b.t1[:], b.x[:, c, :], b.mrep[:], ALU.subtract)
        dve.tensor_tensor(b.xln[:, c, 2:], b.t1[:], b.rrep[:], ALU.mult)
        out.append(dve.sig())
    out.append(mz)
    ctx.dump("xln", b.xln[:], out)
    ctx.dump("mrep", b.mrep[:], out)
    ctx.dump("rrep", b.rrep[:], out)
    return out


def _z_src_get(b, z_toks):
    if not isinstance(z_toks, (list, tuple)):
        z_toks = [z_toks]
    if len(z_toks) == DCH:
        return lambda c: (b.zbuf[:, c, :], z_toks[c])
    if len(z_toks) == 4 * DCH:
        return lambda c: (b.zbuf[:, c, :], z_toks[4 * c + 3])
    return lambda c: (b.zbuf[:, c, :], z_toks[-1])


def emit_transition(ctx, b, z_toks):
    pe, dve, act, pool = ctx.pe, ctx.dve, ctx.act, ctx.pool
    psA = ctx.phase_psum("trA", [128, 512], F32)
    psB = ctx.phase_psum("trB", [128, 512], F32)
    psC = ctx.phase_psum("trC", [128, 512], F32)
    psD = ctx.phase_psum("trD", [128, 512], F32)
    st = emit_stats_rows(ctx, b, _z_src_get(b, z_toks), psA, psB)
    pool.memset(b.xln[:, :, 0:2], 0.0)
    mz = pool.sig()
    ctoks = {}
    for c in range(DCH):
        dve.wait(st)
        dve.tensor_tensor(b.t1[:], b.zbuf[:, c, :], b.mrep[:], ALU.subtract)
        dve.tensor_tensor(b.t1[:], b.t1[:], b.rrep[:], ALU.mult)
        dve.tensor_tensor(b.x[:, c, :], b.x[:, c, :], b.t1[:], ALU.add)
        xt = dve.sig()
        act.wait(xt)
        act.copy(b.xln[:, c, 2:], b.x[:, c, :])
        ctoks[c] = act.sig()
    st2 = emit_stats_rows(ctx, b, lambda c: (b.xln[:, c, 2:], ctoks[c]), psC, psD)
    out = []
    for c in range(DCH):
        dve.wait(st2)
        dve.tensor_tensor(b.t1[:], b.x[:, c, :], b.mrep[:], ALU.subtract)
        dve.tensor_tensor(b.xln[:, c, 2:], b.t1[:], b.rrep[:], ALU.mult)
        out.append(dve.sig())
    out.append(mz)
    return out


def emit_postln_residual(ctx, b, z_toks):
    pe, dve, act, pool = ctx.pe, ctx.dve, ctx.act, ctx.pool
    psA = ctx.phase_psum("plA", [128, 512], F32)
    psB = ctx.phase_psum("plB", [128, 512], F32)
    st = emit_stats_rows(ctx, b, _z_src_get(b, z_toks), psA, psB)
    ctx.dump("pl_mrep", b.mrep[:], [st])
    ctx.dump("pl_rrep", b.rrep[:], [st])
    out = []
    for c in range(DCH):
        dve.wait(st)
        dve.tensor_tensor(b.t1[:], b.zbuf[:, c, :], b.mrep[:], ALU.subtract)
        dve.tensor_tensor(b.t1[:], b.t1[:], b.rrep[:], ALU.mult)
        dve.tensor_tensor(b.x[:, c, :], b.x[:, c, :], b.t1[:], ALU.add)
        out.append(dve.sig())
    ctx.dump("pl_x", b.x[:], out)
    return out




def emit_l1f_weights(ctx, dram, b):
    dve, act, sp = ctx.dve, ctx.act, ctx.sp
    b.l1f_wt = ctx.phase_sbuf("l1f_wt", [128, DCH, D], BF16)
    pk_s = ctx.phase_sbuf("l1f_pks", [128, 2, D], I32)
    sh_s = ctx.phase_sbuf("l1f_sh", [128, 1], I32)
    m3_s = ctx.phase_sbuf("l1f_m3", [128, 1], I32)
    b.l1f_ws = ctx.phase_sbuf("l1f_wss", [1, 1], F32)
    b.l1f_bias = ctx.phase_sbuf("l1f_bs", [128, DCH], F32)
    sp.dma_start(out=sh_s[:], in_=dram["shamt"])
    sp.sig_dma("dW")
    sp.dma_start(out=m3_s[:], in_=dram["mask3"])
    sp.sig_dma("dW")
    sp.dma_start(out=b.l1f_ws[:], in_=dram["l1f_ws"])
    sp.sig_dma("dW")
    sp.dma_start(out=b.l1f_bias[:], in_=dram["l1f_bias"])
    tl = sp.sig_dma("dW")
    pkv = dram["l1f_pk"].rearrange("(c p) o -> c p o", p=128)
    toks = []
    prev = {}
    for c in range(DCH):
        sl = c % 2
        if prev.get(sl) is not None:
            sp.wait(prev[sl])
        sp.dma_start(out=pk_s[:, sl, :], in_=pkv[c])
        tp = sp.sig_dma("dW")
        dve.wait(tp)
        dve.wait(tl)
        for s4 in range(1, 4):
            dve.tensor_single_scalar(pk_s[:, sl, 256 * s4:256 * (s4 + 1)],
                                     pk_s[:, sl, 256 * s4:256 * (s4 + 1)],
                                     2 * s4, ALU.logical_shift_right)
            dve.sig()
        dve.tensor_single_scalar(pk_s[:, sl, :], pk_s[:, sl, :], m3_s[:, 0:1],
                                 ALU.bitwise_and)
        tc = dve.sig()
        act.wait(tc)
        act.activation(b.l1f_wt[:, c, :], pk_s[:, sl, :], AF.Identity, bias=-1.0)
        ta = act.sig()
        prev[sl] = ta
        toks.append(ta)
    return toks


def emit_l1f(ctx, b, xln_toks, wt_toks):
    pe, dve, act, pool, sp = ctx.pe, ctx.dve, ctx.act, ctx.pool, ctx.sp
    xt = b.xln
    act.wait_all(xln_toks)
    act.activation(b.t1[:], xt[:, 0, 2:], AF.Abs)
    t_a = act.sig()
    dve.wait(t_a)
    for c in range(1, DCH):
        act.wait_all(xln_toks)
        act.activation(b.rrep[:], xt[:, c, 2:], AF.Abs)
        t_a = act.sig()
        dve.wait(t_a)
        dve.tensor_max(b.t1[:], b.t1[:], b.rrep[:])
        t_d = dve.sig()
        act.wait(t_d)
    fold_bf = b.s16[:, 0:T]
    dve.tensor_copy(fold_bf, b.t1[:])
    tfold = dve.sig()
    ps_t = ctx.phase_psum("l1f_pst", [128, 128], BF16)
    cols = ctx.phase_sbuf("l1f_cols", [128, NT], F32)
    colsb = ctx.phase_sbuf("l1f_colsb", [128, NT], BF16)
    rows = ctx.phase_sbuf("l1f_rows", [NT, 128], F32)
    tcol = None
    for j in range(NT):
        pe.wait(tfold)
        if tcol is not None:
            pe.wait(tcol)
        pe.transpose(ps_t[:], fold_bf[:, 128 * j:128 * j + 128], b.ident[:])
        tt = pe.sig()
        dve.wait(tt)
        dve.reduce_max(cols[:, j:j + 1], ps_t[:], AX.X)
        tcol = dve.sig()
    dve.tensor_copy(colsb[:], cols[:])
    tcb = dve.sig()
    ps_r = ctx.phase_psum("l1f_psr", [128, 128], BF16)
    pe.wait(tcb)
    pe.transpose(ps_r[0:NT, :], colsb[:], b.ident[:])
    tr = pe.sig()
    dve.wait(tr)
    dve.tensor_copy(rows[:], ps_r[0:NT, :])
    trows = dve.sig()
    amaxrow = b.t1[0:1, 0:T]
    asrow = b.rrep[0:1, 0:T]
    asrow_bf = b.s16[0:1, T:T + T]
    wsasrow_bf = b.s16[0:1, 0:T]
    sp.wait(trows)
    sp.dma_start(out=amaxrow, in_=rows[:])
    tdma = sp.sig_dma("dS")
    dve.wait(tdma)
    dve.tensor_single_scalar(amaxrow, amaxrow, 1e-5, ALU.max)
    dve.sig()
    dve.reciprocal(asrow, amaxrow)
    dve.sig()
    dve.tensor_single_scalar(asrow, asrow, 127.0, ALU.mult)
    dve.sig()
    dve.tensor_copy(asrow_bf, asrow)
    tas = dve.sig()
    dve.tensor_single_scalar(amaxrow, amaxrow, b.l1f_ws[0:1, 0:1], ALU.mult)
    dve.sig()
    dve.tensor_single_scalar(amaxrow, amaxrow, 1.0 / 127.0, ALU.mult)
    dve.sig()
    dve.tensor_copy(wsasrow_bf, amaxrow)
    twsas = dve.sig()
    ps_bc = ctx.phase_psum("l1f_psbc", [128, 512], F32)
    asrep = b.mrep
    ev = None
    for tg in range(T // 512):
        sl = slice(512 * tg, 512 * tg + 512)
        pe.wait(tas)
        if ev is not None:
            pe.wait(ev)
        pe.matmul(ps_bc[:], b.onesrow[:], asrow_bf[0:1, sl], start=True, stop=True)
        t_ = pe.sig()
        dve.wait(t_)
        dve.tensor_copy(asrep[:, sl], ps_bc[:])
        ev = dve.sig()
    ctx.dump("asrep", asrep[:], [ev])
    qi = ctx.phase_sbuf("l1f_qi", [128, T], I32)
    qtoks = []
    for c in range(DCH):
        dve.wait(ev)
        dve.wait_all(xln_toks)
        dve.tensor_tensor(b.t1[:], xt[:, c, 2:], asrep[:], ALU.mult)
        dve.sig()
        dve.tensor_scalar(b.t1[:], b.t1[:], -128.0, 127.0, ALU.max, ALU.min)
        dve.sig()
        dve.tensor_copy(qi[:], b.t1[:])
        tq = dve.sig()
        act.wait(tq)
        act.copy(xt[:, c, 2:], qi[:])
        qtoks.append(act.sig())
    ctx.dump("qbuf", xt[:], qtoks)
    wsasrep = b.s16[:, T:2 * T]
    ev2 = None
    for tg in range(T // 512):
        sl = slice(512 * tg, 512 * tg + 512)
        pe.wait(twsas)
        pe.wait(ev2 if ev2 is not None else ev)
        pe.matmul(ps_bc[:], b.onesrow[:], wsasrow_bf[0:1, sl], start=True, stop=True)
        t_ = pe.sig()
        dve.wait(t_)
        dve.tensor_copy(wsasrep[:, sl], ps_bc[:])
        ev2 = dve.sig()
    trep = ev2
    ps_z = [ctx.phase_psum("l1f_z0", [128, 512], F32), ctx.phase_psum("l1f_z1", [128, 512], F32)]
    ev_prev = [None, None]
    ztoks = []
    for m in range(DCH):
        for tg in range(T // 512):
            sl = slice(512 * tg, 512 * tg + 512)
            idx = (m * (T // 512) + tg) % 2
            pz = ps_z[idx]
            for c in range(DCH):
                pe.wait(qtoks[c])
                pe.wait_all(wt_toks)
                if c == 0 and ev_prev[idx] is not None:
                    pe.wait(ev_prev[idx])
                pe.matmul(pz[:], b.l1f_wt[:, c, 128 * m:128 * (m + 1)],
                          xt[:, c, 2 + 512 * tg:2 + 512 * (tg + 1)],
                          start=(c == 0), stop=(c == DCH - 1))
            tmm = pe.sig()
            dve.wait(tmm)
            dve.wait(trep)
            dve.tensor_tensor(b.t1[:, 0:512], pz[:], wsasrep[:, sl], ALU.mult)
            dve.sig()
            dve.tensor_scalar(b.zbuf[:, m, sl], b.t1[:, 0:512],
                              b.l1f_bias[:, m:m + 1], None, ALU.add)
            zt = dve.sig()
            ev_prev[idx] = zt
            ztoks.append(zt)
    ctx.dump("zbuf", b.zbuf[:], ztoks)
    ctx.dump("wsasrep", wsasrep[:], [trep])
    ctx.dump("l1fwt", b.l1f_wt[:], wt_toks)
    return ztoks




def emit_l0f(ctx, dram, b, xln_toks):
    pe, dve, act, pool, sp = ctx.pe, ctx.dve, ctx.act, ctx.pool, ctx.sp
    NE = 9
    b.mask_fm = ctx.phase_sbuf("l0f_maskfm", [8, T], BF16)
    ctx.push_phase()
    wr = ctx.phase_sbuf("l0f_wr", [128, DCH, 8], BF16)
    pool.dma_start(out=wr[:], in_=dram["l0f_wr"].rearrange("(c p) e -> p c e", p=128))
    twr = pool.sig_dma("dW")
    ps_r = ctx.phase_psum("l0f_psr", [8, 512], F32)
    rlog = ctx.phase_sbuf("l0f_rlog", [8, T], BF16)
    ev = None
    for tg in range(T // 512):
        sl2 = slice(2 + 512 * tg, 2 + 512 * (tg + 1))
        for c in range(DCH):
            pe.wait(twr)
            pe.wait_all(xln_toks)
            if c == 0 and ev is not None:
                pe.wait(ev)
            pe.matmul(ps_r[:], wr[:, c, :], b.xln[:, c, sl2],
                      start=(c == 0), stop=(c == DCH - 1))
        tm_ = pe.sig()
        act.wait(tm_)
        act.copy(rlog[:, 512 * tg:512 * (tg + 1)], ps_r[:])
        ev = act.sig()
    trlog = ev
    ps_g = ctx.phase_psum("l0f_psg", [128, 8], BF16)
    ps_gb = ctx.phase_psum("l0f_psgb", [8, 128], BF16)
    ltm = ctx.phase_sbuf("l0f_ltm", [128, 8], F32)
    g1 = ctx.phase_sbuf("l0f_g1", [128, 8], F32)
    g2 = ctx.phase_sbuf("l0f_g2", [128, 8], F32)
    gc = ctx.phase_sbuf("l0f_gc", [128, 12], F32)
    mask_tm = ctx.phase_sbuf("l0f_masktm", [128, 8], BF16)
    mask_fm = b.mask_fm
    tmask = None
    tp_g = None
    tp_gb = None
    for j in range(NT):
        pe.wait(trlog)
        if tp_g is not None:
            pe.wait(tp_g)
        pe.transpose(ps_g[:], rlog[:, 128 * j:128 * (j + 1)], b.ident[0:8, 0:8])
        tt = pe.sig()
        act.wait(tt)
        act.copy(ltm[:], ps_g[:])
        tp_g = act.sig()
        dve.wait(tp_g)
        dve.reduce_max(gc[:, 0:1], ltm[:], AX.X)
        dve.tensor_single_scalar(g1[:], ltm[:], gc[:, 0:1], ALU.is_ge)
        dve.reduce_sum(gc[:, 1:2], g1[:], AX.X)
        dve.scalar_tensor_tensor(g2[:], g1[:], -1e30, ltm[:], ALU.mult, ALU.add)
        dve.reduce_max(gc[:, 2:3], g2[:], AX.X)
        dve.tensor_single_scalar(gc[:, 3:4], gc[:, 1:2], 1.5, ALU.is_gt)
        dve.tensor_tensor(gc[:, 4:5], gc[:, 0:1], gc[:, 2:3], ALU.subtract)
        dve.tensor_tensor(gc[:, 5:6], gc[:, 3:4], gc[:, 4:5], ALU.mult)
        dve.tensor_tensor(gc[:, 6:7], gc[:, 2:3], gc[:, 5:6], ALU.add)
        dve.tensor_tensor(gc[:, 7:8], gc[:, 6:7], gc[:, 0:1], ALU.subtract)
        td = dve.sig()
        act.wait(td)
        act.activation(gc[:, 8:9], gc[:, 7:8], AF.Exp)
        te = act.sig()
        dve.wait(te)
        dve.tensor_single_scalar(gc[:, 9:10], gc[:, 8:9], 1.0, ALU.add)
        dve.tensor_single_scalar(gc[:, 10:11], gc[:, 9:10], 1.0, ALU.is_equal)
        dve.tensor_single_scalar(mask_tm[:], g1[:], gc[:, 10:11], ALU.mult)
        tmtm = dve.sig()
        pe.wait(tmtm)
        if tp_gb is not None:
            pe.wait(tp_gb)
        pe.transpose(ps_gb[0:8, :], mask_tm[:], b.ident[:])
        tb = pe.sig()
        act.wait(tb)
        act.copy(mask_fm[0:8, 128 * j:128 * (j + 1)], ps_gb[0:8, :])
        tp_gb = act.sig()
        tmask = tp_gb
    ctx.barrier()
    ctx.pop_phase()
    w1b = ctx.phase_sbuf("l0f_w1b", [128, 2, DCH, 512], BF16)
    w2b = ctx.phase_sbuf("l0f_w2b", [128, 2, 4, D], BF16)
    rowbuf = ctx.phase_sbuf("l0f_rowb", [1, T], BF16)
    maskrep = ctx.phase_sbuf("l0f_mrep", [128, T], BF16)
    hraw = b.s16[:, 0:T]
    hm = b.s16[:, T:2 * T]
    _h01 = b.t1[:].bitcast(BF16)
    _h23 = b.mrep[:].bitcast(BF16)
    hsq = lambda fc, sl=slice(0, T): (_h01 if fc < 2 else _h23)[
        :, T * (fc % 2) + (sl.start or 0):T * (fc % 2) + (sl.stop if sl.stop is not None else T)]
    ps_h = ctx.phase_psum("l0f_psh", [128, T], F32)
    ps_z = ctx.phase_psum("l0f_psz", [128, T], F32)
    w_used = [None, None]
    tmr_used = None
    tmr = None
    th_evac = None
    tz_evac = None
    hsq_read = [None] * 4
    ztoks = []
    for e in range(NE):
        slot = e % 2
        pool.wait(w_used[slot])
        pool.dma_start(out=w1b[:, slot], in_=dram["l0f_w1"][e].rearrange(
            "(c p) f -> p c f", p=128))
        pool.sig_dma("dW")
        pool.dma_start(out=w2b[:, slot], in_=dram["l0f_w2"][e].rearrange(
            "(c p) o -> p c o", p=128))
        tw = pool.sig_dma("dW")
        if e < 8:
            sp.wait(tmask)
            sp.wait(tmr)
            sp.dma_start(out=rowbuf[:], in_=mask_fm[e:e + 1, :])
            trow = sp.sig_dma("dS")
            pe.wait(trow)
            pe.wait(th_evac)
            if tmr is not None:
                pe.wait(tmr)
            for tg in range(T // 512):
                sl = slice(512 * tg, 512 * (tg + 1))
                pe.matmul(ps_h[:, sl], b.onesrow[:], rowbuf[0:1, sl],
                          start=True, stop=True)
            tb_ = pe.sig()
            dve.wait(tb_)
            dve.wait(tmr_used)
            for tg in range(T // 512):
                sl = slice(512 * tg, 512 * (tg + 1))
                dve.tensor_copy(maskrep[:, sl], ps_h[:, sl])
            tmr = dve.sig()
        hsq_toks = []
        for fc in range(4):
            for d in range(DCH):
                pe.wait(tw)
                pe.wait_all(xln_toks)
                if d == 0:
                    pe.wait(th_evac)
                    pe.wait(tmr)
                lh = w1b[:, slot, d, 128 * fc:128 * (fc + 1)]
                for tg in range(T // 512):
                    pe.matmul(ps_h[:, 512 * tg:512 * (tg + 1)], lh,
                              b.xln[:, d, 2 + 512 * tg:2 + 512 * (tg + 1)],
                              start=(d == 0), stop=(d == DCH - 1))
            tmm = pe.sig()
            act.wait(tmm)
            if hsq_toks:
                act.wait(hsq_toks[-1])
            elif tmr_used is not None:
                act.wait(tmr_used)
            act.activation(hraw[:], ps_h[:], AF.Relu)
            th_evac = act.sig()
            dve.wait(th_evac)
            dve.wait(hsq_read[fc])
            if e < 8:
                dve.wait(tmr)
                dve.tensor_tensor(hm[:], hraw[:], maskrep[:], ALU.mult)
                dve.tensor_tensor(hsq(fc), hm[:], hraw[:], ALU.mult)
            else:
                dve.tensor_tensor(hsq(fc), hraw[:], hraw[:], ALU.mult)
            hsq_toks.append(dve.sig())
        tmr_used = hsq_toks[-1]
        if e == 8:
            pass
        for m in range(DCH):
            for fc in range(4):
                pe.wait(hsq_toks[fc])
                if fc == 0:
                    pe.wait(tz_evac)
                lh = w2b[:, slot, fc, 128 * m:128 * (m + 1)]
                for tg in range(T // 512):
                    pe.matmul(ps_z[:, 512 * tg:512 * (tg + 1)], lh,
                              hsq(fc, slice(512 * tg, 512 * (tg + 1))),
                              start=(fc == 0), stop=(fc == 3))
            tmm = pe.sig()
            dve.wait(tmm)
            if e == 0:
                dve.tensor_copy(b.zbuf[:, m, :], ps_z[:])
            else:
                dve.tensor_tensor(b.zbuf[:, m, :], b.zbuf[:, m, :], ps_z[:], ALU.add)
            tz_evac = dve.sig()
            if e == NE - 1:
                ztoks.append(tz_evac)
        for fc in range(4):
            hsq_read[fc] = tz_evac
        w_used[slot] = tz_evac
    ctx.dump("moe_maskfm", mask_fm[:], [tmask])
    ctx.dump("moe_z", b.zbuf[:], ztoks)
    return ztoks



PAIRS = ((0, 1), (0, 2), (0, 3), (1, 2), (1, 3), (2, 3))
JSIGN = (1, -1, 1, 1, -1, 1)
ATT_SCALE = float(np.float32(6.0 ** (-0.5)))


def emit_l0a(ctx, dram, b, xln_toks):
    pe, dve, act, pool, sp = ctx.pe, ctx.dve, ctx.act, ctx.pool, ctx.sp
    bandm = ctx.phase_sbuf("l0a_band", [128, 16], F32)
    tri16 = ctx.phase_sbuf("l0a_tri16", [128, T], BF16)
    half01 = ctx.phase_sbuf("l0a_half", [1, 256], BF16)
    sp.dma_start(out=bandm[:], in_=dram["bandmask"])
    sp.sig_dma("dW")
    pool.dma_start(out=tri16[:], in_=dram["trimask16"])
    pool.sig_dma("dW")
    pool.dma_start(out=half01[0:1, 0:128], in_=dram["half0"])
    pool.sig_dma("dW")
    pool.dma_start(out=half01[0:1, 128:256], in_=dram["half1"])
    tw0 = pool.sig_dma("dW")
    vtm = b.zbuf[:].rearrange("p c t -> p (c t)").rearrange("p (j o) -> p j o", o=D)
    qspec = ctx.phase_sbuf("l0a_qspec", [128, T], BF16)
    kspec = ctx.phase_sbuf("l0a_kspec", [128, T], BF16)
    ctx.push_phase()
    wq = ctx.phase_sbuf("l0a_wqb", [128, 16, 128], BF16)
    wk = ctx.phase_sbuf("l0a_wkb", [128, DCH, 128], BF16)
    wv = ctx.phase_sbuf("l0a_wvb", [128, DCH, D], BF16)
    bvr = ctx.phase_sbuf("l0a_bvr", [1, D], BF16)
    pool.dma_start(out=wq[:], in_=dram["l0a_wq"].rearrange("(c p) o -> p c o", p=128))
    pool.sig_dma("dW")
    pool.dma_start(out=wk[:], in_=dram["l0a_wk"].rearrange("(c p) o -> p c o", p=128))
    pool.sig_dma("dW")
    pool.dma_start(out=wv[:], in_=dram["l0a_wv"].rearrange("(c p) o -> p c o", p=128))
    pool.sig_dma("dW")
    pool.dma_start(out=bvr[:], in_=dram["l0a_bv"])
    tw = pool.sig_dma("dW")
    q12f = ctx.phase_sbuf("l0a_q12f", [128, T], BF16)
    k12f = ctx.phase_sbuf("l0a_k12f", [128, T], BF16)
    _rbf = b.rrep[:].bitcast(BF16)
    ps_p = ctx.phase_psum("l0a_psp", [128, 2, 512], F32)
    ps_t = ctx.phase_psum("l0a_pst", [128, 128], BF16)
    ps_v = ctx.phase_psum("l0a_psv", [128, 2, 512], F32)
    tvev = {}
    for j in range(NT):
        tsl = slice(2 + 128 * j, 2 + 128 * (j + 1))
        for half in range(2):
            pv = ps_v[:, half, :]
            osl = slice(512 * half, 512 * (half + 1))
            for dc in range(DCH):
                pe.wait(tw)
                pe.wait_all(xln_toks)
                if dc == 0 and j >= 1:
                    pe.wait(tvev[(j - 1, half)])
                pe.matmul(pv[:], b.xln[:, dc, tsl], wv[:, dc, osl],
                          start=(dc == 0), stop=False)
            pe.matmul(pv[:], b.onesrow[:, 0:128], bvr[0:1, osl],
                      start=False, stop=True)
            tmm = pe.sig()
            act.wait(tmm)
            act.copy(vtm[:, j, osl], pv[:])
            tvev[(j, half)] = act.sig()
    tqf = None
    for tg in range(T // 512):
        sl = slice(512 * tg, 512 * (tg + 1))
        for src, wgt, nch, off in ((q12f, wq, 16, None), (k12f, wk, DCH, None)):
            pp = ps_p[:, 0 if src is q12f else 1, :]
            for c in range(nch):
                pe.wait(tw)
                pe.wait_all(xln_toks)
                if c == 0 and tqf is not None:
                    pe.wait(tqf)
                if src is q12f and c >= DCH:
                    rhs = b.xln[:, c - DCH, 1 + 512 * tg:1 + 512 * (tg + 1)]
                else:
                    rhs = b.xln[:, c, 2 + 512 * tg:2 + 512 * (tg + 1)]
                pe.matmul(pp[:], wgt[:, c, :], rhs, start=(c == 0), stop=(c == nch - 1))
            tmm = pe.sig()
            act.wait(tmm)
            act.copy(src[:, sl], pp[:])
            tqf = act.sig()
    q12tmF = _rbf[:, 0:T]
    k12tmF = _rbf[:, T:2 * T]
    _tbf = b.t1[:].bitcast(BF16)
    qlinF = _tbf[:, 0:T]
    klinF = _tbf[:, T:2 * T]
    pool.memset(qlinF, 0.0)
    pool.memset(klinF, 0.0)
    tlz = pool.sig()
    ttm = None
    for j in range(NT):
        for which, (f_, dstF) in enumerate(((q12f, q12tmF), (k12f, k12tmF))):
            pe.wait(tqf)
            if ttm is not None:
                pe.wait(ttm)
            pe.transpose(ps_t[:], f_[:, 128 * j:128 * (j + 1)], b.ident[:])
            tt = pe.sig()
            act.wait(tt)
            act.copy(dstF[:, 128 * j:128 * (j + 1)], ps_t[:])
            ttm = act.sig()
    qa = q12tmF.rearrange("p (j x) -> p j x", x=128)[:, :, 0:64]\
        .rearrange("p j (h i) -> p j h i", i=4)
    qb2 = q12tmF.rearrange("p (j x) -> p j x", x=128)[:, :, 64:128]\
        .rearrange("p j (h i) -> p j h i", i=4)
    ka = k12tmF.rearrange("p (j x) -> p j x", x=128)[:, :, 0:64]\
        .rearrange("p j (h i) -> p j h i", i=4)
    kb2 = k12tmF.rearrange("p (j x) -> p j x", x=128)[:, :, 64:128]\
        .rearrange("p j (h i) -> p j h i", i=4)
    lvq = qlinF.rearrange("p (j x) -> p j x", x=128)\
        .rearrange("p j (h c) -> p j h c", c=8)
    lvk = klinF.rearrange("p (j x) -> p j x", x=128)\
        .rearrange("p j (h c) -> p j h c", c=8)
    t_a = b.mrep[:, 0:256].rearrange("p (j h u) -> p j h u", h=16, u=1)
    t_b = b.mrep[:, 256:512].rearrange("p (j h u) -> p j h u", h=16, u=1)
    dve.wait(ttm)
    dve.wait(tlz)
    for which in (0, 1):
        a1, a2, lv = (qa, qb2, lvq) if which == 0 else (ka, kb2, lvk)
        for c6 in range(6):
            if which == 0:
                i_, j_ = PAIRS[c6]
                neg = False
            else:
                i_, j_ = PAIRS[5 - c6]
                neg = (JSIGN[c6] < 0)
            dve.tensor_tensor(t_a, a1[:, :, :, i_:i_ + 1], a2[:, :, :, j_:j_ + 1], ALU.mult)
            dve.tensor_tensor(t_b, a1[:, :, :, j_:j_ + 1], a2[:, :, :, i_:i_ + 1], ALU.mult)
            if neg:
                dve.tensor_tensor(lv[:, :, :, c6:c6 + 1], t_b, t_a, ALU.subtract)
            else:
                dve.tensor_tensor(lv[:, :, :, c6:c6 + 1], t_a, t_b, ALU.subtract)
    tpl = dve.sig()
    ncol = b.rrep
    for which in (0, 1):
        linF = qlinF if which == 0 else klinF
        lv = lvq if which == 0 else lvk
        dve.wait(tpl)
        dve.tensor_tensor(b.mrep[:], linF, linF, ALU.mult)
        dve.tensor_reduce(ncol[:, 0:256].rearrange("p (x u) -> p x u", u=1),
                          b.mrep[:].rearrange("p (x c) -> p x c", c=8), AX.X, ALU.add)
        tn_ = dve.sig()
        act.wait(tn_)
        act.activation(ncol[:, 256:512], ncol[:, 0:256], AF.Sqrt)
        tsq2 = act.sig()
        dve.wait(tsq2)
        dve.tensor_single_scalar(ncol[:, 256:512], ncol[:, 256:512], 1e-12, ALU.max)
        dve.reciprocal(ncol[:, 512:768], ncol[:, 256:512])
        dve.tensor_tensor(lv, lv, ncol[:, 512:768]
                          .rearrange("p (j h u) -> p j h u", h=16, u=1)
                          .broadcast_to([128, NT, 16, 8]), ALU.mult)
        tnorm = dve.sig()
    tspec = None
    for j in range(NT):
        for which in (0, 1):
            linF = qlinF if which == 0 else klinF
            pe.wait(tnorm)
            if tspec is not None:
                pe.wait(tspec)
            pe.transpose(ps_t[:], linF[:, 128 * j:128 * (j + 1)], b.ident[:])
            tt2 = pe.sig()
            act.wait(tt2)
            act.copy((qspec if which == 0 else kspec)[:, 128 * j:128 * (j + 1)], ps_t[:])
            tspec = act.sig()
    ctx.barrier()
    ctx.pop_phase()
    ctx.push_phase()
    psb = [b.s16[:, 0:T], b.s16[:, T:2 * T]]
    qb = ctx.phase_sbuf("l0a_qb", [128, T], BF16)
    brd_all = ctx.phase_sbuf("l0a_brdall", [128, 2, 512], BF16)
    zrow = ctx.phase_sbuf("l0a_zrow", [1, 512], BF16)
    pool.memset(zrow[:], 0.0)
    tzr = pool.sig()
    rrow = b.t1[0:1, 0:T]
    rrowb = ctx.phase_sbuf("l0a_rrowb", [1, T], BF16)
    ps_l = ctx.phase_psum("l0a_psl", [128, 2, 512], F32)
    ps_oa = ctx.phase_psum("l0a_psoa", [128, 4, 128], F32)
    ps_ob = ctx.phase_psum("l0a_psob", [128, 4, 128], F32)
    ps_s = ctx.phase_psum("l0a_pss", [128, T], F32)
    oview = [b.xln[:, p, 2:2 + T] for p in range(DCH)]

    def oview_a(q):
        return b.xln[:, 0:4, 2 + 128 * q:2 + 128 * (q + 1)]

    def oview_b(q):
        return b.xln[:, 4:8, 2 + 128 * q:2 + 128 * (q + 1)]
    tqb = None
    texp = [None, None]
    tpread = [None, None]
    pslw = [None, None]
    eread = [None, None]
    ebuf = b.mrep[:].bitcast(BF16)
    tsum_ev = None
    to_ev = None
    tbrd_ev = None
    ztoks_oready = []
    for q in range(NT):
        dve.wait(tspec)
        dve.wait(tqb)
        for h in range(16):
            dve.tensor_single_scalar(qb[:, 128 * h:128 * (h + 1)],
                                     qspec[:, 128 * q:128 * (q + 1)],
                                     bandm[:, h:h + 1], ALU.mult)
        tqbv = dve.sig()
        pe.wait(tzr)
        pe.wait(to_ev)
        pe.matmul(ps_oa[:].rearrange("p a b -> p (a b)"), b.onesrow[:, 0:128],
                  zrow[0:1, :], start=True, stop=True)
        pe.matmul(ps_ob[:].rearrange("p a b -> p (a b)"), b.onesrow[:, 0:128],
                  zrow[0:1, :], start=True, stop=True)

        def emit_sums_attnv(k):
            pb2 = k % 2
            pe.wait(texp[pb2])
            if k == 0:
                pe.wait(tsum_ev)
            for sg in range(T // 512):
                pe.matmul(ps_s[:, 512 * sg:512 * (sg + 1)], b.ones128[:],
                          psb[pb2][:, 512 * sg:512 * (sg + 1)],
                          start=(k == 0), stop=(k == q))
            for p in range(DCH):
                po = (ps_oa if p < 4 else ps_ob)[:, p % 4, :]
                pe.matmul(po[0:64, :], vtm[:, k, 128 * p:128 * p + 64],
                          psb[pb2][:, 256 * p:256 * p + 128],
                          start=False, stop=(k == q), tile_position=(0, 0))
                pe.matmul(po[64:128, :], vtm[:, k, 128 * p + 64:128 * (p + 1)],
                          psb[pb2][:, 256 * p + 128:256 * (p + 1)],
                          start=False, stop=(k == q), tile_position=(0, 64))
            tpread[pb2] = pe.sig()

        for k in range(q + 1):
            pb = k % 2
            for ch in range(4):
                sl_ = ch % 2
                pe.wait(tqbv)
                pe.wait(tspec)
                pe.wait(pslw[sl_])
                pe.matmul(ps_l[:, sl_, :],
                          kspec[:, 128 * k:128 * (k + 1)],
                          qb[:, 512 * ch:512 * (ch + 1)],
                          start=True, stop=True)
                tmm = pe.sig()
                dve.wait(tmm)
                dve.wait(eread[sl_])
                dve.tensor_copy(ebuf[:, 512 * sl_:512 * (sl_ + 1)], ps_l[:, sl_, :])
                pslw[sl_] = dve.sig()
                act.wait(pslw[sl_])
                act.wait(tpread[pb])
                act.activation(psb[pb][:, 512 * ch:512 * (ch + 1)],
                               ebuf[:, 512 * sl_:512 * (sl_ + 1)],
                               AF.Exp, scale=ATT_SCALE)
                eread[sl_] = act.sig()
            texp[pb] = eread[1]
            if k == q:
                dve.wait(texp[pb])
                dve.tensor_tensor(psb[pb][:], psb[pb][:], tri16[:], ALU.mult)
                texp[pb] = dve.sig()
            if q == 3 and k == 1:
                ctx.dump("l0a_P31", psb[pb][:, :], [texp[pb]])
            if k >= 1:
                emit_sums_attnv(k - 1)
        emit_sums_attnv(q)
        tka = tpread[q % 2]
        dve.wait(tka)
        dve.tensor_copy(rrow[0:1, :], ps_s[0:1, :])
        tsr = dve.sig()
        act.wait(tsr)
        act.activation(rrow[0:1, :], rrow[0:1, :], AF.Ln)
        act.activation(rrowb[0:1, :], rrow[0:1, :], AF.Exp, scale=-1.0)
        trr = act.sig()
        if q == 3:
            ctx.dump("l0a_r3", rrowb[0:1, :], [trr])
        tsum_ev = trr
        rview = rrowb[0:1, :].rearrange("u (a b q) -> u a b q", a=8, b=2)
        for half in range(2):
            pe.wait(trr)
            pe.wait(pslw[half])
            pe.wait(tbrd_ev)
            pe.matmul(ps_l[:, half, :], half01[0:1, 0:128],
                      rview[:, 4 * half:4 * (half + 1), 0, :], start=True, stop=True)
            pe.matmul(ps_l[:, half, :], half01[0:1, 128:256],
                      rview[:, 4 * half:4 * (half + 1), 1, :], start=False, stop=True)
        tbb = pe.sig()
        dve.wait(tbb)
        dve.wait(to_ev)
        dve.tensor_copy(brd_all[:], ps_l[:])
        tbrd_ev = dve.sig()
        pslw = [tbrd_ev, tbrd_ev]
        dve.wait(tbrd_ev)
        dve.wait(tka)
        dve.tensor_tensor(oview_a(q), ps_oa[:],
                          brd_all[:, 0, :].rearrange("p (a q) -> p a q", a=4), ALU.mult)
        dve.tensor_tensor(oview_b(q), ps_ob[:],
                          brd_all[:, 1, :].rearrange("p (a q) -> p a q", a=4), ALU.mult)
        to_ev = dve.sig()
        ztoks_oready.append(to_ev)
    ctx.dump("l0a_o", b.xln[:], ztoks_oready)
    ctx.barrier()
    ctx.pop_phase()
    ctx.push_phase()
    wo = ctx.phase_sbuf("l0a_wob", [128, DCH, D], BF16)
    bor = ctx.phase_sbuf("l0a_bor", [1, D], BF16)
    pool.dma_start(out=wo[:], in_=dram["l0a_wo"].rearrange("(c p) o -> p c o", p=128))
    pool.sig_dma("dW")
    pool.dma_start(out=bor[:], in_=dram["l0a_bo"])
    two = pool.sig_dma("dW")
    ones512 = ctx.phase_sbuf("l0a_ones512", [1, 512], BF16)
    pool.memset(ones512[:], 1.0)
    ton = pool.sig()
    ps_z = ctx.phase_psum("l0a_psz", [128, T], F32)
    ztoks = []
    tz_ev = None
    for m in range(DCH):
        for p in range(DCH):
            pe.wait(two)
            pe.wait_all(ztoks_oready)
            if p == 0:
                pe.wait(tz_ev)
            lh = wo[:, p, 128 * m:128 * (m + 1)]
            for tg in range(T // 512):
                pe.matmul(ps_z[:, 512 * tg:512 * (tg + 1)], lh,
                          oview[p][:, 512 * tg:512 * (tg + 1)],
                          start=(p == 0), stop=False)
        for tg in range(T // 512):
            pe.wait(ton)
            pe.matmul(ps_z[:, 512 * tg:512 * (tg + 1)],
                      bor[0:1, 128 * m:128 * (m + 1)], ones512[0:1, :],
                      start=False, stop=True)
        tmm = pe.sig()
        dve.wait(tmm)
        dve.tensor_copy(b.zbuf[:, m, :], ps_z[:])
        tz_ev = dve.sig()
        ztoks.append(tz_ev)
    ctx.barrier()
    ctx.pop_phase()
    ctx.dump("l0a_z", b.zbuf[:], ztoks)
    return ztoks



L1A_HALVES = ((0, 1, 2), (3, 4))


def emit_l1a_fold(ctx, dram, b):
    pe, dve, act, pool, sp = ctx.pe, ctx.dve, ctx.act, ctx.pool, ctx.sp
    ctx.push_phase()
    b.bias_tilde = ctx.phase_sbuf("l1a_btil", [1, 6 * KPAD], BF16)
    wxb = ctx.phase_sbuf("l1a_wxb", [128, DCH, D], BF16)
    cb = ctx.phase_sbuf("l1a_cb", [128, DCH, KBINS], BF16)
    bcol = ctx.phase_sbuf("l1a_bcol", [128, DCH], BF16)
    stg = ctx.phase_sbuf("l1a_stg", [128, KPAD], BF16)
    ps_f = ctx.phase_psum("l1a_psf", [128, KBINS], F32)
    ps_b = ctx.phase_psum("l1a_psb", [1, KBINS], F32)
    pool.memset(stg[:], 0.0)
    tz = pool.sig()
    tprev_dma = None
    tstg_free = tz
    for wi, (wname, bname, plane) in enumerate((
            ("l1a_wq", "l1a_bq", "dftC"), ("l1a_wq", "l1a_bq", "dftS"),
            ("l1a_wk", "l1a_bk", "dftC"), ("l1a_wk", "l1a_bk", "dftS"),
            ("l1a_wv", "l1a_bv", "dftC"), ("l1a_wv", "l1a_bv", "dftS"))):
        pool.wait(tprev_dma)
        pool.dma_start(out=wxb[:], in_=dram[wname].rearrange("(c p) d -> p c d", p=128))
        pool.sig_dma("dW")
        pool.dma_start(out=cb[:], in_=dram[plane].rearrange("(c p) k -> p c k", p=128))
        pool.sig_dma("dW")
        pool.dma_start(out=bcol[:], in_=dram[bname])
        tld = pool.sig_dma("dW")
        tev = None
        for dc in range(DCH):
            for ec in range(DCH):
                pe.wait(tld)
                if ec == 0 and tev is not None:
                    pe.wait(tev)
                pe.matmul(ps_f[:, 0:512], wxb[:, ec, 128 * dc:128 * (dc + 1)],
                          cb[:, ec, 0:512], start=(ec == 0), stop=(ec == DCH - 1))
                pe.matmul(ps_f[:, 512:513], wxb[:, ec, 128 * dc:128 * (dc + 1)],
                          cb[:, ec, 512:513], start=(ec == 0), stop=(ec == DCH - 1))
            tmm = pe.sig()
            act.wait(tmm)
            act.wait(tstg_free)
            act.copy(stg[:, 0:KBINS], ps_f[:])
            tev = act.sig()
            sp.wait(tev)
            sp.dma_start(out=dram["wtil"][wi, 128 * dc:128 * (dc + 1), :], in_=stg[:])
            tstg_free = sp.sig_dma("dS")
        tevb = None
        for ec in range(DCH):
            pe.wait(tld)
            pe.matmul(ps_b[:, 0:512], bcol[:, ec:ec + 1], cb[:, ec, 0:512],
                      start=(ec == 0), stop=(ec == DCH - 1))
            pe.matmul(ps_b[:, 512:513], bcol[:, ec:ec + 1], cb[:, ec, 512:513],
                      start=(ec == 0), stop=(ec == DCH - 1))
        tmm = pe.sig()
        dve.wait(tmm)
        dve.memset(b.bias_tilde[0:1, KPAD * wi:KPAD * (wi + 1)], 0.0)
        dve.sig()
        dve.tensor_copy(b.bias_tilde[0:1, KPAD * wi:KPAD * wi + KBINS], ps_b[:])
        tbt = dve.sig()
        pe.wait(tbt)
        tprev_dma = Tok(tld.sem, tld.val, tld.snap)
        tprev_dma = tev
    ctx.dump("l1a_btil", b.bias_tilde[:], [tbt])
    sp.wait(tbt)
    sp.dma_start(out=dram["btil"], in_=b.bias_tilde[:])
    tst = sp.sig_dma("dS")
    for e_ in ENGS:
        ctx.eng[e_].wait(tst)
    ctx.pop_phase()
    return None


def _complex_norm(ctx, b, re_src, im_src, re_dst, im_dst, width, deps):
    dve, act = ctx.dve, ctx.act
    n2 = b.t1[:, 0:width]
    t2 = b.mrep[:, 0:width]
    dve.wait_all(deps)
    dve.tensor_tensor(n2, re_src, re_src, ALU.mult)
    dve.tensor_tensor(t2, im_src, im_src, ALU.mult)
    dve.tensor_tensor(n2, n2, t2, ALU.add)
    tn = dve.sig()
    act.wait(tn)
    dve.tensor_single_scalar(n2, n2, 1e-30, ALU.max)
    tm_ = dve.sig()
    act.wait(tm_)
    act.activation(n2, n2, AF.Ln)
    act.activation(n2, n2, AF.Exp, scale=0.5)
    act.activation(n2, n2, AF.Ln, bias=1e-8)
    act.activation(t2, n2, AF.Exp, scale=-1.0)
    tc_ = act.sig()
    dve.wait(tc_)
    dve.tensor_tensor(re_dst, re_src, t2, ALU.mult)
    dve.sig()
    dve.tensor_tensor(im_dst, im_src, t2, ALU.mult)
    return dve.sig()


L1A_PARTS = ((0, 1), (2, 3), (4,))


def emit_l1a(ctx, dram, b, xln_toks, seq):
    pe, dve, act, pool, sp = ctx.pe, ctx.dve, ctx.act, ctx.pool, ctx.sp
    onesT = ctx.phase_sbuf("l1a_onesT", [1, 512], BF16)
    utri = ctx.phase_sbuf("l1a_utri", [128, 128], BF16)
    carry = ctx.phase_sbuf("l1a_carry", [128, 12], F32)
    pool.memset(onesT[:], 1.0)
    tone = pool.sig()
    pool.dma_start(out=utri[:], in_=dram["utri"])
    tutri = pool.sig_dma("dW")
    zfirst = [True]
    ztoks = []
    for part_i, kcs in enumerate(L1A_PARTS):
        nk = len(kcs)
        W = 128 * nk
        ko = 128 * kcs[0]
        ctx.push_phase()
        btlp = ctx.phase_sbuf("l1a_btlp", [1, 6 * W], BF16)
        sp.dma_start(
            out=btlp[:].rearrange("u (w k) -> u w k", k=W),
            in_=dram["btil"].rearrange("u (w k) -> u w k", k=KPAD)[:, :, ko:ko + W])
        tbtl = sp.sig_dma("dW")
        xk = ctx.phase_sbuf("l1a_xk", [128, NT, 2 * W], BF16)
        _qre = b.rrep[:].bitcast(BF16).rearrange("p (a t) -> p a t", t=T)[:, 0:nk, :]
        _qim = ctx.phase_sbuf("l1a_qhi", [128, nk, T], BF16)

        def qh(ci):
            return _qre[:, ci, :] if ci < nk else _qim[:, ci - nk, :]
        ctx.push_phase()
        wkb = ctx.phase_sbuf("l1a_wkb", [128, DCH, 4, W], BF16)
        for j, wi in enumerate((2, 3, 4, 5)):
            sp.dma_start(out=wkb[:, :, j, :], in_=dram["wtil"][wi, :, ko:ko + W]
                         .rearrange("(c p) k -> p c k", p=128))
            twk = sp.sig_dma("dW")

        dve.wait(tbtl)
        dve.engine_nop()
        tbc = tbtl
        ps4d = [ctx.phase_psum("l1a_ps4a", [128, 4 * W], F32),
                ctx.phase_psum("l1a_ps4b", [128, 4 * W], F32)]
        kvtd = [b.sq[:].rearrange("p a w -> p (a w)")[:, 0:4 * W]
                .rearrange("p (a w) -> p a w", a=4),
                b.t1[:].bitcast(BF16)[:, 1024:1024 + 4 * W]
                .rearrange("p (a w) -> p a w", a=4)]
        ps_free = [None, None]
        kvt_free = [None, None]
        tbind_prev = None
        for j in range(NT):
            sl2 = j % 2
            ps4 = ps4d[sl2]
            kvt = kvtd[sl2]
            tsl = slice(2 + 128 * j, 2 + 128 * (j + 1))
            for dc in range(DCH):
                pe.wait(twk)
                pe.wait_all(xln_toks)
                pe.wait(tone)
                pe.wait(tbc)
                if dc == 0:
                    pe.wait(ps_free[sl2])
                wr_ = wkb[:, dc, :, :].rearrange("p a w -> p (a w)")
                for sg in range(0, 4 * W, 512):
                    pe.matmul(ps4[:, sg:sg + 512], b.xln[:, dc, tsl],
                              wr_[:, sg:sg + 512], start=(dc == 0), stop=False)
            for sg in range(0, 4 * W, 512):
                pe.matmul(ps4[:, sg:sg + 512], onesT[0:1, 0:128],
                          btlp[0:1, 2 * W + sg:2 * W + sg + 512],
                          start=False, stop=True)
            tmm = pe.sig()
            act.wait(tmm)
            act.wait(kvt_free[sl2])
            act.copy(kvt[:], ps4[:])
            ps_free[sl2] = act.sig()
            kv4 = kvt[:].rearrange("p (a b) w -> p a b w", b=2)
            res_ = kv4[:, :, 0, :]
            ims_ = kv4[:, :, 1, :]
            n2 = b.t1[:, 0:2 * W].rearrange("p (a w) -> p a w", a=2)
            t2 = b.mrep[:, 0:2 * W].rearrange("p (a w) -> p a w", a=2)
            dve.wait(ps_free[sl2])
            dve.tensor_tensor(n2, res_, res_, ALU.mult)
            dve.tensor_tensor(t2, ims_, ims_, ALU.mult)
            dve.tensor_tensor(n2, n2, t2, ALU.add)
            dve.tensor_single_scalar(n2, n2, 1e-30, ALU.max)
            tn_ = dve.sig()
            act.wait(tn_)
            act.activation(n2, n2, AF.Ln)
            act.activation(n2, n2, AF.Exp, scale=0.5)
            act.activation(n2, n2, AF.Ln, bias=1e-8)
            act.activation(t2, n2, AF.Exp, scale=-1.0)
            tc_ = act.sig()
            dve.wait(tc_)
            dve.tensor_tensor(res_, res_, t2, ALU.mult)
            dve.tensor_tensor(ims_, ims_, t2, ALU.mult)
            dve.sig()
            ta = b.s16[:, 0:W]
            tb_ = b.s16[:, W:2 * W]
            dve.tensor_tensor(ta, kvt[:, 0, :], kvt[:, 2, :], ALU.mult)
            dve.tensor_tensor(tb_, kvt[:, 1, :], kvt[:, 3, :], ALU.mult)
            dve.tensor_tensor(xk[:, j, 0:W], ta, tb_, ALU.subtract)
            dve.tensor_tensor(ta, kvt[:, 0, :], kvt[:, 3, :], ALU.mult)
            dve.tensor_tensor(tb_, kvt[:, 1, :], kvt[:, 2, :], ALU.mult)
            dve.tensor_tensor(xk[:, j, W:2 * W], ta, tb_, ALU.add)
            tbind_prev = dve.sig()
            kvt_free[sl2] = tbind_prev
        ctx.barrier()
        ctx.pop_phase()
        ctx.push_phase()
        wqb = ctx.phase_sbuf("l1a_wqb", [128, 2, DCH, W], BF16)
        for j, wi in enumerate((0, 1)):
            sp.dma_start(out=wqb[:, j], in_=dram["wtil"][wi, :, ko:ko + W]
                         .rearrange("(c p) k -> p c k", p=128))
            twq = sp.sig_dma("dW")
        ps_q = ctx.phase_psum("l1a_psq", [128, T], F32)
        tq_free = None
        for ci in range(2 * nk):
            pl, kci = (0, ci) if ci < nk else (1, ci - nk)
            for dc in range(DCH):
                pe.wait(twq)
                pe.wait_all(xln_toks)
                if dc == 0:
                    pe.wait(tq_free)
                lh = wqb[:, pl, dc, 128 * kci:128 * (kci + 1)]
                for tg in range(T // 512):
                    pe.matmul(ps_q[:, 512 * tg:512 * (tg + 1)], lh,
                              b.xln[:, dc, 2 + 512 * tg:2 + 512 * (tg + 1)],
                              start=(dc == 0), stop=False)
            pe.wait(tbtl)
            seg = btlp[0:1, W * pl + 128 * kci:W * pl + 128 * (kci + 1)]
            for tg in range(T // 512):
                pe.matmul(ps_q[:, 512 * tg:512 * (tg + 1)], seg,
                          onesT[0:1, 0:512], start=False, stop=True)
            tmm = pe.sig()
            act.wait(tmm)
            act.copy(qh(ci), ps_q[:])
            tq_free = act.sig()
        tqn = []
        for kci in range(nk):
            tn = _complex_norm(ctx, b, qh(kci), qh(nk + kci),
                               qh(kci), qh(nk + kci), T, [tq_free])
            tqn.append(tn)
        ctx.barrier()
        ctx.pop_phase()
        ctx.push_phase()
        ps_c = [ctx.phase_psum(f"l1a_psc{i}", [128, 128], F32) for i in range(4)]
        ps_ts = ctx.phase_psum("l1a_psts", [128, 2 * nk, NT], F32)
        tsum = ctx.phase_sbuf("l1a_tsum", [128, 2 * nk, NT + 1], F32)
        onecol = ctx.phase_sbuf("l1a_onecol", [128, 1], BF16)
        pool.memset(onecol[:], 1.0)
        toc = pool.sig()
        pe.wait_all(tqn)
        pe.wait(toc)
        for ci in range(2 * nk):
            for j in range(NT):
                pe.matmul(ps_ts[:, ci, j:j + 1], xk[:, j, 128 * ci:128 * (ci + 1)],
                          onecol[:], start=True, stop=True)
        tts = pe.sig()
        tsum2 = ctx.phase_sbuf("l1a_tsum2", [128, 2 * nk, NT + 1], F32)
        dve.wait(tts)
        dve.memset(tsum[:, :, 0:1], 0.0)
        dve.memset(tsum2[:, :, 0:1], 0.0)
        dve.tensor_copy(tsum[:, :, 1:NT + 1], ps_ts[:, :, :])
        dve.sig()
        bufs = (tsum, tsum2)
        for si, sh in enumerate((1, 2, 4, 8)):
            a_, b2_ = bufs[si % 2], bufs[(si + 1) % 2]
            dve.tensor_copy(b2_[:, :, 1:sh + 1], a_[:, :, 1:sh + 1])
            dve.tensor_tensor(b2_[:, :, sh + 1:NT + 1], a_[:, :, sh + 1:NT + 1],
                              a_[:, :, 1:NT + 1 - sh], ALU.add)
            dve.sig()
        tsum = bufs[0]
        tpfx = dve.sig()
        tev4 = [None] * 4
        for ci in range(2 * nk):
            for j in range(NT):
                sl4 = (ci * NT + j) % 4
                pe.wait(tutri)
                pe.wait(tev4[sl4])
                pe.matmul(ps_c[sl4][:], xk[:, j, 128 * ci:128 * (ci + 1)], utri[:],
                          start=True, stop=True)
                tmm = pe.sig()
                dve.wait(tmm)
                dve.wait(tpfx)
                dve.tensor_single_scalar(xk[:, j, 128 * ci:128 * (ci + 1)],
                                         ps_c[sl4][:], tsum[:, ci, j:j + 1], ALU.add)
                tev4[sl4] = dve.sig()
        ctx.barrier()
        ctx.pop_phase()
        ctx.push_phase()
        cwb = ctx.phase_sbuf("l1a_cwb", [128, 2, nk, D], BF16)
        for j, nm in enumerate(("idftC", "idftS")):
            pool.dma_start(out=cwb[:, j], in_=dram[nm][ko:ko + W, :]
                           .rearrange("(c p) e -> p c e", p=128))
            tcw = pool.sig_dma("dW")
        ps_o = ctx.phase_psum("l1a_pso", [128, T], F32)
        gtmp = ctx.phase_sbuf("l1a_gtmp", [128, 2, T], BF16)
        gr = b.s16[:, 0:T]
        gi = b.s16[:, T:2 * T]
        tz_ev = None
        for kci in range(nk):
            sre = xk[:, :, 128 * kci:128 * (kci + 1)]
            sim = xk[:, :, 128 * (nk + kci):128 * (nk + kci + 1)]
            qre = qh(kci).rearrange("p (j q) -> p j q", q=128)
            qim = qh(nk + kci).rearrange("p (j q) -> p j q", q=128)
            grv = gr.rearrange("p (j q) -> p j q", q=128)
            giv = gi.rearrange("p (j q) -> p j q", q=128)
            ta = gtmp[:, 0, :].rearrange("p (j q) -> p j q", q=128)
            tb_ = gtmp[:, 1, :].rearrange("p (j q) -> p j q", q=128)
            dve.wait(tz_ev)
            dve.tensor_tensor(ta, sre, qre, ALU.mult)
            dve.tensor_tensor(tb_, sim, qim, ALU.mult)
            dve.tensor_tensor(grv, ta, tb_, ALU.add)
            dve.tensor_tensor(ta, sim, qre, ALU.mult)
            dve.tensor_tensor(tb_, sre, qim, ALU.mult)
            dve.tensor_tensor(giv, ta, tb_, ALU.subtract)
            tg_ = dve.sig()
            for m in range(DCH):
                for pl, g_ in ((0, gr), (1, gi)):
                    for tg2 in range(T // 512):
                        pe.wait(tcw)
                        pe.wait(tg_)
                        if pl == 0 and tg2 == 0:
                            pe.wait(tz_ev)
                        pe.matmul(ps_o[:, 512 * tg2:512 * (tg2 + 1)],
                                  cwb[:, pl, kci, 128 * m:128 * (m + 1)],
                                  g_[:, 512 * tg2:512 * (tg2 + 1)],
                                  start=(pl == 0), stop=(pl == 1))
                tmm = pe.sig()
                dve.wait(tmm)
                if zfirst[0]:
                    dve.tensor_copy(b.zbuf[:, m, :], ps_o[:])
                else:
                    dve.tensor_tensor(b.zbuf[:, m, :], b.zbuf[:, m, :], ps_o[:], ALU.add)
                tz_ev = dve.sig()
                if part_i == len(L1A_PARTS) - 1 and kci == nk - 1:
                    ztoks.append(tz_ev)
            zfirst[0] = False
        ctx.barrier()
        ctx.pop_phase()
        ctx.pop_phase()
    ctx.dump("l1a_z", b.zbuf[:], ztoks)
    return ztoks


def emit_launch(ctx, dram, launch_blocks):
    b = alloc_persistent(ctx, launch_blocks)
    ctoks = emit_load_const(ctx, dram, b)
    for e in ENGS:
        ctx.eng[e].wait_all(ctoks)
    if 2 in launch_blocks:
        emit_l1a_fold(ctx, dram, b)
        ctx.barrier()
        ctx.end_phase()
    for seq in range(NSEQ):
        xtok = emit_load_x(ctx, dram, b, seq)
        ctx.barrier()
        ln_toks = emit_preln(ctx, b, [xtok])
        ctx.barrier()
        ctx.end_phase()
        for i, blk in enumerate(launch_blocks):
            if blk == 0:
                ztoks = emit_l0a(ctx, dram, b, ln_toks)
            elif blk == 1:
                ztoks = emit_l0f(ctx, dram, b, ln_toks)
            elif blk == 2:
                ztoks = emit_l1a(ctx, dram, b, ln_toks, seq)
            elif blk == 3:
                wt_toks = emit_l1f_weights(ctx, dram, b)
                ztoks = emit_l1f(ctx, b, ln_toks, wt_toks)
            else:
                raise NotImplementedError(f"block {blk}")
            ctx.barrier()
            ctx.end_phase()
            if i < len(launch_blocks) - 1:
                ln_toks = emit_transition(ctx, b, ztoks)
            else:
                xdeps = emit_postln_residual(ctx, b, ztoks)
            ctx.barrier()
            ctx.end_phase()
        st = emit_store_x(ctx, dram, b, seq, xdeps)
        for e in ENGS:
            ctx.eng[e].wait(st)
    ctx.sp.wait(Tok("dO", ctx.dma_counts["dO"], {}))


def build_launch(g, launch_blocks):
    nc = bass.Bass("TRN2", target_bir_lowering=False)
    dram = declare_dram(nc, g, launch_blocks)
    stack = ExitStack()
    with stack:
        ctx = Ctx(nc, stack)
        emit_launch(ctx, dram, launch_blocks)
        block = stack.enter_context(nc.Block())
        ctx.replay(block)
    return nc


LAUNCH_PLAN = [[0, 1, 2, 3]]


def kernel(**inputs):
    g, shards = stage_inputs(inputs)
    cur = shards
    for launch_blocks in LAUNCH_PLAN:
        nc = build_launch(g, launch_blocks)
        in_maps = []
        need = {"xin"}
        need.update(("ident", "ones128", "onesrow"))
        for blk in launch_blocks:
            need.update(BLOCK_INPUTS[blk])
        for c in range(NCORES):
            m = {k: g[k] for k in need if k != "xin"}
            m["xin"] = cur[c]
            in_maps.append(m)
        res = run_bass_kernel_spmd(nc, in_maps, core_ids=list(range(NCORES)))
        cur = [r["xout"] for r in res.results]
    out = np.empty((16, T, D), np.float32)
    for c in range(NCORES):
        out[2 * c:2 * c + 2] = cur[c].transpose(0, 2, 1)
    return out


# version 39
# speedup vs baseline: 1.0464x; 1.0144x over previous
import numpy as np
from contextlib import ExitStack
import concourse.bass as bass
import concourse.mybir as mybir
from concourse.bass_utils import run_bass_kernel_spmd

F32 = mybir.dt.float32
BF16 = mybir.dt.bfloat16
I32 = mybir.dt.int32
AF = mybir.ActivationFunctionType
ALU = mybir.AluOpType
AX = mybir.AxisListType

T = 2048
D = 1024
NCORES = 8
NSEQ = 2
DCH = D // 128
NT = T // 128
KBINS = 513
KPAD = 640
KCH = KPAD // 128
LN_EPS = 1e-5

ENGS = ("pe", "dve", "act", "pool", "sp")



class Tok:
    __slots__ = ("sem", "val", "snap")

    def __init__(self, sem, val, snap):
        self.sem, self.val, self.snap = sem, val, snap


_NO_AUTO = {"dma_start", "indirect_dma_start", "wait_ge", "nop", "sem_inc",
            "print", "dma_start_transpose"}


class EngCtx:

    _FIELDS = ("ctx", "name", "sem_name", "count", "clock", "recs", "auto",
               "self_waited", "uninced")

    def __init__(self, ctx, name, auto):
        self.ctx, self.name = ctx, name
        self.sem_name = "s_" + name
        self.count = 0
        self.clock = {}
        self.recs = []
        self.auto = auto
        self.self_waited = 0
        self.uninced = False

    def _rec(self, fn, auto_inced=False):
        entry = [fn, [], auto_inced]
        self.recs.append(entry)
        return entry

    def __getattr__(self, attr):
        if attr.startswith("_") or attr in EngCtx._FIELDS:
            raise AttributeError(attr)

        def wrap(*a, **k):
            if self.auto and attr not in _NO_AUTO and self.count > self.self_waited:
                sem, val = self.sem_name, self.count
                self._rec(lambda eng: eng.wait_ge(self.ctx.sems[sem], val))
                self.self_waited = val
            auto_inc = self.auto and attr not in _NO_AUTO
            e = self._rec(lambda eng: getattr(eng, attr)(*a, **k), auto_inc)
            if auto_inc:
                self.count += 1
                e[1].append((self.sem_name, 1))
                self.clock[self.sem_name] = self.count
            elif attr not in _NO_AUTO:
                self.uninced = True
            return None
        return wrap

    def wait(self, tok):
        if tok is None:
            return
        if tok.sem == self.sem_name:
            if tok.val <= self.self_waited:
                return
            sem, val = tok.sem, tok.val
            self._rec(lambda eng: eng.wait_ge(self.ctx.sems[sem], val))
            self.self_waited = val
            return
        if self.clock.get(tok.sem, 0) >= tok.val:
            return
        sem, val = tok.sem, tok.val
        self._rec(lambda eng: eng.wait_ge(self.ctx.sems[sem], val))
        for s, v in tok.snap.items():
            if s == self.sem_name:
                if v > self.self_waited:
                    self.self_waited = v
                continue
            if self.clock.get(s, 0) < v:
                self.clock[s] = v
        if self.clock.get(tok.sem, 0) < tok.val:
            self.clock[tok.sem] = tok.val

    def wait_all(self, toks):
        for t in toks:
            self.wait(t)

    def sig(self):
        assert self.recs, "sig() with no recorded instruction"
        if self.recs[-1][2] or any(s == self.sem_name for s, _ in self.recs[-1][1]):
            return Tok(self.sem_name, self.count, dict(self.clock))
        self.count += 1
        self.recs[-1][1].append((self.sem_name, 1))
        self.clock[self.sem_name] = self.count
        self.uninced = False
        return Tok(self.sem_name, self.count, dict(self.clock))

    def sig_dma(self, stream):
        c = self.ctx.dma_counts[stream] = self.ctx.dma_counts.get(stream, 0) + 16
        self.recs[-1][1].append((stream, 16))
        snap = dict(self.clock)
        snap[stream] = c
        return Tok(stream, c, snap)


DEBUG = False


class Ctx:
    def __init__(self, nc, stack):
        self.dbg = []
        self.dbg_names = {}
        self.nc = nc
        self.stack = stack
        self.sems = {}
        self.dma_counts = {}
        self.eng = {n: EngCtx(self, n, auto=(n in ("dve", "act", "pool")))
                    for n in ENGS}
        for n in ENGS:
            self.sems["s_" + n] = stack.enter_context(nc.semaphore("s_" + n))
        for s in ("dW", "dX", "dO", "dS"):
            self.sems[s] = stack.enter_context(nc.semaphore(s))
        self.pe = self.eng["pe"]
        self.dve = self.eng["dve"]
        self.act = self.eng["act"]
        self.pool = self.eng["pool"]
        self.sp = self.eng["sp"]
        self._sbuf_guards = []
        self._psum_guards = []
        self._tensors = {}
        self._phase_guards = []
        self._phase_ctr = 0
        self._phase_stack = []

    def sbuf(self, name, shape, dtype, side="left"):
        if name in self._tensors:
            return self._tensors[name]
        g = self.nc.sbuf_tensor(name, shape, dtype)
        h = g.__enter__()
        self._sbuf_guards.append(g)
        self._tensors[name] = h
        return h

    def psum(self, name, shape, dtype=F32):
        if name in self._tensors:
            return self._tensors[name]
        g = self.nc.psum_tensor(name, shape, dtype)
        h = g.__enter__()
        self._psum_guards.append(g)
        self._tensors[name] = h
        return h

    def barrier(self):
        toks = []
        for n in ENGS:
            e = self.eng[n]
            if e.uninced:
                toks.append(e.sig())
            elif e.count > 0:
                toks.append(Tok(e.sem_name, e.count, dict(e.clock)))
        for s, c in self.dma_counts.items():
            toks.append(Tok(s, c, {s: c}))
        for n in ENGS:
            self.eng[n].wait_all(toks)

    def phase_psum(self, name, shape, dtype=F32):
        self._phase_ctr += 1
        g = self.nc.psum_tensor(f"{name}_{self._phase_ctr}", shape, dtype)
        h = g.__enter__()
        self._phase_guards.append(g)
        return h

    def phase_sbuf(self, name, shape, dtype):
        self._phase_ctr += 1
        g = self.nc.sbuf_tensor(f"{name}_{self._phase_ctr}", shape, dtype)
        h = g.__enter__()
        self._phase_guards.append(g)
        return h

    def push_phase(self):
        self._phase_stack.append(len(self._phase_guards))

    def pop_phase(self):
        k = self._phase_stack.pop() if self._phase_stack else 0
        for g in reversed(self._phase_guards[k:]):
            g.__exit__(None, None, None)
        del self._phase_guards[k:]

    def end_phase(self):
        assert not self._phase_stack, "unbalanced push_phase"
        for g in reversed(self._phase_guards):
            g.__exit__(None, None, None)
        self._phase_guards = []

    def dump(self, name, ap, deps=()):
        if not DEBUG or name in self.dbg_names:
            return
        self.dbg_names[name] = True
        shape = list(ap.shape)
        dt = ap.dtype
        dst = self.nc.dram_tensor("dbg_" + name, shape, dt, kind="ExternalOutput").ap()
        self.sp.wait_all(list(deps))
        self.sp.dma_start(out=dst, in_=ap)
        self.sp.sig_dma("dO")

    def replay(self, block):
        raws = {"pe": block.tensor, "dve": block.vector, "act": block.scalar,
                "pool": block.gpsimd, "sp": block.sync}

        def make_body(name):
            def body(eng):
                for fn, incs, _auto in self.eng[name].recs:
                    inst = fn(eng)
                    for (sem, v) in incs:
                        assert inst is not None, f"then_inc on None inst ({name})"
                        inst.then_inc(self.sems[sem], v)
            return body
        for n in ENGS:
            if self.eng[n].recs:
                raws[n](make_body(n))




def host_constants():
    c = {}
    e = np.arange(D)[:, None]
    k = np.arange(KBINS)[None, :]
    ang = 2.0 * np.pi * ((e * k) % D).astype(np.float64) / D
    c["dftC"] = np.cos(ang).astype(np.float32)
    c["dftS"] = (-np.sin(ang)).astype(np.float32)
    w = np.full(KBINS, 2.0)
    w[0] = 1.0
    w[512] = 1.0
    ki = np.arange(KBINS)[:, None]
    ee = np.arange(D)[None, :]
    ang2 = 2.0 * np.pi * ((ki * ee) % D).astype(np.float64) / D
    cw = (w[:, None] * np.cos(ang2) / D).astype(np.float32)
    sw = (-w[:, None] * np.sin(ang2) / D).astype(np.float32)
    c["idftC"] = np.zeros((KPAD, D), np.float32)
    c["idftC"][:KBINS] = cw
    c["idftS"] = np.zeros((KPAD, D), np.float32)
    c["idftS"][:KBINS] = sw
    c["ident"] = np.eye(128, dtype=np.float32)
    tri = (np.arange(128)[:, None] <= np.arange(128)[None, :]).astype(np.float32)
    c["trimask"] = tri
    c["utri"] = tri.copy()
    c["ones128"] = np.ones((128, 128), np.float32)
    c["onesrow"] = np.ones((1, 128), np.float32)
    c["shamt"] = np.tile(np.array([0, 2, 4, 6], np.int32), 32)[:, None].copy()
    c["mask3"] = np.full((128, 1), 3, np.int32)
    bandmask = np.zeros((128, 16), np.float32)
    for h in range(16):
        bandmask[8 * h:8 * h + 6, h] = 1.0
    c["bandmask"] = bandmask
    tri16 = np.tile(tri, (1, 16))
    c["trimask16"] = np.ascontiguousarray(tri16)
    half0 = np.zeros((1, 128), np.float32)
    half0[0, :64] = 1.0
    half1 = np.zeros((1, 128), np.float32)
    half1[0, 64:] = 1.0
    c["half0"] = half0
    c["half1"] = half1
    c["ones16"] = np.ones((128, 16), np.float32)
    return c


def stage_inputs(inputs):
    g = {}
    g["l0a_wq"] = np.ascontiguousarray(
        np.concatenate([inputs["L0_attn_W1_q"], inputs["L0_attn_W2_q"]], 0).T)
    g["l0a_wk"] = np.ascontiguousarray(
        np.concatenate([inputs["L0_attn_W1_k"], inputs["L0_attn_W2_k"]], 0).T)
    g["l0a_wv"] = np.ascontiguousarray(np.asarray(inputs["L0_attn_Wv"]).T)
    g["l0a_bv"] = np.asarray(inputs["L0_attn_bv"]).reshape(1, D)
    g["l0a_wo"] = np.ascontiguousarray(np.asarray(inputs["L0_attn_Wo"]).T)
    g["l0a_bo"] = np.asarray(inputs["L0_attn_bo"]).reshape(1, D)
    g["l0f_w1"] = np.ascontiguousarray(np.concatenate(
        [np.asarray(inputs["L0_ffn_W1"]), np.asarray(inputs["L0_ffn_Ws1"])[None]], 0))
    g["l0f_w2"] = np.ascontiguousarray(np.concatenate(
        [np.asarray(inputs["L0_ffn_W2"]), np.asarray(inputs["L0_ffn_Ws2"])[None]], 0))
    g["l0f_wr"] = np.ascontiguousarray(np.asarray(inputs["L0_ffn_Wr"]))
    g["l1a_wq"] = np.ascontiguousarray(np.asarray(inputs["L1_attn_Wq"]))
    g["l1a_wk"] = np.ascontiguousarray(np.asarray(inputs["L1_attn_Wk"]))
    g["l1a_wv"] = np.ascontiguousarray(np.asarray(inputs["L1_attn_Wv"]))
    g["l1a_bq"] = np.ascontiguousarray(np.asarray(inputs["L1_attn_bq"]).reshape(DCH, 128).T)
    g["l1a_bk"] = np.ascontiguousarray(np.asarray(inputs["L1_attn_bk"]).reshape(DCH, 128).T)
    g["l1a_bv"] = np.ascontiguousarray(np.asarray(inputs["L1_attn_bv"]).reshape(DCH, 128).T)
    pk1 = np.asarray(inputs["L1_ffn_packed_weight"]).reshape(256, D)
    g["l1f_pk"] = np.ascontiguousarray(np.tile(pk1.T, (1, 4)))
    g["l1f_ws"] = np.asarray(inputs["L1_ffn_weight_scale"]).reshape(1, 1).astype(np.float32)
    g["l1f_bias"] = np.ascontiguousarray(
        np.asarray(inputs["L1_ffn_bias"]).reshape(DCH, 128).T)
    g.update(host_constants())
    x = np.asarray(inputs["L0_attn_x"])
    shards = [np.ascontiguousarray(x[2 * c:2 * c + 2].transpose(0, 2, 1))
              for c in range(NCORES)]
    return g, shards


BLOCK_INPUTS = {
    0: ("l0a_wq", "l0a_wk", "l0a_wv", "l0a_bv", "l0a_wo", "l0a_bo",
        "trimask16", "bandmask", "half0", "half1"),
    1: ("l0f_w1", "l0f_w2", "l0f_wr"),
    2: ("l1a_wq", "l1a_wk", "l1a_wv", "l1a_bq", "l1a_bk", "l1a_bv",
        "dftC", "dftS", "idftC", "idftS", "utri"),
    3: ("l1f_pk", "l1f_ws", "l1f_bias", "shamt", "mask3"),
}


def declare_dram(nc, g, launch_blocks):
    d = {}

    def inp(name):
        arr = g[name]
        dt = I32 if arr.dtype == np.int32 else F32
        d[name] = nc.dram_tensor(name, list(arr.shape), dt, kind="ExternalInput").ap()

    d["xin"] = nc.dram_tensor("xin", [NSEQ, D, T], F32, kind="ExternalInput").ap()
    d["xout"] = nc.dram_tensor("xout", [NSEQ, D, T], F32, kind="ExternalOutput").ap()
    for k in ("ident", "ones128", "onesrow"):
        inp(k)
    names = set()
    for blk in launch_blocks:
        names.update(BLOCK_INPUTS[blk])
    for k in sorted(names):
        inp(k)
    if 2 in launch_blocks:
        d["wtil"] = nc.dram_tensor("wtil", [6, D, KPAD], BF16, kind="Internal").ap()
        d["btil"] = nc.dram_tensor("btil", [1, 6 * KPAD], BF16, kind="Internal").ap()
    return d


class Bufs:
    pass


def alloc_persistent(ctx, launch_blocks=()):
    b = Bufs()
    b.x = ctx.sbuf("x_m", [128, DCH, T], F32)
    b.xln = ctx.sbuf("xln", [128, DCH, T + 2], BF16)
    b.ident = ctx.sbuf("c_ident", [128, 128], BF16)
    b.ones128 = ctx.sbuf("c_ones", [128, 128], BF16)
    b.onesrow = ctx.sbuf("c_onesrow", [1, 128], BF16)
    b.mrep = ctx.sbuf("mrep", [128, T], F32)
    b.rrep = ctx.sbuf("rrep", [128, T], F32)
    b.t1 = ctx.sbuf("t1rep", [128, T], F32)
    b.sq = ctx.sbuf("sqch", [128, 2, 512], BF16)
    b.s16 = ctx.sbuf("s16", [128, 2 * T], BF16)
    b.zbuf = ctx.sbuf("zbuf", [128, DCH, T], BF16)
    b.cpool = ctx.sbuf("cpool", [128, 8], F32)
    return b


CONST_VALS = (-1.0, LN_EPS, 1e-8)


def emit_load_const(ctx, dram, b):
    sp, pool = ctx.sp, ctx.pool
    toks = []
    for name, dst in (("ident", b.ident), ("ones128", b.ones128), ("onesrow", b.onesrow)):
        pool.dma_start(out=dst[:], in_=dram[name])
        toks.append(pool.sig_dma("dW"))
    for j, v in enumerate(CONST_VALS):
        pool.memset(b.cpool[:, j:j + 1], v)
        ctx.nc.const_aps.register(F32, v, b.cpool[:, j:j + 1])
    toks.append(pool.sig())
    return toks


def emit_load_x(ctx, dram, b, seq):
    sp = ctx.sp
    sp.dma_start(out=b.x[:], in_=dram["xin"][seq].rearrange("(c p) t -> p c t", p=128))
    return sp.sig_dma("dX")


def emit_store_x(ctx, dram, b, seq, deps):
    sp = ctx.sp
    sp.wait_all(deps)
    sp.dma_start(out=dram["xout"][seq].rearrange("(c p) t -> p c t", p=128), in_=b.x[:])
    return sp.sig_dma("dO")




def emit_stats_rows(ctx, b, src_get, psA, psB):
    pe, dve, act = ctx.pe, ctx.dve, ctx.act
    last = None
    sq_read = None
    for tg in range(T // 512):
        sl = slice(512 * tg, 512 * tg + 512)
        for c in range(DCH):
            src, rdy = src_get(c)
            pe.wait(rdy)
            if c == 0 and last is not None:
                pe.wait(last)
            pe.matmul(psA[:, 0:512], b.ones128[:], src[:, sl],
                      start=(c == 0), stop=(c == DCH - 1))
            act.wait(rdy)
            if c == 0 and last is not None:
                act.wait(last)
            slq = c % 2
            if sq_read is not None:
                act.wait(sq_read)
            act.activation(b.sq[:, slq, :], src[:, sl], AF.Square)
            sqt = act.sig()
            pe.wait(sqt)
            pe.matmul(psB[:, 0:512], b.ones128[:], b.sq[:, slq, :],
                      start=(c == 0), stop=(c == DCH - 1))
            if c >= 1:
                sq_read = pe.sig()
        tokp = pe.sig()
        dve.wait(tokp)
        dve.tensor_single_scalar(b.mrep[:, sl], psA[:, 0:512], 1.0 / D, ALU.mult)
        dve.tensor_single_scalar(b.t1[:, sl], psB[:, 0:512], 1.0 / D, ALU.mult)
        last = dve.sig()
    dve.tensor_tensor(b.rrep[:], b.mrep[:], b.mrep[:], ALU.mult)
    dve.tensor_tensor(b.t1[:], b.t1[:], b.rrep[:], ALU.subtract)
    tokv = dve.sig()
    act.wait(tokv)
    act.activation(b.t1[:], b.t1[:], AF.Ln, bias=LN_EPS)
    act.activation(b.rrep[:], b.t1[:], AF.Exp, scale=-0.5)
    last = act.sig()
    pe.wait(last)
    return last


def emit_preln(ctx, b, deps):
    pe, dve, act, pool = ctx.pe, ctx.dve, ctx.act, ctx.pool
    psA = ctx.phase_psum("lnA", [128, 512], F32)
    psB = ctx.phase_psum("lnB", [128, 512], F32)
    pool.wait_all(deps)
    pool.memset(b.xln[:, :, 0:2], 0.0)
    mz = pool.sig()
    ctoks = {}
    for c in range(DCH):
        act.wait_all(deps)
        act.copy(b.xln[:, c, 2:], b.x[:, c, :])
        ctoks[c] = act.sig()
    st = emit_stats_rows(ctx, b, lambda c: (b.xln[:, c, 2:], ctoks[c]), psA, psB)
    out = []
    for c in range(DCH):
        dve.wait(st)
        dve.wait_all(deps)
        dve.tensor_tensor(b.t1[:], b.x[:, c, :], b.mrep[:], ALU.subtract)
        dve.tensor_tensor(b.xln[:, c, 2:], b.t1[:], b.rrep[:], ALU.mult)
        out.append(dve.sig())
    out.append(mz)
    ctx.dump("xln", b.xln[:], out)
    ctx.dump("mrep", b.mrep[:], out)
    ctx.dump("rrep", b.rrep[:], out)
    return out


def _z_src_get(b, z_toks):
    if not isinstance(z_toks, (list, tuple)):
        z_toks = [z_toks]
    if len(z_toks) == DCH:
        return lambda c: (b.zbuf[:, c, :], z_toks[c])
    if len(z_toks) == 4 * DCH:
        return lambda c: (b.zbuf[:, c, :], z_toks[4 * c + 3])
    return lambda c: (b.zbuf[:, c, :], z_toks[-1])


def emit_transition(ctx, b, z_toks):
    pe, dve, act, pool = ctx.pe, ctx.dve, ctx.act, ctx.pool
    psA = ctx.phase_psum("trA", [128, 512], F32)
    psB = ctx.phase_psum("trB", [128, 512], F32)
    psC = ctx.phase_psum("trC", [128, 512], F32)
    psD = ctx.phase_psum("trD", [128, 512], F32)
    st = emit_stats_rows(ctx, b, _z_src_get(b, z_toks), psA, psB)
    pool.memset(b.xln[:, :, 0:2], 0.0)
    mz = pool.sig()
    ctoks = {}
    for c in range(DCH):
        dve.wait(st)
        dve.tensor_tensor(b.t1[:], b.zbuf[:, c, :], b.mrep[:], ALU.subtract)
        dve.tensor_tensor(b.t1[:], b.t1[:], b.rrep[:], ALU.mult)
        dve.tensor_tensor(b.x[:, c, :], b.x[:, c, :], b.t1[:], ALU.add)
        xt = dve.sig()
        act.wait(xt)
        act.copy(b.xln[:, c, 2:], b.x[:, c, :])
        ctoks[c] = act.sig()
    st2 = emit_stats_rows(ctx, b, lambda c: (b.xln[:, c, 2:], ctoks[c]), psC, psD)
    out = []
    for c in range(DCH):
        dve.wait(st2)
        dve.tensor_tensor(b.t1[:], b.x[:, c, :], b.mrep[:], ALU.subtract)
        dve.tensor_tensor(b.xln[:, c, 2:], b.t1[:], b.rrep[:], ALU.mult)
        out.append(dve.sig())
    out.append(mz)
    return out


def emit_postln_residual(ctx, b, z_toks):
    pe, dve, act, pool = ctx.pe, ctx.dve, ctx.act, ctx.pool
    psA = ctx.phase_psum("plA", [128, 512], F32)
    psB = ctx.phase_psum("plB", [128, 512], F32)
    st = emit_stats_rows(ctx, b, _z_src_get(b, z_toks), psA, psB)
    ctx.dump("pl_mrep", b.mrep[:], [st])
    ctx.dump("pl_rrep", b.rrep[:], [st])
    out = []
    for c in range(DCH):
        dve.wait(st)
        dve.tensor_tensor(b.t1[:], b.zbuf[:, c, :], b.mrep[:], ALU.subtract)
        dve.tensor_tensor(b.t1[:], b.t1[:], b.rrep[:], ALU.mult)
        dve.tensor_tensor(b.x[:, c, :], b.x[:, c, :], b.t1[:], ALU.add)
        out.append(dve.sig())
    ctx.dump("pl_x", b.x[:], out)
    return out




def emit_l1f_weights(ctx, dram, b):
    dve, act, sp = ctx.dve, ctx.act, ctx.sp
    b.l1f_wt = ctx.phase_sbuf("l1f_wt", [128, DCH, D], BF16)
    pk_s = ctx.phase_sbuf("l1f_pks", [128, 2, D], I32)
    sh_s = ctx.phase_sbuf("l1f_sh", [128, 1], I32)
    m3_s = ctx.phase_sbuf("l1f_m3", [128, 1], I32)
    b.l1f_ws = ctx.phase_sbuf("l1f_wss", [1, 1], F32)
    b.l1f_bias = ctx.phase_sbuf("l1f_bs", [128, DCH], F32)
    sp.dma_start(out=sh_s[:], in_=dram["shamt"])
    sp.sig_dma("dW")
    sp.dma_start(out=m3_s[:], in_=dram["mask3"])
    sp.sig_dma("dW")
    sp.dma_start(out=b.l1f_ws[:], in_=dram["l1f_ws"])
    sp.sig_dma("dW")
    sp.dma_start(out=b.l1f_bias[:], in_=dram["l1f_bias"])
    tl = sp.sig_dma("dW")
    pkv = dram["l1f_pk"].rearrange("(c p) o -> c p o", p=128)
    toks = []
    prev = {}
    for c in range(DCH):
        sl = c % 2
        if prev.get(sl) is not None:
            sp.wait(prev[sl])
        sp.dma_start(out=pk_s[:, sl, :], in_=pkv[c])
        tp = sp.sig_dma("dW")
        dve.wait(tp)
        dve.wait(tl)
        for s4 in range(1, 4):
            dve.tensor_single_scalar(pk_s[:, sl, 256 * s4:256 * (s4 + 1)],
                                     pk_s[:, sl, 256 * s4:256 * (s4 + 1)],
                                     2 * s4, ALU.logical_shift_right)
            dve.sig()
        dve.tensor_single_scalar(pk_s[:, sl, :], pk_s[:, sl, :], m3_s[:, 0:1],
                                 ALU.bitwise_and)
        tc = dve.sig()
        act.wait(tc)
        act.activation(b.l1f_wt[:, c, :], pk_s[:, sl, :], AF.Identity, bias=-1.0)
        ta = act.sig()
        prev[sl] = ta
        toks.append(ta)
    return toks


def emit_l1f(ctx, b, xln_toks, wt_toks):
    pe, dve, act, pool, sp = ctx.pe, ctx.dve, ctx.act, ctx.pool, ctx.sp
    xt = b.xln
    act.wait_all(xln_toks)
    act.activation(b.t1[:], xt[:, 0, 2:], AF.Abs)
    t_a = act.sig()
    dve.wait(t_a)
    for c in range(1, DCH):
        act.wait_all(xln_toks)
        act.activation(b.rrep[:], xt[:, c, 2:], AF.Abs)
        t_a = act.sig()
        dve.wait(t_a)
        dve.tensor_max(b.t1[:], b.t1[:], b.rrep[:])
        t_d = dve.sig()
        act.wait(t_d)
    fold_bf = b.s16[:, 0:T]
    dve.tensor_copy(fold_bf, b.t1[:])
    tfold = dve.sig()
    ps_t = ctx.phase_psum("l1f_pst", [128, 128], BF16)
    cols = ctx.phase_sbuf("l1f_cols", [128, NT], F32)
    colsb = ctx.phase_sbuf("l1f_colsb", [128, NT], BF16)
    rows = ctx.phase_sbuf("l1f_rows", [NT, 128], F32)
    tcol = None
    for j in range(NT):
        pe.wait(tfold)
        if tcol is not None:
            pe.wait(tcol)
        pe.transpose(ps_t[:], fold_bf[:, 128 * j:128 * j + 128], b.ident[:])
        tt = pe.sig()
        dve.wait(tt)
        dve.reduce_max(cols[:, j:j + 1], ps_t[:], AX.X)
        tcol = dve.sig()
    dve.tensor_copy(colsb[:], cols[:])
    tcb = dve.sig()
    ps_r = ctx.phase_psum("l1f_psr", [128, 128], BF16)
    pe.wait(tcb)
    pe.transpose(ps_r[0:NT, :], colsb[:], b.ident[:])
    tr = pe.sig()
    dve.wait(tr)
    dve.tensor_copy(rows[:], ps_r[0:NT, :])
    trows = dve.sig()
    amaxrow = b.t1[0:1, 0:T]
    asrow = b.rrep[0:1, 0:T]
    asrow_bf = b.s16[0:1, T:T + T]
    wsasrow_bf = b.s16[0:1, 0:T]
    sp.wait(trows)
    sp.dma_start(out=amaxrow, in_=rows[:])
    tdma = sp.sig_dma("dS")
    dve.wait(tdma)
    dve.tensor_single_scalar(amaxrow, amaxrow, 1e-5, ALU.max)
    dve.sig()
    dve.reciprocal(asrow, amaxrow)
    dve.sig()
    dve.tensor_single_scalar(asrow, asrow, 127.0, ALU.mult)
    dve.sig()
    dve.tensor_copy(asrow_bf, asrow)
    tas = dve.sig()
    dve.tensor_single_scalar(amaxrow, amaxrow, b.l1f_ws[0:1, 0:1], ALU.mult)
    dve.sig()
    dve.tensor_single_scalar(amaxrow, amaxrow, 1.0 / 127.0, ALU.mult)
    dve.sig()
    dve.tensor_copy(wsasrow_bf, amaxrow)
    twsas = dve.sig()
    ps_bc = ctx.phase_psum("l1f_psbc", [128, 512], F32)
    asrep = b.mrep
    ev = None
    for tg in range(T // 512):
        sl = slice(512 * tg, 512 * tg + 512)
        pe.wait(tas)
        if ev is not None:
            pe.wait(ev)
        pe.matmul(ps_bc[:], b.onesrow[:], asrow_bf[0:1, sl], start=True, stop=True)
        t_ = pe.sig()
        dve.wait(t_)
        dve.tensor_copy(asrep[:, sl], ps_bc[:])
        ev = dve.sig()
    ctx.dump("asrep", asrep[:], [ev])
    qi = ctx.phase_sbuf("l1f_qi", [128, T], I32)
    qtoks = []
    for c in range(DCH):
        dve.wait(ev)
        dve.wait_all(xln_toks)
        dve.tensor_tensor(b.t1[:], xt[:, c, 2:], asrep[:], ALU.mult)
        dve.sig()
        dve.tensor_scalar(b.t1[:], b.t1[:], -128.0, 127.0, ALU.max, ALU.min)
        dve.sig()
        dve.tensor_copy(qi[:], b.t1[:])
        tq = dve.sig()
        act.wait(tq)
        act.copy(xt[:, c, 2:], qi[:])
        qtoks.append(act.sig())
    ctx.dump("qbuf", xt[:], qtoks)
    wsasrep = b.s16[:, T:2 * T]
    ev2 = None
    for tg in range(T // 512):
        sl = slice(512 * tg, 512 * tg + 512)
        pe.wait(twsas)
        pe.wait(ev2 if ev2 is not None else ev)
        pe.matmul(ps_bc[:], b.onesrow[:], wsasrow_bf[0:1, sl], start=True, stop=True)
        t_ = pe.sig()
        dve.wait(t_)
        dve.tensor_copy(wsasrep[:, sl], ps_bc[:])
        ev2 = dve.sig()
    trep = ev2
    ps_z = [ctx.phase_psum("l1f_z0", [128, 512], F32), ctx.phase_psum("l1f_z1", [128, 512], F32)]
    ev_prev = [None, None]
    ztoks = []
    for m in range(DCH):
        for tg in range(T // 512):
            sl = slice(512 * tg, 512 * tg + 512)
            idx = (m * (T // 512) + tg) % 2
            pz = ps_z[idx]
            for c in range(DCH):
                pe.wait(qtoks[c])
                pe.wait(wt_toks[c])
                if c == 0 and ev_prev[idx] is not None:
                    pe.wait(ev_prev[idx])
                pe.matmul(pz[:], b.l1f_wt[:, c, 128 * m:128 * (m + 1)],
                          xt[:, c, 2 + 512 * tg:2 + 512 * (tg + 1)],
                          start=(c == 0), stop=(c == DCH - 1))
            tmm = pe.sig()
            dve.wait(tmm)
            dve.wait(trep)
            dve.tensor_tensor(b.t1[:, 0:512], pz[:], wsasrep[:, sl], ALU.mult)
            dve.sig()
            dve.tensor_scalar(b.zbuf[:, m, sl], b.t1[:, 0:512],
                              b.l1f_bias[:, m:m + 1], None, ALU.add)
            zt = dve.sig()
            ev_prev[idx] = zt
            ztoks.append(zt)
    ctx.dump("zbuf", b.zbuf[:], ztoks)
    ctx.dump("wsasrep", wsasrep[:], [trep])
    ctx.dump("l1fwt", b.l1f_wt[:], wt_toks)
    return ztoks




def emit_l0f(ctx, dram, b, xln_toks):
    pe, dve, act, pool, sp = ctx.pe, ctx.dve, ctx.act, ctx.pool, ctx.sp
    NE = 9
    b.mask_fm = ctx.phase_sbuf("l0f_maskfm", [8, T], BF16)
    ctx.push_phase()
    wr = ctx.phase_sbuf("l0f_wr", [128, DCH, 8], BF16)
    pool.dma_start(out=wr[:], in_=dram["l0f_wr"].rearrange("(c p) e -> p c e", p=128))
    twr = pool.sig_dma("dW")
    ps_r = ctx.phase_psum("l0f_psr", [8, 512], F32)
    rlog = ctx.phase_sbuf("l0f_rlog", [8, T], BF16)
    ev = None
    for tg in range(T // 512):
        sl2 = slice(2 + 512 * tg, 2 + 512 * (tg + 1))
        for c in range(DCH):
            pe.wait(twr)
            pe.wait_all(xln_toks)
            if c == 0 and ev is not None:
                pe.wait(ev)
            pe.matmul(ps_r[:], wr[:, c, :], b.xln[:, c, sl2],
                      start=(c == 0), stop=(c == DCH - 1))
        tm_ = pe.sig()
        act.wait(tm_)
        act.copy(rlog[:, 512 * tg:512 * (tg + 1)], ps_r[:])
        ev = act.sig()
    trlog = ev
    ps_g = ctx.phase_psum("l0f_psg", [128, 8], BF16)
    ps_gb = ctx.phase_psum("l0f_psgb", [8, 128], BF16)
    ltm = ctx.phase_sbuf("l0f_ltm", [128, 8], F32)
    g1 = ctx.phase_sbuf("l0f_g1", [128, 8], F32)
    g2 = ctx.phase_sbuf("l0f_g2", [128, 8], F32)
    gc = ctx.phase_sbuf("l0f_gc", [128, 12], F32)
    mask_tm = ctx.phase_sbuf("l0f_masktm", [128, 8], BF16)
    mask_fm = b.mask_fm
    tmask = None
    tp_g = None
    tp_gb = None
    for j in range(NT):
        pe.wait(trlog)
        if tp_g is not None:
            pe.wait(tp_g)
        pe.transpose(ps_g[:], rlog[:, 128 * j:128 * (j + 1)], b.ident[0:8, 0:8])
        tt = pe.sig()
        act.wait(tt)
        act.copy(ltm[:], ps_g[:])
        tp_g = act.sig()
        dve.wait(tp_g)
        dve.reduce_max(gc[:, 0:1], ltm[:], AX.X)
        dve.tensor_single_scalar(g1[:], ltm[:], gc[:, 0:1], ALU.is_ge)
        dve.reduce_sum(gc[:, 1:2], g1[:], AX.X)
        dve.scalar_tensor_tensor(g2[:], g1[:], -1e30, ltm[:], ALU.mult, ALU.add)
        dve.reduce_max(gc[:, 2:3], g2[:], AX.X)
        dve.tensor_single_scalar(gc[:, 3:4], gc[:, 1:2], 1.5, ALU.is_gt)
        dve.tensor_tensor(gc[:, 4:5], gc[:, 0:1], gc[:, 2:3], ALU.subtract)
        dve.tensor_tensor(gc[:, 5:6], gc[:, 3:4], gc[:, 4:5], ALU.mult)
        dve.tensor_tensor(gc[:, 6:7], gc[:, 2:3], gc[:, 5:6], ALU.add)
        dve.tensor_tensor(gc[:, 7:8], gc[:, 6:7], gc[:, 0:1], ALU.subtract)
        td = dve.sig()
        act.wait(td)
        act.activation(gc[:, 8:9], gc[:, 7:8], AF.Exp)
        te = act.sig()
        dve.wait(te)
        dve.tensor_single_scalar(gc[:, 9:10], gc[:, 8:9], 1.0, ALU.add)
        dve.tensor_single_scalar(gc[:, 10:11], gc[:, 9:10], 1.0, ALU.is_equal)
        dve.tensor_single_scalar(mask_tm[:], g1[:], gc[:, 10:11], ALU.mult)
        tmtm = dve.sig()
        pe.wait(tmtm)
        if tp_gb is not None:
            pe.wait(tp_gb)
        pe.transpose(ps_gb[0:8, :], mask_tm[:], b.ident[:])
        tb = pe.sig()
        act.wait(tb)
        act.copy(mask_fm[0:8, 128 * j:128 * (j + 1)], ps_gb[0:8, :])
        tp_gb = act.sig()
        tmask = tp_gb
    ctx.barrier()
    ctx.pop_phase()
    w1b = ctx.phase_sbuf("l0f_w1b", [128, 2, DCH, 512], BF16)
    w2b = ctx.phase_sbuf("l0f_w2b", [128, 2, 4, D], BF16)
    rowbuf = ctx.phase_sbuf("l0f_rowb", [1, T], BF16)
    maskrep = ctx.phase_sbuf("l0f_mrep", [128, T], BF16)
    hraw = b.s16[:, 0:T]
    hm = b.s16[:, T:2 * T]
    _h01 = b.t1[:].bitcast(BF16)
    _h23 = b.mrep[:].bitcast(BF16)
    hsq = lambda fc, sl=slice(0, T): (_h01 if fc < 2 else _h23)[
        :, T * (fc % 2) + (sl.start or 0):T * (fc % 2) + (sl.stop if sl.stop is not None else T)]
    ps_h = ctx.phase_psum("l0f_psh", [128, T], F32)
    ps_z = ctx.phase_psum("l0f_psz", [128, T], F32)
    w_used = [None, None]
    tmr_used = None
    tmr = None
    th_half = [None, None]
    tz_evac = None
    hsq_read = [None] * 4
    ztoks = []
    for e in range(NE):
        slot = e % 2
        pool.wait(w_used[slot])
        pool.dma_start(out=w1b[:, slot], in_=dram["l0f_w1"][e].rearrange(
            "(c p) f -> p c f", p=128))
        pool.sig_dma("dW")
        pool.dma_start(out=w2b[:, slot], in_=dram["l0f_w2"][e].rearrange(
            "(c p) o -> p c o", p=128))
        tw = pool.sig_dma("dW")
        if e < 8:
            sp.wait(tmask)
            sp.wait(tmr)
            sp.dma_start(out=rowbuf[:], in_=mask_fm[e:e + 1, :])
            trow = sp.sig_dma("dS")
            pe.wait(trow)
            pe.wait(th_half[0])
            pe.wait(th_half[1])
            if tmr is not None:
                pe.wait(tmr)
            for tg in range(T // 512):
                sl = slice(512 * tg, 512 * (tg + 1))
                pe.matmul(ps_h[:, sl], b.onesrow[:], rowbuf[0:1, sl],
                          start=True, stop=True)
            tb_ = pe.sig()
            dve.wait(tb_)
            dve.wait(tmr_used)
            for tg in range(T // 512):
                sl = slice(512 * tg, 512 * (tg + 1))
                dve.tensor_copy(maskrep[:, sl], ps_h[:, sl])
            tmr = dve.sig()
        hsq_toks = []
        for fc in range(4):
            for half in range(2):
                for d in range(DCH):
                    pe.wait(tw)
                    pe.wait_all(xln_toks)
                    if d == 0:
                        pe.wait(th_half[half])
                        if fc == 0:
                            pe.wait(tmr)
                    lh = w1b[:, slot, d, 128 * fc:128 * (fc + 1)]
                    for tg in (2 * half, 2 * half + 1):
                        pe.matmul(ps_h[:, 512 * tg:512 * (tg + 1)], lh,
                                  b.xln[:, d, 2 + 512 * tg:2 + 512 * (tg + 1)],
                                  start=(d == 0), stop=(d == DCH - 1))
                tmm = pe.sig()
                hsl = slice(1024 * half, 1024 * (half + 1))
                dve.wait(tmm)
                dve.wait(hsq_read[fc])
                dve.tensor_single_scalar(hraw[:, hsl], ps_h[:, hsl], 0.0,
                                         ALU.max)
                th_half[half] = dve.sig()
                if e < 8:
                    dve.tensor_tensor(hm[:, hsl], hraw[:, hsl],
                                      maskrep[:, hsl], ALU.mult)
                    dve.tensor_tensor(hsq(fc, hsl), hm[:, hsl], hraw[:, hsl],
                                      ALU.mult)
                else:
                    dve.tensor_tensor(hsq(fc, hsl), hraw[:, hsl], hraw[:, hsl],
                                      ALU.mult)
            hsq_toks.append(dve.sig())
        tmr_used = hsq_toks[-1]
        if e == 8:
            pass
        for m in range(DCH):
            for fc in range(4):
                pe.wait(hsq_toks[fc])
                if fc == 0:
                    pe.wait(tz_evac)
                lh = w2b[:, slot, fc, 128 * m:128 * (m + 1)]
                for tg in range(T // 512):
                    pe.matmul(ps_z[:, 512 * tg:512 * (tg + 1)], lh,
                              hsq(fc, slice(512 * tg, 512 * (tg + 1))),
                              start=(fc == 0), stop=(fc == 3))
            tmm = pe.sig()
            dve.wait(tmm)
            if e == 0:
                dve.tensor_copy(b.zbuf[:, m, :], ps_z[:])
            else:
                dve.tensor_tensor(b.zbuf[:, m, :], b.zbuf[:, m, :], ps_z[:], ALU.add)
            tz_evac = dve.sig()
            if e == NE - 1:
                ztoks.append(tz_evac)
        for fc in range(4):
            hsq_read[fc] = tz_evac
        w_used[slot] = tz_evac
    ctx.dump("moe_maskfm", mask_fm[:], [tmask])
    ctx.dump("moe_z", b.zbuf[:], ztoks)
    return ztoks



PAIRS = ((0, 1), (0, 2), (0, 3), (1, 2), (1, 3), (2, 3))
JSIGN = (1, -1, 1, 1, -1, 1)
ATT_SCALE = float(np.float32(6.0 ** (-0.5)))


def emit_l0a(ctx, dram, b, xln_toks):
    pe, dve, act, pool, sp = ctx.pe, ctx.dve, ctx.act, ctx.pool, ctx.sp
    bandm = ctx.phase_sbuf("l0a_band", [128, 16], F32)
    tri16 = ctx.phase_sbuf("l0a_tri16", [128, T], BF16)
    half01 = ctx.phase_sbuf("l0a_half", [1, 256], BF16)
    sp.dma_start(out=bandm[:], in_=dram["bandmask"])
    sp.sig_dma("dW")
    pool.dma_start(out=tri16[:], in_=dram["trimask16"])
    pool.sig_dma("dW")
    pool.dma_start(out=half01[0:1, 0:128], in_=dram["half0"])
    pool.sig_dma("dW")
    pool.dma_start(out=half01[0:1, 128:256], in_=dram["half1"])
    tw0 = pool.sig_dma("dW")
    vtm = b.zbuf[:].rearrange("p c t -> p (c t)").rearrange("p (j o) -> p j o", o=D)
    qspec = ctx.phase_sbuf("l0a_qspec", [128, T], BF16)
    kspec = ctx.phase_sbuf("l0a_kspec", [128, T], BF16)
    ctx.push_phase()
    wq = ctx.phase_sbuf("l0a_wqb", [128, 16, 128], BF16)
    wk = ctx.phase_sbuf("l0a_wkb", [128, DCH, 128], BF16)
    wv = ctx.phase_sbuf("l0a_wvb", [128, DCH, D], BF16)
    bvr = ctx.phase_sbuf("l0a_bvr", [1, D], BF16)
    pool.dma_start(out=wq[:], in_=dram["l0a_wq"].rearrange("(c p) o -> p c o", p=128))
    pool.sig_dma("dW")
    pool.dma_start(out=wk[:], in_=dram["l0a_wk"].rearrange("(c p) o -> p c o", p=128))
    pool.sig_dma("dW")
    pool.dma_start(out=wv[:], in_=dram["l0a_wv"].rearrange("(c p) o -> p c o", p=128))
    pool.sig_dma("dW")
    pool.dma_start(out=bvr[:], in_=dram["l0a_bv"])
    tw = pool.sig_dma("dW")
    q12f = ctx.phase_sbuf("l0a_q12f", [128, T], BF16)
    k12f = ctx.phase_sbuf("l0a_k12f", [128, T], BF16)
    _rbf = b.rrep[:].bitcast(BF16)
    ps_p = ctx.phase_psum("l0a_psp", [128, 2, 512], F32)
    ps_t = ctx.phase_psum("l0a_pst", [128, 128], BF16)
    ps_v = ctx.phase_psum("l0a_psv", [128, 2, 512], F32)
    tvev = {}
    for j in range(NT):
        tsl = slice(2 + 128 * j, 2 + 128 * (j + 1))
        for half in range(2):
            pv = ps_v[:, half, :]
            osl = slice(512 * half, 512 * (half + 1))
            for dc in range(DCH):
                pe.wait(tw)
                pe.wait_all(xln_toks)
                if dc == 0 and j >= 1:
                    pe.wait(tvev[(j - 1, half)])
                pe.matmul(pv[:], b.xln[:, dc, tsl], wv[:, dc, osl],
                          start=(dc == 0), stop=False)
            pe.matmul(pv[:], b.onesrow[:, 0:128], bvr[0:1, osl],
                      start=False, stop=True)
            tmm = pe.sig()
            act.wait(tmm)
            act.copy(vtm[:, j, osl], pv[:])
            tvev[(j, half)] = act.sig()
    tqf = None
    for tg in range(T // 512):
        sl = slice(512 * tg, 512 * (tg + 1))
        for src, wgt, nch, off in ((q12f, wq, 16, None), (k12f, wk, DCH, None)):
            pp = ps_p[:, 0 if src is q12f else 1, :]
            for c in range(nch):
                pe.wait(tw)
                pe.wait_all(xln_toks)
                if c == 0 and tqf is not None:
                    pe.wait(tqf)
                if src is q12f and c >= DCH:
                    rhs = b.xln[:, c - DCH, 1 + 512 * tg:1 + 512 * (tg + 1)]
                else:
                    rhs = b.xln[:, c, 2 + 512 * tg:2 + 512 * (tg + 1)]
                pe.matmul(pp[:], wgt[:, c, :], rhs, start=(c == 0), stop=(c == nch - 1))
            tmm = pe.sig()
            act.wait(tmm)
            act.copy(src[:, sl], pp[:])
            tqf = act.sig()
    q12tmF = _rbf[:, 0:T]
    k12tmF = _rbf[:, T:2 * T]
    _tbf = b.t1[:].bitcast(BF16)
    qlinF = _tbf[:, 0:T]
    klinF = _tbf[:, T:2 * T]
    pool.memset(qlinF, 0.0)
    pool.memset(klinF, 0.0)
    tlz = pool.sig()
    ttm = None
    for j in range(NT):
        for which, (f_, dstF) in enumerate(((q12f, q12tmF), (k12f, k12tmF))):
            pe.wait(tqf)
            if ttm is not None:
                pe.wait(ttm)
            pe.transpose(ps_t[:], f_[:, 128 * j:128 * (j + 1)], b.ident[:])
            tt = pe.sig()
            act.wait(tt)
            act.copy(dstF[:, 128 * j:128 * (j + 1)], ps_t[:])
            ttm = act.sig()
    qa = q12tmF.rearrange("p (j x) -> p j x", x=128)[:, :, 0:64]\
        .rearrange("p j (h i) -> p j h i", i=4)
    qb2 = q12tmF.rearrange("p (j x) -> p j x", x=128)[:, :, 64:128]\
        .rearrange("p j (h i) -> p j h i", i=4)
    ka = k12tmF.rearrange("p (j x) -> p j x", x=128)[:, :, 0:64]\
        .rearrange("p j (h i) -> p j h i", i=4)
    kb2 = k12tmF.rearrange("p (j x) -> p j x", x=128)[:, :, 64:128]\
        .rearrange("p j (h i) -> p j h i", i=4)
    lvq = qlinF.rearrange("p (j x) -> p j x", x=128)\
        .rearrange("p j (h c) -> p j h c", c=8)
    lvk = klinF.rearrange("p (j x) -> p j x", x=128)\
        .rearrange("p j (h c) -> p j h c", c=8)
    t_a = b.mrep[:, 0:256].rearrange("p (j h u) -> p j h u", h=16, u=1)
    t_b = b.mrep[:, 256:512].rearrange("p (j h u) -> p j h u", h=16, u=1)
    dve.wait(ttm)
    dve.wait(tlz)
    for which in (0, 1):
        a1, a2, lv = (qa, qb2, lvq) if which == 0 else (ka, kb2, lvk)
        for c6 in range(6):
            if which == 0:
                i_, j_ = PAIRS[c6]
                neg = False
            else:
                i_, j_ = PAIRS[5 - c6]
                neg = (JSIGN[c6] < 0)
            dve.tensor_tensor(t_a, a1[:, :, :, i_:i_ + 1], a2[:, :, :, j_:j_ + 1], ALU.mult)
            dve.tensor_tensor(t_b, a1[:, :, :, j_:j_ + 1], a2[:, :, :, i_:i_ + 1], ALU.mult)
            if neg:
                dve.tensor_tensor(lv[:, :, :, c6:c6 + 1], t_b, t_a, ALU.subtract)
            else:
                dve.tensor_tensor(lv[:, :, :, c6:c6 + 1], t_a, t_b, ALU.subtract)
    tpl = dve.sig()
    ncol = b.rrep
    for which in (0, 1):
        linF = qlinF if which == 0 else klinF
        lv = lvq if which == 0 else lvk
        dve.wait(tpl)
        dve.tensor_tensor(b.mrep[:], linF, linF, ALU.mult)
        dve.tensor_reduce(ncol[:, 0:256].rearrange("p (x u) -> p x u", u=1),
                          b.mrep[:].rearrange("p (x c) -> p x c", c=8), AX.X, ALU.add)
        tn_ = dve.sig()
        act.wait(tn_)
        act.activation(ncol[:, 256:512], ncol[:, 0:256], AF.Sqrt)
        tsq2 = act.sig()
        dve.wait(tsq2)
        dve.tensor_single_scalar(ncol[:, 256:512], ncol[:, 256:512], 1e-12, ALU.max)
        dve.reciprocal(ncol[:, 512:768], ncol[:, 256:512])
        dve.tensor_tensor(lv, lv, ncol[:, 512:768]
                          .rearrange("p (j h u) -> p j h u", h=16, u=1)
                          .broadcast_to([128, NT, 16, 8]), ALU.mult)
        tnorm = dve.sig()
    tspec = None
    for j in range(NT):
        for which in (0, 1):
            linF = qlinF if which == 0 else klinF
            pe.wait(tnorm)
            if tspec is not None:
                pe.wait(tspec)
            pe.transpose(ps_t[:], linF[:, 128 * j:128 * (j + 1)], b.ident[:])
            tt2 = pe.sig()
            act.wait(tt2)
            act.copy((qspec if which == 0 else kspec)[:, 128 * j:128 * (j + 1)], ps_t[:])
            tspec = act.sig()
    ctx.barrier()
    ctx.pop_phase()
    ctx.push_phase()
    psb = [b.s16[:, 0:T], b.s16[:, T:2 * T]]
    qb = ctx.phase_sbuf("l0a_qb", [128, T], BF16)
    brd_all = ctx.phase_sbuf("l0a_brdall", [128, 2, 512], BF16)
    zrow = ctx.phase_sbuf("l0a_zrow", [1, 512], BF16)
    pool.memset(zrow[:], 0.0)
    tzr = pool.sig()
    rrow = b.t1[0:1, 0:T]
    rrowb = ctx.phase_sbuf("l0a_rrowb", [1, T], BF16)
    ps_l = ctx.phase_psum("l0a_psl", [128, 2, 512], F32)
    ps_oa = ctx.phase_psum("l0a_psoa", [128, 4, 128], F32)
    ps_ob = ctx.phase_psum("l0a_psob", [128, 4, 128], F32)
    ps_s = ctx.phase_psum("l0a_pss", [128, T], F32)
    oview = [b.xln[:, p, 2:2 + T] for p in range(DCH)]

    def oview_a(q):
        return b.xln[:, 0:4, 2 + 128 * q:2 + 128 * (q + 1)]

    def oview_b(q):
        return b.xln[:, 4:8, 2 + 128 * q:2 + 128 * (q + 1)]
    tqb = None
    texp = [None, None]
    tpread = [None, None]
    pslw = [None, None]
    eread = [None, None]
    ebuf = b.mrep[:].bitcast(BF16)
    tsum_ev = None
    to_ev = None
    tbrd_ev = None
    ztoks_oready = []
    for q in range(NT):
        dve.wait(tspec)
        dve.wait(tqb)
        for h in range(16):
            dve.tensor_single_scalar(qb[:, 128 * h:128 * (h + 1)],
                                     qspec[:, 128 * q:128 * (q + 1)],
                                     bandm[:, h:h + 1], ALU.mult)
        tqbv = dve.sig()
        pe.wait(tzr)
        pe.wait(to_ev)
        pe.matmul(ps_oa[:].rearrange("p a b -> p (a b)"), b.onesrow[:, 0:128],
                  zrow[0:1, :], start=True, stop=True)
        pe.matmul(ps_ob[:].rearrange("p a b -> p (a b)"), b.onesrow[:, 0:128],
                  zrow[0:1, :], start=True, stop=True)

        def emit_sums_attnv(k):
            pb2 = k % 2
            pe.wait(texp[pb2])
            if k == 0:
                pe.wait(tsum_ev)
            for sg in range(T // 512):
                pe.matmul(ps_s[:, 512 * sg:512 * (sg + 1)], b.ones128[:],
                          psb[pb2][:, 512 * sg:512 * (sg + 1)],
                          start=(k == 0), stop=(k == q))
            for p in range(DCH):
                po = (ps_oa if p < 4 else ps_ob)[:, p % 4, :]
                pe.matmul(po[0:64, :], vtm[:, k, 128 * p:128 * p + 64],
                          psb[pb2][:, 256 * p:256 * p + 128],
                          start=False, stop=(k == q), tile_position=(0, 0))
                pe.matmul(po[64:128, :], vtm[:, k, 128 * p + 64:128 * (p + 1)],
                          psb[pb2][:, 256 * p + 128:256 * (p + 1)],
                          start=False, stop=(k == q), tile_position=(0, 64))
            tpread[pb2] = pe.sig()

        for k in range(q + 1):
            pb = k % 2
            for ch in range(4):
                sl_ = ch % 2
                pe.wait(tqbv)
                pe.wait(tspec)
                pe.wait(pslw[sl_])
                pe.matmul(ps_l[:, sl_, :],
                          kspec[:, 128 * k:128 * (k + 1)],
                          qb[:, 512 * ch:512 * (ch + 1)],
                          start=True, stop=True)
                tmm = pe.sig()
                dve.wait(tmm)
                dve.wait(eread[sl_])
                dve.tensor_copy(ebuf[:, 512 * sl_:512 * (sl_ + 1)], ps_l[:, sl_, :])
                pslw[sl_] = dve.sig()
                act.wait(pslw[sl_])
                act.wait(tpread[pb])
                act.activation(psb[pb][:, 512 * ch:512 * (ch + 1)],
                               ebuf[:, 512 * sl_:512 * (sl_ + 1)],
                               AF.Exp, scale=ATT_SCALE)
                eread[sl_] = act.sig()
            texp[pb] = eread[1]
            if k == q:
                dve.wait(texp[pb])
                dve.tensor_tensor(psb[pb][:], psb[pb][:], tri16[:], ALU.mult)
                texp[pb] = dve.sig()
            if q == 3 and k == 1:
                ctx.dump("l0a_P31", psb[pb][:, :], [texp[pb]])
            if k >= 1:
                emit_sums_attnv(k - 1)
        emit_sums_attnv(q)
        tka = tpread[q % 2]
        dve.wait(tka)
        dve.tensor_copy(rrow[0:1, :], ps_s[0:1, :])
        tsr = dve.sig()
        act.wait(tsr)
        act.activation(rrow[0:1, :], rrow[0:1, :], AF.Ln)
        act.activation(rrowb[0:1, :], rrow[0:1, :], AF.Exp, scale=-1.0)
        trr = act.sig()
        if q == 3:
            ctx.dump("l0a_r3", rrowb[0:1, :], [trr])
        tsum_ev = trr
        rview = rrowb[0:1, :].rearrange("u (a b q) -> u a b q", a=8, b=2)
        for half in range(2):
            pe.wait(trr)
            pe.wait(pslw[half])
            pe.wait(tbrd_ev)
            pe.matmul(ps_l[:, half, :], half01[0:1, 0:128],
                      rview[:, 4 * half:4 * (half + 1), 0, :], start=True, stop=True)
            pe.matmul(ps_l[:, half, :], half01[0:1, 128:256],
                      rview[:, 4 * half:4 * (half + 1), 1, :], start=False, stop=True)
        tbb = pe.sig()
        dve.wait(tbb)
        dve.wait(to_ev)
        dve.tensor_copy(brd_all[:], ps_l[:])
        tbrd_ev = dve.sig()
        pslw = [tbrd_ev, tbrd_ev]
        dve.wait(tbrd_ev)
        dve.wait(tka)
        dve.tensor_tensor(oview_a(q), ps_oa[:],
                          brd_all[:, 0, :].rearrange("p (a q) -> p a q", a=4), ALU.mult)
        dve.tensor_tensor(oview_b(q), ps_ob[:],
                          brd_all[:, 1, :].rearrange("p (a q) -> p a q", a=4), ALU.mult)
        to_ev = dve.sig()
        ztoks_oready.append(to_ev)
    ctx.dump("l0a_o", b.xln[:], ztoks_oready)
    ctx.barrier()
    ctx.pop_phase()
    ctx.push_phase()
    wo = ctx.phase_sbuf("l0a_wob", [128, DCH, D], BF16)
    bor = ctx.phase_sbuf("l0a_bor", [1, D], BF16)
    pool.dma_start(out=wo[:], in_=dram["l0a_wo"].rearrange("(c p) o -> p c o", p=128))
    pool.sig_dma("dW")
    pool.dma_start(out=bor[:], in_=dram["l0a_bo"])
    two = pool.sig_dma("dW")
    ones512 = ctx.phase_sbuf("l0a_ones512", [1, 512], BF16)
    pool.memset(ones512[:], 1.0)
    ton = pool.sig()
    ps_z = ctx.phase_psum("l0a_psz", [128, T], F32)
    ztoks = []
    tz_ev = None
    for m in range(DCH):
        for p in range(DCH):
            pe.wait(two)
            pe.wait_all(ztoks_oready)
            if p == 0:
                pe.wait(tz_ev)
            lh = wo[:, p, 128 * m:128 * (m + 1)]
            for tg in range(T // 512):
                pe.matmul(ps_z[:, 512 * tg:512 * (tg + 1)], lh,
                          oview[p][:, 512 * tg:512 * (tg + 1)],
                          start=(p == 0), stop=False)
        for tg in range(T // 512):
            pe.wait(ton)
            pe.matmul(ps_z[:, 512 * tg:512 * (tg + 1)],
                      bor[0:1, 128 * m:128 * (m + 1)], ones512[0:1, :],
                      start=False, stop=True)
        tmm = pe.sig()
        dve.wait(tmm)
        dve.tensor_copy(b.zbuf[:, m, :], ps_z[:])
        tz_ev = dve.sig()
        ztoks.append(tz_ev)
    ctx.barrier()
    ctx.pop_phase()
    ctx.dump("l0a_z", b.zbuf[:], ztoks)
    return ztoks



L1A_HALVES = ((0, 1, 2), (3, 4))


def emit_l1a_fold(ctx, dram, b):
    pe, dve, act, pool, sp = ctx.pe, ctx.dve, ctx.act, ctx.pool, ctx.sp
    ctx.push_phase()
    b.bias_tilde = ctx.phase_sbuf("l1a_btil", [1, 6 * KPAD], BF16)
    wxb = ctx.phase_sbuf("l1a_wxb", [128, DCH, D], BF16)
    cb = ctx.phase_sbuf("l1a_cb", [128, DCH, KBINS], BF16)
    bcol = ctx.phase_sbuf("l1a_bcol", [128, DCH], BF16)
    stg = ctx.phase_sbuf("l1a_stg", [128, KPAD], BF16)
    ps_f = ctx.phase_psum("l1a_psf", [128, KBINS], F32)
    ps_b = ctx.phase_psum("l1a_psb", [1, KBINS], F32)
    pool.memset(stg[:], 0.0)
    tz = pool.sig()
    tprev_dma = None
    tstg_free = tz
    for wi, (wname, bname, plane) in enumerate((
            ("l1a_wq", "l1a_bq", "dftC"), ("l1a_wq", "l1a_bq", "dftS"),
            ("l1a_wk", "l1a_bk", "dftC"), ("l1a_wk", "l1a_bk", "dftS"),
            ("l1a_wv", "l1a_bv", "dftC"), ("l1a_wv", "l1a_bv", "dftS"))):
        pool.wait(tprev_dma)
        pool.dma_start(out=wxb[:], in_=dram[wname].rearrange("(c p) d -> p c d", p=128))
        pool.sig_dma("dW")
        pool.dma_start(out=cb[:], in_=dram[plane].rearrange("(c p) k -> p c k", p=128))
        pool.sig_dma("dW")
        pool.dma_start(out=bcol[:], in_=dram[bname])
        tld = pool.sig_dma("dW")
        tev = None
        for dc in range(DCH):
            for ec in range(DCH):
                pe.wait(tld)
                if ec == 0 and tev is not None:
                    pe.wait(tev)
                pe.matmul(ps_f[:, 0:512], wxb[:, ec, 128 * dc:128 * (dc + 1)],
                          cb[:, ec, 0:512], start=(ec == 0), stop=(ec == DCH - 1))
                pe.matmul(ps_f[:, 512:513], wxb[:, ec, 128 * dc:128 * (dc + 1)],
                          cb[:, ec, 512:513], start=(ec == 0), stop=(ec == DCH - 1))
            tmm = pe.sig()
            act.wait(tmm)
            act.wait(tstg_free)
            act.copy(stg[:, 0:KBINS], ps_f[:])
            tev = act.sig()
            sp.wait(tev)
            sp.dma_start(out=dram["wtil"][wi, 128 * dc:128 * (dc + 1), :], in_=stg[:])
            tstg_free = sp.sig_dma("dS")
        tevb = None
        for ec in range(DCH):
            pe.wait(tld)
            pe.matmul(ps_b[:, 0:512], bcol[:, ec:ec + 1], cb[:, ec, 0:512],
                      start=(ec == 0), stop=(ec == DCH - 1))
            pe.matmul(ps_b[:, 512:513], bcol[:, ec:ec + 1], cb[:, ec, 512:513],
                      start=(ec == 0), stop=(ec == DCH - 1))
        tmm = pe.sig()
        dve.wait(tmm)
        dve.memset(b.bias_tilde[0:1, KPAD * wi:KPAD * (wi + 1)], 0.0)
        dve.sig()
        dve.tensor_copy(b.bias_tilde[0:1, KPAD * wi:KPAD * wi + KBINS], ps_b[:])
        tbt = dve.sig()
        pe.wait(tbt)
        tprev_dma = Tok(tld.sem, tld.val, tld.snap)
        tprev_dma = tev
    ctx.dump("l1a_btil", b.bias_tilde[:], [tbt])
    sp.wait(tbt)
    sp.dma_start(out=dram["btil"], in_=b.bias_tilde[:])
    tst = sp.sig_dma("dS")
    for e_ in ENGS:
        ctx.eng[e_].wait(tst)
    ctx.pop_phase()
    return None


def _complex_norm(ctx, b, re_src, im_src, re_dst, im_dst, width, deps):
    dve, act = ctx.dve, ctx.act
    n2 = b.t1[:, 0:width]
    t2 = b.mrep[:, 0:width]
    dve.wait_all(deps)
    dve.tensor_tensor(n2, re_src, re_src, ALU.mult)
    dve.tensor_tensor(t2, im_src, im_src, ALU.mult)
    dve.tensor_tensor(n2, n2, t2, ALU.add)
    tn = dve.sig()
    act.wait(tn)
    dve.tensor_single_scalar(n2, n2, 1e-30, ALU.max)
    tm_ = dve.sig()
    act.wait(tm_)
    act.activation(n2, n2, AF.Ln)
    act.activation(n2, n2, AF.Exp, scale=0.5)
    act.activation(n2, n2, AF.Ln, bias=1e-8)
    act.activation(t2, n2, AF.Exp, scale=-1.0)
    tc_ = act.sig()
    dve.wait(tc_)
    dve.tensor_tensor(re_dst, re_src, t2, ALU.mult)
    dve.sig()
    dve.tensor_tensor(im_dst, im_src, t2, ALU.mult)
    return dve.sig()


L1A_PARTS = ((0, 1), (2, 3), (4,))


def emit_l1a(ctx, dram, b, xln_toks, seq):
    pe, dve, act, pool, sp = ctx.pe, ctx.dve, ctx.act, ctx.pool, ctx.sp
    onesT = ctx.phase_sbuf("l1a_onesT", [1, 512], BF16)
    utri = ctx.phase_sbuf("l1a_utri", [128, 128], BF16)
    carry = ctx.phase_sbuf("l1a_carry", [128, 12], F32)
    pool.memset(onesT[:], 1.0)
    tone = pool.sig()
    pool.dma_start(out=utri[:], in_=dram["utri"])
    tutri = pool.sig_dma("dW")
    zfirst = [True]
    ztoks = []
    for part_i, kcs in enumerate(L1A_PARTS):
        nk = len(kcs)
        W = 128 * nk
        ko = 128 * kcs[0]
        ctx.push_phase()
        btlp = ctx.phase_sbuf("l1a_btlp", [1, 6 * W], BF16)
        sp.dma_start(
            out=btlp[:].rearrange("u (w k) -> u w k", k=W),
            in_=dram["btil"].rearrange("u (w k) -> u w k", k=KPAD)[:, :, ko:ko + W])
        tbtl = sp.sig_dma("dW")
        xk = ctx.phase_sbuf("l1a_xk", [128, NT, 2 * W], BF16)
        _qre = b.rrep[:].bitcast(BF16).rearrange("p (a t) -> p a t", t=T)[:, 0:nk, :]
        _qim = ctx.phase_sbuf("l1a_qhi", [128, nk, T], BF16)

        def qh(ci):
            return _qre[:, ci, :] if ci < nk else _qim[:, ci - nk, :]
        ctx.push_phase()
        wkb = ctx.phase_sbuf("l1a_wkb", [128, DCH, 4, W], BF16)
        for j, wi in enumerate((2, 3, 4, 5)):
            sp.dma_start(out=wkb[:, :, j, :], in_=dram["wtil"][wi, :, ko:ko + W]
                         .rearrange("(c p) k -> p c k", p=128))
            twk = sp.sig_dma("dW")

        dve.wait(tbtl)
        dve.engine_nop()
        tbc = tbtl
        ps4d = [ctx.phase_psum("l1a_ps4a", [128, 4 * W], F32),
                ctx.phase_psum("l1a_ps4b", [128, 4 * W], F32)]
        kvtd = [b.sq[:].rearrange("p a w -> p (a w)")[:, 0:4 * W]
                .rearrange("p (a w) -> p a w", a=4),
                b.t1[:].bitcast(BF16)[:, 1024:1024 + 4 * W]
                .rearrange("p (a w) -> p a w", a=4)]
        ps_free = [None, None]
        kvt_free = [None, None]
        tbind_prev = None
        for j in range(NT):
            sl2 = j % 2
            ps4 = ps4d[sl2]
            kvt = kvtd[sl2]
            tsl = slice(2 + 128 * j, 2 + 128 * (j + 1))
            for dc in range(DCH):
                pe.wait(twk)
                pe.wait_all(xln_toks)
                pe.wait(tone)
                pe.wait(tbc)
                if dc == 0:
                    pe.wait(ps_free[sl2])
                wr_ = wkb[:, dc, :, :].rearrange("p a w -> p (a w)")
                for sg in range(0, 4 * W, 512):
                    pe.matmul(ps4[:, sg:sg + 512], b.xln[:, dc, tsl],
                              wr_[:, sg:sg + 512], start=(dc == 0), stop=False)
            for sg in range(0, 4 * W, 512):
                pe.matmul(ps4[:, sg:sg + 512], onesT[0:1, 0:128],
                          btlp[0:1, 2 * W + sg:2 * W + sg + 512],
                          start=False, stop=True)
            tmm = pe.sig()
            act.wait(tmm)
            act.wait(kvt_free[sl2])
            act.copy(kvt[:], ps4[:])
            ps_free[sl2] = act.sig()
            kv4 = kvt[:].rearrange("p (a b) w -> p a b w", b=2)
            res_ = kv4[:, :, 0, :]
            ims_ = kv4[:, :, 1, :]
            n2 = b.t1[:, 0:2 * W].rearrange("p (a w) -> p a w", a=2)
            t2 = b.mrep[:, 0:2 * W].rearrange("p (a w) -> p a w", a=2)
            dve.wait(ps_free[sl2])
            dve.tensor_tensor(n2, res_, res_, ALU.mult)
            dve.tensor_tensor(t2, ims_, ims_, ALU.mult)
            dve.tensor_tensor(n2, n2, t2, ALU.add)
            dve.tensor_single_scalar(n2, n2, 1e-30, ALU.max)
            tn_ = dve.sig()
            act.wait(tn_)
            act.activation(n2, n2, AF.Ln)
            act.activation(n2, n2, AF.Exp, scale=0.5)
            act.activation(n2, n2, AF.Ln, bias=1e-8)
            act.activation(t2, n2, AF.Exp, scale=-1.0)
            tc_ = act.sig()
            dve.wait(tc_)
            dve.tensor_tensor(res_, res_, t2, ALU.mult)
            dve.tensor_tensor(ims_, ims_, t2, ALU.mult)
            dve.sig()
            ta = b.s16[:, 0:W]
            tb_ = b.s16[:, W:2 * W]
            dve.tensor_tensor(ta, kvt[:, 0, :], kvt[:, 2, :], ALU.mult)
            dve.tensor_tensor(tb_, kvt[:, 1, :], kvt[:, 3, :], ALU.mult)
            dve.tensor_tensor(xk[:, j, 0:W], ta, tb_, ALU.subtract)
            dve.tensor_tensor(ta, kvt[:, 0, :], kvt[:, 3, :], ALU.mult)
            dve.tensor_tensor(tb_, kvt[:, 1, :], kvt[:, 2, :], ALU.mult)
            dve.tensor_tensor(xk[:, j, W:2 * W], ta, tb_, ALU.add)
            tbind_prev = dve.sig()
            kvt_free[sl2] = tbind_prev
        ctx.barrier()
        ctx.pop_phase()
        ctx.push_phase()
        wqb = ctx.phase_sbuf("l1a_wqb", [128, 2, DCH, W], BF16)
        for j, wi in enumerate((0, 1)):
            sp.dma_start(out=wqb[:, j], in_=dram["wtil"][wi, :, ko:ko + W]
                         .rearrange("(c p) k -> p c k", p=128))
            twq = sp.sig_dma("dW")
        ps_q = ctx.phase_psum("l1a_psq", [128, T], F32)
        tq_free = None
        for ci in range(2 * nk):
            pl, kci = (0, ci) if ci < nk else (1, ci - nk)
            for dc in range(DCH):
                pe.wait(twq)
                pe.wait_all(xln_toks)
                if dc == 0:
                    pe.wait(tq_free)
                lh = wqb[:, pl, dc, 128 * kci:128 * (kci + 1)]
                for tg in range(T // 512):
                    pe.matmul(ps_q[:, 512 * tg:512 * (tg + 1)], lh,
                              b.xln[:, dc, 2 + 512 * tg:2 + 512 * (tg + 1)],
                              start=(dc == 0), stop=False)
            pe.wait(tbtl)
            seg = btlp[0:1, W * pl + 128 * kci:W * pl + 128 * (kci + 1)]
            for tg in range(T // 512):
                pe.matmul(ps_q[:, 512 * tg:512 * (tg + 1)], seg,
                          onesT[0:1, 0:512], start=False, stop=True)
            tmm = pe.sig()
            act.wait(tmm)
            act.copy(qh(ci), ps_q[:])
            tq_free = act.sig()
        tqn = []
        for kci in range(nk):
            tn = _complex_norm(ctx, b, qh(kci), qh(nk + kci),
                               qh(kci), qh(nk + kci), T, [tq_free])
            tqn.append(tn)
        ctx.barrier()
        ctx.pop_phase()
        ctx.push_phase()
        ps_c = [ctx.phase_psum(f"l1a_psc{i}", [128, 128], F32) for i in range(4)]
        ps_ts = ctx.phase_psum("l1a_psts", [128, 2 * nk, NT], F32)
        tsum = ctx.phase_sbuf("l1a_tsum", [128, 2 * nk, NT + 1], F32)
        onecol = ctx.phase_sbuf("l1a_onecol", [128, 1], BF16)
        pool.memset(onecol[:], 1.0)
        toc = pool.sig()
        pe.wait_all(tqn)
        pe.wait(toc)
        for ci in range(2 * nk):
            for j in range(NT):
                pe.matmul(ps_ts[:, ci, j:j + 1], xk[:, j, 128 * ci:128 * (ci + 1)],
                          onecol[:], start=True, stop=True)
        tts = pe.sig()
        tsum2 = ctx.phase_sbuf("l1a_tsum2", [128, 2 * nk, NT + 1], F32)
        dve.wait(tts)
        dve.memset(tsum[:, :, 0:1], 0.0)
        dve.memset(tsum2[:, :, 0:1], 0.0)
        dve.tensor_copy(tsum[:, :, 1:NT + 1], ps_ts[:, :, :])
        dve.sig()
        bufs = (tsum, tsum2)
        for si, sh in enumerate((1, 2, 4, 8)):
            a_, b2_ = bufs[si % 2], bufs[(si + 1) % 2]
            dve.tensor_copy(b2_[:, :, 1:sh + 1], a_[:, :, 1:sh + 1])
            dve.tensor_tensor(b2_[:, :, sh + 1:NT + 1], a_[:, :, sh + 1:NT + 1],
                              a_[:, :, 1:NT + 1 - sh], ALU.add)
            dve.sig()
        tsum = bufs[0]
        tpfx = dve.sig()
        tev4 = [None] * 4
        for ci in range(2 * nk):
            for j in range(NT):
                sl4 = (ci * NT + j) % 4
                pe.wait(tutri)
                pe.wait(tev4[sl4])
                pe.matmul(ps_c[sl4][:], xk[:, j, 128 * ci:128 * (ci + 1)], utri[:],
                          start=True, stop=True)
                tmm = pe.sig()
                dve.wait(tmm)
                dve.wait(tpfx)
                dve.tensor_single_scalar(xk[:, j, 128 * ci:128 * (ci + 1)],
                                         ps_c[sl4][:], tsum[:, ci, j:j + 1], ALU.add)
                tev4[sl4] = dve.sig()
        ctx.barrier()
        ctx.pop_phase()
        ctx.push_phase()
        cwb = ctx.phase_sbuf("l1a_cwb", [128, 2, nk, D], BF16)
        for j, nm in enumerate(("idftC", "idftS")):
            pool.dma_start(out=cwb[:, j], in_=dram[nm][ko:ko + W, :]
                           .rearrange("(c p) e -> p c e", p=128))
            tcw = pool.sig_dma("dW")
        ps_o = ctx.phase_psum("l1a_pso", [128, T], F32)
        gtmp = ctx.phase_sbuf("l1a_gtmp", [128, 2, T], BF16)
        gr = b.s16[:, 0:T]
        gi = b.s16[:, T:2 * T]
        tz_ev = None
        for kci in range(nk):
            sre = xk[:, :, 128 * kci:128 * (kci + 1)]
            sim = xk[:, :, 128 * (nk + kci):128 * (nk + kci + 1)]
            qre = qh(kci).rearrange("p (j q) -> p j q", q=128)
            qim = qh(nk + kci).rearrange("p (j q) -> p j q", q=128)
            grv = gr.rearrange("p (j q) -> p j q", q=128)
            giv = gi.rearrange("p (j q) -> p j q", q=128)
            ta = gtmp[:, 0, :].rearrange("p (j q) -> p j q", q=128)
            tb_ = gtmp[:, 1, :].rearrange("p (j q) -> p j q", q=128)
            dve.wait(tz_ev)
            dve.tensor_tensor(ta, sre, qre, ALU.mult)
            dve.tensor_tensor(tb_, sim, qim, ALU.mult)
            dve.tensor_tensor(grv, ta, tb_, ALU.add)
            dve.tensor_tensor(ta, sim, qre, ALU.mult)
            dve.tensor_tensor(tb_, sre, qim, ALU.mult)
            dve.tensor_tensor(giv, ta, tb_, ALU.subtract)
            tg_ = dve.sig()
            for m in range(DCH):
                for pl, g_ in ((0, gr), (1, gi)):
                    for tg2 in range(T // 512):
                        pe.wait(tcw)
                        pe.wait(tg_)
                        if pl == 0 and tg2 == 0:
                            pe.wait(tz_ev)
                        pe.matmul(ps_o[:, 512 * tg2:512 * (tg2 + 1)],
                                  cwb[:, pl, kci, 128 * m:128 * (m + 1)],
                                  g_[:, 512 * tg2:512 * (tg2 + 1)],
                                  start=(pl == 0), stop=(pl == 1))
                tmm = pe.sig()
                dve.wait(tmm)
                if zfirst[0]:
                    dve.tensor_copy(b.zbuf[:, m, :], ps_o[:])
                else:
                    dve.tensor_tensor(b.zbuf[:, m, :], b.zbuf[:, m, :], ps_o[:], ALU.add)
                tz_ev = dve.sig()
                if part_i == len(L1A_PARTS) - 1 and kci == nk - 1:
                    ztoks.append(tz_ev)
            zfirst[0] = False
        ctx.barrier()
        ctx.pop_phase()
        ctx.pop_phase()
    ctx.dump("l1a_z", b.zbuf[:], ztoks)
    return ztoks


def emit_launch(ctx, dram, launch_blocks):
    b = alloc_persistent(ctx, launch_blocks)
    ctoks = emit_load_const(ctx, dram, b)
    for e in ENGS:
        ctx.eng[e].wait_all(ctoks)
    if 2 in launch_blocks:
        emit_l1a_fold(ctx, dram, b)
        ctx.barrier()
        ctx.end_phase()
    for seq in range(NSEQ):
        xtok = emit_load_x(ctx, dram, b, seq)
        ctx.barrier()
        ln_toks = emit_preln(ctx, b, [xtok])
        ctx.barrier()
        ctx.end_phase()
        for i, blk in enumerate(launch_blocks):
            if blk == 0:
                ztoks = emit_l0a(ctx, dram, b, ln_toks)
            elif blk == 1:
                ztoks = emit_l0f(ctx, dram, b, ln_toks)
            elif blk == 2:
                ztoks = emit_l1a(ctx, dram, b, ln_toks, seq)
            elif blk == 3:
                wt_toks = emit_l1f_weights(ctx, dram, b)
                ztoks = emit_l1f(ctx, b, ln_toks, wt_toks)
            else:
                raise NotImplementedError(f"block {blk}")
            ctx.barrier()
            ctx.end_phase()
            if i < len(launch_blocks) - 1:
                ln_toks = emit_transition(ctx, b, ztoks)
            else:
                xdeps = emit_postln_residual(ctx, b, ztoks)
            ctx.barrier()
            ctx.end_phase()
        st = emit_store_x(ctx, dram, b, seq, xdeps)
        for e in ENGS:
            ctx.eng[e].wait(st)
    ctx.sp.wait(Tok("dO", ctx.dma_counts["dO"], {}))


def build_launch(g, launch_blocks):
    nc = bass.Bass("TRN2", target_bir_lowering=False)
    dram = declare_dram(nc, g, launch_blocks)
    stack = ExitStack()
    with stack:
        ctx = Ctx(nc, stack)
        emit_launch(ctx, dram, launch_blocks)
        block = stack.enter_context(nc.Block())
        ctx.replay(block)
    return nc


LAUNCH_PLAN = [[0, 1, 2, 3]]


def kernel(**inputs):
    g, shards = stage_inputs(inputs)
    cur = shards
    for launch_blocks in LAUNCH_PLAN:
        nc = build_launch(g, launch_blocks)
        in_maps = []
        need = {"xin"}
        need.update(("ident", "ones128", "onesrow"))
        for blk in launch_blocks:
            need.update(BLOCK_INPUTS[blk])
        for c in range(NCORES):
            m = {k: g[k] for k in need if k != "xin"}
            m["xin"] = cur[c]
            in_maps.append(m)
        res = run_bass_kernel_spmd(nc, in_maps, core_ids=list(range(NCORES)))
        cur = [r["xout"] for r in res.results]
    out = np.empty((16, T, D), np.float32)
    for c in range(NCORES):
        out[2 * c:2 * c + 2] = cur[c].transpose(0, 2, 1)
    return out


# version 40
# speedup vs baseline: 1.1008x; 1.0520x over previous
import numpy as np
from contextlib import ExitStack
import concourse.bass as bass
import concourse.mybir as mybir
from concourse.bass_utils import run_bass_kernel_spmd

F32 = mybir.dt.float32
BF16 = mybir.dt.bfloat16
I32 = mybir.dt.int32
AF = mybir.ActivationFunctionType
ALU = mybir.AluOpType
AX = mybir.AxisListType

T = 2048
D = 1024
NCORES = 8
NSEQ = 2
DCH = D // 128
NT = T // 128
KBINS = 513
KPAD = 640
KCH = KPAD // 128
LN_EPS = 1e-5

ENGS = ("pe", "dve", "act", "pool", "sp")



class Tok:
    __slots__ = ("sem", "val", "snap")

    def __init__(self, sem, val, snap):
        self.sem, self.val, self.snap = sem, val, snap


_NO_AUTO = {"dma_start", "indirect_dma_start", "wait_ge", "nop", "sem_inc",
            "print", "dma_start_transpose"}


class EngCtx:

    _FIELDS = ("ctx", "name", "sem_name", "count", "clock", "recs", "auto",
               "self_waited", "uninced")

    def __init__(self, ctx, name, auto):
        self.ctx, self.name = ctx, name
        self.sem_name = "s_" + name
        self.count = 0
        self.clock = {}
        self.recs = []
        self.auto = auto
        self.self_waited = 0
        self.uninced = False

    def _rec(self, fn, auto_inced=False):
        entry = [fn, [], auto_inced]
        self.recs.append(entry)
        return entry

    def __getattr__(self, attr):
        if attr.startswith("_") or attr in EngCtx._FIELDS:
            raise AttributeError(attr)

        def wrap(*a, **k):
            if self.auto and attr not in _NO_AUTO and self.count > self.self_waited:
                sem, val = self.sem_name, self.count
                self._rec(lambda eng: eng.wait_ge(self.ctx.sems[sem], val))
                self.self_waited = val
            auto_inc = self.auto and attr not in _NO_AUTO
            e = self._rec(lambda eng: getattr(eng, attr)(*a, **k), auto_inc)
            if auto_inc:
                self.count += 1
                e[1].append((self.sem_name, 1))
                self.clock[self.sem_name] = self.count
            elif attr not in _NO_AUTO:
                self.uninced = True
            return None
        return wrap

    def wait(self, tok):
        if tok is None:
            return
        if tok.sem == self.sem_name:
            if tok.val <= self.self_waited:
                return
            sem, val = tok.sem, tok.val
            self._rec(lambda eng: eng.wait_ge(self.ctx.sems[sem], val))
            self.self_waited = val
            return
        if self.clock.get(tok.sem, 0) >= tok.val:
            return
        sem, val = tok.sem, tok.val
        self._rec(lambda eng: eng.wait_ge(self.ctx.sems[sem], val))
        for s, v in tok.snap.items():
            if s == self.sem_name:
                if v > self.self_waited:
                    self.self_waited = v
                continue
            if self.clock.get(s, 0) < v:
                self.clock[s] = v
        if self.clock.get(tok.sem, 0) < tok.val:
            self.clock[tok.sem] = tok.val

    def wait_all(self, toks):
        for t in toks:
            self.wait(t)

    def sig(self):
        assert self.recs, "sig() with no recorded instruction"
        if self.recs[-1][2] or any(s == self.sem_name for s, _ in self.recs[-1][1]):
            return Tok(self.sem_name, self.count, dict(self.clock))
        self.count += 1
        self.recs[-1][1].append((self.sem_name, 1))
        self.clock[self.sem_name] = self.count
        self.uninced = False
        return Tok(self.sem_name, self.count, dict(self.clock))

    def sig_dma(self, stream):
        c = self.ctx.dma_counts[stream] = self.ctx.dma_counts.get(stream, 0) + 16
        self.recs[-1][1].append((stream, 16))
        snap = dict(self.clock)
        snap[stream] = c
        return Tok(stream, c, snap)


DEBUG = False


class Ctx:
    def __init__(self, nc, stack):
        self.dbg = []
        self.dbg_names = {}
        self.nc = nc
        self.stack = stack
        self.sems = {}
        self.dma_counts = {}
        self.eng = {n: EngCtx(self, n, auto=(n in ("dve", "act", "pool")))
                    for n in ENGS}
        for n in ENGS:
            self.sems["s_" + n] = stack.enter_context(nc.semaphore("s_" + n))
        for s in ("dW", "dX", "dO", "dS"):
            self.sems[s] = stack.enter_context(nc.semaphore(s))
        self.pe = self.eng["pe"]
        self.dve = self.eng["dve"]
        self.act = self.eng["act"]
        self.pool = self.eng["pool"]
        self.sp = self.eng["sp"]
        self._sbuf_guards = []
        self._psum_guards = []
        self._tensors = {}
        self._phase_guards = []
        self._phase_ctr = 0
        self._phase_stack = []

    def sbuf(self, name, shape, dtype, side="left"):
        if name in self._tensors:
            return self._tensors[name]
        g = self.nc.sbuf_tensor(name, shape, dtype)
        h = g.__enter__()
        self._sbuf_guards.append(g)
        self._tensors[name] = h
        return h

    def psum(self, name, shape, dtype=F32):
        if name in self._tensors:
            return self._tensors[name]
        g = self.nc.psum_tensor(name, shape, dtype)
        h = g.__enter__()
        self._psum_guards.append(g)
        self._tensors[name] = h
        return h

    def barrier(self):
        toks = []
        for n in ENGS:
            e = self.eng[n]
            if e.uninced:
                toks.append(e.sig())
            elif e.count > 0:
                toks.append(Tok(e.sem_name, e.count, dict(e.clock)))
        for s, c in self.dma_counts.items():
            toks.append(Tok(s, c, {s: c}))
        for n in ENGS:
            self.eng[n].wait_all(toks)

    def phase_psum(self, name, shape, dtype=F32):
        self._phase_ctr += 1
        g = self.nc.psum_tensor(f"{name}_{self._phase_ctr}", shape, dtype)
        h = g.__enter__()
        self._phase_guards.append(g)
        return h

    def phase_sbuf(self, name, shape, dtype):
        self._phase_ctr += 1
        g = self.nc.sbuf_tensor(f"{name}_{self._phase_ctr}", shape, dtype)
        h = g.__enter__()
        self._phase_guards.append(g)
        return h

    def push_phase(self):
        self._phase_stack.append(len(self._phase_guards))

    def pop_phase(self):
        k = self._phase_stack.pop() if self._phase_stack else 0
        for g in reversed(self._phase_guards[k:]):
            g.__exit__(None, None, None)
        del self._phase_guards[k:]

    def end_phase(self):
        assert not self._phase_stack, "unbalanced push_phase"
        for g in reversed(self._phase_guards):
            g.__exit__(None, None, None)
        self._phase_guards = []

    def dump(self, name, ap, deps=()):
        if not DEBUG or name in self.dbg_names:
            return
        self.dbg_names[name] = True
        shape = list(ap.shape)
        dt = ap.dtype
        dst = self.nc.dram_tensor("dbg_" + name, shape, dt, kind="ExternalOutput").ap()
        self.sp.wait_all(list(deps))
        self.sp.dma_start(out=dst, in_=ap)
        self.sp.sig_dma("dO")

    def replay(self, block):
        raws = {"pe": block.tensor, "dve": block.vector, "act": block.scalar,
                "pool": block.gpsimd, "sp": block.sync}

        def make_body(name):
            def body(eng):
                for fn, incs, _auto in self.eng[name].recs:
                    inst = fn(eng)
                    for (sem, v) in incs:
                        assert inst is not None, f"then_inc on None inst ({name})"
                        inst.then_inc(self.sems[sem], v)
            return body
        for n in ENGS:
            if self.eng[n].recs:
                raws[n](make_body(n))




def host_constants():
    c = {}
    e = np.arange(D)[:, None]
    k = np.arange(KBINS)[None, :]
    ang = 2.0 * np.pi * ((e * k) % D).astype(np.float64) / D
    c["dftC"] = np.cos(ang).astype(np.float32)
    c["dftS"] = (-np.sin(ang)).astype(np.float32)
    w = np.full(KBINS, 2.0)
    w[0] = 1.0
    w[512] = 1.0
    ki = np.arange(KBINS)[:, None]
    ee = np.arange(D)[None, :]
    ang2 = 2.0 * np.pi * ((ki * ee) % D).astype(np.float64) / D
    cw = (w[:, None] * np.cos(ang2) / D).astype(np.float32)
    sw = (-w[:, None] * np.sin(ang2) / D).astype(np.float32)
    c["idftC"] = np.zeros((KPAD, D), np.float32)
    c["idftC"][:KBINS] = cw
    c["idftS"] = np.zeros((KPAD, D), np.float32)
    c["idftS"][:KBINS] = sw
    c["ident"] = np.eye(128, dtype=np.float32)
    tri = (np.arange(128)[:, None] <= np.arange(128)[None, :]).astype(np.float32)
    c["trimask"] = tri
    c["utri"] = tri.copy()
    c["ones128"] = np.ones((128, 128), np.float32)
    c["onesrow"] = np.ones((1, 128), np.float32)
    c["shamt"] = np.tile(np.array([0, 2, 4, 6], np.int32), 32)[:, None].copy()
    c["mask3"] = np.full((128, 1), 3, np.int32)
    bandmask = np.zeros((128, 16), np.float32)
    for h in range(16):
        bandmask[8 * h:8 * h + 6, h] = 1.0
    c["bandmask"] = bandmask
    tri16 = np.tile(tri, (1, 16))
    c["trimask16"] = np.ascontiguousarray(tri16)
    half0 = np.zeros((1, 128), np.float32)
    half0[0, :64] = 1.0
    half1 = np.zeros((1, 128), np.float32)
    half1[0, 64:] = 1.0
    c["half0"] = half0
    c["half1"] = half1
    c["ones16"] = np.ones((128, 16), np.float32)
    return c


def stage_inputs(inputs):
    g = {}
    g["l0a_wq"] = np.ascontiguousarray(
        np.concatenate([inputs["L0_attn_W1_q"], inputs["L0_attn_W2_q"]], 0).T)
    g["l0a_wk"] = np.ascontiguousarray(
        np.concatenate([inputs["L0_attn_W1_k"], inputs["L0_attn_W2_k"]], 0).T)
    g["l0a_wv"] = np.ascontiguousarray(np.asarray(inputs["L0_attn_Wv"]).T)
    g["l0a_bv"] = np.asarray(inputs["L0_attn_bv"]).reshape(1, D)
    g["l0a_wo"] = np.ascontiguousarray(np.asarray(inputs["L0_attn_Wo"]).T)
    g["l0a_bo"] = np.asarray(inputs["L0_attn_bo"]).reshape(1, D)
    g["l0f_w1"] = np.ascontiguousarray(np.concatenate(
        [np.asarray(inputs["L0_ffn_W1"]), np.asarray(inputs["L0_ffn_Ws1"])[None]], 0))
    g["l0f_w2"] = np.ascontiguousarray(np.concatenate(
        [np.asarray(inputs["L0_ffn_W2"]), np.asarray(inputs["L0_ffn_Ws2"])[None]], 0))
    g["l0f_wr"] = np.ascontiguousarray(np.asarray(inputs["L0_ffn_Wr"]))
    g["l1a_wq"] = np.ascontiguousarray(np.asarray(inputs["L1_attn_Wq"]))
    g["l1a_wk"] = np.ascontiguousarray(np.asarray(inputs["L1_attn_Wk"]))
    g["l1a_wv"] = np.ascontiguousarray(np.asarray(inputs["L1_attn_Wv"]))
    g["l1a_bq"] = np.ascontiguousarray(np.asarray(inputs["L1_attn_bq"]).reshape(DCH, 128).T)
    g["l1a_bk"] = np.ascontiguousarray(np.asarray(inputs["L1_attn_bk"]).reshape(DCH, 128).T)
    g["l1a_bv"] = np.ascontiguousarray(np.asarray(inputs["L1_attn_bv"]).reshape(DCH, 128).T)
    pk1 = np.asarray(inputs["L1_ffn_packed_weight"]).reshape(256, D)
    g["l1f_pk"] = np.ascontiguousarray(np.tile(pk1.T, (1, 4)))
    g["l1f_ws"] = np.asarray(inputs["L1_ffn_weight_scale"]).reshape(1, 1).astype(np.float32)
    g["l1f_bias"] = np.ascontiguousarray(
        np.asarray(inputs["L1_ffn_bias"]).reshape(DCH, 128).T)
    g.update(host_constants())
    x = np.asarray(inputs["L0_attn_x"])
    shards = [np.ascontiguousarray(x[2 * c:2 * c + 2].transpose(0, 2, 1))
              for c in range(NCORES)]
    return g, shards


BLOCK_INPUTS = {
    0: ("l0a_wq", "l0a_wk", "l0a_wv", "l0a_bv", "l0a_wo", "l0a_bo",
        "trimask16", "bandmask", "half0", "half1"),
    1: ("l0f_w1", "l0f_w2", "l0f_wr"),
    2: ("l1a_wq", "l1a_wk", "l1a_wv", "l1a_bq", "l1a_bk", "l1a_bv",
        "dftC", "dftS", "idftC", "idftS", "utri"),
    3: ("l1f_pk", "l1f_ws", "l1f_bias", "shamt", "mask3"),
}


def declare_dram(nc, g, launch_blocks):
    d = {}

    def inp(name):
        arr = g[name]
        dt = I32 if arr.dtype == np.int32 else F32
        d[name] = nc.dram_tensor(name, list(arr.shape), dt, kind="ExternalInput").ap()

    d["xin"] = nc.dram_tensor("xin", [NSEQ, D, T], F32, kind="ExternalInput").ap()
    d["xout"] = nc.dram_tensor("xout", [NSEQ, D, T], F32, kind="ExternalOutput").ap()
    for k in ("ident", "ones128", "onesrow"):
        inp(k)
    names = set()
    for blk in launch_blocks:
        names.update(BLOCK_INPUTS[blk])
    for k in sorted(names):
        inp(k)
    if 2 in launch_blocks:
        d["wtil"] = nc.dram_tensor("wtil", [6, D, KPAD], BF16, kind="Internal").ap()
        d["btil"] = nc.dram_tensor("btil", [1, 6 * KPAD], BF16, kind="Internal").ap()
    return d


class Bufs:
    pass


def alloc_persistent(ctx, launch_blocks=()):
    b = Bufs()
    b.x = ctx.sbuf("x_m", [128, DCH, T], F32)
    b.xln = ctx.sbuf("xln", [128, DCH, T + 2], BF16)
    b.ident = ctx.sbuf("c_ident", [128, 128], BF16)
    b.ones128 = ctx.sbuf("c_ones", [128, 128], BF16)
    b.onesrow = ctx.sbuf("c_onesrow", [1, 128], BF16)
    b.mrep = ctx.sbuf("mrep", [128, T], F32)
    b.rrep = ctx.sbuf("rrep", [128, T], F32)
    b.t1 = ctx.sbuf("t1rep", [128, T], F32)
    b.sq = ctx.sbuf("sqch", [128, 2, 512], BF16)
    b.s16 = ctx.sbuf("s16", [128, 2 * T], BF16)
    b.zbuf = ctx.sbuf("zbuf", [128, DCH, T], BF16)
    b.cpool = ctx.sbuf("cpool", [128, 8], F32)
    return b


CONST_VALS = (-1.0, LN_EPS, 1e-8)


def emit_load_const(ctx, dram, b):
    sp, pool = ctx.sp, ctx.pool
    toks = []
    for name, dst in (("ident", b.ident), ("ones128", b.ones128), ("onesrow", b.onesrow)):
        pool.dma_start(out=dst[:], in_=dram[name])
        toks.append(pool.sig_dma("dW"))
    for j, v in enumerate(CONST_VALS):
        pool.memset(b.cpool[:, j:j + 1], v)
        ctx.nc.const_aps.register(F32, v, b.cpool[:, j:j + 1])
    toks.append(pool.sig())
    return toks


def emit_load_x(ctx, dram, b, seq):
    sp = ctx.sp
    sp.dma_start(out=b.x[:], in_=dram["xin"][seq].rearrange("(c p) t -> p c t", p=128))
    return sp.sig_dma("dX")


def emit_store_x(ctx, dram, b, seq, deps):
    sp = ctx.sp
    sp.wait_all(deps)
    sp.dma_start(out=dram["xout"][seq].rearrange("(c p) t -> p c t", p=128), in_=b.x[:])
    return sp.sig_dma("dO")




def emit_stats_rows(ctx, b, src_get, psA, psB):
    pe, dve, act = ctx.pe, ctx.dve, ctx.act
    last = None
    sq_read = None
    for tg in range(T // 512):
        sl = slice(512 * tg, 512 * tg + 512)
        for c in range(DCH):
            src, rdy = src_get(c)
            pe.wait(rdy)
            if c == 0 and last is not None:
                pe.wait(last)
            pe.matmul(psA[:, 0:512], b.ones128[:], src[:, sl],
                      start=(c == 0), stop=(c == DCH - 1))
            act.wait(rdy)
            if c == 0 and last is not None:
                act.wait(last)
            slq = c % 2
            if sq_read is not None:
                act.wait(sq_read)
            act.activation(b.sq[:, slq, :], src[:, sl], AF.Square)
            sqt = act.sig()
            pe.wait(sqt)
            pe.matmul(psB[:, 0:512], b.ones128[:], b.sq[:, slq, :],
                      start=(c == 0), stop=(c == DCH - 1))
            if c >= 1:
                sq_read = pe.sig()
        tokp = pe.sig()
        dve.wait(tokp)
        dve.tensor_single_scalar(b.mrep[:, sl], psA[:, 0:512], 1.0 / D, ALU.mult)
        dve.tensor_single_scalar(b.t1[:, sl], psB[:, 0:512], 1.0 / D, ALU.mult)
        last = dve.sig()
    dve.tensor_tensor(b.rrep[:], b.mrep[:], b.mrep[:], ALU.mult)
    dve.tensor_tensor(b.t1[:], b.t1[:], b.rrep[:], ALU.subtract)
    tokv = dve.sig()
    act.wait(tokv)
    act.activation(b.t1[:], b.t1[:], AF.Ln, bias=LN_EPS)
    act.activation(b.rrep[:], b.t1[:], AF.Exp, scale=-0.5)
    last = act.sig()
    pe.wait(last)
    return last


def emit_preln(ctx, b, deps):
    pe, dve, act, pool = ctx.pe, ctx.dve, ctx.act, ctx.pool
    psA = ctx.phase_psum("lnA", [128, 512], F32)
    psB = ctx.phase_psum("lnB", [128, 512], F32)
    pool.wait_all(deps)
    pool.memset(b.xln[:, :, 0:2], 0.0)
    mz = pool.sig()
    ctoks = {}
    for c in range(DCH):
        act.wait_all(deps)
        act.copy(b.xln[:, c, 2:], b.x[:, c, :])
        ctoks[c] = act.sig()
    st = emit_stats_rows(ctx, b, lambda c: (b.xln[:, c, 2:], ctoks[c]), psA, psB)
    out = []
    for c in range(DCH):
        dve.wait(st)
        dve.wait_all(deps)
        dve.tensor_tensor(b.t1[:], b.x[:, c, :], b.mrep[:], ALU.subtract)
        dve.tensor_tensor(b.xln[:, c, 2:], b.t1[:], b.rrep[:], ALU.mult)
        out.append(dve.sig())
    out.append(mz)
    ctx.dump("xln", b.xln[:], out)
    ctx.dump("mrep", b.mrep[:], out)
    ctx.dump("rrep", b.rrep[:], out)
    return out


def _z_src_get(b, z_toks):
    if not isinstance(z_toks, (list, tuple)):
        z_toks = [z_toks]
    if len(z_toks) == DCH:
        return lambda c: (b.zbuf[:, c, :], z_toks[c])
    if len(z_toks) == 4 * DCH:
        return lambda c: (b.zbuf[:, c, :], z_toks[4 * c + 3])
    return lambda c: (b.zbuf[:, c, :], z_toks[-1])


def emit_transition(ctx, b, z_toks):
    pe, dve, act, pool = ctx.pe, ctx.dve, ctx.act, ctx.pool
    psA = ctx.phase_psum("trA", [128, 512], F32)
    psB = ctx.phase_psum("trB", [128, 512], F32)
    psC = ctx.phase_psum("trC", [128, 512], F32)
    psD = ctx.phase_psum("trD", [128, 512], F32)
    st = emit_stats_rows(ctx, b, _z_src_get(b, z_toks), psA, psB)
    pool.memset(b.xln[:, :, 0:2], 0.0)
    mz = pool.sig()
    ctoks = {}
    for c in range(DCH):
        dve.wait(st)
        dve.tensor_tensor(b.t1[:], b.zbuf[:, c, :], b.mrep[:], ALU.subtract)
        dve.tensor_tensor(b.t1[:], b.t1[:], b.rrep[:], ALU.mult)
        dve.tensor_tensor(b.x[:, c, :], b.x[:, c, :], b.t1[:], ALU.add)
        xt = dve.sig()
        act.wait(xt)
        act.copy(b.xln[:, c, 2:], b.x[:, c, :])
        ctoks[c] = act.sig()
    st2 = emit_stats_rows(ctx, b, lambda c: (b.xln[:, c, 2:], ctoks[c]), psC, psD)
    out = []
    for c in range(DCH):
        dve.wait(st2)
        dve.tensor_tensor(b.t1[:], b.x[:, c, :], b.mrep[:], ALU.subtract)
        dve.tensor_tensor(b.xln[:, c, 2:], b.t1[:], b.rrep[:], ALU.mult)
        out.append(dve.sig())
    out.append(mz)
    return out


def emit_postln_residual(ctx, b, z_toks):
    pe, dve, act, pool = ctx.pe, ctx.dve, ctx.act, ctx.pool
    psA = ctx.phase_psum("plA", [128, 512], F32)
    psB = ctx.phase_psum("plB", [128, 512], F32)
    st = emit_stats_rows(ctx, b, _z_src_get(b, z_toks), psA, psB)
    ctx.dump("pl_mrep", b.mrep[:], [st])
    ctx.dump("pl_rrep", b.rrep[:], [st])
    out = []
    for c in range(DCH):
        dve.wait(st)
        dve.tensor_tensor(b.t1[:], b.zbuf[:, c, :], b.mrep[:], ALU.subtract)
        dve.tensor_tensor(b.t1[:], b.t1[:], b.rrep[:], ALU.mult)
        dve.tensor_tensor(b.x[:, c, :], b.x[:, c, :], b.t1[:], ALU.add)
        out.append(dve.sig())
    ctx.dump("pl_x", b.x[:], out)
    return out




def emit_l1f_weights(ctx, dram, b):
    dve, act, sp = ctx.dve, ctx.act, ctx.sp
    b.l1f_wt = ctx.phase_sbuf("l1f_wt", [128, DCH, D], BF16)
    pk_s = ctx.phase_sbuf("l1f_pks", [128, 2, D], I32)
    sh_s = ctx.phase_sbuf("l1f_sh", [128, 1], I32)
    m3_s = ctx.phase_sbuf("l1f_m3", [128, 1], I32)
    b.l1f_ws = ctx.phase_sbuf("l1f_wss", [1, 1], F32)
    b.l1f_bias = ctx.phase_sbuf("l1f_bs", [128, DCH], F32)
    sp.dma_start(out=sh_s[:], in_=dram["shamt"])
    sp.sig_dma("dW")
    sp.dma_start(out=m3_s[:], in_=dram["mask3"])
    sp.sig_dma("dW")
    sp.dma_start(out=b.l1f_ws[:], in_=dram["l1f_ws"])
    sp.sig_dma("dW")
    sp.dma_start(out=b.l1f_bias[:], in_=dram["l1f_bias"])
    tl = sp.sig_dma("dW")
    pkv = dram["l1f_pk"].rearrange("(c p) o -> c p o", p=128)
    toks = []
    prev = {}
    for c in range(DCH):
        sl = c % 2
        if prev.get(sl) is not None:
            sp.wait(prev[sl])
        sp.dma_start(out=pk_s[:, sl, :], in_=pkv[c])
        tp = sp.sig_dma("dW")
        dve.wait(tp)
        dve.wait(tl)
        for s4 in range(1, 4):
            dve.tensor_single_scalar(pk_s[:, sl, 256 * s4:256 * (s4 + 1)],
                                     pk_s[:, sl, 256 * s4:256 * (s4 + 1)],
                                     2 * s4, ALU.logical_shift_right)
            dve.sig()
        dve.tensor_single_scalar(pk_s[:, sl, :], pk_s[:, sl, :], m3_s[:, 0:1],
                                 ALU.bitwise_and)
        tc = dve.sig()
        act.wait(tc)
        act.activation(b.l1f_wt[:, c, :], pk_s[:, sl, :], AF.Identity, bias=-1.0)
        ta = act.sig()
        prev[sl] = ta
        toks.append(ta)
    return toks


def emit_l1f(ctx, b, xln_toks, wt_toks):
    pe, dve, act, pool, sp = ctx.pe, ctx.dve, ctx.act, ctx.pool, ctx.sp
    xt = b.xln
    act.wait_all(xln_toks)
    act.activation(b.t1[:], xt[:, 0, 2:], AF.Abs)
    t_a = act.sig()
    dve.wait(t_a)
    for c in range(1, DCH):
        act.wait_all(xln_toks)
        act.activation(b.rrep[:], xt[:, c, 2:], AF.Abs)
        t_a = act.sig()
        dve.wait(t_a)
        dve.tensor_max(b.t1[:], b.t1[:], b.rrep[:])
        t_d = dve.sig()
        act.wait(t_d)
    fold_bf = b.s16[:, 0:T]
    dve.tensor_copy(fold_bf, b.t1[:])
    tfold = dve.sig()
    ps_t = ctx.phase_psum("l1f_pst", [128, 128], BF16)
    cols = ctx.phase_sbuf("l1f_cols", [128, NT], F32)
    colsb = ctx.phase_sbuf("l1f_colsb", [128, NT], BF16)
    rows = ctx.phase_sbuf("l1f_rows", [NT, 128], F32)
    tcol = None
    for j in range(NT):
        pe.wait(tfold)
        if tcol is not None:
            pe.wait(tcol)
        pe.transpose(ps_t[:], fold_bf[:, 128 * j:128 * j + 128], b.ident[:])
        tt = pe.sig()
        dve.wait(tt)
        dve.reduce_max(cols[:, j:j + 1], ps_t[:], AX.X)
        tcol = dve.sig()
    dve.tensor_copy(colsb[:], cols[:])
    tcb = dve.sig()
    ps_r = ctx.phase_psum("l1f_psr", [128, 128], BF16)
    pe.wait(tcb)
    pe.transpose(ps_r[0:NT, :], colsb[:], b.ident[:])
    tr = pe.sig()
    dve.wait(tr)
    dve.tensor_copy(rows[:], ps_r[0:NT, :])
    trows = dve.sig()
    amaxrow = b.t1[0:1, 0:T]
    asrow = b.rrep[0:1, 0:T]
    asrow_bf = b.s16[0:1, T:T + T]
    wsasrow_bf = b.s16[0:1, 0:T]
    sp.wait(trows)
    sp.dma_start(out=amaxrow, in_=rows[:])
    tdma = sp.sig_dma("dS")
    dve.wait(tdma)
    dve.tensor_single_scalar(amaxrow, amaxrow, 1e-5, ALU.max)
    dve.sig()
    dve.reciprocal(asrow, amaxrow)
    dve.sig()
    dve.tensor_single_scalar(asrow, asrow, 127.0, ALU.mult)
    dve.sig()
    dve.tensor_copy(asrow_bf, asrow)
    tas = dve.sig()
    dve.tensor_single_scalar(amaxrow, amaxrow, b.l1f_ws[0:1, 0:1], ALU.mult)
    dve.sig()
    dve.tensor_single_scalar(amaxrow, amaxrow, 1.0 / 127.0, ALU.mult)
    dve.sig()
    dve.tensor_copy(wsasrow_bf, amaxrow)
    twsas = dve.sig()
    ps_bc = ctx.phase_psum("l1f_psbc", [128, 512], F32)
    asrep = b.mrep
    ev = None
    for tg in range(T // 512):
        sl = slice(512 * tg, 512 * tg + 512)
        pe.wait(tas)
        if ev is not None:
            pe.wait(ev)
        pe.matmul(ps_bc[:], b.onesrow[:], asrow_bf[0:1, sl], start=True, stop=True)
        t_ = pe.sig()
        dve.wait(t_)
        dve.tensor_copy(asrep[:, sl], ps_bc[:])
        ev = dve.sig()
    ctx.dump("asrep", asrep[:], [ev])
    qi = ctx.phase_sbuf("l1f_qi", [128, T], I32)
    qtoks = []
    for c in range(DCH):
        dve.wait(ev)
        dve.wait_all(xln_toks)
        dve.tensor_tensor(b.t1[:], xt[:, c, 2:], asrep[:], ALU.mult)
        dve.sig()
        dve.tensor_scalar(b.t1[:], b.t1[:], -128.0, 127.0, ALU.max, ALU.min)
        dve.sig()
        dve.tensor_copy(qi[:], b.t1[:])
        tq = dve.sig()
        act.wait(tq)
        act.copy(xt[:, c, 2:], qi[:])
        qtoks.append(act.sig())
    ctx.dump("qbuf", xt[:], qtoks)
    wsasrep = b.s16[:, T:2 * T]
    ev2 = None
    for tg in range(T // 512):
        sl = slice(512 * tg, 512 * tg + 512)
        pe.wait(twsas)
        pe.wait(ev2 if ev2 is not None else ev)
        pe.matmul(ps_bc[:], b.onesrow[:], wsasrow_bf[0:1, sl], start=True, stop=True)
        t_ = pe.sig()
        dve.wait(t_)
        dve.tensor_copy(wsasrep[:, sl], ps_bc[:])
        ev2 = dve.sig()
    trep = ev2
    ps_z = [ctx.phase_psum("l1f_z0", [128, 512], F32), ctx.phase_psum("l1f_z1", [128, 512], F32)]
    ev_prev = [None, None]
    ztoks = []
    for m in range(DCH):
        for tg in range(T // 512):
            sl = slice(512 * tg, 512 * tg + 512)
            idx = (m * (T // 512) + tg) % 2
            pz = ps_z[idx]
            for c in range(DCH):
                pe.wait(qtoks[c])
                pe.wait(wt_toks[c])
                if c == 0 and ev_prev[idx] is not None:
                    pe.wait(ev_prev[idx])
                pe.matmul(pz[:], b.l1f_wt[:, c, 128 * m:128 * (m + 1)],
                          xt[:, c, 2 + 512 * tg:2 + 512 * (tg + 1)],
                          start=(c == 0), stop=(c == DCH - 1))
            tmm = pe.sig()
            dve.wait(tmm)
            dve.wait(trep)
            dve.tensor_tensor(b.t1[:, 0:512], pz[:], wsasrep[:, sl], ALU.mult)
            dve.sig()
            dve.tensor_scalar(b.zbuf[:, m, sl], b.t1[:, 0:512],
                              b.l1f_bias[:, m:m + 1], None, ALU.add)
            zt = dve.sig()
            ev_prev[idx] = zt
            ztoks.append(zt)
    ctx.dump("zbuf", b.zbuf[:], ztoks)
    ctx.dump("wsasrep", wsasrep[:], [trep])
    ctx.dump("l1fwt", b.l1f_wt[:], wt_toks)
    return ztoks




def emit_l0f(ctx, dram, b, xln_toks):
    pe, dve, act, pool, sp = ctx.pe, ctx.dve, ctx.act, ctx.pool, ctx.sp
    NE = 9
    b.mask_fm = ctx.phase_sbuf("l0f_maskfm", [8, T], BF16)
    ctx.push_phase()
    wr = ctx.phase_sbuf("l0f_wr", [128, DCH, 8], BF16)
    pool.dma_start(out=wr[:], in_=dram["l0f_wr"].rearrange("(c p) e -> p c e", p=128))
    twr = pool.sig_dma("dW")
    ps_r = ctx.phase_psum("l0f_psr", [8, 512], F32)
    rlog = ctx.phase_sbuf("l0f_rlog", [8, T], BF16)
    ev = None
    for tg in range(T // 512):
        sl2 = slice(2 + 512 * tg, 2 + 512 * (tg + 1))
        for c in range(DCH):
            pe.wait(twr)
            pe.wait_all(xln_toks)
            if c == 0 and ev is not None:
                pe.wait(ev)
            pe.matmul(ps_r[:], wr[:, c, :], b.xln[:, c, sl2],
                      start=(c == 0), stop=(c == DCH - 1))
        tm_ = pe.sig()
        act.wait(tm_)
        act.copy(rlog[:, 512 * tg:512 * (tg + 1)], ps_r[:])
        ev = act.sig()
    trlog = ev
    ps_g = ctx.phase_psum("l0f_psg", [128, 8], BF16)
    ps_gb = ctx.phase_psum("l0f_psgb", [8, 128], BF16)
    ltm = ctx.phase_sbuf("l0f_ltm", [128, 8], F32)
    g1 = ctx.phase_sbuf("l0f_g1", [128, 8], F32)
    g2 = ctx.phase_sbuf("l0f_g2", [128, 8], F32)
    gc = ctx.phase_sbuf("l0f_gc", [128, 12], F32)
    mask_tm = ctx.phase_sbuf("l0f_masktm", [128, 8], BF16)
    mask_fm = b.mask_fm
    tmask = None
    tp_g = None
    tp_gb = None
    for j in range(NT):
        pe.wait(trlog)
        if tp_g is not None:
            pe.wait(tp_g)
        pe.transpose(ps_g[:], rlog[:, 128 * j:128 * (j + 1)], b.ident[0:8, 0:8])
        tt = pe.sig()
        act.wait(tt)
        act.copy(ltm[:], ps_g[:])
        tp_g = act.sig()
        dve.wait(tp_g)
        dve.reduce_max(gc[:, 0:1], ltm[:], AX.X)
        dve.tensor_single_scalar(g1[:], ltm[:], gc[:, 0:1], ALU.is_ge)
        dve.reduce_sum(gc[:, 1:2], g1[:], AX.X)
        dve.scalar_tensor_tensor(g2[:], g1[:], -1e30, ltm[:], ALU.mult, ALU.add)
        dve.reduce_max(gc[:, 2:3], g2[:], AX.X)
        dve.tensor_single_scalar(gc[:, 3:4], gc[:, 1:2], 1.5, ALU.is_gt)
        dve.tensor_tensor(gc[:, 4:5], gc[:, 0:1], gc[:, 2:3], ALU.subtract)
        dve.tensor_tensor(gc[:, 5:6], gc[:, 3:4], gc[:, 4:5], ALU.mult)
        dve.tensor_tensor(gc[:, 6:7], gc[:, 2:3], gc[:, 5:6], ALU.add)
        dve.tensor_tensor(gc[:, 7:8], gc[:, 6:7], gc[:, 0:1], ALU.subtract)
        td = dve.sig()
        act.wait(td)
        act.activation(gc[:, 8:9], gc[:, 7:8], AF.Exp)
        te = act.sig()
        dve.wait(te)
        dve.tensor_single_scalar(gc[:, 9:10], gc[:, 8:9], 1.0, ALU.add)
        dve.tensor_single_scalar(gc[:, 10:11], gc[:, 9:10], 1.0, ALU.is_equal)
        dve.tensor_single_scalar(mask_tm[:], g1[:], gc[:, 10:11], ALU.mult)
        tmtm = dve.sig()
        pe.wait(tmtm)
        if tp_gb is not None:
            pe.wait(tp_gb)
        pe.transpose(ps_gb[0:8, :], mask_tm[:], b.ident[:])
        tb = pe.sig()
        act.wait(tb)
        act.copy(mask_fm[0:8, 128 * j:128 * (j + 1)], ps_gb[0:8, :])
        tp_gb = act.sig()
        tmask = tp_gb
    ctx.barrier()
    ctx.pop_phase()
    w1b = ctx.phase_sbuf("l0f_w1b", [128, 2, DCH, 512], BF16)
    w2b = ctx.phase_sbuf("l0f_w2b", [128, 2, 4, D], BF16)
    rowbuf = ctx.phase_sbuf("l0f_rowb", [1, T], BF16)
    maskrep = ctx.phase_sbuf("l0f_mrep", [128, T], BF16)
    hraw = b.s16[:, 0:T]
    hm = b.s16[:, T:2 * T]
    _h01 = b.t1[:].bitcast(BF16)
    _h23 = b.mrep[:].bitcast(BF16)
    hsq = lambda fc, sl=slice(0, T): (_h01 if fc < 2 else _h23)[
        :, T * (fc % 2) + (sl.start or 0):T * (fc % 2) + (sl.stop if sl.stop is not None else T)]
    ps_h = ctx.phase_psum("l0f_psh", [128, T], F32)
    ps_z = ctx.phase_psum("l0f_psz", [128, T], F32)
    w_used = [None, None]
    tmr_used = None
    tmr = None
    th_half = [None, None]
    tz_half = [None, None]
    hsq_read = [None] * 4
    ztoks = []
    for e in range(NE):
        slot = e % 2
        pool.wait(w_used[slot])
        pool.dma_start(out=w1b[:, slot], in_=dram["l0f_w1"][e].rearrange(
            "(c p) f -> p c f", p=128))
        pool.sig_dma("dW")
        pool.dma_start(out=w2b[:, slot], in_=dram["l0f_w2"][e].rearrange(
            "(c p) o -> p c o", p=128))
        tw = pool.sig_dma("dW")
        if e < 8:
            sp.wait(tmask)
            sp.wait(tmr)
            sp.dma_start(out=rowbuf[:], in_=mask_fm[e:e + 1, :])
            trow = sp.sig_dma("dS")
            pe.wait(trow)
            pe.wait(th_half[0])
            pe.wait(th_half[1])
            if tmr is not None:
                pe.wait(tmr)
            for tg in range(T // 512):
                sl = slice(512 * tg, 512 * (tg + 1))
                pe.matmul(ps_h[:, sl], b.onesrow[:], rowbuf[0:1, sl],
                          start=True, stop=True)
            tb_ = pe.sig()
            dve.wait(tb_)
            dve.wait(tmr_used)
            for tg in range(T // 512):
                sl = slice(512 * tg, 512 * (tg + 1))
                dve.tensor_copy(maskrep[:, sl], ps_h[:, sl])
            tmr = dve.sig()
        hsq_toks = []
        for fc in range(4):
            for half in range(2):
                for d in range(DCH):
                    pe.wait(tw)
                    pe.wait_all(xln_toks)
                    if d == 0:
                        pe.wait(th_half[half])
                        if fc == 0:
                            pe.wait(tmr)
                    lh = w1b[:, slot, d, 128 * fc:128 * (fc + 1)]
                    for tg in (2 * half, 2 * half + 1):
                        pe.matmul(ps_h[:, 512 * tg:512 * (tg + 1)], lh,
                                  b.xln[:, d, 2 + 512 * tg:2 + 512 * (tg + 1)],
                                  start=(d == 0), stop=(d == DCH - 1))
                tmm = pe.sig()
                hsl = slice(1024 * half, 1024 * (half + 1))
                dve.wait(tmm)
                dve.wait(hsq_read[fc])
                dve.tensor_single_scalar(hraw[:, hsl], ps_h[:, hsl], 0.0,
                                         ALU.max)
                th_half[half] = dve.sig()
                if e < 8:
                    dve.tensor_tensor(hm[:, hsl], hraw[:, hsl],
                                      maskrep[:, hsl], ALU.mult)
                    dve.tensor_tensor(hsq(fc, hsl), hm[:, hsl], hraw[:, hsl],
                                      ALU.mult)
                else:
                    dve.tensor_tensor(hsq(fc, hsl), hraw[:, hsl], hraw[:, hsl],
                                      ALU.mult)
            hsq_toks.append(dve.sig())
        tmr_used = hsq_toks[-1]
        if e == 8:
            pass
        for m in range(DCH):
            for half in range(2):
                for fc in range(4):
                    pe.wait(hsq_toks[fc])
                    if fc == 0:
                        pe.wait(tz_half[half])
                    lh = w2b[:, slot, fc, 128 * m:128 * (m + 1)]
                    for tg in (2 * half, 2 * half + 1):
                        pe.matmul(ps_z[:, 512 * tg:512 * (tg + 1)], lh,
                                  hsq(fc, slice(512 * tg, 512 * (tg + 1))),
                                  start=(fc == 0), stop=(fc == 3))
                tmm = pe.sig()
                hsl = slice(1024 * half, 1024 * (half + 1))
                dve.wait(tmm)
                if e == 0:
                    dve.tensor_copy(b.zbuf[:, m, hsl], ps_z[:, hsl])
                else:
                    dve.tensor_tensor(b.zbuf[:, m, hsl], b.zbuf[:, m, hsl],
                                      ps_z[:, hsl], ALU.add)
                tz_half[half] = dve.sig()
            if e == NE - 1:
                ztoks.append(tz_half[1])
        for fc in range(4):
            hsq_read[fc] = tz_half[1]
        w_used[slot] = tz_half[1]
    ctx.dump("moe_maskfm", mask_fm[:], [tmask])
    ctx.dump("moe_z", b.zbuf[:], ztoks)
    return ztoks



PAIRS = ((0, 1), (0, 2), (0, 3), (1, 2), (1, 3), (2, 3))
JSIGN = (1, -1, 1, 1, -1, 1)
ATT_SCALE = float(np.float32(6.0 ** (-0.5)))


def emit_l0a(ctx, dram, b, xln_toks):
    pe, dve, act, pool, sp = ctx.pe, ctx.dve, ctx.act, ctx.pool, ctx.sp
    bandm = ctx.phase_sbuf("l0a_band", [128, 16], F32)
    tri16 = ctx.phase_sbuf("l0a_tri16", [128, T], BF16)
    half01 = ctx.phase_sbuf("l0a_half", [1, 256], BF16)
    sp.dma_start(out=bandm[:], in_=dram["bandmask"])
    sp.sig_dma("dW")
    pool.dma_start(out=tri16[:], in_=dram["trimask16"])
    pool.sig_dma("dW")
    pool.dma_start(out=half01[0:1, 0:128], in_=dram["half0"])
    pool.sig_dma("dW")
    pool.dma_start(out=half01[0:1, 128:256], in_=dram["half1"])
    tw0 = pool.sig_dma("dW")
    vtm = b.zbuf[:].rearrange("p c t -> p (c t)").rearrange("p (j o) -> p j o", o=D)
    qspec = ctx.phase_sbuf("l0a_qspec", [128, T], BF16)
    kspec = ctx.phase_sbuf("l0a_kspec", [128, T], BF16)
    ctx.push_phase()
    wq = ctx.phase_sbuf("l0a_wqb", [128, 16, 128], BF16)
    wk = ctx.phase_sbuf("l0a_wkb", [128, DCH, 128], BF16)
    wv = ctx.phase_sbuf("l0a_wvb", [128, DCH, D], BF16)
    bvr = ctx.phase_sbuf("l0a_bvr", [1, D], BF16)
    pool.dma_start(out=wq[:], in_=dram["l0a_wq"].rearrange("(c p) o -> p c o", p=128))
    pool.sig_dma("dW")
    pool.dma_start(out=wk[:], in_=dram["l0a_wk"].rearrange("(c p) o -> p c o", p=128))
    pool.sig_dma("dW")
    pool.dma_start(out=wv[:], in_=dram["l0a_wv"].rearrange("(c p) o -> p c o", p=128))
    pool.sig_dma("dW")
    pool.dma_start(out=bvr[:], in_=dram["l0a_bv"])
    tw = pool.sig_dma("dW")
    q12f = ctx.phase_sbuf("l0a_q12f", [128, T], BF16)
    k12f = ctx.phase_sbuf("l0a_k12f", [128, T], BF16)
    _rbf = b.rrep[:].bitcast(BF16)
    ps_p = ctx.phase_psum("l0a_psp", [128, 2, 512], F32)
    ps_t = ctx.phase_psum("l0a_pst", [128, 128], BF16)
    ps_v = ctx.phase_psum("l0a_psv", [128, 2, 512], F32)
    tvev = {}
    for j in range(NT):
        tsl = slice(2 + 128 * j, 2 + 128 * (j + 1))
        for half in range(2):
            pv = ps_v[:, half, :]
            osl = slice(512 * half, 512 * (half + 1))
            for dc in range(DCH):
                pe.wait(tw)
                pe.wait_all(xln_toks)
                if dc == 0 and j >= 1:
                    pe.wait(tvev[(j - 1, half)])
                pe.matmul(pv[:], b.xln[:, dc, tsl], wv[:, dc, osl],
                          start=(dc == 0), stop=False)
            pe.matmul(pv[:], b.onesrow[:, 0:128], bvr[0:1, osl],
                      start=False, stop=True)
            tmm = pe.sig()
            act.wait(tmm)
            act.copy(vtm[:, j, osl], pv[:])
            tvev[(j, half)] = act.sig()
    tqf = None
    for tg in range(T // 512):
        sl = slice(512 * tg, 512 * (tg + 1))
        for src, wgt, nch, off in ((q12f, wq, 16, None), (k12f, wk, DCH, None)):
            pp = ps_p[:, 0 if src is q12f else 1, :]
            for c in range(nch):
                pe.wait(tw)
                pe.wait_all(xln_toks)
                if c == 0 and tqf is not None:
                    pe.wait(tqf)
                if src is q12f and c >= DCH:
                    rhs = b.xln[:, c - DCH, 1 + 512 * tg:1 + 512 * (tg + 1)]
                else:
                    rhs = b.xln[:, c, 2 + 512 * tg:2 + 512 * (tg + 1)]
                pe.matmul(pp[:], wgt[:, c, :], rhs, start=(c == 0), stop=(c == nch - 1))
            tmm = pe.sig()
            act.wait(tmm)
            act.copy(src[:, sl], pp[:])
            tqf = act.sig()
    q12tmF = _rbf[:, 0:T]
    k12tmF = _rbf[:, T:2 * T]
    _tbf = b.t1[:].bitcast(BF16)
    qlinF = _tbf[:, 0:T]
    klinF = _tbf[:, T:2 * T]
    pool.memset(qlinF, 0.0)
    pool.memset(klinF, 0.0)
    tlz = pool.sig()
    ttm = None
    for j in range(NT):
        for which, (f_, dstF) in enumerate(((q12f, q12tmF), (k12f, k12tmF))):
            pe.wait(tqf)
            if ttm is not None:
                pe.wait(ttm)
            pe.transpose(ps_t[:], f_[:, 128 * j:128 * (j + 1)], b.ident[:])
            tt = pe.sig()
            act.wait(tt)
            act.copy(dstF[:, 128 * j:128 * (j + 1)], ps_t[:])
            ttm = act.sig()
    qa = q12tmF.rearrange("p (j x) -> p j x", x=128)[:, :, 0:64]\
        .rearrange("p j (h i) -> p j h i", i=4)
    qb2 = q12tmF.rearrange("p (j x) -> p j x", x=128)[:, :, 64:128]\
        .rearrange("p j (h i) -> p j h i", i=4)
    ka = k12tmF.rearrange("p (j x) -> p j x", x=128)[:, :, 0:64]\
        .rearrange("p j (h i) -> p j h i", i=4)
    kb2 = k12tmF.rearrange("p (j x) -> p j x", x=128)[:, :, 64:128]\
        .rearrange("p j (h i) -> p j h i", i=4)
    lvq = qlinF.rearrange("p (j x) -> p j x", x=128)\
        .rearrange("p j (h c) -> p j h c", c=8)
    lvk = klinF.rearrange("p (j x) -> p j x", x=128)\
        .rearrange("p j (h c) -> p j h c", c=8)
    t_a = b.mrep[:, 0:256].rearrange("p (j h u) -> p j h u", h=16, u=1)
    t_b = b.mrep[:, 256:512].rearrange("p (j h u) -> p j h u", h=16, u=1)
    dve.wait(ttm)
    dve.wait(tlz)
    for which in (0, 1):
        a1, a2, lv = (qa, qb2, lvq) if which == 0 else (ka, kb2, lvk)
        for c6 in range(6):
            if which == 0:
                i_, j_ = PAIRS[c6]
                neg = False
            else:
                i_, j_ = PAIRS[5 - c6]
                neg = (JSIGN[c6] < 0)
            dve.tensor_tensor(t_a, a1[:, :, :, i_:i_ + 1], a2[:, :, :, j_:j_ + 1], ALU.mult)
            dve.tensor_tensor(t_b, a1[:, :, :, j_:j_ + 1], a2[:, :, :, i_:i_ + 1], ALU.mult)
            if neg:
                dve.tensor_tensor(lv[:, :, :, c6:c6 + 1], t_b, t_a, ALU.subtract)
            else:
                dve.tensor_tensor(lv[:, :, :, c6:c6 + 1], t_a, t_b, ALU.subtract)
    tpl = dve.sig()
    ncol = b.rrep
    for which in (0, 1):
        linF = qlinF if which == 0 else klinF
        lv = lvq if which == 0 else lvk
        dve.wait(tpl)
        dve.tensor_tensor(b.mrep[:], linF, linF, ALU.mult)
        dve.tensor_reduce(ncol[:, 0:256].rearrange("p (x u) -> p x u", u=1),
                          b.mrep[:].rearrange("p (x c) -> p x c", c=8), AX.X, ALU.add)
        tn_ = dve.sig()
        act.wait(tn_)
        act.activation(ncol[:, 256:512], ncol[:, 0:256], AF.Sqrt)
        tsq2 = act.sig()
        dve.wait(tsq2)
        dve.tensor_single_scalar(ncol[:, 256:512], ncol[:, 256:512], 1e-12, ALU.max)
        dve.reciprocal(ncol[:, 512:768], ncol[:, 256:512])
        dve.tensor_tensor(lv, lv, ncol[:, 512:768]
                          .rearrange("p (j h u) -> p j h u", h=16, u=1)
                          .broadcast_to([128, NT, 16, 8]), ALU.mult)
        tnorm = dve.sig()
    tspec = None
    for j in range(NT):
        for which in (0, 1):
            linF = qlinF if which == 0 else klinF
            pe.wait(tnorm)
            if tspec is not None:
                pe.wait(tspec)
            pe.transpose(ps_t[:], linF[:, 128 * j:128 * (j + 1)], b.ident[:])
            tt2 = pe.sig()
            act.wait(tt2)
            act.copy((qspec if which == 0 else kspec)[:, 128 * j:128 * (j + 1)], ps_t[:])
            tspec = act.sig()
    ctx.barrier()
    ctx.pop_phase()
    ctx.push_phase()
    psb = [b.s16[:, 0:T], b.s16[:, T:2 * T]]
    qb = ctx.phase_sbuf("l0a_qb", [128, T], BF16)
    brd_all = ctx.phase_sbuf("l0a_brdall", [128, 2, 512], BF16)
    zrow = ctx.phase_sbuf("l0a_zrow", [1, 512], BF16)
    pool.memset(zrow[:], 0.0)
    tzr = pool.sig()
    rrow = b.t1[0:1, 0:T]
    rrowb = ctx.phase_sbuf("l0a_rrowb", [1, T], BF16)
    ps_l = ctx.phase_psum("l0a_psl", [128, 2, 512], F32)
    ps_oa = ctx.phase_psum("l0a_psoa", [128, 4, 128], F32)
    ps_ob = ctx.phase_psum("l0a_psob", [128, 4, 128], F32)
    ps_s = ctx.phase_psum("l0a_pss", [128, T], F32)
    oview = [b.xln[:, p, 2:2 + T] for p in range(DCH)]

    def oview_a(q):
        return b.xln[:, 0:4, 2 + 128 * q:2 + 128 * (q + 1)]

    def oview_b(q):
        return b.xln[:, 4:8, 2 + 128 * q:2 + 128 * (q + 1)]
    tqb = None
    texp = [None, None]
    tpread = [None, None]
    pslw = [None, None]
    eread = [None, None]
    ebuf = b.mrep[:].bitcast(BF16)
    tsum_ev = None
    to_ev = None
    tbrd_ev = None
    ztoks_oready = []
    for q in range(NT):
        dve.wait(tspec)
        dve.wait(tqb)
        for h in range(16):
            dve.tensor_single_scalar(qb[:, 128 * h:128 * (h + 1)],
                                     qspec[:, 128 * q:128 * (q + 1)],
                                     bandm[:, h:h + 1], ALU.mult)
        tqbv = dve.sig()
        pe.wait(tzr)
        pe.wait(to_ev)
        pe.matmul(ps_oa[:].rearrange("p a b -> p (a b)"), b.onesrow[:, 0:128],
                  zrow[0:1, :], start=True, stop=True)
        pe.matmul(ps_ob[:].rearrange("p a b -> p (a b)"), b.onesrow[:, 0:128],
                  zrow[0:1, :], start=True, stop=True)

        def emit_sums_attnv(k):
            pb2 = k % 2
            pe.wait(texp[pb2])
            if k == 0:
                pe.wait(tsum_ev)
            for sg in range(T // 512):
                pe.matmul(ps_s[:, 512 * sg:512 * (sg + 1)], b.ones128[:],
                          psb[pb2][:, 512 * sg:512 * (sg + 1)],
                          start=(k == 0), stop=(k == q))
            for p in range(DCH):
                po = (ps_oa if p < 4 else ps_ob)[:, p % 4, :]
                pe.matmul(po[0:64, :], vtm[:, k, 128 * p:128 * p + 64],
                          psb[pb2][:, 256 * p:256 * p + 128],
                          start=False, stop=(k == q), tile_position=(0, 0))
                pe.matmul(po[64:128, :], vtm[:, k, 128 * p + 64:128 * (p + 1)],
                          psb[pb2][:, 256 * p + 128:256 * (p + 1)],
                          start=False, stop=(k == q), tile_position=(0, 64))
            tpread[pb2] = pe.sig()

        for k in range(q + 1):
            pb = k % 2
            for ch in range(4):
                sl_ = ch % 2
                pe.wait(tqbv)
                pe.wait(tspec)
                pe.wait(pslw[sl_])
                pe.matmul(ps_l[:, sl_, :],
                          kspec[:, 128 * k:128 * (k + 1)],
                          qb[:, 512 * ch:512 * (ch + 1)],
                          start=True, stop=True)
                tmm = pe.sig()
                dve.wait(tmm)
                dve.wait(eread[sl_])
                dve.tensor_copy(ebuf[:, 512 * sl_:512 * (sl_ + 1)], ps_l[:, sl_, :])
                pslw[sl_] = dve.sig()
                act.wait(pslw[sl_])
                act.wait(tpread[pb])
                act.activation(psb[pb][:, 512 * ch:512 * (ch + 1)],
                               ebuf[:, 512 * sl_:512 * (sl_ + 1)],
                               AF.Exp, scale=ATT_SCALE)
                eread[sl_] = act.sig()
            texp[pb] = eread[1]
            if k == q:
                dve.wait(texp[pb])
                dve.tensor_tensor(psb[pb][:], psb[pb][:], tri16[:], ALU.mult)
                texp[pb] = dve.sig()
            if q == 3 and k == 1:
                ctx.dump("l0a_P31", psb[pb][:, :], [texp[pb]])
            if k >= 1:
                emit_sums_attnv(k - 1)
        emit_sums_attnv(q)
        tka = tpread[q % 2]
        dve.wait(tka)
        dve.tensor_copy(rrow[0:1, :], ps_s[0:1, :])
        tsr = dve.sig()
        act.wait(tsr)
        act.activation(rrow[0:1, :], rrow[0:1, :], AF.Ln)
        act.activation(rrowb[0:1, :], rrow[0:1, :], AF.Exp, scale=-1.0)
        trr = act.sig()
        if q == 3:
            ctx.dump("l0a_r3", rrowb[0:1, :], [trr])
        tsum_ev = trr
        rview = rrowb[0:1, :].rearrange("u (a b q) -> u a b q", a=8, b=2)
        for half in range(2):
            pe.wait(trr)
            pe.wait(pslw[half])
            pe.wait(tbrd_ev)
            pe.matmul(ps_l[:, half, :], half01[0:1, 0:128],
                      rview[:, 4 * half:4 * (half + 1), 0, :], start=True, stop=True)
            pe.matmul(ps_l[:, half, :], half01[0:1, 128:256],
                      rview[:, 4 * half:4 * (half + 1), 1, :], start=False, stop=True)
        tbb = pe.sig()
        dve.wait(tbb)
        dve.wait(to_ev)
        dve.tensor_copy(brd_all[:], ps_l[:])
        tbrd_ev = dve.sig()
        pslw = [tbrd_ev, tbrd_ev]
        dve.wait(tbrd_ev)
        dve.wait(tka)
        dve.tensor_tensor(oview_a(q), ps_oa[:],
                          brd_all[:, 0, :].rearrange("p (a q) -> p a q", a=4), ALU.mult)
        dve.tensor_tensor(oview_b(q), ps_ob[:],
                          brd_all[:, 1, :].rearrange("p (a q) -> p a q", a=4), ALU.mult)
        to_ev = dve.sig()
        ztoks_oready.append(to_ev)
    ctx.dump("l0a_o", b.xln[:], ztoks_oready)
    ctx.barrier()
    ctx.pop_phase()
    ctx.push_phase()
    wo = ctx.phase_sbuf("l0a_wob", [128, DCH, D], BF16)
    bor = ctx.phase_sbuf("l0a_bor", [1, D], BF16)
    pool.dma_start(out=wo[:], in_=dram["l0a_wo"].rearrange("(c p) o -> p c o", p=128))
    pool.sig_dma("dW")
    pool.dma_start(out=bor[:], in_=dram["l0a_bo"])
    two = pool.sig_dma("dW")
    ones512 = ctx.phase_sbuf("l0a_ones512", [1, 512], BF16)
    pool.memset(ones512[:], 1.0)
    ton = pool.sig()
    ps_z = ctx.phase_psum("l0a_psz", [128, T], F32)
    ztoks = []
    tz_ev = None
    for m in range(DCH):
        for p in range(DCH):
            pe.wait(two)
            pe.wait_all(ztoks_oready)
            if p == 0:
                pe.wait(tz_ev)
            lh = wo[:, p, 128 * m:128 * (m + 1)]
            for tg in range(T // 512):
                pe.matmul(ps_z[:, 512 * tg:512 * (tg + 1)], lh,
                          oview[p][:, 512 * tg:512 * (tg + 1)],
                          start=(p == 0), stop=False)
        for tg in range(T // 512):
            pe.wait(ton)
            pe.matmul(ps_z[:, 512 * tg:512 * (tg + 1)],
                      bor[0:1, 128 * m:128 * (m + 1)], ones512[0:1, :],
                      start=False, stop=True)
        tmm = pe.sig()
        dve.wait(tmm)
        dve.tensor_copy(b.zbuf[:, m, :], ps_z[:])
        tz_ev = dve.sig()
        ztoks.append(tz_ev)
    ctx.barrier()
    ctx.pop_phase()
    ctx.dump("l0a_z", b.zbuf[:], ztoks)
    return ztoks



L1A_HALVES = ((0, 1, 2), (3, 4))


def emit_l1a_fold(ctx, dram, b):
    pe, dve, act, pool, sp = ctx.pe, ctx.dve, ctx.act, ctx.pool, ctx.sp
    ctx.push_phase()
    b.bias_tilde = ctx.phase_sbuf("l1a_btil", [1, 6 * KPAD], BF16)
    wxb = ctx.phase_sbuf("l1a_wxb", [128, DCH, D], BF16)
    cb = ctx.phase_sbuf("l1a_cb", [128, DCH, KBINS], BF16)
    bcol = ctx.phase_sbuf("l1a_bcol", [128, DCH], BF16)
    stg = ctx.phase_sbuf("l1a_stg", [128, KPAD], BF16)
    ps_f = ctx.phase_psum("l1a_psf", [128, KBINS], F32)
    ps_b = ctx.phase_psum("l1a_psb", [1, KBINS], F32)
    pool.memset(stg[:], 0.0)
    tz = pool.sig()
    tprev_dma = None
    tstg_free = tz
    for wi, (wname, bname, plane) in enumerate((
            ("l1a_wq", "l1a_bq", "dftC"), ("l1a_wq", "l1a_bq", "dftS"),
            ("l1a_wk", "l1a_bk", "dftC"), ("l1a_wk", "l1a_bk", "dftS"),
            ("l1a_wv", "l1a_bv", "dftC"), ("l1a_wv", "l1a_bv", "dftS"))):
        pool.wait(tprev_dma)
        pool.dma_start(out=wxb[:], in_=dram[wname].rearrange("(c p) d -> p c d", p=128))
        pool.sig_dma("dW")
        pool.dma_start(out=cb[:], in_=dram[plane].rearrange("(c p) k -> p c k", p=128))
        pool.sig_dma("dW")
        pool.dma_start(out=bcol[:], in_=dram[bname])
        tld = pool.sig_dma("dW")
        tev = None
        for dc in range(DCH):
            for ec in range(DCH):
                pe.wait(tld)
                if ec == 0 and tev is not None:
                    pe.wait(tev)
                pe.matmul(ps_f[:, 0:512], wxb[:, ec, 128 * dc:128 * (dc + 1)],
                          cb[:, ec, 0:512], start=(ec == 0), stop=(ec == DCH - 1))
                pe.matmul(ps_f[:, 512:513], wxb[:, ec, 128 * dc:128 * (dc + 1)],
                          cb[:, ec, 512:513], start=(ec == 0), stop=(ec == DCH - 1))
            tmm = pe.sig()
            act.wait(tmm)
            act.wait(tstg_free)
            act.copy(stg[:, 0:KBINS], ps_f[:])
            tev = act.sig()
            sp.wait(tev)
            sp.dma_start(out=dram["wtil"][wi, 128 * dc:128 * (dc + 1), :], in_=stg[:])
            tstg_free = sp.sig_dma("dS")
        tevb = None
        for ec in range(DCH):
            pe.wait(tld)
            pe.matmul(ps_b[:, 0:512], bcol[:, ec:ec + 1], cb[:, ec, 0:512],
                      start=(ec == 0), stop=(ec == DCH - 1))
            pe.matmul(ps_b[:, 512:513], bcol[:, ec:ec + 1], cb[:, ec, 512:513],
                      start=(ec == 0), stop=(ec == DCH - 1))
        tmm = pe.sig()
        dve.wait(tmm)
        dve.memset(b.bias_tilde[0:1, KPAD * wi:KPAD * (wi + 1)], 0.0)
        dve.sig()
        dve.tensor_copy(b.bias_tilde[0:1, KPAD * wi:KPAD * wi + KBINS], ps_b[:])
        tbt = dve.sig()
        pe.wait(tbt)
        tprev_dma = Tok(tld.sem, tld.val, tld.snap)
        tprev_dma = tev
    ctx.dump("l1a_btil", b.bias_tilde[:], [tbt])
    sp.wait(tbt)
    sp.dma_start(out=dram["btil"], in_=b.bias_tilde[:])
    tst = sp.sig_dma("dS")
    for e_ in ENGS:
        ctx.eng[e_].wait(tst)
    ctx.pop_phase()
    return None


def _complex_norm(ctx, b, re_src, im_src, re_dst, im_dst, width, deps):
    dve, act = ctx.dve, ctx.act
    n2 = b.t1[:, 0:width]
    t2 = b.mrep[:, 0:width]
    dve.wait_all(deps)
    dve.tensor_tensor(n2, re_src, re_src, ALU.mult)
    dve.tensor_tensor(t2, im_src, im_src, ALU.mult)
    dve.tensor_tensor(n2, n2, t2, ALU.add)
    tn = dve.sig()
    act.wait(tn)
    dve.tensor_single_scalar(n2, n2, 1e-30, ALU.max)
    tm_ = dve.sig()
    act.wait(tm_)
    act.activation(n2, n2, AF.Ln)
    act.activation(n2, n2, AF.Exp, scale=0.5)
    act.activation(n2, n2, AF.Ln, bias=1e-8)
    act.activation(t2, n2, AF.Exp, scale=-1.0)
    tc_ = act.sig()
    dve.wait(tc_)
    dve.tensor_tensor(re_dst, re_src, t2, ALU.mult)
    dve.sig()
    dve.tensor_tensor(im_dst, im_src, t2, ALU.mult)
    return dve.sig()


L1A_PARTS = ((0, 1), (2, 3), (4,))


def emit_l1a(ctx, dram, b, xln_toks, seq):
    pe, dve, act, pool, sp = ctx.pe, ctx.dve, ctx.act, ctx.pool, ctx.sp
    onesT = ctx.phase_sbuf("l1a_onesT", [1, 512], BF16)
    utri = ctx.phase_sbuf("l1a_utri", [128, 128], BF16)
    carry = ctx.phase_sbuf("l1a_carry", [128, 12], F32)
    pool.memset(onesT[:], 1.0)
    tone = pool.sig()
    pool.dma_start(out=utri[:], in_=dram["utri"])
    tutri = pool.sig_dma("dW")
    zfirst = [True]
    ztoks = []
    for part_i, kcs in enumerate(L1A_PARTS):
        nk = len(kcs)
        W = 128 * nk
        ko = 128 * kcs[0]
        ctx.push_phase()
        btlp = ctx.phase_sbuf("l1a_btlp", [1, 6 * W], BF16)
        sp.dma_start(
            out=btlp[:].rearrange("u (w k) -> u w k", k=W),
            in_=dram["btil"].rearrange("u (w k) -> u w k", k=KPAD)[:, :, ko:ko + W])
        tbtl = sp.sig_dma("dW")
        xk = ctx.phase_sbuf("l1a_xk", [128, NT, 2 * W], BF16)
        _qre = b.rrep[:].bitcast(BF16).rearrange("p (a t) -> p a t", t=T)[:, 0:nk, :]
        _qim = ctx.phase_sbuf("l1a_qhi", [128, nk, T], BF16)

        def qh(ci):
            return _qre[:, ci, :] if ci < nk else _qim[:, ci - nk, :]
        ctx.push_phase()
        wkb = ctx.phase_sbuf("l1a_wkb", [128, DCH, 4, W], BF16)
        for j, wi in enumerate((2, 3, 4, 5)):
            sp.dma_start(out=wkb[:, :, j, :], in_=dram["wtil"][wi, :, ko:ko + W]
                         .rearrange("(c p) k -> p c k", p=128))
            twk = sp.sig_dma("dW")

        dve.wait(tbtl)
        dve.engine_nop()
        tbc = tbtl
        ps4d = [ctx.phase_psum("l1a_ps4a", [128, 4 * W], F32),
                ctx.phase_psum("l1a_ps4b", [128, 4 * W], F32)]
        kvtd = [b.sq[:].rearrange("p a w -> p (a w)")[:, 0:4 * W]
                .rearrange("p (a w) -> p a w", a=4),
                b.t1[:].bitcast(BF16)[:, 1024:1024 + 4 * W]
                .rearrange("p (a w) -> p a w", a=4)]
        ps_free = [None, None]
        kvt_free = [None, None]
        tbind_prev = None
        for j in range(NT):
            sl2 = j % 2
            ps4 = ps4d[sl2]
            kvt = kvtd[sl2]
            tsl = slice(2 + 128 * j, 2 + 128 * (j + 1))
            for dc in range(DCH):
                pe.wait(twk)
                pe.wait_all(xln_toks)
                pe.wait(tone)
                pe.wait(tbc)
                if dc == 0:
                    pe.wait(ps_free[sl2])
                wr_ = wkb[:, dc, :, :].rearrange("p a w -> p (a w)")
                for sg in range(0, 4 * W, 512):
                    pe.matmul(ps4[:, sg:sg + 512], b.xln[:, dc, tsl],
                              wr_[:, sg:sg + 512], start=(dc == 0), stop=False)
            for sg in range(0, 4 * W, 512):
                pe.matmul(ps4[:, sg:sg + 512], onesT[0:1, 0:128],
                          btlp[0:1, 2 * W + sg:2 * W + sg + 512],
                          start=False, stop=True)
            tmm = pe.sig()
            act.wait(tmm)
            act.wait(kvt_free[sl2])
            act.copy(kvt[:], ps4[:])
            ps_free[sl2] = act.sig()
            kv4 = kvt[:].rearrange("p (a b) w -> p a b w", b=2)
            res_ = kv4[:, :, 0, :]
            ims_ = kv4[:, :, 1, :]
            n2 = b.t1[:, 0:2 * W].rearrange("p (a w) -> p a w", a=2)
            t2 = b.mrep[:, 0:2 * W].rearrange("p (a w) -> p a w", a=2)
            dve.wait(ps_free[sl2])
            dve.tensor_tensor(n2, res_, res_, ALU.mult)
            dve.tensor_tensor(t2, ims_, ims_, ALU.mult)
            dve.tensor_tensor(n2, n2, t2, ALU.add)
            dve.tensor_single_scalar(n2, n2, 1e-30, ALU.max)
            tn_ = dve.sig()
            act.wait(tn_)
            act.activation(n2, n2, AF.Ln)
            act.activation(n2, n2, AF.Exp, scale=0.5)
            act.activation(n2, n2, AF.Ln, bias=1e-8)
            act.activation(t2, n2, AF.Exp, scale=-1.0)
            tc_ = act.sig()
            dve.wait(tc_)
            dve.tensor_tensor(res_, res_, t2, ALU.mult)
            dve.tensor_tensor(ims_, ims_, t2, ALU.mult)
            dve.sig()
            ta = b.s16[:, 0:W]
            tb_ = b.s16[:, W:2 * W]
            dve.tensor_tensor(ta, kvt[:, 0, :], kvt[:, 2, :], ALU.mult)
            dve.tensor_tensor(tb_, kvt[:, 1, :], kvt[:, 3, :], ALU.mult)
            dve.tensor_tensor(xk[:, j, 0:W], ta, tb_, ALU.subtract)
            dve.tensor_tensor(ta, kvt[:, 0, :], kvt[:, 3, :], ALU.mult)
            dve.tensor_tensor(tb_, kvt[:, 1, :], kvt[:, 2, :], ALU.mult)
            dve.tensor_tensor(xk[:, j, W:2 * W], ta, tb_, ALU.add)
            tbind_prev = dve.sig()
            kvt_free[sl2] = tbind_prev
        ctx.barrier()
        ctx.pop_phase()
        ctx.push_phase()
        wqb = ctx.phase_sbuf("l1a_wqb", [128, 2, DCH, W], BF16)
        for j, wi in enumerate((0, 1)):
            sp.dma_start(out=wqb[:, j], in_=dram["wtil"][wi, :, ko:ko + W]
                         .rearrange("(c p) k -> p c k", p=128))
            twq = sp.sig_dma("dW")
        ps_q = ctx.phase_psum("l1a_psq", [128, T], F32)
        tq_free = None
        for ci in range(2 * nk):
            pl, kci = (0, ci) if ci < nk else (1, ci - nk)
            for dc in range(DCH):
                pe.wait(twq)
                pe.wait_all(xln_toks)
                if dc == 0:
                    pe.wait(tq_free)
                lh = wqb[:, pl, dc, 128 * kci:128 * (kci + 1)]
                for tg in range(T // 512):
                    pe.matmul(ps_q[:, 512 * tg:512 * (tg + 1)], lh,
                              b.xln[:, dc, 2 + 512 * tg:2 + 512 * (tg + 1)],
                              start=(dc == 0), stop=False)
            pe.wait(tbtl)
            seg = btlp[0:1, W * pl + 128 * kci:W * pl + 128 * (kci + 1)]
            for tg in range(T // 512):
                pe.matmul(ps_q[:, 512 * tg:512 * (tg + 1)], seg,
                          onesT[0:1, 0:512], start=False, stop=True)
            tmm = pe.sig()
            act.wait(tmm)
            act.copy(qh(ci), ps_q[:])
            tq_free = act.sig()
        tqn = []
        for kci in range(nk):
            tn = _complex_norm(ctx, b, qh(kci), qh(nk + kci),
                               qh(kci), qh(nk + kci), T, [tq_free])
            tqn.append(tn)
        ctx.barrier()
        ctx.pop_phase()
        ctx.push_phase()
        ps_c = [ctx.phase_psum(f"l1a_psc{i}", [128, 128], F32) for i in range(4)]
        ps_ts = ctx.phase_psum("l1a_psts", [128, 2 * nk, NT], F32)
        tsum = ctx.phase_sbuf("l1a_tsum", [128, 2 * nk, NT + 1], F32)
        onecol = ctx.phase_sbuf("l1a_onecol", [128, 1], BF16)
        pool.memset(onecol[:], 1.0)
        toc = pool.sig()
        pe.wait_all(tqn)
        pe.wait(toc)
        for ci in range(2 * nk):
            for j in range(NT):
                pe.matmul(ps_ts[:, ci, j:j + 1], xk[:, j, 128 * ci:128 * (ci + 1)],
                          onecol[:], start=True, stop=True)
        tts = pe.sig()
        tsum2 = ctx.phase_sbuf("l1a_tsum2", [128, 2 * nk, NT + 1], F32)
        dve.wait(tts)
        dve.memset(tsum[:, :, 0:1], 0.0)
        dve.memset(tsum2[:, :, 0:1], 0.0)
        dve.tensor_copy(tsum[:, :, 1:NT + 1], ps_ts[:, :, :])
        dve.sig()
        bufs = (tsum, tsum2)
        for si, sh in enumerate((1, 2, 4, 8)):
            a_, b2_ = bufs[si % 2], bufs[(si + 1) % 2]
            dve.tensor_copy(b2_[:, :, 1:sh + 1], a_[:, :, 1:sh + 1])
            dve.tensor_tensor(b2_[:, :, sh + 1:NT + 1], a_[:, :, sh + 1:NT + 1],
                              a_[:, :, 1:NT + 1 - sh], ALU.add)
            dve.sig()
        tsum = bufs[0]
        tpfx = dve.sig()
        tev4 = [None] * 4
        for ci in range(2 * nk):
            for j in range(NT):
                sl4 = (ci * NT + j) % 4
                pe.wait(tutri)
                pe.wait(tev4[sl4])
                pe.matmul(ps_c[sl4][:], xk[:, j, 128 * ci:128 * (ci + 1)], utri[:],
                          start=True, stop=True)
                tmm = pe.sig()
                dve.wait(tmm)
                dve.wait(tpfx)
                dve.tensor_single_scalar(xk[:, j, 128 * ci:128 * (ci + 1)],
                                         ps_c[sl4][:], tsum[:, ci, j:j + 1], ALU.add)
                tev4[sl4] = dve.sig()
        ctx.barrier()
        ctx.pop_phase()
        ctx.push_phase()
        cwb = ctx.phase_sbuf("l1a_cwb", [128, 2, nk, D], BF16)
        for j, nm in enumerate(("idftC", "idftS")):
            pool.dma_start(out=cwb[:, j], in_=dram[nm][ko:ko + W, :]
                           .rearrange("(c p) e -> p c e", p=128))
            tcw = pool.sig_dma("dW")
        ps_o = ctx.phase_psum("l1a_pso", [128, T], F32)
        gtmp = ctx.phase_sbuf("l1a_gtmp", [128, 2, T], BF16)
        gr = b.s16[:, 0:T]
        gi = b.s16[:, T:2 * T]
        tz_ev = None
        for kci in range(nk):
            sre = xk[:, :, 128 * kci:128 * (kci + 1)]
            sim = xk[:, :, 128 * (nk + kci):128 * (nk + kci + 1)]
            qre = qh(kci).rearrange("p (j q) -> p j q", q=128)
            qim = qh(nk + kci).rearrange("p (j q) -> p j q", q=128)
            grv = gr.rearrange("p (j q) -> p j q", q=128)
            giv = gi.rearrange("p (j q) -> p j q", q=128)
            ta = gtmp[:, 0, :].rearrange("p (j q) -> p j q", q=128)
            tb_ = gtmp[:, 1, :].rearrange("p (j q) -> p j q", q=128)
            dve.wait(tz_ev)
            dve.tensor_tensor(ta, sre, qre, ALU.mult)
            dve.tensor_tensor(tb_, sim, qim, ALU.mult)
            dve.tensor_tensor(grv, ta, tb_, ALU.add)
            dve.tensor_tensor(ta, sim, qre, ALU.mult)
            dve.tensor_tensor(tb_, sre, qim, ALU.mult)
            dve.tensor_tensor(giv, ta, tb_, ALU.subtract)
            tg_ = dve.sig()
            for m in range(DCH):
                for pl, g_ in ((0, gr), (1, gi)):
                    for tg2 in range(T // 512):
                        pe.wait(tcw)
                        pe.wait(tg_)
                        if pl == 0 and tg2 == 0:
                            pe.wait(tz_ev)
                        pe.matmul(ps_o[:, 512 * tg2:512 * (tg2 + 1)],
                                  cwb[:, pl, kci, 128 * m:128 * (m + 1)],
                                  g_[:, 512 * tg2:512 * (tg2 + 1)],
                                  start=(pl == 0), stop=(pl == 1))
                tmm = pe.sig()
                dve.wait(tmm)
                if zfirst[0]:
                    dve.tensor_copy(b.zbuf[:, m, :], ps_o[:])
                else:
                    dve.tensor_tensor(b.zbuf[:, m, :], b.zbuf[:, m, :], ps_o[:], ALU.add)
                tz_ev = dve.sig()
                if part_i == len(L1A_PARTS) - 1 and kci == nk - 1:
                    ztoks.append(tz_ev)
            zfirst[0] = False
        ctx.barrier()
        ctx.pop_phase()
        ctx.pop_phase()
    ctx.dump("l1a_z", b.zbuf[:], ztoks)
    return ztoks


def emit_launch(ctx, dram, launch_blocks):
    b = alloc_persistent(ctx, launch_blocks)
    ctoks = emit_load_const(ctx, dram, b)
    for e in ENGS:
        ctx.eng[e].wait_all(ctoks)
    if 2 in launch_blocks:
        emit_l1a_fold(ctx, dram, b)
        ctx.barrier()
        ctx.end_phase()
    for seq in range(NSEQ):
        xtok = emit_load_x(ctx, dram, b, seq)
        ctx.barrier()
        ln_toks = emit_preln(ctx, b, [xtok])
        ctx.barrier()
        ctx.end_phase()
        for i, blk in enumerate(launch_blocks):
            if blk == 0:
                ztoks = emit_l0a(ctx, dram, b, ln_toks)
            elif blk == 1:
                ztoks = emit_l0f(ctx, dram, b, ln_toks)
            elif blk == 2:
                ztoks = emit_l1a(ctx, dram, b, ln_toks, seq)
            elif blk == 3:
                wt_toks = emit_l1f_weights(ctx, dram, b)
                ztoks = emit_l1f(ctx, b, ln_toks, wt_toks)
            else:
                raise NotImplementedError(f"block {blk}")
            ctx.barrier()
            ctx.end_phase()
            if i < len(launch_blocks) - 1:
                ln_toks = emit_transition(ctx, b, ztoks)
            else:
                xdeps = emit_postln_residual(ctx, b, ztoks)
            ctx.barrier()
            ctx.end_phase()
        st = emit_store_x(ctx, dram, b, seq, xdeps)
        for e in ENGS:
            ctx.eng[e].wait(st)
    ctx.sp.wait(Tok("dO", ctx.dma_counts["dO"], {}))


def build_launch(g, launch_blocks):
    nc = bass.Bass("TRN2", target_bir_lowering=False)
    dram = declare_dram(nc, g, launch_blocks)
    stack = ExitStack()
    with stack:
        ctx = Ctx(nc, stack)
        emit_launch(ctx, dram, launch_blocks)
        block = stack.enter_context(nc.Block())
        ctx.replay(block)
    return nc


LAUNCH_PLAN = [[0, 1, 2, 3]]


def kernel(**inputs):
    g, shards = stage_inputs(inputs)
    cur = shards
    for launch_blocks in LAUNCH_PLAN:
        nc = build_launch(g, launch_blocks)
        in_maps = []
        need = {"xin"}
        need.update(("ident", "ones128", "onesrow"))
        for blk in launch_blocks:
            need.update(BLOCK_INPUTS[blk])
        for c in range(NCORES):
            m = {k: g[k] for k in need if k != "xin"}
            m["xin"] = cur[c]
            in_maps.append(m)
        res = run_bass_kernel_spmd(nc, in_maps, core_ids=list(range(NCORES)))
        cur = [r["xout"] for r in res.results]
    out = np.empty((16, T, D), np.float32)
    for c in range(NCORES):
        out[2 * c:2 * c + 2] = cur[c].transpose(0, 2, 1)
    return out


# version 41
# speedup vs baseline: 1.1388x; 1.0344x over previous
import numpy as np
from contextlib import ExitStack
import concourse.bass as bass
import concourse.mybir as mybir
from concourse.bass_utils import run_bass_kernel_spmd

F32 = mybir.dt.float32
BF16 = mybir.dt.bfloat16
I32 = mybir.dt.int32
AF = mybir.ActivationFunctionType
ALU = mybir.AluOpType
AX = mybir.AxisListType

T = 2048
D = 1024
NCORES = 8
NSEQ = 2
DCH = D // 128
NT = T // 128
KBINS = 513
KPAD = 640
KCH = KPAD // 128
LN_EPS = 1e-5

ENGS = ("pe", "dve", "act", "pool", "sp")



class Tok:
    __slots__ = ("sem", "val", "snap")

    def __init__(self, sem, val, snap):
        self.sem, self.val, self.snap = sem, val, snap


_NO_AUTO = {"dma_start", "indirect_dma_start", "wait_ge", "nop", "sem_inc",
            "print", "dma_start_transpose"}


class EngCtx:

    _FIELDS = ("ctx", "name", "sem_name", "count", "clock", "recs", "auto",
               "self_waited", "uninced")

    def __init__(self, ctx, name, auto):
        self.ctx, self.name = ctx, name
        self.sem_name = "s_" + name
        self.count = 0
        self.clock = {}
        self.recs = []
        self.auto = auto
        self.self_waited = 0
        self.uninced = False

    def _rec(self, fn, auto_inced=False):
        entry = [fn, [], auto_inced]
        self.recs.append(entry)
        return entry

    def __getattr__(self, attr):
        if attr.startswith("_") or attr in EngCtx._FIELDS:
            raise AttributeError(attr)

        def wrap(*a, **k):
            if self.auto and attr not in _NO_AUTO and self.count > self.self_waited:
                sem, val = self.sem_name, self.count
                self._rec(lambda eng: eng.wait_ge(self.ctx.sems[sem], val))
                self.self_waited = val
            auto_inc = self.auto and attr not in _NO_AUTO
            e = self._rec(lambda eng: getattr(eng, attr)(*a, **k), auto_inc)
            if auto_inc:
                self.count += 1
                e[1].append((self.sem_name, 1))
                self.clock[self.sem_name] = self.count
            elif attr not in _NO_AUTO:
                self.uninced = True
            return None
        return wrap

    def wait(self, tok):
        if tok is None:
            return
        if tok.sem == self.sem_name:
            if tok.val <= self.self_waited:
                return
            sem, val = tok.sem, tok.val
            self._rec(lambda eng: eng.wait_ge(self.ctx.sems[sem], val))
            self.self_waited = val
            return
        if self.clock.get(tok.sem, 0) >= tok.val:
            return
        sem, val = tok.sem, tok.val
        self._rec(lambda eng: eng.wait_ge(self.ctx.sems[sem], val))
        for s, v in tok.snap.items():
            if s == self.sem_name:
                if v > self.self_waited:
                    self.self_waited = v
                continue
            if self.clock.get(s, 0) < v:
                self.clock[s] = v
        if self.clock.get(tok.sem, 0) < tok.val:
            self.clock[tok.sem] = tok.val

    def wait_all(self, toks):
        for t in toks:
            self.wait(t)

    def sig(self):
        assert self.recs, "sig() with no recorded instruction"
        if self.recs[-1][2] or any(s == self.sem_name for s, _ in self.recs[-1][1]):
            return Tok(self.sem_name, self.count, dict(self.clock))
        self.count += 1
        self.recs[-1][1].append((self.sem_name, 1))
        self.clock[self.sem_name] = self.count
        self.uninced = False
        return Tok(self.sem_name, self.count, dict(self.clock))

    def sig_dma(self, stream):
        c = self.ctx.dma_counts[stream] = self.ctx.dma_counts.get(stream, 0) + 16
        self.recs[-1][1].append((stream, 16))
        snap = dict(self.clock)
        snap[stream] = c
        return Tok(stream, c, snap)


DEBUG = False


class Ctx:
    def __init__(self, nc, stack):
        self.dbg = []
        self.dbg_names = {}
        self.nc = nc
        self.stack = stack
        self.sems = {}
        self.dma_counts = {}
        self.eng = {n: EngCtx(self, n, auto=(n in ("dve", "act", "pool")))
                    for n in ENGS}
        for n in ENGS:
            self.sems["s_" + n] = stack.enter_context(nc.semaphore("s_" + n))
        for s in ("dW", "dX", "dO", "dS"):
            self.sems[s] = stack.enter_context(nc.semaphore(s))
        self.pe = self.eng["pe"]
        self.dve = self.eng["dve"]
        self.act = self.eng["act"]
        self.pool = self.eng["pool"]
        self.sp = self.eng["sp"]
        self._sbuf_guards = []
        self._psum_guards = []
        self._tensors = {}
        self._phase_guards = []
        self._phase_ctr = 0
        self._phase_stack = []

    def sbuf(self, name, shape, dtype, side="left"):
        if name in self._tensors:
            return self._tensors[name]
        g = self.nc.sbuf_tensor(name, shape, dtype)
        h = g.__enter__()
        self._sbuf_guards.append(g)
        self._tensors[name] = h
        return h

    def psum(self, name, shape, dtype=F32):
        if name in self._tensors:
            return self._tensors[name]
        g = self.nc.psum_tensor(name, shape, dtype)
        h = g.__enter__()
        self._psum_guards.append(g)
        self._tensors[name] = h
        return h

    def barrier(self):
        toks = []
        for n in ENGS:
            e = self.eng[n]
            if e.uninced:
                toks.append(e.sig())
            elif e.count > 0:
                toks.append(Tok(e.sem_name, e.count, dict(e.clock)))
        for s, c in self.dma_counts.items():
            toks.append(Tok(s, c, {s: c}))
        for n in ENGS:
            self.eng[n].wait_all(toks)

    def phase_psum(self, name, shape, dtype=F32):
        self._phase_ctr += 1
        g = self.nc.psum_tensor(f"{name}_{self._phase_ctr}", shape, dtype)
        h = g.__enter__()
        self._phase_guards.append(g)
        return h

    def phase_sbuf(self, name, shape, dtype):
        self._phase_ctr += 1
        g = self.nc.sbuf_tensor(f"{name}_{self._phase_ctr}", shape, dtype)
        h = g.__enter__()
        self._phase_guards.append(g)
        return h

    def push_phase(self):
        self._phase_stack.append(len(self._phase_guards))

    def pop_phase(self):
        k = self._phase_stack.pop() if self._phase_stack else 0
        for g in reversed(self._phase_guards[k:]):
            g.__exit__(None, None, None)
        del self._phase_guards[k:]

    def end_phase(self):
        assert not self._phase_stack, "unbalanced push_phase"
        for g in reversed(self._phase_guards):
            g.__exit__(None, None, None)
        self._phase_guards = []

    def dump(self, name, ap, deps=()):
        if not DEBUG or name in self.dbg_names:
            return
        self.dbg_names[name] = True
        shape = list(ap.shape)
        dt = ap.dtype
        dst = self.nc.dram_tensor("dbg_" + name, shape, dt, kind="ExternalOutput").ap()
        self.sp.wait_all(list(deps))
        self.sp.dma_start(out=dst, in_=ap)
        self.sp.sig_dma("dO")

    def replay(self, block):
        raws = {"pe": block.tensor, "dve": block.vector, "act": block.scalar,
                "pool": block.gpsimd, "sp": block.sync}

        def make_body(name):
            def body(eng):
                for fn, incs, _auto in self.eng[name].recs:
                    inst = fn(eng)
                    for (sem, v) in incs:
                        assert inst is not None, f"then_inc on None inst ({name})"
                        inst.then_inc(self.sems[sem], v)
            return body
        for n in ENGS:
            if self.eng[n].recs:
                raws[n](make_body(n))




def host_constants():
    c = {}
    e = np.arange(D)[:, None]
    k = np.arange(KBINS)[None, :]
    ang = 2.0 * np.pi * ((e * k) % D).astype(np.float64) / D
    c["dftC"] = np.cos(ang).astype(np.float32)
    c["dftS"] = (-np.sin(ang)).astype(np.float32)
    w = np.full(KBINS, 2.0)
    w[0] = 1.0
    w[512] = 1.0
    ki = np.arange(KBINS)[:, None]
    ee = np.arange(D)[None, :]
    ang2 = 2.0 * np.pi * ((ki * ee) % D).astype(np.float64) / D
    cw = (w[:, None] * np.cos(ang2) / D).astype(np.float32)
    sw = (-w[:, None] * np.sin(ang2) / D).astype(np.float32)
    c["idftC"] = np.zeros((KPAD, D), np.float32)
    c["idftC"][:KBINS] = cw
    c["idftS"] = np.zeros((KPAD, D), np.float32)
    c["idftS"][:KBINS] = sw
    c["ident"] = np.eye(128, dtype=np.float32)
    tri = (np.arange(128)[:, None] <= np.arange(128)[None, :]).astype(np.float32)
    c["trimask"] = tri
    c["utri"] = tri.copy()
    c["ones128"] = np.ones((128, 128), np.float32)
    c["onesrow"] = np.ones((1, 128), np.float32)
    c["shamt"] = np.tile(np.array([0, 2, 4, 6], np.int32), 32)[:, None].copy()
    c["mask3"] = np.full((128, 1), 3, np.int32)
    bandmask = np.zeros((128, 16), np.float32)
    for h in range(16):
        bandmask[8 * h:8 * h + 6, h] = 1.0
    c["bandmask"] = bandmask
    tri16 = np.tile(tri, (1, 16))
    c["trimask16"] = np.ascontiguousarray(tri16)
    half0 = np.zeros((1, 128), np.float32)
    half0[0, :64] = 1.0
    half1 = np.zeros((1, 128), np.float32)
    half1[0, 64:] = 1.0
    c["half0"] = half0
    c["half1"] = half1
    c["ones16"] = np.ones((128, 16), np.float32)
    return c


def stage_inputs(inputs):
    g = {}
    g["l0a_wq"] = np.ascontiguousarray(
        np.concatenate([inputs["L0_attn_W1_q"], inputs["L0_attn_W2_q"]], 0).T)
    g["l0a_wk"] = np.ascontiguousarray(
        np.concatenate([inputs["L0_attn_W1_k"], inputs["L0_attn_W2_k"]], 0).T)
    g["l0a_wv"] = np.ascontiguousarray(np.asarray(inputs["L0_attn_Wv"]).T)
    g["l0a_bv"] = np.asarray(inputs["L0_attn_bv"]).reshape(1, D)
    g["l0a_wo"] = np.ascontiguousarray(np.asarray(inputs["L0_attn_Wo"]).T)
    g["l0a_bo"] = np.asarray(inputs["L0_attn_bo"]).reshape(1, D)
    g["l0f_w1"] = np.ascontiguousarray(np.concatenate(
        [np.asarray(inputs["L0_ffn_W1"]), np.asarray(inputs["L0_ffn_Ws1"])[None]], 0))
    g["l0f_w2"] = np.ascontiguousarray(np.concatenate(
        [np.asarray(inputs["L0_ffn_W2"]), np.asarray(inputs["L0_ffn_Ws2"])[None]], 0))
    g["l0f_wr"] = np.ascontiguousarray(np.asarray(inputs["L0_ffn_Wr"]))
    g["l1a_wq"] = np.ascontiguousarray(np.asarray(inputs["L1_attn_Wq"]))
    g["l1a_wk"] = np.ascontiguousarray(np.asarray(inputs["L1_attn_Wk"]))
    g["l1a_wv"] = np.ascontiguousarray(np.asarray(inputs["L1_attn_Wv"]))
    g["l1a_bq"] = np.ascontiguousarray(np.asarray(inputs["L1_attn_bq"]).reshape(DCH, 128).T)
    g["l1a_bk"] = np.ascontiguousarray(np.asarray(inputs["L1_attn_bk"]).reshape(DCH, 128).T)
    g["l1a_bv"] = np.ascontiguousarray(np.asarray(inputs["L1_attn_bv"]).reshape(DCH, 128).T)
    pk1 = np.asarray(inputs["L1_ffn_packed_weight"]).reshape(256, D)
    g["l1f_pk"] = np.ascontiguousarray(np.tile(pk1.T, (1, 4)))
    g["l1f_ws"] = np.asarray(inputs["L1_ffn_weight_scale"]).reshape(1, 1).astype(np.float32)
    g["l1f_bias"] = np.ascontiguousarray(
        np.asarray(inputs["L1_ffn_bias"]).reshape(DCH, 128).T)
    g.update(host_constants())
    x = np.asarray(inputs["L0_attn_x"])
    shards = [np.ascontiguousarray(x[2 * c:2 * c + 2].transpose(0, 2, 1))
              for c in range(NCORES)]
    return g, shards


BLOCK_INPUTS = {
    0: ("l0a_wq", "l0a_wk", "l0a_wv", "l0a_bv", "l0a_wo", "l0a_bo",
        "trimask16", "bandmask", "half0", "half1"),
    1: ("l0f_w1", "l0f_w2", "l0f_wr"),
    2: ("l1a_wq", "l1a_wk", "l1a_wv", "l1a_bq", "l1a_bk", "l1a_bv",
        "dftC", "dftS", "idftC", "idftS", "utri"),
    3: ("l1f_pk", "l1f_ws", "l1f_bias", "shamt", "mask3"),
}


def declare_dram(nc, g, launch_blocks):
    d = {}

    def inp(name):
        arr = g[name]
        dt = I32 if arr.dtype == np.int32 else F32
        d[name] = nc.dram_tensor(name, list(arr.shape), dt, kind="ExternalInput").ap()

    d["xin"] = nc.dram_tensor("xin", [NSEQ, D, T], F32, kind="ExternalInput").ap()
    d["xout"] = nc.dram_tensor("xout", [NSEQ, D, T], F32, kind="ExternalOutput").ap()
    for k in ("ident", "ones128", "onesrow"):
        inp(k)
    names = set()
    for blk in launch_blocks:
        names.update(BLOCK_INPUTS[blk])
    for k in sorted(names):
        inp(k)
    if 2 in launch_blocks:
        d["wtil"] = nc.dram_tensor("wtil", [6, D, KPAD], BF16, kind="Internal").ap()
        d["btil"] = nc.dram_tensor("btil", [1, 6 * KPAD], BF16, kind="Internal").ap()
    return d


class Bufs:
    pass


def alloc_persistent(ctx, launch_blocks=()):
    b = Bufs()
    b.x = ctx.sbuf("x_m", [128, DCH, T], F32)
    b.xln = ctx.sbuf("xln", [128, DCH, T + 2], BF16)
    b.ident = ctx.sbuf("c_ident", [128, 128], BF16)
    b.ones128 = ctx.sbuf("c_ones", [128, 128], BF16)
    b.onesrow = ctx.sbuf("c_onesrow", [1, 128], BF16)
    b.mrep = ctx.sbuf("mrep", [128, T], F32)
    b.rrep = ctx.sbuf("rrep", [128, T], F32)
    b.t1 = ctx.sbuf("t1rep", [128, T], F32)
    b.sq = ctx.sbuf("sqch", [128, 2, 512], BF16)
    b.s16 = ctx.sbuf("s16", [128, 2 * T], BF16)
    b.zbuf = ctx.sbuf("zbuf", [128, DCH, T], BF16)
    b.cpool = ctx.sbuf("cpool", [128, 8], F32)
    return b


CONST_VALS = (-1.0, LN_EPS, 1e-8)


def emit_load_const(ctx, dram, b):
    sp, pool = ctx.sp, ctx.pool
    toks = []
    for name, dst in (("ident", b.ident), ("ones128", b.ones128), ("onesrow", b.onesrow)):
        pool.dma_start(out=dst[:], in_=dram[name])
        toks.append(pool.sig_dma("dW"))
    for j, v in enumerate(CONST_VALS):
        pool.memset(b.cpool[:, j:j + 1], v)
        ctx.nc.const_aps.register(F32, v, b.cpool[:, j:j + 1])
    toks.append(pool.sig())
    return toks


def emit_load_x(ctx, dram, b, seq):
    sp = ctx.sp
    sp.dma_start(out=b.x[:], in_=dram["xin"][seq].rearrange("(c p) t -> p c t", p=128))
    return sp.sig_dma("dX")


def emit_store_x(ctx, dram, b, seq, deps):
    sp = ctx.sp
    sp.wait_all(deps)
    sp.dma_start(out=dram["xout"][seq].rearrange("(c p) t -> p c t", p=128), in_=b.x[:])
    return sp.sig_dma("dO")




def emit_stats_rows(ctx, b, src_get, psA, psB):
    pe, dve, act = ctx.pe, ctx.dve, ctx.act
    last = None
    sq_read = None
    for tg in range(T // 512):
        sl = slice(512 * tg, 512 * tg + 512)
        for c in range(DCH):
            src, rdy = src_get(c)
            pe.wait(rdy)
            if c == 0 and last is not None:
                pe.wait(last)
            pe.matmul(psA[:, 0:512], b.ones128[:], src[:, sl],
                      start=(c == 0), stop=(c == DCH - 1))
            act.wait(rdy)
            if c == 0 and last is not None:
                act.wait(last)
            slq = c % 2
            if sq_read is not None:
                act.wait(sq_read)
            act.activation(b.sq[:, slq, :], src[:, sl], AF.Square)
            sqt = act.sig()
            pe.wait(sqt)
            pe.matmul(psB[:, 0:512], b.ones128[:], b.sq[:, slq, :],
                      start=(c == 0), stop=(c == DCH - 1))
            if c >= 1:
                sq_read = pe.sig()
        tokp = pe.sig()
        dve.wait(tokp)
        dve.tensor_single_scalar(b.mrep[:, sl], psA[:, 0:512], 1.0 / D, ALU.mult)
        dve.tensor_single_scalar(b.t1[:, sl], psB[:, 0:512], 1.0 / D, ALU.mult)
        last = dve.sig()
    dve.tensor_tensor(b.rrep[:], b.mrep[:], b.mrep[:], ALU.mult)
    dve.tensor_tensor(b.t1[:], b.t1[:], b.rrep[:], ALU.subtract)
    tokv = dve.sig()
    act.wait(tokv)
    act.activation(b.t1[:], b.t1[:], AF.Ln, bias=LN_EPS)
    act.activation(b.rrep[:], b.t1[:], AF.Exp, scale=-0.5)
    last = act.sig()
    pe.wait(last)
    return last


def emit_preln(ctx, b, deps):
    pe, dve, act, pool = ctx.pe, ctx.dve, ctx.act, ctx.pool
    psA = ctx.phase_psum("lnA", [128, 512], F32)
    psB = ctx.phase_psum("lnB", [128, 512], F32)
    pool.wait_all(deps)
    pool.memset(b.xln[:, :, 0:2], 0.0)
    mz = pool.sig()
    ctoks = {}
    for c in range(DCH):
        act.wait_all(deps)
        act.copy(b.xln[:, c, 2:], b.x[:, c, :])
        ctoks[c] = act.sig()
    st = emit_stats_rows(ctx, b, lambda c: (b.xln[:, c, 2:], ctoks[c]), psA, psB)
    out = []
    for c in range(DCH):
        dve.wait(st)
        dve.wait_all(deps)
        dve.tensor_tensor(b.t1[:], b.x[:, c, :], b.mrep[:], ALU.subtract)
        dve.tensor_tensor(b.xln[:, c, 2:], b.t1[:], b.rrep[:], ALU.mult)
        out.append(dve.sig())
    out.append(mz)
    ctx.dump("xln", b.xln[:], out)
    ctx.dump("mrep", b.mrep[:], out)
    ctx.dump("rrep", b.rrep[:], out)
    return out


def _z_src_get(b, z_toks):
    if not isinstance(z_toks, (list, tuple)):
        z_toks = [z_toks]
    if len(z_toks) == DCH:
        return lambda c: (b.zbuf[:, c, :], z_toks[c])
    if len(z_toks) == 4 * DCH:
        return lambda c: (b.zbuf[:, c, :], z_toks[4 * c + 3])
    return lambda c: (b.zbuf[:, c, :], z_toks[-1])


def emit_transition(ctx, b, z_toks):
    pe, dve, act, pool = ctx.pe, ctx.dve, ctx.act, ctx.pool
    psA = ctx.phase_psum("trA", [128, 512], F32)
    psB = ctx.phase_psum("trB", [128, 512], F32)
    psC = ctx.phase_psum("trC", [128, 512], F32)
    psD = ctx.phase_psum("trD", [128, 512], F32)
    st = emit_stats_rows(ctx, b, _z_src_get(b, z_toks), psA, psB)
    pool.memset(b.xln[:, :, 0:2], 0.0)
    mz = pool.sig()
    ctoks = {}
    for c in range(DCH):
        dve.wait(st)
        dve.tensor_tensor(b.t1[:], b.zbuf[:, c, :], b.mrep[:], ALU.subtract)
        dve.tensor_tensor(b.t1[:], b.t1[:], b.rrep[:], ALU.mult)
        dve.tensor_tensor(b.x[:, c, :], b.x[:, c, :], b.t1[:], ALU.add)
        xt = dve.sig()
        act.wait(xt)
        act.copy(b.xln[:, c, 2:], b.x[:, c, :])
        ctoks[c] = act.sig()
    st2 = emit_stats_rows(ctx, b, lambda c: (b.xln[:, c, 2:], ctoks[c]), psC, psD)
    out = []
    for c in range(DCH):
        dve.wait(st2)
        dve.tensor_tensor(b.t1[:], b.x[:, c, :], b.mrep[:], ALU.subtract)
        dve.tensor_tensor(b.xln[:, c, 2:], b.t1[:], b.rrep[:], ALU.mult)
        out.append(dve.sig())
    out.append(mz)
    return out


def emit_postln_residual(ctx, b, z_toks):
    pe, dve, act, pool = ctx.pe, ctx.dve, ctx.act, ctx.pool
    psA = ctx.phase_psum("plA", [128, 512], F32)
    psB = ctx.phase_psum("plB", [128, 512], F32)
    st = emit_stats_rows(ctx, b, _z_src_get(b, z_toks), psA, psB)
    ctx.dump("pl_mrep", b.mrep[:], [st])
    ctx.dump("pl_rrep", b.rrep[:], [st])
    out = []
    for c in range(DCH):
        dve.wait(st)
        dve.tensor_tensor(b.t1[:], b.zbuf[:, c, :], b.mrep[:], ALU.subtract)
        dve.tensor_tensor(b.t1[:], b.t1[:], b.rrep[:], ALU.mult)
        dve.tensor_tensor(b.x[:, c, :], b.x[:, c, :], b.t1[:], ALU.add)
        out.append(dve.sig())
    ctx.dump("pl_x", b.x[:], out)
    return out




def emit_l1f_weights(ctx, dram, b):
    dve, act, sp = ctx.dve, ctx.act, ctx.sp
    b.l1f_wt = ctx.phase_sbuf("l1f_wt", [128, DCH, D], BF16)
    pk_s = ctx.phase_sbuf("l1f_pks", [128, 2, D], I32)
    sh_s = ctx.phase_sbuf("l1f_sh", [128, 1], I32)
    m3_s = ctx.phase_sbuf("l1f_m3", [128, 1], I32)
    b.l1f_ws = ctx.phase_sbuf("l1f_wss", [1, 1], F32)
    b.l1f_bias = ctx.phase_sbuf("l1f_bs", [128, DCH], F32)
    sp.dma_start(out=sh_s[:], in_=dram["shamt"])
    sp.sig_dma("dW")
    sp.dma_start(out=m3_s[:], in_=dram["mask3"])
    sp.sig_dma("dW")
    sp.dma_start(out=b.l1f_ws[:], in_=dram["l1f_ws"])
    sp.sig_dma("dW")
    sp.dma_start(out=b.l1f_bias[:], in_=dram["l1f_bias"])
    tl = sp.sig_dma("dW")
    pkv = dram["l1f_pk"].rearrange("(c p) o -> c p o", p=128)
    toks = []
    prev = {}
    for c in range(DCH):
        sl = c % 2
        if prev.get(sl) is not None:
            sp.wait(prev[sl])
        sp.dma_start(out=pk_s[:, sl, :], in_=pkv[c])
        tp = sp.sig_dma("dW")
        dve.wait(tp)
        dve.wait(tl)
        for s4 in range(1, 4):
            dve.tensor_single_scalar(pk_s[:, sl, 256 * s4:256 * (s4 + 1)],
                                     pk_s[:, sl, 256 * s4:256 * (s4 + 1)],
                                     2 * s4, ALU.logical_shift_right)
            dve.sig()
        dve.tensor_single_scalar(pk_s[:, sl, :], pk_s[:, sl, :], m3_s[:, 0:1],
                                 ALU.bitwise_and)
        tc = dve.sig()
        act.wait(tc)
        act.activation(b.l1f_wt[:, c, :], pk_s[:, sl, :], AF.Identity, bias=-1.0)
        ta = act.sig()
        prev[sl] = ta
        toks.append(ta)
    return toks


def emit_l1f(ctx, b, xln_toks, wt_toks):
    pe, dve, act, pool, sp = ctx.pe, ctx.dve, ctx.act, ctx.pool, ctx.sp
    xt = b.xln
    act.wait_all(xln_toks)
    act.activation(b.t1[:], xt[:, 0, 2:], AF.Abs)
    t_a = act.sig()
    dve.wait(t_a)
    for c in range(1, DCH):
        act.wait_all(xln_toks)
        act.activation(b.rrep[:], xt[:, c, 2:], AF.Abs)
        t_a = act.sig()
        dve.wait(t_a)
        dve.tensor_max(b.t1[:], b.t1[:], b.rrep[:])
        t_d = dve.sig()
        act.wait(t_d)
    fold_bf = b.s16[:, 0:T]
    dve.tensor_copy(fold_bf, b.t1[:])
    tfold = dve.sig()
    ps_t = ctx.phase_psum("l1f_pst", [128, 128], BF16)
    cols = ctx.phase_sbuf("l1f_cols", [128, NT], F32)
    colsb = ctx.phase_sbuf("l1f_colsb", [128, NT], BF16)
    rows = ctx.phase_sbuf("l1f_rows", [NT, 128], F32)
    tcol = None
    for j in range(NT):
        pe.wait(tfold)
        if tcol is not None:
            pe.wait(tcol)
        pe.transpose(ps_t[:], fold_bf[:, 128 * j:128 * j + 128], b.ident[:])
        tt = pe.sig()
        dve.wait(tt)
        dve.reduce_max(cols[:, j:j + 1], ps_t[:], AX.X)
        tcol = dve.sig()
    dve.tensor_copy(colsb[:], cols[:])
    tcb = dve.sig()
    ps_r = ctx.phase_psum("l1f_psr", [128, 128], BF16)
    pe.wait(tcb)
    pe.transpose(ps_r[0:NT, :], colsb[:], b.ident[:])
    tr = pe.sig()
    dve.wait(tr)
    dve.tensor_copy(rows[:], ps_r[0:NT, :])
    trows = dve.sig()
    amaxrow = b.t1[0:1, 0:T]
    asrow = b.rrep[0:1, 0:T]
    asrow_bf = b.s16[0:1, T:T + T]
    wsasrow_bf = b.s16[0:1, 0:T]
    sp.wait(trows)
    sp.dma_start(out=amaxrow, in_=rows[:])
    tdma = sp.sig_dma("dS")
    dve.wait(tdma)
    dve.tensor_single_scalar(amaxrow, amaxrow, 1e-5, ALU.max)
    dve.sig()
    dve.reciprocal(asrow, amaxrow)
    dve.sig()
    dve.tensor_single_scalar(asrow, asrow, 127.0, ALU.mult)
    dve.sig()
    dve.tensor_copy(asrow_bf, asrow)
    tas = dve.sig()
    dve.tensor_single_scalar(amaxrow, amaxrow, b.l1f_ws[0:1, 0:1], ALU.mult)
    dve.sig()
    dve.tensor_single_scalar(amaxrow, amaxrow, 1.0 / 127.0, ALU.mult)
    dve.sig()
    dve.tensor_copy(wsasrow_bf, amaxrow)
    twsas = dve.sig()
    ps_bc = ctx.phase_psum("l1f_psbc", [128, 512], F32)
    asrep = b.mrep
    ev = None
    for tg in range(T // 512):
        sl = slice(512 * tg, 512 * tg + 512)
        pe.wait(tas)
        if ev is not None:
            pe.wait(ev)
        pe.matmul(ps_bc[:], b.onesrow[:], asrow_bf[0:1, sl], start=True, stop=True)
        t_ = pe.sig()
        dve.wait(t_)
        dve.tensor_copy(asrep[:, sl], ps_bc[:])
        ev = dve.sig()
    ctx.dump("asrep", asrep[:], [ev])
    qi = ctx.phase_sbuf("l1f_qi", [128, T], I32)
    qtoks = []
    for c in range(DCH):
        dve.wait(ev)
        dve.wait_all(xln_toks)
        dve.tensor_tensor(b.t1[:], xt[:, c, 2:], asrep[:], ALU.mult)
        dve.sig()
        dve.tensor_scalar(b.t1[:], b.t1[:], -128.0, 127.0, ALU.max, ALU.min)
        dve.sig()
        dve.tensor_copy(qi[:], b.t1[:])
        tq = dve.sig()
        act.wait(tq)
        act.copy(xt[:, c, 2:], qi[:])
        qtoks.append(act.sig())
    ctx.dump("qbuf", xt[:], qtoks)
    wsasrep = b.s16[:, T:2 * T]
    ev2 = None
    for tg in range(T // 512):
        sl = slice(512 * tg, 512 * tg + 512)
        pe.wait(twsas)
        pe.wait(ev2 if ev2 is not None else ev)
        pe.matmul(ps_bc[:], b.onesrow[:], wsasrow_bf[0:1, sl], start=True, stop=True)
        t_ = pe.sig()
        dve.wait(t_)
        dve.tensor_copy(wsasrep[:, sl], ps_bc[:])
        ev2 = dve.sig()
    trep = ev2
    ps_z = [ctx.phase_psum("l1f_z0", [128, 512], F32), ctx.phase_psum("l1f_z1", [128, 512], F32)]
    ev_prev = [None, None]
    ztoks = []
    for m in range(DCH):
        for tg in range(T // 512):
            sl = slice(512 * tg, 512 * tg + 512)
            idx = (m * (T // 512) + tg) % 2
            pz = ps_z[idx]
            for c in range(DCH):
                pe.wait(qtoks[c])
                pe.wait(wt_toks[c])
                if c == 0 and ev_prev[idx] is not None:
                    pe.wait(ev_prev[idx])
                pe.matmul(pz[:], b.l1f_wt[:, c, 128 * m:128 * (m + 1)],
                          xt[:, c, 2 + 512 * tg:2 + 512 * (tg + 1)],
                          start=(c == 0), stop=(c == DCH - 1))
            tmm = pe.sig()
            dve.wait(tmm)
            dve.wait(trep)
            dve.tensor_tensor(b.t1[:, 0:512], pz[:], wsasrep[:, sl], ALU.mult)
            dve.sig()
            dve.tensor_scalar(b.zbuf[:, m, sl], b.t1[:, 0:512],
                              b.l1f_bias[:, m:m + 1], None, ALU.add)
            zt = dve.sig()
            ev_prev[idx] = zt
            ztoks.append(zt)
    ctx.dump("zbuf", b.zbuf[:], ztoks)
    ctx.dump("wsasrep", wsasrep[:], [trep])
    ctx.dump("l1fwt", b.l1f_wt[:], wt_toks)
    return ztoks




def emit_l0f(ctx, dram, b, xln_toks):
    pe, dve, act, pool, sp = ctx.pe, ctx.dve, ctx.act, ctx.pool, ctx.sp
    NE = 9
    b.mask_fm = ctx.phase_sbuf("l0f_maskfm", [8, T], BF16)
    ctx.push_phase()
    wr = ctx.phase_sbuf("l0f_wr", [128, DCH, 8], BF16)
    pool.dma_start(out=wr[:], in_=dram["l0f_wr"].rearrange("(c p) e -> p c e", p=128))
    twr = pool.sig_dma("dW")
    ps_r = ctx.phase_psum("l0f_psr", [8, 512], F32)
    rlog = ctx.phase_sbuf("l0f_rlog", [8, T], BF16)
    ev = None
    for tg in range(T // 512):
        sl2 = slice(2 + 512 * tg, 2 + 512 * (tg + 1))
        for c in range(DCH):
            pe.wait(twr)
            pe.wait_all(xln_toks)
            if c == 0 and ev is not None:
                pe.wait(ev)
            pe.matmul(ps_r[:], wr[:, c, :], b.xln[:, c, sl2],
                      start=(c == 0), stop=(c == DCH - 1))
        tm_ = pe.sig()
        act.wait(tm_)
        act.copy(rlog[:, 512 * tg:512 * (tg + 1)], ps_r[:])
        ev = act.sig()
    trlog = ev
    ps_g = ctx.phase_psum("l0f_psg", [128, 8], BF16)
    ps_gb = ctx.phase_psum("l0f_psgb", [8, 128], BF16)
    ltm = ctx.phase_sbuf("l0f_ltm", [128, 8], F32)
    g1 = ctx.phase_sbuf("l0f_g1", [128, 8], F32)
    g2 = ctx.phase_sbuf("l0f_g2", [128, 8], F32)
    gc = ctx.phase_sbuf("l0f_gc", [128, 12], F32)
    mask_tm = ctx.phase_sbuf("l0f_masktm", [128, 8], BF16)
    mask_fm = b.mask_fm
    tmask = None
    tp_g = None
    tp_gb = None
    for j in range(NT):
        pe.wait(trlog)
        if tp_g is not None:
            pe.wait(tp_g)
        pe.transpose(ps_g[:], rlog[:, 128 * j:128 * (j + 1)], b.ident[0:8, 0:8])
        tt = pe.sig()
        act.wait(tt)
        act.copy(ltm[:], ps_g[:])
        tp_g = act.sig()
        dve.wait(tp_g)
        dve.reduce_max(gc[:, 0:1], ltm[:], AX.X)
        dve.tensor_single_scalar(g1[:], ltm[:], gc[:, 0:1], ALU.is_ge)
        dve.reduce_sum(gc[:, 1:2], g1[:], AX.X)
        dve.scalar_tensor_tensor(g2[:], g1[:], -1e30, ltm[:], ALU.mult, ALU.add)
        dve.reduce_max(gc[:, 2:3], g2[:], AX.X)
        dve.tensor_single_scalar(gc[:, 3:4], gc[:, 1:2], 1.5, ALU.is_gt)
        dve.tensor_tensor(gc[:, 4:5], gc[:, 0:1], gc[:, 2:3], ALU.subtract)
        dve.tensor_tensor(gc[:, 5:6], gc[:, 3:4], gc[:, 4:5], ALU.mult)
        dve.tensor_tensor(gc[:, 6:7], gc[:, 2:3], gc[:, 5:6], ALU.add)
        dve.tensor_tensor(gc[:, 7:8], gc[:, 6:7], gc[:, 0:1], ALU.subtract)
        td = dve.sig()
        act.wait(td)
        act.activation(gc[:, 8:9], gc[:, 7:8], AF.Exp)
        te = act.sig()
        dve.wait(te)
        dve.tensor_single_scalar(gc[:, 9:10], gc[:, 8:9], 1.0, ALU.add)
        dve.tensor_single_scalar(gc[:, 10:11], gc[:, 9:10], 1.0, ALU.is_equal)
        dve.tensor_single_scalar(mask_tm[:], g1[:], gc[:, 10:11], ALU.mult)
        tmtm = dve.sig()
        pe.wait(tmtm)
        if tp_gb is not None:
            pe.wait(tp_gb)
        pe.transpose(ps_gb[0:8, :], mask_tm[:], b.ident[:])
        tb = pe.sig()
        act.wait(tb)
        act.copy(mask_fm[0:8, 128 * j:128 * (j + 1)], ps_gb[0:8, :])
        tp_gb = act.sig()
        tmask = tp_gb
    ctx.barrier()
    ctx.pop_phase()
    w1b = ctx.phase_sbuf("l0f_w1b", [128, 2, DCH, 512], BF16)
    w2b = ctx.phase_sbuf("l0f_w2b", [128, 2, 4, D], BF16)
    rowbuf = ctx.phase_sbuf("l0f_rowb", [1, T], BF16)
    maskrep = ctx.phase_sbuf("l0f_mrep", [128, T], BF16)
    hraw = b.s16[:, 0:T]
    hm = b.s16[:, T:2 * T]
    _h01 = b.t1[:].bitcast(BF16)
    _h23 = b.mrep[:].bitcast(BF16)
    hsq = lambda fc, sl=slice(0, T): (_h01 if fc < 2 else _h23)[
        :, T * (fc % 2) + (sl.start or 0):T * (fc % 2) + (sl.stop if sl.stop is not None else T)]
    ps_h = ctx.phase_psum("l0f_psh", [128, T], F32)
    ps_z = ctx.phase_psum("l0f_psz", [128, T], F32)
    w_used = [None, None]
    tmr_used = None
    tmr = None
    th_half = [None, None]
    tz_half = [None, None]
    hsq_read = [None] * 4
    ztoks = []
    for e in range(NE):
        slot = e % 2
        pool.wait(w_used[slot])
        pool.dma_start(out=w1b[:, slot], in_=dram["l0f_w1"][e].rearrange(
            "(c p) f -> p c f", p=128))
        pool.sig_dma("dW")
        pool.dma_start(out=w2b[:, slot], in_=dram["l0f_w2"][e].rearrange(
            "(c p) o -> p c o", p=128))
        tw = pool.sig_dma("dW")
        if e < 8:
            sp.wait(tmask)
            sp.wait(tmr)
            sp.dma_start(out=rowbuf[:], in_=mask_fm[e:e + 1, :])
            trow = sp.sig_dma("dS")
            pe.wait(trow)
            pe.wait(th_half[0])
            pe.wait(th_half[1])
            if tmr is not None:
                pe.wait(tmr)
            for tg in range(T // 512):
                sl = slice(512 * tg, 512 * (tg + 1))
                pe.matmul(ps_h[:, sl], b.onesrow[:], rowbuf[0:1, sl],
                          start=True, stop=True)
            tb_ = pe.sig()
            dve.wait(tb_)
            dve.wait(tmr_used)
            for tg in range(T // 512):
                sl = slice(512 * tg, 512 * (tg + 1))
                dve.tensor_copy(maskrep[:, sl], ps_h[:, sl])
            tmr = dve.sig()
        hsq_toks = []
        for fc in range(4):
            for half in range(2):
                for d in range(DCH):
                    pe.wait(tw)
                    pe.wait_all(xln_toks)
                    if d == 0:
                        pe.wait(th_half[half])
                        if fc == 0:
                            pe.wait(tmr)
                    lh = w1b[:, slot, d, 128 * fc:128 * (fc + 1)]
                    for tg in (2 * half, 2 * half + 1):
                        pe.matmul(ps_h[:, 512 * tg:512 * (tg + 1)], lh,
                                  b.xln[:, d, 2 + 512 * tg:2 + 512 * (tg + 1)],
                                  start=(d == 0), stop=(d == DCH - 1))
                tmm = pe.sig()
                hsl = slice(1024 * half, 1024 * (half + 1))
                dve.wait(tmm)
                dve.wait(hsq_read[fc])
                dve.tensor_single_scalar(hraw[:, hsl], ps_h[:, hsl], 0.0,
                                         ALU.max)
                th_half[half] = dve.sig()
                if e < 8:
                    dve.tensor_tensor(hm[:, hsl], hraw[:, hsl],
                                      maskrep[:, hsl], ALU.mult)
                    dve.tensor_tensor(hsq(fc, hsl), hm[:, hsl], hraw[:, hsl],
                                      ALU.mult)
                else:
                    dve.tensor_tensor(hsq(fc, hsl), hraw[:, hsl], hraw[:, hsl],
                                      ALU.mult)
            hsq_toks.append(dve.sig())
        tmr_used = hsq_toks[-1]
        if e == 8:
            pass
        for m in range(DCH):
            for half in range(2):
                for fc in range(4):
                    pe.wait(hsq_toks[fc])
                    if fc == 0:
                        pe.wait(tz_half[half])
                    lh = w2b[:, slot, fc, 128 * m:128 * (m + 1)]
                    for tg in (2 * half, 2 * half + 1):
                        pe.matmul(ps_z[:, 512 * tg:512 * (tg + 1)], lh,
                                  hsq(fc, slice(512 * tg, 512 * (tg + 1))),
                                  start=(fc == 0), stop=(fc == 3))
                tmm = pe.sig()
                hsl = slice(1024 * half, 1024 * (half + 1))
                dve.wait(tmm)
                if e == 0:
                    dve.tensor_copy(b.zbuf[:, m, hsl], ps_z[:, hsl])
                else:
                    dve.tensor_tensor(b.zbuf[:, m, hsl], b.zbuf[:, m, hsl],
                                      ps_z[:, hsl], ALU.add)
                tz_half[half] = dve.sig()
            if e == NE - 1:
                ztoks.append(tz_half[1])
        for fc in range(4):
            hsq_read[fc] = tz_half[1]
        w_used[slot] = tz_half[1]
    ctx.dump("moe_maskfm", mask_fm[:], [tmask])
    ctx.dump("moe_z", b.zbuf[:], ztoks)
    return ztoks



PAIRS = ((0, 1), (0, 2), (0, 3), (1, 2), (1, 3), (2, 3))
JSIGN = (1, -1, 1, 1, -1, 1)
ATT_SCALE = float(np.float32(6.0 ** (-0.5)))


def emit_l0a(ctx, dram, b, xln_toks):
    pe, dve, act, pool, sp = ctx.pe, ctx.dve, ctx.act, ctx.pool, ctx.sp
    bandm = ctx.phase_sbuf("l0a_band", [128, 16], F32)
    tri16 = ctx.phase_sbuf("l0a_tri16", [128, T], BF16)
    half01 = ctx.phase_sbuf("l0a_half", [1, 256], BF16)
    sp.dma_start(out=bandm[:], in_=dram["bandmask"])
    sp.sig_dma("dW")
    pool.dma_start(out=tri16[:], in_=dram["trimask16"])
    pool.sig_dma("dW")
    pool.dma_start(out=half01[0:1, 0:128], in_=dram["half0"])
    pool.sig_dma("dW")
    pool.dma_start(out=half01[0:1, 128:256], in_=dram["half1"])
    tw0 = pool.sig_dma("dW")
    vtm = b.zbuf[:].rearrange("p c t -> p (c t)").rearrange("p (j o) -> p j o", o=D)
    qspec = ctx.phase_sbuf("l0a_qspec", [128, T], BF16)
    kspec = ctx.phase_sbuf("l0a_kspec", [128, T], BF16)
    ctx.push_phase()
    wq = ctx.phase_sbuf("l0a_wqb", [128, 16, 128], BF16)
    wk = ctx.phase_sbuf("l0a_wkb", [128, DCH, 128], BF16)
    wv = ctx.phase_sbuf("l0a_wvb", [128, DCH, D], BF16)
    bvr = ctx.phase_sbuf("l0a_bvr", [1, D], BF16)
    pool.dma_start(out=wq[:], in_=dram["l0a_wq"].rearrange("(c p) o -> p c o", p=128))
    pool.sig_dma("dW")
    pool.dma_start(out=wk[:], in_=dram["l0a_wk"].rearrange("(c p) o -> p c o", p=128))
    pool.sig_dma("dW")
    pool.dma_start(out=wv[:], in_=dram["l0a_wv"].rearrange("(c p) o -> p c o", p=128))
    pool.sig_dma("dW")
    pool.dma_start(out=bvr[:], in_=dram["l0a_bv"])
    tw = pool.sig_dma("dW")
    q12f = ctx.phase_sbuf("l0a_q12f", [128, T], BF16)
    k12f = ctx.phase_sbuf("l0a_k12f", [128, T], BF16)
    _rbf = b.rrep[:].bitcast(BF16)
    ps_p = ctx.phase_psum("l0a_psp", [128, 2, 512], F32)
    ps_t = ctx.phase_psum("l0a_pst", [128, 128], BF16)
    ps_v = ctx.phase_psum("l0a_psv", [128, 2, 512], F32)
    tvev = {}
    for j in range(NT):
        tsl = slice(2 + 128 * j, 2 + 128 * (j + 1))
        for half in range(2):
            pv = ps_v[:, half, :]
            osl = slice(512 * half, 512 * (half + 1))
            for dc in range(DCH):
                pe.wait(tw)
                pe.wait_all(xln_toks)
                if dc == 0 and j >= 1:
                    pe.wait(tvev[(j - 1, half)])
                pe.matmul(pv[:], b.xln[:, dc, tsl], wv[:, dc, osl],
                          start=(dc == 0), stop=False)
            pe.matmul(pv[:], b.onesrow[:, 0:128], bvr[0:1, osl],
                      start=False, stop=True)
            tmm = pe.sig()
            act.wait(tmm)
            act.copy(vtm[:, j, osl], pv[:])
            tvev[(j, half)] = act.sig()
    tqf = None
    for tg in range(T // 512):
        sl = slice(512 * tg, 512 * (tg + 1))
        for src, wgt, nch, off in ((q12f, wq, 16, None), (k12f, wk, DCH, None)):
            pp = ps_p[:, 0 if src is q12f else 1, :]
            for c in range(nch):
                pe.wait(tw)
                pe.wait_all(xln_toks)
                if c == 0 and tqf is not None:
                    pe.wait(tqf)
                if src is q12f and c >= DCH:
                    rhs = b.xln[:, c - DCH, 1 + 512 * tg:1 + 512 * (tg + 1)]
                else:
                    rhs = b.xln[:, c, 2 + 512 * tg:2 + 512 * (tg + 1)]
                pe.matmul(pp[:], wgt[:, c, :], rhs, start=(c == 0), stop=(c == nch - 1))
            tmm = pe.sig()
            act.wait(tmm)
            act.copy(src[:, sl], pp[:])
            tqf = act.sig()
    q12tmF = _rbf[:, 0:T]
    k12tmF = _rbf[:, T:2 * T]
    _tbf = b.t1[:].bitcast(BF16)
    qlinF = _tbf[:, 0:T]
    klinF = _tbf[:, T:2 * T]
    pool.memset(qlinF, 0.0)
    pool.memset(klinF, 0.0)
    tlz = pool.sig()
    ttm = None
    for j in range(NT):
        for which, (f_, dstF) in enumerate(((q12f, q12tmF), (k12f, k12tmF))):
            pe.wait(tqf)
            if ttm is not None:
                pe.wait(ttm)
            pe.transpose(ps_t[:], f_[:, 128 * j:128 * (j + 1)], b.ident[:])
            tt = pe.sig()
            act.wait(tt)
            act.copy(dstF[:, 128 * j:128 * (j + 1)], ps_t[:])
            ttm = act.sig()
    qa = q12tmF.rearrange("p (j x) -> p j x", x=128)[:, :, 0:64]\
        .rearrange("p j (h i) -> p j h i", i=4)
    qb2 = q12tmF.rearrange("p (j x) -> p j x", x=128)[:, :, 64:128]\
        .rearrange("p j (h i) -> p j h i", i=4)
    ka = k12tmF.rearrange("p (j x) -> p j x", x=128)[:, :, 0:64]\
        .rearrange("p j (h i) -> p j h i", i=4)
    kb2 = k12tmF.rearrange("p (j x) -> p j x", x=128)[:, :, 64:128]\
        .rearrange("p j (h i) -> p j h i", i=4)
    lvq = qlinF.rearrange("p (j x) -> p j x", x=128)\
        .rearrange("p j (h c) -> p j h c", c=8)
    lvk = klinF.rearrange("p (j x) -> p j x", x=128)\
        .rearrange("p j (h c) -> p j h c", c=8)
    t_a = b.mrep[:, 0:256].rearrange("p (j h u) -> p j h u", h=16, u=1)
    t_b = b.mrep[:, 256:512].rearrange("p (j h u) -> p j h u", h=16, u=1)
    dve.wait(ttm)
    dve.wait(tlz)
    for which in (0, 1):
        a1, a2, lv = (qa, qb2, lvq) if which == 0 else (ka, kb2, lvk)
        for c6 in range(6):
            if which == 0:
                i_, j_ = PAIRS[c6]
                neg = False
            else:
                i_, j_ = PAIRS[5 - c6]
                neg = (JSIGN[c6] < 0)
            dve.tensor_tensor(t_a, a1[:, :, :, i_:i_ + 1], a2[:, :, :, j_:j_ + 1], ALU.mult)
            dve.tensor_tensor(t_b, a1[:, :, :, j_:j_ + 1], a2[:, :, :, i_:i_ + 1], ALU.mult)
            if neg:
                dve.tensor_tensor(lv[:, :, :, c6:c6 + 1], t_b, t_a, ALU.subtract)
            else:
                dve.tensor_tensor(lv[:, :, :, c6:c6 + 1], t_a, t_b, ALU.subtract)
    tpl = dve.sig()
    ncol = b.rrep
    for which in (0, 1):
        linF = qlinF if which == 0 else klinF
        lv = lvq if which == 0 else lvk
        dve.wait(tpl)
        dve.tensor_tensor(b.mrep[:], linF, linF, ALU.mult)
        dve.tensor_reduce(ncol[:, 0:256].rearrange("p (x u) -> p x u", u=1),
                          b.mrep[:].rearrange("p (x c) -> p x c", c=8), AX.X, ALU.add)
        tn_ = dve.sig()
        act.wait(tn_)
        act.activation(ncol[:, 256:512], ncol[:, 0:256], AF.Sqrt)
        tsq2 = act.sig()
        dve.wait(tsq2)
        dve.tensor_single_scalar(ncol[:, 256:512], ncol[:, 256:512], 1e-12, ALU.max)
        dve.reciprocal(ncol[:, 512:768], ncol[:, 256:512])
        dve.tensor_tensor(lv, lv, ncol[:, 512:768]
                          .rearrange("p (j h u) -> p j h u", h=16, u=1)
                          .broadcast_to([128, NT, 16, 8]), ALU.mult)
        tnorm = dve.sig()
    tspec = None
    for j in range(NT):
        for which in (0, 1):
            linF = qlinF if which == 0 else klinF
            pe.wait(tnorm)
            if tspec is not None:
                pe.wait(tspec)
            pe.transpose(ps_t[:], linF[:, 128 * j:128 * (j + 1)], b.ident[:])
            tt2 = pe.sig()
            act.wait(tt2)
            act.copy((qspec if which == 0 else kspec)[:, 128 * j:128 * (j + 1)], ps_t[:])
            tspec = act.sig()
    ctx.barrier()
    ctx.pop_phase()
    ctx.push_phase()
    psb = [b.s16[:, 0:T], b.s16[:, T:2 * T]]
    qb = ctx.phase_sbuf("l0a_qb", [128, T], BF16)
    brd_all = ctx.phase_sbuf("l0a_brdall", [128, 2, 512], BF16)
    zrow = ctx.phase_sbuf("l0a_zrow", [1, 512], BF16)
    pool.memset(zrow[:], 0.0)
    tzr = pool.sig()
    rrow = b.t1[0:1, 0:T]
    rrowb = ctx.phase_sbuf("l0a_rrowb", [1, T], BF16)
    ps_l = ctx.phase_psum("l0a_psl", [128, 2, 512], F32)
    ps_oa = ctx.phase_psum("l0a_psoa", [128, 4, 128], F32)
    ps_ob = ctx.phase_psum("l0a_psob", [128, 4, 128], F32)
    ps_s = ctx.phase_psum("l0a_pss", [128, T], F32)
    oview = [b.xln[:, p, 2:2 + T] for p in range(DCH)]

    def oview_a(q):
        return b.xln[:, 0:4, 2 + 128 * q:2 + 128 * (q + 1)]

    def oview_b(q):
        return b.xln[:, 4:8, 2 + 128 * q:2 + 128 * (q + 1)]
    tqb = None
    texp = [None, None]
    tpread = [None, None]
    pslw = [None, None]
    eread = [None, None]
    ebuf = b.mrep[:].bitcast(BF16)
    tsum_ev = None
    to_ev = None
    tbrd_ev = None
    ztoks_oready = []
    for q in range(NT):
        dve.wait(tspec)
        dve.wait(tqb)
        for h in range(16):
            dve.tensor_single_scalar(qb[:, 128 * h:128 * (h + 1)],
                                     qspec[:, 128 * q:128 * (q + 1)],
                                     bandm[:, h:h + 1], ALU.mult)
        tqbv = dve.sig()
        pe.wait(tzr)
        pe.wait(to_ev)
        pe.matmul(ps_oa[:].rearrange("p a b -> p (a b)"), b.onesrow[:, 0:128],
                  zrow[0:1, :], start=True, stop=True)
        pe.matmul(ps_ob[:].rearrange("p a b -> p (a b)"), b.onesrow[:, 0:128],
                  zrow[0:1, :], start=True, stop=True)

        def emit_sums_attnv(k):
            pb2 = k % 2
            pe.wait(texp[pb2])
            if k == 0:
                pe.wait(tsum_ev)
            for sg in range(T // 512):
                pe.matmul(ps_s[:, 512 * sg:512 * (sg + 1)], b.ones128[:],
                          psb[pb2][:, 512 * sg:512 * (sg + 1)],
                          start=(k == 0), stop=(k == q))
            for p in range(DCH):
                po = (ps_oa if p < 4 else ps_ob)[:, p % 4, :]
                pe.matmul(po[0:64, :], vtm[:, k, 128 * p:128 * p + 64],
                          psb[pb2][:, 256 * p:256 * p + 128],
                          start=False, stop=(k == q), tile_position=(0, 0))
                pe.matmul(po[64:128, :], vtm[:, k, 128 * p + 64:128 * (p + 1)],
                          psb[pb2][:, 256 * p + 128:256 * (p + 1)],
                          start=False, stop=(k == q), tile_position=(0, 64))
            tpread[pb2] = pe.sig()

        for k in range(q + 1):
            pb = k % 2
            for ch in range(4):
                sl_ = ch % 2
                pe.wait(tqbv)
                pe.wait(tspec)
                pe.wait(pslw[sl_])
                pe.matmul(ps_l[:, sl_, :],
                          kspec[:, 128 * k:128 * (k + 1)],
                          qb[:, 512 * ch:512 * (ch + 1)],
                          start=True, stop=True)
                tmm = pe.sig()
                dve.wait(tmm)
                dve.wait(eread[sl_])
                dve.tensor_copy(ebuf[:, 512 * sl_:512 * (sl_ + 1)], ps_l[:, sl_, :])
                pslw[sl_] = dve.sig()
                act.wait(pslw[sl_])
                act.wait(tpread[pb])
                act.activation(psb[pb][:, 512 * ch:512 * (ch + 1)],
                               ebuf[:, 512 * sl_:512 * (sl_ + 1)],
                               AF.Exp, scale=ATT_SCALE)
                eread[sl_] = act.sig()
            texp[pb] = eread[1]
            if k == q:
                dve.wait(texp[pb])
                dve.tensor_tensor(psb[pb][:], psb[pb][:], tri16[:], ALU.mult)
                texp[pb] = dve.sig()
            if q == 3 and k == 1:
                ctx.dump("l0a_P31", psb[pb][:, :], [texp[pb]])
            if k >= 1:
                emit_sums_attnv(k - 1)
        emit_sums_attnv(q)
        tka = tpread[q % 2]
        dve.wait(tka)
        dve.tensor_copy(rrow[0:1, :], ps_s[0:1, :])
        tsr = dve.sig()
        act.wait(tsr)
        act.activation(rrow[0:1, :], rrow[0:1, :], AF.Ln)
        act.activation(rrowb[0:1, :], rrow[0:1, :], AF.Exp, scale=-1.0)
        trr = act.sig()
        if q == 3:
            ctx.dump("l0a_r3", rrowb[0:1, :], [trr])
        tsum_ev = trr
        rview = rrowb[0:1, :].rearrange("u (a b q) -> u a b q", a=8, b=2)
        for half in range(2):
            pe.wait(trr)
            pe.wait(pslw[half])
            pe.wait(tbrd_ev)
            pe.matmul(ps_l[:, half, :], half01[0:1, 0:128],
                      rview[:, 4 * half:4 * (half + 1), 0, :], start=True, stop=True)
            pe.matmul(ps_l[:, half, :], half01[0:1, 128:256],
                      rview[:, 4 * half:4 * (half + 1), 1, :], start=False, stop=True)
        tbb = pe.sig()
        dve.wait(tbb)
        dve.wait(to_ev)
        dve.tensor_copy(brd_all[:], ps_l[:])
        tbrd_ev = dve.sig()
        pslw = [tbrd_ev, tbrd_ev]
        dve.wait(tbrd_ev)
        dve.wait(tka)
        dve.tensor_tensor(oview_a(q), ps_oa[:],
                          brd_all[:, 0, :].rearrange("p (a q) -> p a q", a=4), ALU.mult)
        dve.tensor_tensor(oview_b(q), ps_ob[:],
                          brd_all[:, 1, :].rearrange("p (a q) -> p a q", a=4), ALU.mult)
        to_ev = dve.sig()
        ztoks_oready.append(to_ev)
    ctx.dump("l0a_o", b.xln[:], ztoks_oready)
    ctx.barrier()
    ctx.pop_phase()
    ctx.push_phase()
    wo = ctx.phase_sbuf("l0a_wob", [128, DCH, D], BF16)
    bor = ctx.phase_sbuf("l0a_bor", [1, D], BF16)
    pool.dma_start(out=wo[:], in_=dram["l0a_wo"].rearrange("(c p) o -> p c o", p=128))
    pool.sig_dma("dW")
    pool.dma_start(out=bor[:], in_=dram["l0a_bo"])
    two = pool.sig_dma("dW")
    ones512 = ctx.phase_sbuf("l0a_ones512", [1, 512], BF16)
    pool.memset(ones512[:], 1.0)
    ton = pool.sig()
    ps_z = ctx.phase_psum("l0a_psz", [128, T], F32)
    ztoks = []
    tz_half = [None, None]
    for m in range(DCH):
        for half in range(2):
            for p in range(DCH):
                pe.wait(two)
                pe.wait_all(ztoks_oready)
                if p == 0:
                    pe.wait(tz_half[half])
                lh = wo[:, p, 128 * m:128 * (m + 1)]
                for tg in (2 * half, 2 * half + 1):
                    pe.matmul(ps_z[:, 512 * tg:512 * (tg + 1)], lh,
                              oview[p][:, 512 * tg:512 * (tg + 1)],
                              start=(p == 0), stop=False)
            for tg in (2 * half, 2 * half + 1):
                pe.wait(ton)
                pe.matmul(ps_z[:, 512 * tg:512 * (tg + 1)],
                          bor[0:1, 128 * m:128 * (m + 1)], ones512[0:1, :],
                          start=False, stop=True)
            tmm = pe.sig()
            hsl = slice(1024 * half, 1024 * (half + 1))
            dve.wait(tmm)
            dve.tensor_copy(b.zbuf[:, m, hsl], ps_z[:, hsl])
            tz_half[half] = dve.sig()
        ztoks.append(tz_half[1])
    ctx.barrier()
    ctx.pop_phase()
    ctx.dump("l0a_z", b.zbuf[:], ztoks)
    return ztoks



L1A_HALVES = ((0, 1, 2), (3, 4))


def emit_l1a_fold(ctx, dram, b):
    pe, dve, act, pool, sp = ctx.pe, ctx.dve, ctx.act, ctx.pool, ctx.sp
    ctx.push_phase()
    b.bias_tilde = ctx.phase_sbuf("l1a_btil", [1, 6 * KPAD], BF16)
    wxb = ctx.phase_sbuf("l1a_wxb", [128, DCH, D], BF16)
    cb = ctx.phase_sbuf("l1a_cb", [128, DCH, KBINS], BF16)
    bcol = ctx.phase_sbuf("l1a_bcol", [128, DCH], BF16)
    stg = ctx.phase_sbuf("l1a_stg", [128, KPAD], BF16)
    ps_f = ctx.phase_psum("l1a_psf", [128, KBINS], F32)
    ps_b = ctx.phase_psum("l1a_psb", [1, KBINS], F32)
    pool.memset(stg[:], 0.0)
    tz = pool.sig()
    tprev_dma = None
    tstg_free = tz
    for wi, (wname, bname, plane) in enumerate((
            ("l1a_wq", "l1a_bq", "dftC"), ("l1a_wq", "l1a_bq", "dftS"),
            ("l1a_wk", "l1a_bk", "dftC"), ("l1a_wk", "l1a_bk", "dftS"),
            ("l1a_wv", "l1a_bv", "dftC"), ("l1a_wv", "l1a_bv", "dftS"))):
        pool.wait(tprev_dma)
        pool.dma_start(out=wxb[:], in_=dram[wname].rearrange("(c p) d -> p c d", p=128))
        pool.sig_dma("dW")
        pool.dma_start(out=cb[:], in_=dram[plane].rearrange("(c p) k -> p c k", p=128))
        pool.sig_dma("dW")
        pool.dma_start(out=bcol[:], in_=dram[bname])
        tld = pool.sig_dma("dW")
        tev = None
        for dc in range(DCH):
            for ec in range(DCH):
                pe.wait(tld)
                if ec == 0 and tev is not None:
                    pe.wait(tev)
                pe.matmul(ps_f[:, 0:512], wxb[:, ec, 128 * dc:128 * (dc + 1)],
                          cb[:, ec, 0:512], start=(ec == 0), stop=(ec == DCH - 1))
                pe.matmul(ps_f[:, 512:513], wxb[:, ec, 128 * dc:128 * (dc + 1)],
                          cb[:, ec, 512:513], start=(ec == 0), stop=(ec == DCH - 1))
            tmm = pe.sig()
            act.wait(tmm)
            act.wait(tstg_free)
            act.copy(stg[:, 0:KBINS], ps_f[:])
            tev = act.sig()
            sp.wait(tev)
            sp.dma_start(out=dram["wtil"][wi, 128 * dc:128 * (dc + 1), :], in_=stg[:])
            tstg_free = sp.sig_dma("dS")
        tevb = None
        for ec in range(DCH):
            pe.wait(tld)
            pe.matmul(ps_b[:, 0:512], bcol[:, ec:ec + 1], cb[:, ec, 0:512],
                      start=(ec == 0), stop=(ec == DCH - 1))
            pe.matmul(ps_b[:, 512:513], bcol[:, ec:ec + 1], cb[:, ec, 512:513],
                      start=(ec == 0), stop=(ec == DCH - 1))
        tmm = pe.sig()
        dve.wait(tmm)
        dve.memset(b.bias_tilde[0:1, KPAD * wi:KPAD * (wi + 1)], 0.0)
        dve.sig()
        dve.tensor_copy(b.bias_tilde[0:1, KPAD * wi:KPAD * wi + KBINS], ps_b[:])
        tbt = dve.sig()
        pe.wait(tbt)
        tprev_dma = Tok(tld.sem, tld.val, tld.snap)
        tprev_dma = tev
    ctx.dump("l1a_btil", b.bias_tilde[:], [tbt])
    sp.wait(tbt)
    sp.dma_start(out=dram["btil"], in_=b.bias_tilde[:])
    tst = sp.sig_dma("dS")
    for e_ in ENGS:
        ctx.eng[e_].wait(tst)
    ctx.pop_phase()
    return None


def _complex_norm(ctx, b, re_src, im_src, re_dst, im_dst, width, deps):
    dve, act = ctx.dve, ctx.act
    n2 = b.t1[:, 0:width]
    t2 = b.mrep[:, 0:width]
    dve.wait_all(deps)
    dve.tensor_tensor(n2, re_src, re_src, ALU.mult)
    dve.tensor_tensor(t2, im_src, im_src, ALU.mult)
    dve.tensor_tensor(n2, n2, t2, ALU.add)
    tn = dve.sig()
    act.wait(tn)
    dve.tensor_single_scalar(n2, n2, 1e-30, ALU.max)
    tm_ = dve.sig()
    act.wait(tm_)
    act.activation(n2, n2, AF.Ln)
    act.activation(n2, n2, AF.Exp, scale=0.5)
    act.activation(n2, n2, AF.Ln, bias=1e-8)
    act.activation(t2, n2, AF.Exp, scale=-1.0)
    tc_ = act.sig()
    dve.wait(tc_)
    dve.tensor_tensor(re_dst, re_src, t2, ALU.mult)
    dve.sig()
    dve.tensor_tensor(im_dst, im_src, t2, ALU.mult)
    return dve.sig()


L1A_PARTS = ((0, 1), (2, 3), (4,))


def emit_l1a(ctx, dram, b, xln_toks, seq):
    pe, dve, act, pool, sp = ctx.pe, ctx.dve, ctx.act, ctx.pool, ctx.sp
    onesT = ctx.phase_sbuf("l1a_onesT", [1, 512], BF16)
    utri = ctx.phase_sbuf("l1a_utri", [128, 128], BF16)
    carry = ctx.phase_sbuf("l1a_carry", [128, 12], F32)
    pool.memset(onesT[:], 1.0)
    tone = pool.sig()
    pool.dma_start(out=utri[:], in_=dram["utri"])
    tutri = pool.sig_dma("dW")
    zfirst = [True]
    ztoks = []
    for part_i, kcs in enumerate(L1A_PARTS):
        nk = len(kcs)
        W = 128 * nk
        ko = 128 * kcs[0]
        ctx.push_phase()
        btlp = ctx.phase_sbuf("l1a_btlp", [1, 6 * W], BF16)
        sp.dma_start(
            out=btlp[:].rearrange("u (w k) -> u w k", k=W),
            in_=dram["btil"].rearrange("u (w k) -> u w k", k=KPAD)[:, :, ko:ko + W])
        tbtl = sp.sig_dma("dW")
        xk = ctx.phase_sbuf("l1a_xk", [128, NT, 2 * W], BF16)
        _qre = b.rrep[:].bitcast(BF16).rearrange("p (a t) -> p a t", t=T)[:, 0:nk, :]
        _qim = ctx.phase_sbuf("l1a_qhi", [128, nk, T], BF16)

        def qh(ci):
            return _qre[:, ci, :] if ci < nk else _qim[:, ci - nk, :]
        ctx.push_phase()
        wkb = ctx.phase_sbuf("l1a_wkb", [128, DCH, 4, W], BF16)
        for j, wi in enumerate((2, 3, 4, 5)):
            sp.dma_start(out=wkb[:, :, j, :], in_=dram["wtil"][wi, :, ko:ko + W]
                         .rearrange("(c p) k -> p c k", p=128))
            twk = sp.sig_dma("dW")

        dve.wait(tbtl)
        dve.engine_nop()
        tbc = tbtl
        ps4d = [ctx.phase_psum("l1a_ps4a", [128, 4 * W], F32),
                ctx.phase_psum("l1a_ps4b", [128, 4 * W], F32)]
        kvtd = [b.sq[:].rearrange("p a w -> p (a w)")[:, 0:4 * W]
                .rearrange("p (a w) -> p a w", a=4),
                b.t1[:].bitcast(BF16)[:, 1024:1024 + 4 * W]
                .rearrange("p (a w) -> p a w", a=4)]
        ps_free = [None, None]
        kvt_free = [None, None]
        tbind_prev = None
        for j in range(NT):
            sl2 = j % 2
            ps4 = ps4d[sl2]
            kvt = kvtd[sl2]
            tsl = slice(2 + 128 * j, 2 + 128 * (j + 1))
            for dc in range(DCH):
                pe.wait(twk)
                pe.wait_all(xln_toks)
                pe.wait(tone)
                pe.wait(tbc)
                if dc == 0:
                    pe.wait(ps_free[sl2])
                wr_ = wkb[:, dc, :, :].rearrange("p a w -> p (a w)")
                for sg in range(0, 4 * W, 512):
                    pe.matmul(ps4[:, sg:sg + 512], b.xln[:, dc, tsl],
                              wr_[:, sg:sg + 512], start=(dc == 0), stop=False)
            for sg in range(0, 4 * W, 512):
                pe.matmul(ps4[:, sg:sg + 512], onesT[0:1, 0:128],
                          btlp[0:1, 2 * W + sg:2 * W + sg + 512],
                          start=False, stop=True)
            tmm = pe.sig()
            act.wait(tmm)
            act.wait(kvt_free[sl2])
            act.copy(kvt[:], ps4[:])
            ps_free[sl2] = act.sig()
            kv4 = kvt[:].rearrange("p (a b) w -> p a b w", b=2)
            res_ = kv4[:, :, 0, :]
            ims_ = kv4[:, :, 1, :]
            n2 = b.t1[:, 0:2 * W].rearrange("p (a w) -> p a w", a=2)
            t2 = b.mrep[:, 0:2 * W].rearrange("p (a w) -> p a w", a=2)
            dve.wait(ps_free[sl2])
            dve.tensor_tensor(n2, res_, res_, ALU.mult)
            dve.tensor_tensor(t2, ims_, ims_, ALU.mult)
            dve.tensor_tensor(n2, n2, t2, ALU.add)
            dve.tensor_single_scalar(n2, n2, 1e-30, ALU.max)
            tn_ = dve.sig()
            act.wait(tn_)
            act.activation(n2, n2, AF.Ln)
            act.activation(n2, n2, AF.Exp, scale=0.5)
            act.activation(n2, n2, AF.Ln, bias=1e-8)
            act.activation(t2, n2, AF.Exp, scale=-1.0)
            tc_ = act.sig()
            dve.wait(tc_)
            dve.tensor_tensor(res_, res_, t2, ALU.mult)
            dve.tensor_tensor(ims_, ims_, t2, ALU.mult)
            dve.sig()
            ta = b.s16[:, 0:W]
            tb_ = b.s16[:, W:2 * W]
            dve.tensor_tensor(ta, kvt[:, 0, :], kvt[:, 2, :], ALU.mult)
            dve.tensor_tensor(tb_, kvt[:, 1, :], kvt[:, 3, :], ALU.mult)
            dve.tensor_tensor(xk[:, j, 0:W], ta, tb_, ALU.subtract)
            dve.tensor_tensor(ta, kvt[:, 0, :], kvt[:, 3, :], ALU.mult)
            dve.tensor_tensor(tb_, kvt[:, 1, :], kvt[:, 2, :], ALU.mult)
            dve.tensor_tensor(xk[:, j, W:2 * W], ta, tb_, ALU.add)
            tbind_prev = dve.sig()
            kvt_free[sl2] = tbind_prev
        ctx.barrier()
        ctx.pop_phase()
        ctx.push_phase()
        wqb = ctx.phase_sbuf("l1a_wqb", [128, 2, DCH, W], BF16)
        for j, wi in enumerate((0, 1)):
            sp.dma_start(out=wqb[:, j], in_=dram["wtil"][wi, :, ko:ko + W]
                         .rearrange("(c p) k -> p c k", p=128))
            twq = sp.sig_dma("dW")
        ps_q = ctx.phase_psum("l1a_psq", [128, T], F32)
        tq_free = None
        for ci in range(2 * nk):
            pl, kci = (0, ci) if ci < nk else (1, ci - nk)
            for dc in range(DCH):
                pe.wait(twq)
                pe.wait_all(xln_toks)
                if dc == 0:
                    pe.wait(tq_free)
                lh = wqb[:, pl, dc, 128 * kci:128 * (kci + 1)]
                for tg in range(T // 512):
                    pe.matmul(ps_q[:, 512 * tg:512 * (tg + 1)], lh,
                              b.xln[:, dc, 2 + 512 * tg:2 + 512 * (tg + 1)],
                              start=(dc == 0), stop=False)
            pe.wait(tbtl)
            seg = btlp[0:1, W * pl + 128 * kci:W * pl + 128 * (kci + 1)]
            for tg in range(T // 512):
                pe.matmul(ps_q[:, 512 * tg:512 * (tg + 1)], seg,
                          onesT[0:1, 0:512], start=False, stop=True)
            tmm = pe.sig()
            act.wait(tmm)
            act.copy(qh(ci), ps_q[:])
            tq_free = act.sig()
        tqn = []
        for kci in range(nk):
            tn = _complex_norm(ctx, b, qh(kci), qh(nk + kci),
                               qh(kci), qh(nk + kci), T, [tq_free])
            tqn.append(tn)
        ctx.barrier()
        ctx.pop_phase()
        ctx.push_phase()
        ps_c = [ctx.phase_psum(f"l1a_psc{i}", [128, 128], F32) for i in range(4)]
        ps_ts = ctx.phase_psum("l1a_psts", [128, 2 * nk, NT], F32)
        tsum = ctx.phase_sbuf("l1a_tsum", [128, 2 * nk, NT + 1], F32)
        onecol = ctx.phase_sbuf("l1a_onecol", [128, 1], BF16)
        pool.memset(onecol[:], 1.0)
        toc = pool.sig()
        pe.wait_all(tqn)
        pe.wait(toc)
        for ci in range(2 * nk):
            for j in range(NT):
                pe.matmul(ps_ts[:, ci, j:j + 1], xk[:, j, 128 * ci:128 * (ci + 1)],
                          onecol[:], start=True, stop=True)
        tts = pe.sig()
        tsum2 = ctx.phase_sbuf("l1a_tsum2", [128, 2 * nk, NT + 1], F32)
        dve.wait(tts)
        dve.memset(tsum[:, :, 0:1], 0.0)
        dve.memset(tsum2[:, :, 0:1], 0.0)
        dve.tensor_copy(tsum[:, :, 1:NT + 1], ps_ts[:, :, :])
        dve.sig()
        bufs = (tsum, tsum2)
        for si, sh in enumerate((1, 2, 4, 8)):
            a_, b2_ = bufs[si % 2], bufs[(si + 1) % 2]
            dve.tensor_copy(b2_[:, :, 1:sh + 1], a_[:, :, 1:sh + 1])
            dve.tensor_tensor(b2_[:, :, sh + 1:NT + 1], a_[:, :, sh + 1:NT + 1],
                              a_[:, :, 1:NT + 1 - sh], ALU.add)
            dve.sig()
        tsum = bufs[0]
        tpfx = dve.sig()
        tev4 = [None] * 4
        for ci in range(2 * nk):
            for j in range(NT):
                sl4 = (ci * NT + j) % 4
                pe.wait(tutri)
                pe.wait(tev4[sl4])
                pe.matmul(ps_c[sl4][:], xk[:, j, 128 * ci:128 * (ci + 1)], utri[:],
                          start=True, stop=True)
                tmm = pe.sig()
                dve.wait(tmm)
                dve.wait(tpfx)
                dve.tensor_single_scalar(xk[:, j, 128 * ci:128 * (ci + 1)],
                                         ps_c[sl4][:], tsum[:, ci, j:j + 1], ALU.add)
                tev4[sl4] = dve.sig()
        ctx.barrier()
        ctx.pop_phase()
        ctx.push_phase()
        cwb = ctx.phase_sbuf("l1a_cwb", [128, 2, nk, D], BF16)
        for j, nm in enumerate(("idftC", "idftS")):
            pool.dma_start(out=cwb[:, j], in_=dram[nm][ko:ko + W, :]
                           .rearrange("(c p) e -> p c e", p=128))
            tcw = pool.sig_dma("dW")
        ps_o = ctx.phase_psum("l1a_pso", [128, T], F32)
        gtmp = ctx.phase_sbuf("l1a_gtmp", [128, 2, T], BF16)
        gr = b.s16[:, 0:T]
        gi = b.s16[:, T:2 * T]
        tz_half = [None, None]
        for kci in range(nk):
            sre = xk[:, :, 128 * kci:128 * (kci + 1)]
            sim = xk[:, :, 128 * (nk + kci):128 * (nk + kci + 1)]
            qre = qh(kci).rearrange("p (j q) -> p j q", q=128)
            qim = qh(nk + kci).rearrange("p (j q) -> p j q", q=128)
            grv = gr.rearrange("p (j q) -> p j q", q=128)
            giv = gi.rearrange("p (j q) -> p j q", q=128)
            ta = gtmp[:, 0, :].rearrange("p (j q) -> p j q", q=128)
            tb_ = gtmp[:, 1, :].rearrange("p (j q) -> p j q", q=128)
            dve.wait(tz_half[1])
            dve.tensor_tensor(ta, sre, qre, ALU.mult)
            dve.tensor_tensor(tb_, sim, qim, ALU.mult)
            dve.tensor_tensor(grv, ta, tb_, ALU.add)
            dve.tensor_tensor(ta, sim, qre, ALU.mult)
            dve.tensor_tensor(tb_, sre, qim, ALU.mult)
            dve.tensor_tensor(giv, ta, tb_, ALU.subtract)
            tg_ = dve.sig()
            for m in range(DCH):
                for half in range(2):
                    for pl, g_ in ((0, gr), (1, gi)):
                        for tg2 in (2 * half, 2 * half + 1):
                            pe.wait(tcw)
                            pe.wait(tg_)
                            if pl == 0 and tg2 == 2 * half:
                                pe.wait(tz_half[half])
                            pe.matmul(ps_o[:, 512 * tg2:512 * (tg2 + 1)],
                                      cwb[:, pl, kci, 128 * m:128 * (m + 1)],
                                      g_[:, 512 * tg2:512 * (tg2 + 1)],
                                      start=(pl == 0), stop=(pl == 1))
                    tmm = pe.sig()
                    hsl = slice(1024 * half, 1024 * (half + 1))
                    dve.wait(tmm)
                    if zfirst[0]:
                        dve.tensor_copy(b.zbuf[:, m, hsl], ps_o[:, hsl])
                    else:
                        dve.tensor_tensor(b.zbuf[:, m, hsl], b.zbuf[:, m, hsl],
                                          ps_o[:, hsl], ALU.add)
                    tz_half[half] = dve.sig()
                if part_i == len(L1A_PARTS) - 1 and kci == nk - 1:
                    ztoks.append(tz_half[1])
            zfirst[0] = False
        ctx.barrier()
        ctx.pop_phase()
        ctx.pop_phase()
    ctx.dump("l1a_z", b.zbuf[:], ztoks)
    return ztoks


def emit_launch(ctx, dram, launch_blocks):
    b = alloc_persistent(ctx, launch_blocks)
    ctoks = emit_load_const(ctx, dram, b)
    for e in ENGS:
        ctx.eng[e].wait_all(ctoks)
    if 2 in launch_blocks:
        emit_l1a_fold(ctx, dram, b)
        ctx.barrier()
        ctx.end_phase()
    for seq in range(NSEQ):
        xtok = emit_load_x(ctx, dram, b, seq)
        ctx.barrier()
        ln_toks = emit_preln(ctx, b, [xtok])
        ctx.barrier()
        ctx.end_phase()
        for i, blk in enumerate(launch_blocks):
            if blk == 0:
                ztoks = emit_l0a(ctx, dram, b, ln_toks)
            elif blk == 1:
                ztoks = emit_l0f(ctx, dram, b, ln_toks)
            elif blk == 2:
                ztoks = emit_l1a(ctx, dram, b, ln_toks, seq)
            elif blk == 3:
                wt_toks = emit_l1f_weights(ctx, dram, b)
                ztoks = emit_l1f(ctx, b, ln_toks, wt_toks)
            else:
                raise NotImplementedError(f"block {blk}")
            ctx.barrier()
            ctx.end_phase()
            if i < len(launch_blocks) - 1:
                ln_toks = emit_transition(ctx, b, ztoks)
            else:
                xdeps = emit_postln_residual(ctx, b, ztoks)
            ctx.barrier()
            ctx.end_phase()
        st = emit_store_x(ctx, dram, b, seq, xdeps)
        for e in ENGS:
            ctx.eng[e].wait(st)
    ctx.sp.wait(Tok("dO", ctx.dma_counts["dO"], {}))


def build_launch(g, launch_blocks):
    nc = bass.Bass("TRN2", target_bir_lowering=False)
    dram = declare_dram(nc, g, launch_blocks)
    stack = ExitStack()
    with stack:
        ctx = Ctx(nc, stack)
        emit_launch(ctx, dram, launch_blocks)
        block = stack.enter_context(nc.Block())
        ctx.replay(block)
    return nc


LAUNCH_PLAN = [[0, 1, 2, 3]]


def kernel(**inputs):
    g, shards = stage_inputs(inputs)
    cur = shards
    for launch_blocks in LAUNCH_PLAN:
        nc = build_launch(g, launch_blocks)
        in_maps = []
        need = {"xin"}
        need.update(("ident", "ones128", "onesrow"))
        for blk in launch_blocks:
            need.update(BLOCK_INPUTS[blk])
        for c in range(NCORES):
            m = {k: g[k] for k in need if k != "xin"}
            m["xin"] = cur[c]
            in_maps.append(m)
        res = run_bass_kernel_spmd(nc, in_maps, core_ids=list(range(NCORES)))
        cur = [r["xout"] for r in res.results]
    out = np.empty((16, T, D), np.float32)
    for c in range(NCORES):
        out[2 * c:2 * c + 2] = cur[c].transpose(0, 2, 1)
    return out


# version 42 (final)
# speedup vs baseline: 1.1493x; 1.0093x over previous
import numpy as np
from contextlib import ExitStack
import concourse.bass as bass
import concourse.mybir as mybir
from concourse.bass_utils import run_bass_kernel_spmd

F32 = mybir.dt.float32
BF16 = mybir.dt.bfloat16
I32 = mybir.dt.int32
AF = mybir.ActivationFunctionType
ALU = mybir.AluOpType
AX = mybir.AxisListType

T = 2048
D = 1024
NCORES = 8
NSEQ = 2
DCH = D // 128
NT = T // 128
KBINS = 513
KPAD = 640
KCH = KPAD // 128
LN_EPS = 1e-5

ENGS = ("pe", "dve", "act", "pool", "sp")



class Tok:
    __slots__ = ("sem", "val", "snap")

    def __init__(self, sem, val, snap):
        self.sem, self.val, self.snap = sem, val, snap


_NO_AUTO = {"dma_start", "indirect_dma_start", "wait_ge", "nop", "sem_inc",
            "print", "dma_start_transpose"}


class EngCtx:

    _FIELDS = ("ctx", "name", "sem_name", "count", "clock", "recs", "auto",
               "self_waited", "uninced")

    def __init__(self, ctx, name, auto):
        self.ctx, self.name = ctx, name
        self.sem_name = "s_" + name
        self.count = 0
        self.clock = {}
        self.recs = []
        self.auto = auto
        self.self_waited = 0
        self.uninced = False

    def _rec(self, fn, auto_inced=False):
        entry = [fn, [], auto_inced]
        self.recs.append(entry)
        return entry

    def __getattr__(self, attr):
        if attr.startswith("_") or attr in EngCtx._FIELDS:
            raise AttributeError(attr)

        def wrap(*a, **k):
            if self.auto and attr not in _NO_AUTO and self.count > self.self_waited:
                sem, val = self.sem_name, self.count
                self._rec(lambda eng: eng.wait_ge(self.ctx.sems[sem], val))
                self.self_waited = val
            auto_inc = self.auto and attr not in _NO_AUTO
            e = self._rec(lambda eng: getattr(eng, attr)(*a, **k), auto_inc)
            if auto_inc:
                self.count += 1
                e[1].append((self.sem_name, 1))
                self.clock[self.sem_name] = self.count
            elif attr not in _NO_AUTO:
                self.uninced = True
            return None
        return wrap

    def wait(self, tok):
        if tok is None:
            return
        if tok.sem == self.sem_name:
            if tok.val <= self.self_waited:
                return
            sem, val = tok.sem, tok.val
            self._rec(lambda eng: eng.wait_ge(self.ctx.sems[sem], val))
            self.self_waited = val
            return
        if self.clock.get(tok.sem, 0) >= tok.val:
            return
        sem, val = tok.sem, tok.val
        self._rec(lambda eng: eng.wait_ge(self.ctx.sems[sem], val))
        for s, v in tok.snap.items():
            if s == self.sem_name:
                if v > self.self_waited:
                    self.self_waited = v
                continue
            if self.clock.get(s, 0) < v:
                self.clock[s] = v
        if self.clock.get(tok.sem, 0) < tok.val:
            self.clock[tok.sem] = tok.val

    def wait_all(self, toks):
        for t in toks:
            self.wait(t)

    def sig(self):
        assert self.recs, "sig() with no recorded instruction"
        if self.recs[-1][2] or any(s == self.sem_name for s, _ in self.recs[-1][1]):
            return Tok(self.sem_name, self.count, dict(self.clock))
        self.count += 1
        self.recs[-1][1].append((self.sem_name, 1))
        self.clock[self.sem_name] = self.count
        self.uninced = False
        return Tok(self.sem_name, self.count, dict(self.clock))

    def sig_dma(self, stream):
        c = self.ctx.dma_counts[stream] = self.ctx.dma_counts.get(stream, 0) + 16
        self.recs[-1][1].append((stream, 16))
        snap = dict(self.clock)
        snap[stream] = c
        return Tok(stream, c, snap)


DEBUG = False


class Ctx:
    def __init__(self, nc, stack):
        self.dbg = []
        self.dbg_names = {}
        self.nc = nc
        self.stack = stack
        self.sems = {}
        self.dma_counts = {}
        self.eng = {n: EngCtx(self, n, auto=(n in ("dve", "act", "pool")))
                    for n in ENGS}
        for n in ENGS:
            self.sems["s_" + n] = stack.enter_context(nc.semaphore("s_" + n))
        for s in ("dW", "dX", "dO", "dS"):
            self.sems[s] = stack.enter_context(nc.semaphore(s))
        self.pe = self.eng["pe"]
        self.dve = self.eng["dve"]
        self.act = self.eng["act"]
        self.pool = self.eng["pool"]
        self.sp = self.eng["sp"]
        self._sbuf_guards = []
        self._psum_guards = []
        self._tensors = {}
        self._phase_guards = []
        self._phase_ctr = 0
        self._phase_stack = []

    def sbuf(self, name, shape, dtype, side="left"):
        if name in self._tensors:
            return self._tensors[name]
        g = self.nc.sbuf_tensor(name, shape, dtype)
        h = g.__enter__()
        self._sbuf_guards.append(g)
        self._tensors[name] = h
        return h

    def psum(self, name, shape, dtype=F32):
        if name in self._tensors:
            return self._tensors[name]
        g = self.nc.psum_tensor(name, shape, dtype)
        h = g.__enter__()
        self._psum_guards.append(g)
        self._tensors[name] = h
        return h

    def barrier(self):
        toks = []
        for n in ENGS:
            e = self.eng[n]
            if e.uninced:
                toks.append(e.sig())
            elif e.count > 0:
                toks.append(Tok(e.sem_name, e.count, dict(e.clock)))
        for s, c in self.dma_counts.items():
            toks.append(Tok(s, c, {s: c}))
        for n in ENGS:
            self.eng[n].wait_all(toks)

    def phase_psum(self, name, shape, dtype=F32):
        self._phase_ctr += 1
        g = self.nc.psum_tensor(f"{name}_{self._phase_ctr}", shape, dtype)
        h = g.__enter__()
        self._phase_guards.append(g)
        return h

    def phase_sbuf(self, name, shape, dtype):
        self._phase_ctr += 1
        g = self.nc.sbuf_tensor(f"{name}_{self._phase_ctr}", shape, dtype)
        h = g.__enter__()
        self._phase_guards.append(g)
        return h

    def push_phase(self):
        self._phase_stack.append(len(self._phase_guards))

    def pop_phase(self):
        k = self._phase_stack.pop() if self._phase_stack else 0
        for g in reversed(self._phase_guards[k:]):
            g.__exit__(None, None, None)
        del self._phase_guards[k:]

    def end_phase(self):
        assert not self._phase_stack, "unbalanced push_phase"
        for g in reversed(self._phase_guards):
            g.__exit__(None, None, None)
        self._phase_guards = []

    def dump(self, name, ap, deps=()):
        if not DEBUG or name in self.dbg_names:
            return
        self.dbg_names[name] = True
        shape = list(ap.shape)
        dt = ap.dtype
        dst = self.nc.dram_tensor("dbg_" + name, shape, dt, kind="ExternalOutput").ap()
        self.sp.wait_all(list(deps))
        self.sp.dma_start(out=dst, in_=ap)
        self.sp.sig_dma("dO")

    def replay(self, block):
        raws = {"pe": block.tensor, "dve": block.vector, "act": block.scalar,
                "pool": block.gpsimd, "sp": block.sync}

        def make_body(name):
            def body(eng):
                for fn, incs, _auto in self.eng[name].recs:
                    inst = fn(eng)
                    for (sem, v) in incs:
                        assert inst is not None, f"then_inc on None inst ({name})"
                        inst.then_inc(self.sems[sem], v)
            return body
        for n in ENGS:
            if self.eng[n].recs:
                raws[n](make_body(n))




def host_constants():
    c = {}
    e = np.arange(D)[:, None]
    k = np.arange(KBINS)[None, :]
    ang = 2.0 * np.pi * ((e * k) % D).astype(np.float64) / D
    c["dftC"] = np.cos(ang).astype(np.float32)
    c["dftS"] = (-np.sin(ang)).astype(np.float32)
    w = np.full(KBINS, 2.0)
    w[0] = 1.0
    w[512] = 1.0
    ki = np.arange(KBINS)[:, None]
    ee = np.arange(D)[None, :]
    ang2 = 2.0 * np.pi * ((ki * ee) % D).astype(np.float64) / D
    cw = (w[:, None] * np.cos(ang2) / D).astype(np.float32)
    sw = (-w[:, None] * np.sin(ang2) / D).astype(np.float32)
    c["idftC"] = np.zeros((KPAD, D), np.float32)
    c["idftC"][:KBINS] = cw
    c["idftS"] = np.zeros((KPAD, D), np.float32)
    c["idftS"][:KBINS] = sw
    c["ident"] = np.eye(128, dtype=np.float32)
    tri = (np.arange(128)[:, None] <= np.arange(128)[None, :]).astype(np.float32)
    c["trimask"] = tri
    c["utri"] = tri.copy()
    c["ones128"] = np.ones((128, 128), np.float32)
    c["onesrow"] = np.ones((1, 128), np.float32)
    c["shamt"] = np.tile(np.array([0, 2, 4, 6], np.int32), 32)[:, None].copy()
    c["mask3"] = np.full((128, 1), 3, np.int32)
    bandmask = np.zeros((128, 16), np.float32)
    for h in range(16):
        bandmask[8 * h:8 * h + 6, h] = 1.0
    c["bandmask"] = bandmask
    tri16 = np.tile(tri, (1, 16))
    c["trimask16"] = np.ascontiguousarray(tri16)
    half0 = np.zeros((1, 128), np.float32)
    half0[0, :64] = 1.0
    half1 = np.zeros((1, 128), np.float32)
    half1[0, 64:] = 1.0
    c["half0"] = half0
    c["half1"] = half1
    c["ones16"] = np.ones((128, 16), np.float32)
    return c


def stage_inputs(inputs):
    g = {}
    g["l0a_wq"] = np.ascontiguousarray(
        np.concatenate([inputs["L0_attn_W1_q"], inputs["L0_attn_W2_q"]], 0).T)
    g["l0a_wk"] = np.ascontiguousarray(
        np.concatenate([inputs["L0_attn_W1_k"], inputs["L0_attn_W2_k"]], 0).T)
    g["l0a_wv"] = np.ascontiguousarray(np.asarray(inputs["L0_attn_Wv"]).T)
    g["l0a_bv"] = np.asarray(inputs["L0_attn_bv"]).reshape(1, D)
    g["l0a_wo"] = np.ascontiguousarray(np.asarray(inputs["L0_attn_Wo"]).T)
    g["l0a_bo"] = np.asarray(inputs["L0_attn_bo"]).reshape(1, D)
    g["l0f_w1"] = np.ascontiguousarray(np.concatenate(
        [np.asarray(inputs["L0_ffn_W1"]), np.asarray(inputs["L0_ffn_Ws1"])[None]], 0))
    g["l0f_w2"] = np.ascontiguousarray(np.concatenate(
        [np.asarray(inputs["L0_ffn_W2"]), np.asarray(inputs["L0_ffn_Ws2"])[None]], 0))
    g["l0f_wr"] = np.ascontiguousarray(np.asarray(inputs["L0_ffn_Wr"]))
    g["l1a_wq"] = np.ascontiguousarray(np.asarray(inputs["L1_attn_Wq"]))
    g["l1a_wk"] = np.ascontiguousarray(np.asarray(inputs["L1_attn_Wk"]))
    g["l1a_wv"] = np.ascontiguousarray(np.asarray(inputs["L1_attn_Wv"]))
    g["l1a_bq"] = np.ascontiguousarray(np.asarray(inputs["L1_attn_bq"]).reshape(DCH, 128).T)
    g["l1a_bk"] = np.ascontiguousarray(np.asarray(inputs["L1_attn_bk"]).reshape(DCH, 128).T)
    g["l1a_bv"] = np.ascontiguousarray(np.asarray(inputs["L1_attn_bv"]).reshape(DCH, 128).T)
    pk1 = np.asarray(inputs["L1_ffn_packed_weight"]).reshape(256, D)
    g["l1f_pk"] = np.ascontiguousarray(np.tile(pk1.T, (1, 4)))
    g["l1f_ws"] = np.asarray(inputs["L1_ffn_weight_scale"]).reshape(1, 1).astype(np.float32)
    g["l1f_bias"] = np.ascontiguousarray(
        np.asarray(inputs["L1_ffn_bias"]).reshape(DCH, 128).T)
    g.update(host_constants())
    x = np.asarray(inputs["L0_attn_x"])
    shards = [np.ascontiguousarray(x[2 * c:2 * c + 2].transpose(0, 2, 1))
              for c in range(NCORES)]
    return g, shards


BLOCK_INPUTS = {
    0: ("l0a_wq", "l0a_wk", "l0a_wv", "l0a_bv", "l0a_wo", "l0a_bo",
        "trimask16", "bandmask", "half0", "half1"),
    1: ("l0f_w1", "l0f_w2", "l0f_wr"),
    2: ("l1a_wq", "l1a_wk", "l1a_wv", "l1a_bq", "l1a_bk", "l1a_bv",
        "dftC", "dftS", "idftC", "idftS", "utri"),
    3: ("l1f_pk", "l1f_ws", "l1f_bias", "shamt", "mask3"),
}


def declare_dram(nc, g, launch_blocks):
    d = {}

    def inp(name):
        arr = g[name]
        dt = I32 if arr.dtype == np.int32 else F32
        d[name] = nc.dram_tensor(name, list(arr.shape), dt, kind="ExternalInput").ap()

    d["xin"] = nc.dram_tensor("xin", [NSEQ, D, T], F32, kind="ExternalInput").ap()
    d["xout"] = nc.dram_tensor("xout", [NSEQ, D, T], F32, kind="ExternalOutput").ap()
    for k in ("ident", "ones128", "onesrow"):
        inp(k)
    names = set()
    for blk in launch_blocks:
        names.update(BLOCK_INPUTS[blk])
    for k in sorted(names):
        inp(k)
    if 2 in launch_blocks:
        d["wtil"] = nc.dram_tensor("wtil", [6, D, KPAD], BF16, kind="Internal").ap()
        d["btil"] = nc.dram_tensor("btil", [1, 6 * KPAD], BF16, kind="Internal").ap()
    return d


class Bufs:
    pass


def alloc_persistent(ctx, launch_blocks=()):
    b = Bufs()
    b.x = ctx.sbuf("x_m", [128, DCH, T], F32)
    b.xln = ctx.sbuf("xln", [128, DCH, T + 2], BF16)
    b.ident = ctx.sbuf("c_ident", [128, 128], BF16)
    b.ones128 = ctx.sbuf("c_ones", [128, 128], BF16)
    b.onesrow = ctx.sbuf("c_onesrow", [1, 128], BF16)
    b.mrep = ctx.sbuf("mrep", [128, T], F32)
    b.rrep = ctx.sbuf("rrep", [128, T], F32)
    b.t1 = ctx.sbuf("t1rep", [128, T], F32)
    b.sq = ctx.sbuf("sqch", [128, 2, 512], BF16)
    b.s16 = ctx.sbuf("s16", [128, 2 * T], BF16)
    b.zbuf = ctx.sbuf("zbuf", [128, DCH, T], BF16)
    b.cpool = ctx.sbuf("cpool", [128, 8], F32)
    return b


CONST_VALS = (-1.0, LN_EPS, 1e-8)


def emit_load_const(ctx, dram, b):
    sp, pool = ctx.sp, ctx.pool
    toks = []
    for name, dst in (("ident", b.ident), ("ones128", b.ones128), ("onesrow", b.onesrow)):
        pool.dma_start(out=dst[:], in_=dram[name])
        toks.append(pool.sig_dma("dW"))
    for j, v in enumerate(CONST_VALS):
        pool.memset(b.cpool[:, j:j + 1], v)
        ctx.nc.const_aps.register(F32, v, b.cpool[:, j:j + 1])
    toks.append(pool.sig())
    return toks


def emit_load_x(ctx, dram, b, seq):
    sp = ctx.sp
    sp.dma_start(out=b.x[:], in_=dram["xin"][seq].rearrange("(c p) t -> p c t", p=128))
    return sp.sig_dma("dX")


def emit_store_x(ctx, dram, b, seq, deps):
    sp = ctx.sp
    sp.wait_all(deps)
    sp.dma_start(out=dram["xout"][seq].rearrange("(c p) t -> p c t", p=128), in_=b.x[:])
    return sp.sig_dma("dO")




def emit_stats_rows(ctx, b, src_get, psA, psB):
    pe, dve, act = ctx.pe, ctx.dve, ctx.act
    ev = [None, None]
    sq_read = None
    for tg in range(T // 512):
        sl = slice(512 * tg, 512 * tg + 512)
        sb = tg % 2
        pA = psA[:, sb, :]
        pB = psB[:, sb, :]
        for c in range(DCH):
            src, rdy = src_get(c)
            pe.wait(rdy)
            if c == 0 and ev[sb] is not None:
                pe.wait(ev[sb])
            pe.matmul(pA, b.ones128[:], src[:, sl],
                      start=(c == 0), stop=(c == DCH - 1))
            act.wait(rdy)
            slq = c % 2
            if sq_read is not None:
                act.wait(sq_read)
            act.activation(b.sq[:, slq, :], src[:, sl], AF.Square)
            sqt = act.sig()
            pe.wait(sqt)
            pe.matmul(pB, b.ones128[:], b.sq[:, slq, :],
                      start=(c == 0), stop=(c == DCH - 1))
            if c >= 1:
                sq_read = pe.sig()
        tokp = pe.sig()
        dve.wait(tokp)
        dve.tensor_single_scalar(b.mrep[:, sl], pA, 1.0 / D, ALU.mult)
        dve.tensor_single_scalar(b.t1[:, sl], pB, 1.0 / D, ALU.mult)
        ev[sb] = dve.sig()
    dve.tensor_tensor(b.rrep[:], b.mrep[:], b.mrep[:], ALU.mult)
    dve.tensor_tensor(b.t1[:], b.t1[:], b.rrep[:], ALU.subtract)
    tokv = dve.sig()
    act.wait(tokv)
    act.activation(b.t1[:], b.t1[:], AF.Ln, bias=LN_EPS)
    act.activation(b.rrep[:], b.t1[:], AF.Exp, scale=-0.5)
    last = act.sig()
    pe.wait(last)
    return last


def emit_preln(ctx, b, deps):
    pe, dve, act, pool = ctx.pe, ctx.dve, ctx.act, ctx.pool
    psA = ctx.phase_psum("lnA", [128, 2, 512], F32)
    psB = ctx.phase_psum("lnB", [128, 2, 512], F32)
    pool.wait_all(deps)
    pool.memset(b.xln[:, :, 0:2], 0.0)
    mz = pool.sig()
    ctoks = {}
    for c in range(DCH):
        act.wait_all(deps)
        act.copy(b.xln[:, c, 2:], b.x[:, c, :])
        ctoks[c] = act.sig()
    st = emit_stats_rows(ctx, b, lambda c: (b.xln[:, c, 2:], ctoks[c]), psA, psB)
    out = []
    for c in range(DCH):
        dve.wait(st)
        dve.wait_all(deps)
        dve.tensor_tensor(b.t1[:], b.x[:, c, :], b.mrep[:], ALU.subtract)
        dve.tensor_tensor(b.xln[:, c, 2:], b.t1[:], b.rrep[:], ALU.mult)
        out.append(dve.sig())
    out.append(mz)
    ctx.dump("xln", b.xln[:], out)
    ctx.dump("mrep", b.mrep[:], out)
    ctx.dump("rrep", b.rrep[:], out)
    return out


def _z_src_get(b, z_toks):
    if not isinstance(z_toks, (list, tuple)):
        z_toks = [z_toks]
    if len(z_toks) == DCH:
        return lambda c: (b.zbuf[:, c, :], z_toks[c])
    if len(z_toks) == 4 * DCH:
        return lambda c: (b.zbuf[:, c, :], z_toks[4 * c + 3])
    return lambda c: (b.zbuf[:, c, :], z_toks[-1])


def emit_transition(ctx, b, z_toks):
    pe, dve, act, pool = ctx.pe, ctx.dve, ctx.act, ctx.pool
    psA = ctx.phase_psum("trA", [128, 2, 512], F32)
    psB = ctx.phase_psum("trB", [128, 2, 512], F32)
    psC = ctx.phase_psum("trC", [128, 2, 512], F32)
    psD = ctx.phase_psum("trD", [128, 2, 512], F32)
    st = emit_stats_rows(ctx, b, _z_src_get(b, z_toks), psA, psB)
    pool.memset(b.xln[:, :, 0:2], 0.0)
    mz = pool.sig()
    ctoks = {}
    for c in range(DCH):
        dve.wait(st)
        dve.tensor_tensor(b.t1[:], b.zbuf[:, c, :], b.mrep[:], ALU.subtract)
        dve.tensor_tensor(b.t1[:], b.t1[:], b.rrep[:], ALU.mult)
        dve.tensor_tensor(b.x[:, c, :], b.x[:, c, :], b.t1[:], ALU.add)
        xt = dve.sig()
        act.wait(xt)
        act.copy(b.xln[:, c, 2:], b.x[:, c, :])
        ctoks[c] = act.sig()
    st2 = emit_stats_rows(ctx, b, lambda c: (b.xln[:, c, 2:], ctoks[c]), psC, psD)
    out = []
    for c in range(DCH):
        dve.wait(st2)
        dve.tensor_tensor(b.t1[:], b.x[:, c, :], b.mrep[:], ALU.subtract)
        dve.tensor_tensor(b.xln[:, c, 2:], b.t1[:], b.rrep[:], ALU.mult)
        out.append(dve.sig())
    out.append(mz)
    return out


def emit_postln_residual(ctx, b, z_toks):
    pe, dve, act, pool = ctx.pe, ctx.dve, ctx.act, ctx.pool
    psA = ctx.phase_psum("plA", [128, 2, 512], F32)
    psB = ctx.phase_psum("plB", [128, 2, 512], F32)
    st = emit_stats_rows(ctx, b, _z_src_get(b, z_toks), psA, psB)
    ctx.dump("pl_mrep", b.mrep[:], [st])
    ctx.dump("pl_rrep", b.rrep[:], [st])
    out = []
    for c in range(DCH):
        dve.wait(st)
        dve.tensor_tensor(b.t1[:], b.zbuf[:, c, :], b.mrep[:], ALU.subtract)
        dve.tensor_tensor(b.t1[:], b.t1[:], b.rrep[:], ALU.mult)
        dve.tensor_tensor(b.x[:, c, :], b.x[:, c, :], b.t1[:], ALU.add)
        out.append(dve.sig())
    ctx.dump("pl_x", b.x[:], out)
    return out




def emit_l1f_weights(ctx, dram, b):
    dve, act, sp = ctx.dve, ctx.act, ctx.sp
    b.l1f_wt = ctx.phase_sbuf("l1f_wt", [128, DCH, D], BF16)
    pk_s = ctx.phase_sbuf("l1f_pks", [128, 2, D], I32)
    sh_s = ctx.phase_sbuf("l1f_sh", [128, 1], I32)
    m3_s = ctx.phase_sbuf("l1f_m3", [128, 1], I32)
    b.l1f_ws = ctx.phase_sbuf("l1f_wss", [1, 1], F32)
    b.l1f_bias = ctx.phase_sbuf("l1f_bs", [128, DCH], F32)
    sp.dma_start(out=sh_s[:], in_=dram["shamt"])
    sp.sig_dma("dW")
    sp.dma_start(out=m3_s[:], in_=dram["mask3"])
    sp.sig_dma("dW")
    sp.dma_start(out=b.l1f_ws[:], in_=dram["l1f_ws"])
    sp.sig_dma("dW")
    sp.dma_start(out=b.l1f_bias[:], in_=dram["l1f_bias"])
    tl = sp.sig_dma("dW")
    pkv = dram["l1f_pk"].rearrange("(c p) o -> c p o", p=128)
    toks = []
    prev = {}
    for c in range(DCH):
        sl = c % 2
        if prev.get(sl) is not None:
            sp.wait(prev[sl])
        sp.dma_start(out=pk_s[:, sl, :], in_=pkv[c])
        tp = sp.sig_dma("dW")
        dve.wait(tp)
        dve.wait(tl)
        for s4 in range(1, 4):
            dve.tensor_single_scalar(pk_s[:, sl, 256 * s4:256 * (s4 + 1)],
                                     pk_s[:, sl, 256 * s4:256 * (s4 + 1)],
                                     2 * s4, ALU.logical_shift_right)
            dve.sig()
        dve.tensor_single_scalar(pk_s[:, sl, :], pk_s[:, sl, :], m3_s[:, 0:1],
                                 ALU.bitwise_and)
        tc = dve.sig()
        act.wait(tc)
        act.activation(b.l1f_wt[:, c, :], pk_s[:, sl, :], AF.Identity, bias=-1.0)
        ta = act.sig()
        prev[sl] = ta
        toks.append(ta)
    return toks


def emit_l1f(ctx, b, xln_toks, wt_toks):
    pe, dve, act, pool, sp = ctx.pe, ctx.dve, ctx.act, ctx.pool, ctx.sp
    xt = b.xln
    act.wait_all(xln_toks)
    act.activation(b.t1[:], xt[:, 0, 2:], AF.Abs)
    t_a = act.sig()
    dve.wait(t_a)
    for c in range(1, DCH):
        act.wait_all(xln_toks)
        act.activation(b.rrep[:], xt[:, c, 2:], AF.Abs)
        t_a = act.sig()
        dve.wait(t_a)
        dve.tensor_max(b.t1[:], b.t1[:], b.rrep[:])
        t_d = dve.sig()
        act.wait(t_d)
    fold_bf = b.s16[:, 0:T]
    dve.tensor_copy(fold_bf, b.t1[:])
    tfold = dve.sig()
    ps_t = ctx.phase_psum("l1f_pst", [128, 128], BF16)
    cols = ctx.phase_sbuf("l1f_cols", [128, NT], F32)
    colsb = ctx.phase_sbuf("l1f_colsb", [128, NT], BF16)
    rows = ctx.phase_sbuf("l1f_rows", [NT, 128], F32)
    tcol = None
    for j in range(NT):
        pe.wait(tfold)
        if tcol is not None:
            pe.wait(tcol)
        pe.transpose(ps_t[:], fold_bf[:, 128 * j:128 * j + 128], b.ident[:])
        tt = pe.sig()
        dve.wait(tt)
        dve.reduce_max(cols[:, j:j + 1], ps_t[:], AX.X)
        tcol = dve.sig()
    dve.tensor_copy(colsb[:], cols[:])
    tcb = dve.sig()
    ps_r = ctx.phase_psum("l1f_psr", [128, 128], BF16)
    pe.wait(tcb)
    pe.transpose(ps_r[0:NT, :], colsb[:], b.ident[:])
    tr = pe.sig()
    dve.wait(tr)
    dve.tensor_copy(rows[:], ps_r[0:NT, :])
    trows = dve.sig()
    amaxrow = b.t1[0:1, 0:T]
    asrow = b.rrep[0:1, 0:T]
    asrow_bf = b.s16[0:1, T:T + T]
    wsasrow_bf = b.s16[0:1, 0:T]
    sp.wait(trows)
    sp.dma_start(out=amaxrow, in_=rows[:])
    tdma = sp.sig_dma("dS")
    dve.wait(tdma)
    dve.tensor_single_scalar(amaxrow, amaxrow, 1e-5, ALU.max)
    dve.sig()
    dve.reciprocal(asrow, amaxrow)
    dve.sig()
    dve.tensor_single_scalar(asrow, asrow, 127.0, ALU.mult)
    dve.sig()
    dve.tensor_copy(asrow_bf, asrow)
    tas = dve.sig()
    dve.tensor_single_scalar(amaxrow, amaxrow, b.l1f_ws[0:1, 0:1], ALU.mult)
    dve.sig()
    dve.tensor_single_scalar(amaxrow, amaxrow, 1.0 / 127.0, ALU.mult)
    dve.sig()
    dve.tensor_copy(wsasrow_bf, amaxrow)
    twsas = dve.sig()
    ps_bc = ctx.phase_psum("l1f_psbc", [128, 512], F32)
    asrep = b.mrep
    ev = None
    for tg in range(T // 512):
        sl = slice(512 * tg, 512 * tg + 512)
        pe.wait(tas)
        if ev is not None:
            pe.wait(ev)
        pe.matmul(ps_bc[:], b.onesrow[:], asrow_bf[0:1, sl], start=True, stop=True)
        t_ = pe.sig()
        dve.wait(t_)
        dve.tensor_copy(asrep[:, sl], ps_bc[:])
        ev = dve.sig()
    ctx.dump("asrep", asrep[:], [ev])
    qi = ctx.phase_sbuf("l1f_qi", [128, T], I32)
    qtoks = []
    for c in range(DCH):
        dve.wait(ev)
        dve.wait_all(xln_toks)
        dve.tensor_tensor(b.t1[:], xt[:, c, 2:], asrep[:], ALU.mult)
        dve.sig()
        dve.tensor_scalar(b.t1[:], b.t1[:], -128.0, 127.0, ALU.max, ALU.min)
        dve.sig()
        dve.tensor_copy(qi[:], b.t1[:])
        tq = dve.sig()
        act.wait(tq)
        act.copy(xt[:, c, 2:], qi[:])
        qtoks.append(act.sig())
    ctx.dump("qbuf", xt[:], qtoks)
    wsasrep = b.s16[:, T:2 * T]
    ev2 = None
    for tg in range(T // 512):
        sl = slice(512 * tg, 512 * tg + 512)
        pe.wait(twsas)
        pe.wait(ev2 if ev2 is not None else ev)
        pe.matmul(ps_bc[:], b.onesrow[:], wsasrow_bf[0:1, sl], start=True, stop=True)
        t_ = pe.sig()
        dve.wait(t_)
        dve.tensor_copy(wsasrep[:, sl], ps_bc[:])
        ev2 = dve.sig()
    trep = ev2
    ps_z = [ctx.phase_psum("l1f_z0", [128, 512], F32), ctx.phase_psum("l1f_z1", [128, 512], F32)]
    ev_prev = [None, None]
    ztoks = []
    for m in range(DCH):
        for tg in range(T // 512):
            sl = slice(512 * tg, 512 * tg + 512)
            idx = (m * (T // 512) + tg) % 2
            pz = ps_z[idx]
            for c in range(DCH):
                pe.wait(qtoks[c])
                pe.wait(wt_toks[c])
                if c == 0 and ev_prev[idx] is not None:
                    pe.wait(ev_prev[idx])
                pe.matmul(pz[:], b.l1f_wt[:, c, 128 * m:128 * (m + 1)],
                          xt[:, c, 2 + 512 * tg:2 + 512 * (tg + 1)],
                          start=(c == 0), stop=(c == DCH - 1))
            tmm = pe.sig()
            dve.wait(tmm)
            dve.wait(trep)
            dve.tensor_tensor(b.t1[:, 0:512], pz[:], wsasrep[:, sl], ALU.mult)
            dve.sig()
            dve.tensor_scalar(b.zbuf[:, m, sl], b.t1[:, 0:512],
                              b.l1f_bias[:, m:m + 1], None, ALU.add)
            zt = dve.sig()
            ev_prev[idx] = zt
            ztoks.append(zt)
    ctx.dump("zbuf", b.zbuf[:], ztoks)
    ctx.dump("wsasrep", wsasrep[:], [trep])
    ctx.dump("l1fwt", b.l1f_wt[:], wt_toks)
    return ztoks




def emit_l0f(ctx, dram, b, xln_toks):
    pe, dve, act, pool, sp = ctx.pe, ctx.dve, ctx.act, ctx.pool, ctx.sp
    NE = 9
    b.mask_fm = ctx.phase_sbuf("l0f_maskfm", [8, T], BF16)
    ctx.push_phase()
    wr = ctx.phase_sbuf("l0f_wr", [128, DCH, 8], BF16)
    pool.dma_start(out=wr[:], in_=dram["l0f_wr"].rearrange("(c p) e -> p c e", p=128))
    twr = pool.sig_dma("dW")
    ps_r = ctx.phase_psum("l0f_psr", [8, 512], F32)
    rlog = ctx.phase_sbuf("l0f_rlog", [8, T], BF16)
    ev = None
    for tg in range(T // 512):
        sl2 = slice(2 + 512 * tg, 2 + 512 * (tg + 1))
        for c in range(DCH):
            pe.wait(twr)
            pe.wait_all(xln_toks)
            if c == 0 and ev is not None:
                pe.wait(ev)
            pe.matmul(ps_r[:], wr[:, c, :], b.xln[:, c, sl2],
                      start=(c == 0), stop=(c == DCH - 1))
        tm_ = pe.sig()
        act.wait(tm_)
        act.copy(rlog[:, 512 * tg:512 * (tg + 1)], ps_r[:])
        ev = act.sig()
    trlog = ev
    ps_g = ctx.phase_psum("l0f_psg", [128, 8], BF16)
    ps_gb = ctx.phase_psum("l0f_psgb", [8, 128], BF16)
    ltm = ctx.phase_sbuf("l0f_ltm", [128, 8], F32)
    g1 = ctx.phase_sbuf("l0f_g1", [128, 8], F32)
    g2 = ctx.phase_sbuf("l0f_g2", [128, 8], F32)
    gc = ctx.phase_sbuf("l0f_gc", [128, 12], F32)
    mask_tm = ctx.phase_sbuf("l0f_masktm", [128, 8], BF16)
    mask_fm = b.mask_fm
    tmask = None
    tp_g = None
    tp_gb = None
    for j in range(NT):
        pe.wait(trlog)
        if tp_g is not None:
            pe.wait(tp_g)
        pe.transpose(ps_g[:], rlog[:, 128 * j:128 * (j + 1)], b.ident[0:8, 0:8])
        tt = pe.sig()
        act.wait(tt)
        act.copy(ltm[:], ps_g[:])
        tp_g = act.sig()
        dve.wait(tp_g)
        dve.reduce_max(gc[:, 0:1], ltm[:], AX.X)
        dve.tensor_single_scalar(g1[:], ltm[:], gc[:, 0:1], ALU.is_ge)
        dve.reduce_sum(gc[:, 1:2], g1[:], AX.X)
        dve.scalar_tensor_tensor(g2[:], g1[:], -1e30, ltm[:], ALU.mult, ALU.add)
        dve.reduce_max(gc[:, 2:3], g2[:], AX.X)
        dve.tensor_single_scalar(gc[:, 3:4], gc[:, 1:2], 1.5, ALU.is_gt)
        dve.tensor_tensor(gc[:, 4:5], gc[:, 0:1], gc[:, 2:3], ALU.subtract)
        dve.tensor_tensor(gc[:, 5:6], gc[:, 3:4], gc[:, 4:5], ALU.mult)
        dve.tensor_tensor(gc[:, 6:7], gc[:, 2:3], gc[:, 5:6], ALU.add)
        dve.tensor_tensor(gc[:, 7:8], gc[:, 6:7], gc[:, 0:1], ALU.subtract)
        td = dve.sig()
        act.wait(td)
        act.activation(gc[:, 8:9], gc[:, 7:8], AF.Exp)
        te = act.sig()
        dve.wait(te)
        dve.tensor_single_scalar(gc[:, 9:10], gc[:, 8:9], 1.0, ALU.add)
        dve.tensor_single_scalar(gc[:, 10:11], gc[:, 9:10], 1.0, ALU.is_equal)
        dve.tensor_single_scalar(mask_tm[:], g1[:], gc[:, 10:11], ALU.mult)
        tmtm = dve.sig()
        pe.wait(tmtm)
        if tp_gb is not None:
            pe.wait(tp_gb)
        pe.transpose(ps_gb[0:8, :], mask_tm[:], b.ident[:])
        tb = pe.sig()
        act.wait(tb)
        act.copy(mask_fm[0:8, 128 * j:128 * (j + 1)], ps_gb[0:8, :])
        tp_gb = act.sig()
        tmask = tp_gb
    ctx.barrier()
    ctx.pop_phase()
    w1b = ctx.phase_sbuf("l0f_w1b", [128, 2, DCH, 512], BF16)
    w2b = ctx.phase_sbuf("l0f_w2b", [128, 2, 4, D], BF16)
    rowbuf = ctx.phase_sbuf("l0f_rowb", [1, T], BF16)
    maskrep = ctx.phase_sbuf("l0f_mrep", [128, T], BF16)
    hraw = b.s16[:, 0:T]
    hm = b.s16[:, T:2 * T]
    _h01 = b.t1[:].bitcast(BF16)
    _h23 = b.mrep[:].bitcast(BF16)
    hsq = lambda fc, sl=slice(0, T): (_h01 if fc < 2 else _h23)[
        :, T * (fc % 2) + (sl.start or 0):T * (fc % 2) + (sl.stop if sl.stop is not None else T)]
    ps_h = ctx.phase_psum("l0f_psh", [128, T], F32)
    ps_z = ctx.phase_psum("l0f_psz", [128, T], F32)
    w_used = [None, None]
    tmr_used = None
    tmr = None
    th_half = [None, None]
    tz_half = [None, None]
    hsq_read = [None] * 4
    ztoks = []
    for e in range(NE):
        slot = e % 2
        pool.wait(w_used[slot])
        pool.dma_start(out=w1b[:, slot], in_=dram["l0f_w1"][e].rearrange(
            "(c p) f -> p c f", p=128))
        pool.sig_dma("dW")
        pool.dma_start(out=w2b[:, slot], in_=dram["l0f_w2"][e].rearrange(
            "(c p) o -> p c o", p=128))
        tw = pool.sig_dma("dW")
        if e < 8:
            sp.wait(tmask)
            sp.wait(tmr)
            sp.dma_start(out=rowbuf[:], in_=mask_fm[e:e + 1, :])
            trow = sp.sig_dma("dS")
            pe.wait(trow)
            pe.wait(th_half[0])
            pe.wait(th_half[1])
            if tmr is not None:
                pe.wait(tmr)
            for tg in range(T // 512):
                sl = slice(512 * tg, 512 * (tg + 1))
                pe.matmul(ps_h[:, sl], b.onesrow[:], rowbuf[0:1, sl],
                          start=True, stop=True)
            tb_ = pe.sig()
            dve.wait(tb_)
            dve.wait(tmr_used)
            for tg in range(T // 512):
                sl = slice(512 * tg, 512 * (tg + 1))
                dve.tensor_copy(maskrep[:, sl], ps_h[:, sl])
            tmr = dve.sig()
        hsq_toks = []
        for fc in range(4):
            for half in range(2):
                for d in range(DCH):
                    pe.wait(tw)
                    pe.wait_all(xln_toks)
                    if d == 0:
                        pe.wait(th_half[half])
                        if fc == 0:
                            pe.wait(tmr)
                    lh = w1b[:, slot, d, 128 * fc:128 * (fc + 1)]
                    for tg in (2 * half, 2 * half + 1):
                        pe.matmul(ps_h[:, 512 * tg:512 * (tg + 1)], lh,
                                  b.xln[:, d, 2 + 512 * tg:2 + 512 * (tg + 1)],
                                  start=(d == 0), stop=(d == DCH - 1))
                tmm = pe.sig()
                hsl = slice(1024 * half, 1024 * (half + 1))
                dve.wait(tmm)
                dve.wait(hsq_read[fc])
                dve.tensor_single_scalar(hraw[:, hsl], ps_h[:, hsl], 0.0,
                                         ALU.max)
                th_half[half] = dve.sig()
                if e < 8:
                    dve.tensor_tensor(hm[:, hsl], hraw[:, hsl],
                                      maskrep[:, hsl], ALU.mult)
                    dve.tensor_tensor(hsq(fc, hsl), hm[:, hsl], hraw[:, hsl],
                                      ALU.mult)
                else:
                    dve.tensor_tensor(hsq(fc, hsl), hraw[:, hsl], hraw[:, hsl],
                                      ALU.mult)
            hsq_toks.append(dve.sig())
        tmr_used = hsq_toks[-1]
        if e == 8:
            pass
        for m in range(DCH):
            for half in range(2):
                for fc in range(4):
                    pe.wait(hsq_toks[fc])
                    if fc == 0:
                        pe.wait(tz_half[half])
                    lh = w2b[:, slot, fc, 128 * m:128 * (m + 1)]
                    for tg in (2 * half, 2 * half + 1):
                        pe.matmul(ps_z[:, 512 * tg:512 * (tg + 1)], lh,
                                  hsq(fc, slice(512 * tg, 512 * (tg + 1))),
                                  start=(fc == 0), stop=(fc == 3))
                tmm = pe.sig()
                hsl = slice(1024 * half, 1024 * (half + 1))
                dve.wait(tmm)
                if e == 0:
                    dve.tensor_copy(b.zbuf[:, m, hsl], ps_z[:, hsl])
                else:
                    dve.tensor_tensor(b.zbuf[:, m, hsl], b.zbuf[:, m, hsl],
                                      ps_z[:, hsl], ALU.add)
                tz_half[half] = dve.sig()
            if e == NE - 1:
                ztoks.append(tz_half[1])
        for fc in range(4):
            hsq_read[fc] = tz_half[1]
        w_used[slot] = tz_half[1]
    ctx.dump("moe_maskfm", mask_fm[:], [tmask])
    ctx.dump("moe_z", b.zbuf[:], ztoks)
    return ztoks



PAIRS = ((0, 1), (0, 2), (0, 3), (1, 2), (1, 3), (2, 3))
JSIGN = (1, -1, 1, 1, -1, 1)
ATT_SCALE = float(np.float32(6.0 ** (-0.5)))


def emit_l0a(ctx, dram, b, xln_toks):
    pe, dve, act, pool, sp = ctx.pe, ctx.dve, ctx.act, ctx.pool, ctx.sp
    bandm = ctx.phase_sbuf("l0a_band", [128, 16], F32)
    tri16 = ctx.phase_sbuf("l0a_tri16", [128, T], BF16)
    half01 = ctx.phase_sbuf("l0a_half", [1, 256], BF16)
    sp.dma_start(out=bandm[:], in_=dram["bandmask"])
    sp.sig_dma("dW")
    pool.dma_start(out=tri16[:], in_=dram["trimask16"])
    pool.sig_dma("dW")
    pool.dma_start(out=half01[0:1, 0:128], in_=dram["half0"])
    pool.sig_dma("dW")
    pool.dma_start(out=half01[0:1, 128:256], in_=dram["half1"])
    tw0 = pool.sig_dma("dW")
    vtm = b.zbuf[:].rearrange("p c t -> p (c t)").rearrange("p (j o) -> p j o", o=D)
    qspec = ctx.phase_sbuf("l0a_qspec", [128, T], BF16)
    kspec = ctx.phase_sbuf("l0a_kspec", [128, T], BF16)
    ctx.push_phase()
    wq = ctx.phase_sbuf("l0a_wqb", [128, 16, 128], BF16)
    wk = ctx.phase_sbuf("l0a_wkb", [128, DCH, 128], BF16)
    wv = ctx.phase_sbuf("l0a_wvb", [128, DCH, D], BF16)
    bvr = ctx.phase_sbuf("l0a_bvr", [1, D], BF16)
    pool.dma_start(out=wq[:], in_=dram["l0a_wq"].rearrange("(c p) o -> p c o", p=128))
    pool.sig_dma("dW")
    pool.dma_start(out=wk[:], in_=dram["l0a_wk"].rearrange("(c p) o -> p c o", p=128))
    pool.sig_dma("dW")
    pool.dma_start(out=wv[:], in_=dram["l0a_wv"].rearrange("(c p) o -> p c o", p=128))
    pool.sig_dma("dW")
    pool.dma_start(out=bvr[:], in_=dram["l0a_bv"])
    tw = pool.sig_dma("dW")
    q12f = ctx.phase_sbuf("l0a_q12f", [128, T], BF16)
    k12f = ctx.phase_sbuf("l0a_k12f", [128, T], BF16)
    _rbf = b.rrep[:].bitcast(BF16)
    ps_p = ctx.phase_psum("l0a_psp", [128, 2, 512], F32)
    ps_t = ctx.phase_psum("l0a_pst", [128, 128], BF16)
    ps_v = ctx.phase_psum("l0a_psv", [128, 2, 512], F32)
    tvev = {}
    for j in range(NT):
        tsl = slice(2 + 128 * j, 2 + 128 * (j + 1))
        for half in range(2):
            pv = ps_v[:, half, :]
            osl = slice(512 * half, 512 * (half + 1))
            for dc in range(DCH):
                pe.wait(tw)
                pe.wait_all(xln_toks)
                if dc == 0 and j >= 1:
                    pe.wait(tvev[(j - 1, half)])
                pe.matmul(pv[:], b.xln[:, dc, tsl], wv[:, dc, osl],
                          start=(dc == 0), stop=False)
            pe.matmul(pv[:], b.onesrow[:, 0:128], bvr[0:1, osl],
                      start=False, stop=True)
            tmm = pe.sig()
            act.wait(tmm)
            act.copy(vtm[:, j, osl], pv[:])
            tvev[(j, half)] = act.sig()
    tqf = None
    for tg in range(T // 512):
        sl = slice(512 * tg, 512 * (tg + 1))
        for src, wgt, nch, off in ((q12f, wq, 16, None), (k12f, wk, DCH, None)):
            pp = ps_p[:, 0 if src is q12f else 1, :]
            for c in range(nch):
                pe.wait(tw)
                pe.wait_all(xln_toks)
                if c == 0 and tqf is not None:
                    pe.wait(tqf)
                if src is q12f and c >= DCH:
                    rhs = b.xln[:, c - DCH, 1 + 512 * tg:1 + 512 * (tg + 1)]
                else:
                    rhs = b.xln[:, c, 2 + 512 * tg:2 + 512 * (tg + 1)]
                pe.matmul(pp[:], wgt[:, c, :], rhs, start=(c == 0), stop=(c == nch - 1))
            tmm = pe.sig()
            act.wait(tmm)
            act.copy(src[:, sl], pp[:])
            tqf = act.sig()
    q12tmF = _rbf[:, 0:T]
    k12tmF = _rbf[:, T:2 * T]
    _tbf = b.t1[:].bitcast(BF16)
    qlinF = _tbf[:, 0:T]
    klinF = _tbf[:, T:2 * T]
    pool.memset(qlinF, 0.0)
    pool.memset(klinF, 0.0)
    tlz = pool.sig()
    ttm = None
    for j in range(NT):
        for which, (f_, dstF) in enumerate(((q12f, q12tmF), (k12f, k12tmF))):
            pe.wait(tqf)
            if ttm is not None:
                pe.wait(ttm)
            pe.transpose(ps_t[:], f_[:, 128 * j:128 * (j + 1)], b.ident[:])
            tt = pe.sig()
            act.wait(tt)
            act.copy(dstF[:, 128 * j:128 * (j + 1)], ps_t[:])
            ttm = act.sig()
    qa = q12tmF.rearrange("p (j x) -> p j x", x=128)[:, :, 0:64]\
        .rearrange("p j (h i) -> p j h i", i=4)
    qb2 = q12tmF.rearrange("p (j x) -> p j x", x=128)[:, :, 64:128]\
        .rearrange("p j (h i) -> p j h i", i=4)
    ka = k12tmF.rearrange("p (j x) -> p j x", x=128)[:, :, 0:64]\
        .rearrange("p j (h i) -> p j h i", i=4)
    kb2 = k12tmF.rearrange("p (j x) -> p j x", x=128)[:, :, 64:128]\
        .rearrange("p j (h i) -> p j h i", i=4)
    lvq = qlinF.rearrange("p (j x) -> p j x", x=128)\
        .rearrange("p j (h c) -> p j h c", c=8)
    lvk = klinF.rearrange("p (j x) -> p j x", x=128)\
        .rearrange("p j (h c) -> p j h c", c=8)
    t_a = b.mrep[:, 0:256].rearrange("p (j h u) -> p j h u", h=16, u=1)
    t_b = b.mrep[:, 256:512].rearrange("p (j h u) -> p j h u", h=16, u=1)
    dve.wait(ttm)
    dve.wait(tlz)
    for which in (0, 1):
        a1, a2, lv = (qa, qb2, lvq) if which == 0 else (ka, kb2, lvk)
        for c6 in range(6):
            if which == 0:
                i_, j_ = PAIRS[c6]
                neg = False
            else:
                i_, j_ = PAIRS[5 - c6]
                neg = (JSIGN[c6] < 0)
            dve.tensor_tensor(t_a, a1[:, :, :, i_:i_ + 1], a2[:, :, :, j_:j_ + 1], ALU.mult)
            dve.tensor_tensor(t_b, a1[:, :, :, j_:j_ + 1], a2[:, :, :, i_:i_ + 1], ALU.mult)
            if neg:
                dve.tensor_tensor(lv[:, :, :, c6:c6 + 1], t_b, t_a, ALU.subtract)
            else:
                dve.tensor_tensor(lv[:, :, :, c6:c6 + 1], t_a, t_b, ALU.subtract)
    tpl = dve.sig()
    ncol = b.rrep
    for which in (0, 1):
        linF = qlinF if which == 0 else klinF
        lv = lvq if which == 0 else lvk
        dve.wait(tpl)
        dve.tensor_tensor(b.mrep[:], linF, linF, ALU.mult)
        dve.tensor_reduce(ncol[:, 0:256].rearrange("p (x u) -> p x u", u=1),
                          b.mrep[:].rearrange("p (x c) -> p x c", c=8), AX.X, ALU.add)
        tn_ = dve.sig()
        act.wait(tn_)
        act.activation(ncol[:, 256:512], ncol[:, 0:256], AF.Sqrt)
        tsq2 = act.sig()
        dve.wait(tsq2)
        dve.tensor_single_scalar(ncol[:, 256:512], ncol[:, 256:512], 1e-12, ALU.max)
        dve.reciprocal(ncol[:, 512:768], ncol[:, 256:512])
        dve.tensor_tensor(lv, lv, ncol[:, 512:768]
                          .rearrange("p (j h u) -> p j h u", h=16, u=1)
                          .broadcast_to([128, NT, 16, 8]), ALU.mult)
        tnorm = dve.sig()
    tspec = None
    for j in range(NT):
        for which in (0, 1):
            linF = qlinF if which == 0 else klinF
            pe.wait(tnorm)
            if tspec is not None:
                pe.wait(tspec)
            pe.transpose(ps_t[:], linF[:, 128 * j:128 * (j + 1)], b.ident[:])
            tt2 = pe.sig()
            act.wait(tt2)
            act.copy((qspec if which == 0 else kspec)[:, 128 * j:128 * (j + 1)], ps_t[:])
            tspec = act.sig()
    ctx.barrier()
    ctx.pop_phase()
    ctx.push_phase()
    psb = [b.s16[:, 0:T], b.s16[:, T:2 * T]]
    qb = ctx.phase_sbuf("l0a_qb", [128, T], BF16)
    brd_all = ctx.phase_sbuf("l0a_brdall", [128, 2, 512], BF16)
    zrow = ctx.phase_sbuf("l0a_zrow", [1, 512], BF16)
    pool.memset(zrow[:], 0.0)
    tzr = pool.sig()
    rrow = b.t1[0:1, 0:T]
    rrowb = ctx.phase_sbuf("l0a_rrowb", [1, T], BF16)
    ps_l = ctx.phase_psum("l0a_psl", [128, 2, 512], F32)
    ps_oa = ctx.phase_psum("l0a_psoa", [128, 4, 128], F32)
    ps_ob = ctx.phase_psum("l0a_psob", [128, 4, 128], F32)
    ps_s = ctx.phase_psum("l0a_pss", [128, T], F32)
    oview = [b.xln[:, p, 2:2 + T] for p in range(DCH)]

    def oview_a(q):
        return b.xln[:, 0:4, 2 + 128 * q:2 + 128 * (q + 1)]

    def oview_b(q):
        return b.xln[:, 4:8, 2 + 128 * q:2 + 128 * (q + 1)]
    tqb = None
    texp = [None, None]
    tpread = [None, None]
    pslw = [None, None]
    eread = [None, None]
    ebuf = b.mrep[:].bitcast(BF16)
    tsum_ev = None
    to_ev = None
    tbrd_ev = None
    ztoks_oready = []
    for q in range(NT):
        dve.wait(tspec)
        dve.wait(tqb)
        for h in range(16):
            dve.tensor_single_scalar(qb[:, 128 * h:128 * (h + 1)],
                                     qspec[:, 128 * q:128 * (q + 1)],
                                     bandm[:, h:h + 1], ALU.mult)
        tqbv = dve.sig()
        pe.wait(tzr)
        pe.wait(to_ev)
        pe.matmul(ps_oa[:].rearrange("p a b -> p (a b)"), b.onesrow[:, 0:128],
                  zrow[0:1, :], start=True, stop=True)
        pe.matmul(ps_ob[:].rearrange("p a b -> p (a b)"), b.onesrow[:, 0:128],
                  zrow[0:1, :], start=True, stop=True)

        def emit_sums_attnv(k):
            pb2 = k % 2
            pe.wait(texp[pb2])
            if k == 0:
                pe.wait(tsum_ev)
            for sg in range(T // 512):
                pe.matmul(ps_s[:, 512 * sg:512 * (sg + 1)], b.ones128[:],
                          psb[pb2][:, 512 * sg:512 * (sg + 1)],
                          start=(k == 0), stop=(k == q))
            for p in range(DCH):
                po = (ps_oa if p < 4 else ps_ob)[:, p % 4, :]
                pe.matmul(po[0:64, :], vtm[:, k, 128 * p:128 * p + 64],
                          psb[pb2][:, 256 * p:256 * p + 128],
                          start=False, stop=(k == q), tile_position=(0, 0))
                pe.matmul(po[64:128, :], vtm[:, k, 128 * p + 64:128 * (p + 1)],
                          psb[pb2][:, 256 * p + 128:256 * (p + 1)],
                          start=False, stop=(k == q), tile_position=(0, 64))
            tpread[pb2] = pe.sig()

        for k in range(q + 1):
            pb = k % 2
            for ch in range(4):
                sl_ = ch % 2
                pe.wait(tqbv)
                pe.wait(tspec)
                pe.wait(pslw[sl_])
                pe.matmul(ps_l[:, sl_, :],
                          kspec[:, 128 * k:128 * (k + 1)],
                          qb[:, 512 * ch:512 * (ch + 1)],
                          start=True, stop=True)
                tmm = pe.sig()
                dve.wait(tmm)
                dve.wait(eread[sl_])
                dve.tensor_copy(ebuf[:, 512 * sl_:512 * (sl_ + 1)], ps_l[:, sl_, :])
                pslw[sl_] = dve.sig()
                act.wait(pslw[sl_])
                act.wait(tpread[pb])
                act.activation(psb[pb][:, 512 * ch:512 * (ch + 1)],
                               ebuf[:, 512 * sl_:512 * (sl_ + 1)],
                               AF.Exp, scale=ATT_SCALE)
                eread[sl_] = act.sig()
            texp[pb] = eread[1]
            if k == q:
                dve.wait(texp[pb])
                dve.tensor_tensor(psb[pb][:], psb[pb][:], tri16[:], ALU.mult)
                texp[pb] = dve.sig()
            if q == 3 and k == 1:
                ctx.dump("l0a_P31", psb[pb][:, :], [texp[pb]])
            if k >= 1:
                emit_sums_attnv(k - 1)
        emit_sums_attnv(q)
        tka = tpread[q % 2]
        dve.wait(tka)
        dve.tensor_copy(rrow[0:1, :], ps_s[0:1, :])
        tsr = dve.sig()
        act.wait(tsr)
        act.activation(rrow[0:1, :], rrow[0:1, :], AF.Ln)
        act.activation(rrowb[0:1, :], rrow[0:1, :], AF.Exp, scale=-1.0)
        trr = act.sig()
        if q == 3:
            ctx.dump("l0a_r3", rrowb[0:1, :], [trr])
        tsum_ev = trr
        rview = rrowb[0:1, :].rearrange("u (a b q) -> u a b q", a=8, b=2)
        for half in range(2):
            pe.wait(trr)
            pe.wait(pslw[half])
            pe.wait(tbrd_ev)
            pe.matmul(ps_l[:, half, :], half01[0:1, 0:128],
                      rview[:, 4 * half:4 * (half + 1), 0, :], start=True, stop=True)
            pe.matmul(ps_l[:, half, :], half01[0:1, 128:256],
                      rview[:, 4 * half:4 * (half + 1), 1, :], start=False, stop=True)
        tbb = pe.sig()
        dve.wait(tbb)
        dve.wait(to_ev)
        dve.tensor_copy(brd_all[:], ps_l[:])
        tbrd_ev = dve.sig()
        pslw = [tbrd_ev, tbrd_ev]
        dve.wait(tbrd_ev)
        dve.wait(tka)
        dve.tensor_tensor(oview_a(q), ps_oa[:],
                          brd_all[:, 0, :].rearrange("p (a q) -> p a q", a=4), ALU.mult)
        dve.tensor_tensor(oview_b(q), ps_ob[:],
                          brd_all[:, 1, :].rearrange("p (a q) -> p a q", a=4), ALU.mult)
        to_ev = dve.sig()
        ztoks_oready.append(to_ev)
    ctx.dump("l0a_o", b.xln[:], ztoks_oready)
    ctx.barrier()
    ctx.pop_phase()
    ctx.push_phase()
    wo = ctx.phase_sbuf("l0a_wob", [128, DCH, D], BF16)
    bor = ctx.phase_sbuf("l0a_bor", [1, D], BF16)
    pool.dma_start(out=wo[:], in_=dram["l0a_wo"].rearrange("(c p) o -> p c o", p=128))
    pool.sig_dma("dW")
    pool.dma_start(out=bor[:], in_=dram["l0a_bo"])
    two = pool.sig_dma("dW")
    ones512 = ctx.phase_sbuf("l0a_ones512", [1, 512], BF16)
    pool.memset(ones512[:], 1.0)
    ton = pool.sig()
    ps_z = ctx.phase_psum("l0a_psz", [128, T], F32)
    ztoks = []
    tz_half = [None, None]
    for m in range(DCH):
        for half in range(2):
            for p in range(DCH):
                pe.wait(two)
                pe.wait_all(ztoks_oready)
                if p == 0:
                    pe.wait(tz_half[half])
                lh = wo[:, p, 128 * m:128 * (m + 1)]
                for tg in (2 * half, 2 * half + 1):
                    pe.matmul(ps_z[:, 512 * tg:512 * (tg + 1)], lh,
                              oview[p][:, 512 * tg:512 * (tg + 1)],
                              start=(p == 0), stop=False)
            for tg in (2 * half, 2 * half + 1):
                pe.wait(ton)
                pe.matmul(ps_z[:, 512 * tg:512 * (tg + 1)],
                          bor[0:1, 128 * m:128 * (m + 1)], ones512[0:1, :],
                          start=False, stop=True)
            tmm = pe.sig()
            hsl = slice(1024 * half, 1024 * (half + 1))
            dve.wait(tmm)
            dve.tensor_copy(b.zbuf[:, m, hsl], ps_z[:, hsl])
            tz_half[half] = dve.sig()
        ztoks.append(tz_half[1])
    ctx.barrier()
    ctx.pop_phase()
    ctx.dump("l0a_z", b.zbuf[:], ztoks)
    return ztoks



L1A_HALVES = ((0, 1, 2), (3, 4))


def emit_l1a_fold(ctx, dram, b):
    pe, dve, act, pool, sp = ctx.pe, ctx.dve, ctx.act, ctx.pool, ctx.sp
    ctx.push_phase()
    b.bias_tilde = ctx.phase_sbuf("l1a_btil", [1, 6 * KPAD], BF16)
    wxb = ctx.phase_sbuf("l1a_wxb", [128, DCH, D], BF16)
    cb = ctx.phase_sbuf("l1a_cb", [128, DCH, KBINS], BF16)
    bcol = ctx.phase_sbuf("l1a_bcol", [128, DCH], BF16)
    stg = ctx.phase_sbuf("l1a_stg", [128, KPAD], BF16)
    ps_f = ctx.phase_psum("l1a_psf", [128, KBINS], F32)
    ps_b = ctx.phase_psum("l1a_psb", [1, KBINS], F32)
    pool.memset(stg[:], 0.0)
    tz = pool.sig()
    tprev_dma = None
    tstg_free = tz
    for wi, (wname, bname, plane) in enumerate((
            ("l1a_wq", "l1a_bq", "dftC"), ("l1a_wq", "l1a_bq", "dftS"),
            ("l1a_wk", "l1a_bk", "dftC"), ("l1a_wk", "l1a_bk", "dftS"),
            ("l1a_wv", "l1a_bv", "dftC"), ("l1a_wv", "l1a_bv", "dftS"))):
        pool.wait(tprev_dma)
        pool.dma_start(out=wxb[:], in_=dram[wname].rearrange("(c p) d -> p c d", p=128))
        pool.sig_dma("dW")
        pool.dma_start(out=cb[:], in_=dram[plane].rearrange("(c p) k -> p c k", p=128))
        pool.sig_dma("dW")
        pool.dma_start(out=bcol[:], in_=dram[bname])
        tld = pool.sig_dma("dW")
        tev = None
        for dc in range(DCH):
            for ec in range(DCH):
                pe.wait(tld)
                if ec == 0 and tev is not None:
                    pe.wait(tev)
                pe.matmul(ps_f[:, 0:512], wxb[:, ec, 128 * dc:128 * (dc + 1)],
                          cb[:, ec, 0:512], start=(ec == 0), stop=(ec == DCH - 1))
                pe.matmul(ps_f[:, 512:513], wxb[:, ec, 128 * dc:128 * (dc + 1)],
                          cb[:, ec, 512:513], start=(ec == 0), stop=(ec == DCH - 1))
            tmm = pe.sig()
            act.wait(tmm)
            act.wait(tstg_free)
            act.copy(stg[:, 0:KBINS], ps_f[:])
            tev = act.sig()
            sp.wait(tev)
            sp.dma_start(out=dram["wtil"][wi, 128 * dc:128 * (dc + 1), :], in_=stg[:])
            tstg_free = sp.sig_dma("dS")
        tevb = None
        for ec in range(DCH):
            pe.wait(tld)
            pe.matmul(ps_b[:, 0:512], bcol[:, ec:ec + 1], cb[:, ec, 0:512],
                      start=(ec == 0), stop=(ec == DCH - 1))
            pe.matmul(ps_b[:, 512:513], bcol[:, ec:ec + 1], cb[:, ec, 512:513],
                      start=(ec == 0), stop=(ec == DCH - 1))
        tmm = pe.sig()
        dve.wait(tmm)
        dve.memset(b.bias_tilde[0:1, KPAD * wi:KPAD * (wi + 1)], 0.0)
        dve.sig()
        dve.tensor_copy(b.bias_tilde[0:1, KPAD * wi:KPAD * wi + KBINS], ps_b[:])
        tbt = dve.sig()
        pe.wait(tbt)
        tprev_dma = Tok(tld.sem, tld.val, tld.snap)
        tprev_dma = tev
    ctx.dump("l1a_btil", b.bias_tilde[:], [tbt])
    sp.wait(tbt)
    sp.dma_start(out=dram["btil"], in_=b.bias_tilde[:])
    tst = sp.sig_dma("dS")
    for e_ in ENGS:
        ctx.eng[e_].wait(tst)
    ctx.pop_phase()
    return None


def _complex_norm(ctx, b, re_src, im_src, re_dst, im_dst, width, deps):
    dve, act = ctx.dve, ctx.act
    n2 = b.t1[:, 0:width]
    t2 = b.mrep[:, 0:width]
    dve.wait_all(deps)
    dve.tensor_tensor(n2, re_src, re_src, ALU.mult)
    dve.tensor_tensor(t2, im_src, im_src, ALU.mult)
    dve.tensor_tensor(n2, n2, t2, ALU.add)
    tn = dve.sig()
    act.wait(tn)
    dve.tensor_single_scalar(n2, n2, 1e-30, ALU.max)
    tm_ = dve.sig()
    act.wait(tm_)
    act.activation(n2, n2, AF.Ln)
    act.activation(n2, n2, AF.Exp, scale=0.5)
    act.activation(n2, n2, AF.Ln, bias=1e-8)
    act.activation(t2, n2, AF.Exp, scale=-1.0)
    tc_ = act.sig()
    dve.wait(tc_)
    dve.tensor_tensor(re_dst, re_src, t2, ALU.mult)
    dve.sig()
    dve.tensor_tensor(im_dst, im_src, t2, ALU.mult)
    return dve.sig()


L1A_PARTS = ((0, 1), (2, 3), (4,))


def emit_l1a(ctx, dram, b, xln_toks, seq):
    pe, dve, act, pool, sp = ctx.pe, ctx.dve, ctx.act, ctx.pool, ctx.sp
    onesT = ctx.phase_sbuf("l1a_onesT", [1, 512], BF16)
    utri = ctx.phase_sbuf("l1a_utri", [128, 128], BF16)
    carry = ctx.phase_sbuf("l1a_carry", [128, 12], F32)
    pool.memset(onesT[:], 1.0)
    tone = pool.sig()
    pool.dma_start(out=utri[:], in_=dram["utri"])
    tutri = pool.sig_dma("dW")
    zfirst = [True]
    ztoks = []
    for part_i, kcs in enumerate(L1A_PARTS):
        nk = len(kcs)
        W = 128 * nk
        ko = 128 * kcs[0]
        ctx.push_phase()
        btlp = ctx.phase_sbuf("l1a_btlp", [1, 6 * W], BF16)
        sp.dma_start(
            out=btlp[:].rearrange("u (w k) -> u w k", k=W),
            in_=dram["btil"].rearrange("u (w k) -> u w k", k=KPAD)[:, :, ko:ko + W])
        tbtl = sp.sig_dma("dW")
        xk = ctx.phase_sbuf("l1a_xk", [128, NT, 2 * W], BF16)
        _qre = b.rrep[:].bitcast(BF16).rearrange("p (a t) -> p a t", t=T)[:, 0:nk, :]
        _qim = ctx.phase_sbuf("l1a_qhi", [128, nk, T], BF16)

        def qh(ci):
            return _qre[:, ci, :] if ci < nk else _qim[:, ci - nk, :]
        ctx.push_phase()
        wkb = ctx.phase_sbuf("l1a_wkb", [128, DCH, 4, W], BF16)
        for j, wi in enumerate((2, 3, 4, 5)):
            sp.dma_start(out=wkb[:, :, j, :], in_=dram["wtil"][wi, :, ko:ko + W]
                         .rearrange("(c p) k -> p c k", p=128))
            twk = sp.sig_dma("dW")

        dve.wait(tbtl)
        dve.engine_nop()
        tbc = tbtl
        ps4d = [ctx.phase_psum("l1a_ps4a", [128, 4 * W], F32),
                ctx.phase_psum("l1a_ps4b", [128, 4 * W], F32)]
        kvtd = [b.sq[:].rearrange("p a w -> p (a w)")[:, 0:4 * W]
                .rearrange("p (a w) -> p a w", a=4),
                b.t1[:].bitcast(BF16)[:, 1024:1024 + 4 * W]
                .rearrange("p (a w) -> p a w", a=4)]
        ps_free = [None, None]
        kvt_free = [None, None]
        tbind_prev = None
        for j in range(NT):
            sl2 = j % 2
            ps4 = ps4d[sl2]
            kvt = kvtd[sl2]
            tsl = slice(2 + 128 * j, 2 + 128 * (j + 1))
            for dc in range(DCH):
                pe.wait(twk)
                pe.wait_all(xln_toks)
                pe.wait(tone)
                pe.wait(tbc)
                if dc == 0:
                    pe.wait(ps_free[sl2])
                wr_ = wkb[:, dc, :, :].rearrange("p a w -> p (a w)")
                for sg in range(0, 4 * W, 512):
                    pe.matmul(ps4[:, sg:sg + 512], b.xln[:, dc, tsl],
                              wr_[:, sg:sg + 512], start=(dc == 0), stop=False)
            for sg in range(0, 4 * W, 512):
                pe.matmul(ps4[:, sg:sg + 512], onesT[0:1, 0:128],
                          btlp[0:1, 2 * W + sg:2 * W + sg + 512],
                          start=False, stop=True)
            tmm = pe.sig()
            act.wait(tmm)
            act.wait(kvt_free[sl2])
            act.copy(kvt[:], ps4[:])
            ps_free[sl2] = act.sig()
            kv4 = kvt[:].rearrange("p (a b) w -> p a b w", b=2)
            res_ = kv4[:, :, 0, :]
            ims_ = kv4[:, :, 1, :]
            n2 = b.t1[:, 0:2 * W].rearrange("p (a w) -> p a w", a=2)
            t2 = b.mrep[:, 0:2 * W].rearrange("p (a w) -> p a w", a=2)
            dve.wait(ps_free[sl2])
            dve.tensor_tensor(n2, res_, res_, ALU.mult)
            dve.tensor_tensor(t2, ims_, ims_, ALU.mult)
            dve.tensor_tensor(n2, n2, t2, ALU.add)
            dve.tensor_single_scalar(n2, n2, 1e-30, ALU.max)
            tn_ = dve.sig()
            act.wait(tn_)
            act.activation(n2, n2, AF.Ln)
            act.activation(n2, n2, AF.Exp, scale=0.5)
            act.activation(n2, n2, AF.Ln, bias=1e-8)
            act.activation(t2, n2, AF.Exp, scale=-1.0)
            tc_ = act.sig()
            dve.wait(tc_)
            dve.tensor_tensor(res_, res_, t2, ALU.mult)
            dve.tensor_tensor(ims_, ims_, t2, ALU.mult)
            dve.sig()
            ta = b.s16[:, 0:W]
            tb_ = b.s16[:, W:2 * W]
            dve.tensor_tensor(ta, kvt[:, 0, :], kvt[:, 2, :], ALU.mult)
            dve.tensor_tensor(tb_, kvt[:, 1, :], kvt[:, 3, :], ALU.mult)
            dve.tensor_tensor(xk[:, j, 0:W], ta, tb_, ALU.subtract)
            dve.tensor_tensor(ta, kvt[:, 0, :], kvt[:, 3, :], ALU.mult)
            dve.tensor_tensor(tb_, kvt[:, 1, :], kvt[:, 2, :], ALU.mult)
            dve.tensor_tensor(xk[:, j, W:2 * W], ta, tb_, ALU.add)
            tbind_prev = dve.sig()
            kvt_free[sl2] = tbind_prev
        ctx.barrier()
        ctx.pop_phase()
        ctx.push_phase()
        wqb = ctx.phase_sbuf("l1a_wqb", [128, 2, DCH, W], BF16)
        for j, wi in enumerate((0, 1)):
            sp.dma_start(out=wqb[:, j], in_=dram["wtil"][wi, :, ko:ko + W]
                         .rearrange("(c p) k -> p c k", p=128))
            twq = sp.sig_dma("dW")
        ps_q = ctx.phase_psum("l1a_psq", [128, T], F32)
        tq_free = None
        for ci in range(2 * nk):
            pl, kci = (0, ci) if ci < nk else (1, ci - nk)
            for dc in range(DCH):
                pe.wait(twq)
                pe.wait_all(xln_toks)
                if dc == 0:
                    pe.wait(tq_free)
                lh = wqb[:, pl, dc, 128 * kci:128 * (kci + 1)]
                for tg in range(T // 512):
                    pe.matmul(ps_q[:, 512 * tg:512 * (tg + 1)], lh,
                              b.xln[:, dc, 2 + 512 * tg:2 + 512 * (tg + 1)],
                              start=(dc == 0), stop=False)
            pe.wait(tbtl)
            seg = btlp[0:1, W * pl + 128 * kci:W * pl + 128 * (kci + 1)]
            for tg in range(T // 512):
                pe.matmul(ps_q[:, 512 * tg:512 * (tg + 1)], seg,
                          onesT[0:1, 0:512], start=False, stop=True)
            tmm = pe.sig()
            act.wait(tmm)
            act.copy(qh(ci), ps_q[:])
            tq_free = act.sig()
        tqn = []
        for kci in range(nk):
            tn = _complex_norm(ctx, b, qh(kci), qh(nk + kci),
                               qh(kci), qh(nk + kci), T, [tq_free])
            tqn.append(tn)
        ctx.barrier()
        ctx.pop_phase()
        ctx.push_phase()
        ps_c = [ctx.phase_psum(f"l1a_psc{i}", [128, 128], F32) for i in range(4)]
        ps_ts = ctx.phase_psum("l1a_psts", [128, 2 * nk, NT], F32)
        tsum = ctx.phase_sbuf("l1a_tsum", [128, 2 * nk, NT + 1], F32)
        onecol = ctx.phase_sbuf("l1a_onecol", [128, 1], BF16)
        pool.memset(onecol[:], 1.0)
        toc = pool.sig()
        pe.wait_all(tqn)
        pe.wait(toc)
        for ci in range(2 * nk):
            for j in range(NT):
                pe.matmul(ps_ts[:, ci, j:j + 1], xk[:, j, 128 * ci:128 * (ci + 1)],
                          onecol[:], start=True, stop=True)
        tts = pe.sig()
        tsum2 = ctx.phase_sbuf("l1a_tsum2", [128, 2 * nk, NT + 1], F32)
        dve.wait(tts)
        dve.memset(tsum[:, :, 0:1], 0.0)
        dve.memset(tsum2[:, :, 0:1], 0.0)
        dve.tensor_copy(tsum[:, :, 1:NT + 1], ps_ts[:, :, :])
        dve.sig()
        bufs = (tsum, tsum2)
        for si, sh in enumerate((1, 2, 4, 8)):
            a_, b2_ = bufs[si % 2], bufs[(si + 1) % 2]
            dve.tensor_copy(b2_[:, :, 1:sh + 1], a_[:, :, 1:sh + 1])
            dve.tensor_tensor(b2_[:, :, sh + 1:NT + 1], a_[:, :, sh + 1:NT + 1],
                              a_[:, :, 1:NT + 1 - sh], ALU.add)
            dve.sig()
        tsum = bufs[0]
        tpfx = dve.sig()
        tev4 = [None] * 4
        for ci in range(2 * nk):
            for j in range(NT):
                sl4 = (ci * NT + j) % 4
                pe.wait(tutri)
                pe.wait(tev4[sl4])
                pe.matmul(ps_c[sl4][:], xk[:, j, 128 * ci:128 * (ci + 1)], utri[:],
                          start=True, stop=True)
                tmm = pe.sig()
                dve.wait(tmm)
                dve.wait(tpfx)
                dve.tensor_single_scalar(xk[:, j, 128 * ci:128 * (ci + 1)],
                                         ps_c[sl4][:], tsum[:, ci, j:j + 1], ALU.add)
                tev4[sl4] = dve.sig()
        ctx.barrier()
        ctx.pop_phase()
        ctx.push_phase()
        cwb = ctx.phase_sbuf("l1a_cwb", [128, 2, nk, D], BF16)
        for j, nm in enumerate(("idftC", "idftS")):
            pool.dma_start(out=cwb[:, j], in_=dram[nm][ko:ko + W, :]
                           .rearrange("(c p) e -> p c e", p=128))
            tcw = pool.sig_dma("dW")
        ps_o = ctx.phase_psum("l1a_pso", [128, T], F32)
        gtmp = ctx.phase_sbuf("l1a_gtmp", [128, 2, T], BF16)
        gr = b.s16[:, 0:T]
        gi = b.s16[:, T:2 * T]
        tz_half = [None, None]
        for kci in range(nk):
            sre = xk[:, :, 128 * kci:128 * (kci + 1)]
            sim = xk[:, :, 128 * (nk + kci):128 * (nk + kci + 1)]
            qre = qh(kci).rearrange("p (j q) -> p j q", q=128)
            qim = qh(nk + kci).rearrange("p (j q) -> p j q", q=128)
            grv = gr.rearrange("p (j q) -> p j q", q=128)
            giv = gi.rearrange("p (j q) -> p j q", q=128)
            ta = gtmp[:, 0, :].rearrange("p (j q) -> p j q", q=128)
            tb_ = gtmp[:, 1, :].rearrange("p (j q) -> p j q", q=128)
            dve.wait(tz_half[1])
            dve.tensor_tensor(ta, sre, qre, ALU.mult)
            dve.tensor_tensor(tb_, sim, qim, ALU.mult)
            dve.tensor_tensor(grv, ta, tb_, ALU.add)
            dve.tensor_tensor(ta, sim, qre, ALU.mult)
            dve.tensor_tensor(tb_, sre, qim, ALU.mult)
            dve.tensor_tensor(giv, ta, tb_, ALU.subtract)
            tg_ = dve.sig()
            for m in range(DCH):
                for half in range(2):
                    for pl, g_ in ((0, gr), (1, gi)):
                        for tg2 in (2 * half, 2 * half + 1):
                            pe.wait(tcw)
                            pe.wait(tg_)
                            if pl == 0 and tg2 == 2 * half:
                                pe.wait(tz_half[half])
                            pe.matmul(ps_o[:, 512 * tg2:512 * (tg2 + 1)],
                                      cwb[:, pl, kci, 128 * m:128 * (m + 1)],
                                      g_[:, 512 * tg2:512 * (tg2 + 1)],
                                      start=(pl == 0), stop=(pl == 1))
                    tmm = pe.sig()
                    hsl = slice(1024 * half, 1024 * (half + 1))
                    dve.wait(tmm)
                    if zfirst[0]:
                        dve.tensor_copy(b.zbuf[:, m, hsl], ps_o[:, hsl])
                    else:
                        dve.tensor_tensor(b.zbuf[:, m, hsl], b.zbuf[:, m, hsl],
                                          ps_o[:, hsl], ALU.add)
                    tz_half[half] = dve.sig()
                if part_i == len(L1A_PARTS) - 1 and kci == nk - 1:
                    ztoks.append(tz_half[1])
            zfirst[0] = False
        ctx.barrier()
        ctx.pop_phase()
        ctx.pop_phase()
    ctx.dump("l1a_z", b.zbuf[:], ztoks)
    return ztoks


def emit_launch(ctx, dram, launch_blocks):
    b = alloc_persistent(ctx, launch_blocks)
    ctoks = emit_load_const(ctx, dram, b)
    for e in ENGS:
        ctx.eng[e].wait_all(ctoks)
    if 2 in launch_blocks:
        emit_l1a_fold(ctx, dram, b)
        ctx.barrier()
        ctx.end_phase()
    for seq in range(NSEQ):
        xtok = emit_load_x(ctx, dram, b, seq)
        ctx.barrier()
        ln_toks = emit_preln(ctx, b, [xtok])
        ctx.barrier()
        ctx.end_phase()
        for i, blk in enumerate(launch_blocks):
            if blk == 0:
                ztoks = emit_l0a(ctx, dram, b, ln_toks)
            elif blk == 1:
                ztoks = emit_l0f(ctx, dram, b, ln_toks)
            elif blk == 2:
                ztoks = emit_l1a(ctx, dram, b, ln_toks, seq)
            elif blk == 3:
                wt_toks = emit_l1f_weights(ctx, dram, b)
                ztoks = emit_l1f(ctx, b, ln_toks, wt_toks)
            else:
                raise NotImplementedError(f"block {blk}")
            ctx.barrier()
            ctx.end_phase()
            if i < len(launch_blocks) - 1:
                ln_toks = emit_transition(ctx, b, ztoks)
            else:
                xdeps = emit_postln_residual(ctx, b, ztoks)
            ctx.barrier()
            ctx.end_phase()
        st = emit_store_x(ctx, dram, b, seq, xdeps)
        for e in ENGS:
            ctx.eng[e].wait(st)
    ctx.sp.wait(Tok("dO", ctx.dma_counts["dO"], {}))


def build_launch(g, launch_blocks):
    nc = bass.Bass("TRN2", target_bir_lowering=False)
    dram = declare_dram(nc, g, launch_blocks)
    stack = ExitStack()
    with stack:
        ctx = Ctx(nc, stack)
        emit_launch(ctx, dram, launch_blocks)
        block = stack.enter_context(nc.Block())
        ctx.replay(block)
    return nc


LAUNCH_PLAN = [[0, 1, 2, 3]]


def kernel(**inputs):
    g, shards = stage_inputs(inputs)
    cur = shards
    for launch_blocks in LAUNCH_PLAN:
        nc = build_launch(g, launch_blocks)
        in_maps = []
        need = {"xin"}
        need.update(("ident", "ones128", "onesrow"))
        for blk in launch_blocks:
            need.update(BLOCK_INPUTS[blk])
        for c in range(NCORES):
            m = {k: g[k] for k in need if k != "xin"}
            m["xin"] = cur[c]
            in_maps.append(m)
        res = run_bass_kernel_spmd(nc, in_maps, core_ids=list(range(NCORES)))
        cur = [r["xout"] for r in res.results]
    out = np.empty((16, T, D), np.float32)
    for c in range(NCORES):
        out[2 * c:2 * c + 2] = cur[c].transpose(0, 2, 1)
    return out
